# version 11
# baseline (speedup 1.0000x reference)
import numpy as np
import ml_dtypes
from contextlib import ExitStack

import concourse.bass as bass
import concourse.mybir as mybir
from concourse.bass_utils import run_bass_kernel_spmd
from concourse.bass_core_rust import IndirectOffsetOnAxis

F32 = mybir.dt.float32
BF16 = mybir.dt.bfloat16
I32 = mybir.dt.int32
AF = mybir.ActivationFunctionType
ALU = mybir.AluOpType
BFNP = ml_dtypes.bfloat16

D = 2048
KD = D // 128
S = 4096
TOWN = 2048
NCORES = 8
PI = float(np.pi)
PAIRS = [[0, 1], [2, 3], [4, 5], [6, 7]]

ENGS = ("sync", "tensor", "vector", "scalar", "gpsimd")
_ESEM = {"sync": "e_sy", "tensor": "e_t", "vector": "e_v", "scalar": "e_s",
         "gpsimd": "e_g"}


class _Op:
    __slots__ = ("eng", "emit", "waits", "sem", "by", "marked", "val", "tag")

    def __init__(self, eng, emit, waits, sem, by, tag=""):
        self.eng, self.emit, self.waits = eng, emit, waits
        self.sem, self.by = sem, by
        self.marked = by == 16
        self.val = None
        self.tag = tag


def _flat(waits):
    out = []
    for w in waits:
        if w is None:
            continue
        if isinstance(w, (list, tuple)):
            out.extend(_flat(w))
        else:
            out.append(w)
    return out


class Prog:
    NQ = 8

    def __init__(self, nc):
        self.nc = nc
        self.ops = {e: [] for e in ENGS}
        self.sem_names = set()
        self.qidx = {"q_s": 0, "q_g": 0, "q_c": 0}
        self._phase_fr = None
        self._phase_seen = set()

    def phase_break(self):
        fr = []
        for e in ENGS:
            seen = set()
            for op in reversed(self.ops[e]):
                if op.sem not in seen:
                    seen.add(op.sem)
                    fr.append(op)
        self._phase_fr = fr
        self._phase_seen = set()

    def _add(self, eng, emit, waits, sem, by, tag=""):
        ws = _flat(waits)
        if self._phase_fr and eng not in self._phase_seen:
            self._phase_seen.add(eng)
            ws = ws + self._phase_fr
        for w in ws:
            w.marked = True
        op = _Op(eng, emit, ws, sem, by, tag)
        self.ops[eng].append(op)
        self.sem_names.add(sem)
        return op

    def t(self, emit, waits=(), tag=""):
        return self._add("tensor", emit, waits, "e_t", 1, tag)

    def v(self, emit, waits=(), tag=""):
        return self._add("vector", emit, waits, "e_v", 1, tag)

    def s(self, emit, waits=(), tag=""):
        return self._add("scalar", emit, waits, "e_s", 1, tag)

    def g(self, emit, waits=(), tag=""):
        return self._add("gpsimd", emit, waits, "e_g", 1, tag)

    def _qsem(self, q):
        i = self.qidx[q]
        self.qidx[q] = i + 1
        return f"{q}{i % self.NQ}"

    def dma(self, out, in_, waits=(), tag="", slow=False):
        if slow:
            emit = (lambda e, o=out, i=in_: e.dma_start(
                out=o, in_=i, allow_slow_non_contiguous=True))
        else:
            emit = (lambda e, o=out, i=in_: e.dma_start(out=o, in_=i))
        return self._add("sync", emit, waits, self._qsem("q_s"), 16, tag)

    def dmag(self, out, in_, waits=(), tag=""):
        return self._add("gpsimd", (lambda e, o=out, i=in_: e.dma_start(out=o, in_=i)),
                         waits, self._qsem("q_g"), 16, tag)

    def dmai(self, out, in_2d, idx_col, waits=(), tag=""):
        return self._add(
            "gpsimd",
            (lambda e, o=out, i=in_2d, x=idx_col: e.indirect_dma_start(
                o, None, i, IndirectOffsetOnAxis(x, 0))),
            waits, self._qsem("q_g"), 16, tag)

    def coll(self, kind, op, groups, ins, outs, waits=(), tag=""):
        def emit(e, kind=kind, op=op, groups=groups, ins=ins, outs=outs):
            return e.collective_compute(kind, op, replica_groups=groups,
                                        ins=ins, outs=outs)
        o = self._add("gpsimd", emit, waits, self._qsem("q_c"), 1, tag)
        o.marked = True
        return o

    SERIAL = ("vector", "scalar", "gpsimd")

    def finalize(self, ctx):
        nc = self.nc
        for e in self.SERIAL:
            for op in self.ops[e]:
                if not (op.by == 16 or op.sem == "q_c"):
                    op.marked = True
            self.sem_names.add(_ESEM[e] if e != "gpsimd" else "e_g")
        handles = {n: ctx.enter_context(nc.semaphore(n))
                   for n in sorted(self.sem_names)}
        counts = {n: 0 for n in self.sem_names}
        self._auto = {}
        for eng in ENGS:
            for op in self.ops[eng]:
                if op.marked:
                    if op.sem[0] == "q":
                        self._auto[id(op)] = (op.sem, counts[op.sem])
                    counts[op.sem] += op.by
                    op.val = counts[op.sem]
        block = ctx.enter_context(nc.Block())

        def mkbody(engname, tail):
            ops = self.ops[engname]
            serial = engname in self.SERIAL

            def body(eng):
                satisfied = {}
                prev_compute_val = 0
                for op in ops:
                    if serial and prev_compute_val and op.by == 1:
                        sem = _ESEM[engname]
                        if satisfied.get(sem, 0) < prev_compute_val:
                            eng.wait_ge(handles[sem], prev_compute_val)
                            satisfied[sem] = prev_compute_val
                    aw = self._auto.get(id(op))
                    if aw is not None and aw[1] > 0 and \
                            satisfied.get(aw[0], 0) < aw[1]:
                        eng.wait_ge(handles[aw[0]], aw[1])
                        satisfied[aw[0]] = aw[1]
                    for w in op.waits:
                        if w.eng == op.eng and w.by == 1 and w.sem == _ESEM[op.eng]:
                            continue
                        assert w.val is not None, (w.tag, "unmarked dep")
                        if satisfied.get(w.sem, 0) < w.val:
                            eng.wait_ge(handles[w.sem], w.val)
                            satisfied[w.sem] = w.val
                    inst = op.emit(eng)
                    if op.marked:
                        inst.then_inc(handles[op.sem], op.by)
                        if op.by == 1 and op.sem == _ESEM[engname]:
                            prev_compute_val = op.val
                if tail:
                    for qn in sorted(counts):
                        if qn[0] == "q" and counts[qn] > 0 and \
                                satisfied.get(qn, 0) < counts[qn]:
                            eng.wait_ge(handles[qn], counts[qn])
            return body

        for engname in ENGS:
            getattr(block, engname)(mkbody(engname, engname == "gpsimd"))



def cols_of(h):
    return np.r_[0:1024, 3072:4096] if h == 0 else np.r_[1024:3072]


def to_bf(x):
    return np.ascontiguousarray(np.asarray(x, dtype=np.float32).astype(BFNP))


def tile_lhsT(w):
    din, dout = w.shape
    nk, nm = din // 128, dout // 128
    img = np.ascontiguousarray(
        w.reshape(nk, 128, nm, 128).transpose(2, 1, 0, 3).reshape(nm, 128, nk * 128))
    return to_bf(img)



class LnBufs:

    def __init__(self, nc, ctx, name, ps_st, ps_bc):
        def sb(n, shape, dtype=F32):
            return ctx.enter_context(nc.sbuf_tensor(name + n, shape, dtype))
        self.xst = [sb(f"xst{i}", [128, 512]) for i in range(2)]
        self.sqbf = [sb(f"sqbf{i}", [128, 512], BF16) for i in range(2)]
        self.srow = sb("srow", [1, 1024])
        self.srow_bf = sb("srow_bf", [1, 1024], BF16)
        self.cb1 = [sb(f"cb1_{i}", [128, 512], BF16) for i in range(2)]
        self.cb0 = [sb(f"cb0_{i}", [128, 512], BF16) for i in range(2)]
        self.fscr = sb("fscr", [128, 512])
        self.ones_colb = sb("ones_colb", [128, 1], BF16)
        self.ones_rowb = sb("ones_rowb", [1, 128], BF16)
        self.cns = sb("cns", [1, 1])
        self.ps_st = ps_st
        self.ps_bc = ps_bc
        self.init = None
        self.last = {}

    def emit_init(self, p, eps=1e-5):
        m1 = p.v(lambda e: e.memset(self.ones_colb[:, :], 1.0))
        m2 = p.v(lambda e: e.memset(self.ones_rowb[:, :], 1.0))
        m3 = p.v(lambda e: e.memset(self.cns[:, :], eps))
        self.init = [m1, m2, m3]


def emit_ln_block(p, lb, x_dram, col0, arena, acol0, eps=1e-5, pitch=1024,
                  boff=0, load_fn=None):
    L = lb.last
    stat_toks = []
    streaders = {}
    cast_toks = [None] * KD
    for k in range(KD):
        cts = []
        for sub in range(2):
            si = sub
            if load_fn is not None:
                ld = load_fn(lb.xst[si][:, :], k, (col0 + 512 * sub) // 512,
                             [L.get(("xst", si))])
            else:
                ld = p.dma(lb.xst[si][:, :],
                           x_dram[128 * k:128 * (k + 1),
                                  col0 + 512 * sub:col0 + 512 * (sub + 1)],
                           [L.get(("xst", si))], tag="lnld")
            sq = p.s(lambda e, si=si: e.activation(
                lb.sqbf[si][:, :], lb.xst[si][:, :], AF.Square),
                [ld, L.get(("sq", si))], tag="lnsq")
            av = arena[:, acol0 + pitch * k + boff + 512 * sub:
                       acol0 + pitch * k + boff + 512 * (sub + 1)]
            cast = p.v(lambda e, av=av, si=si: e.tensor_copy(av, lb.xst[si][:, :]),
                       [ld, L.get(("ar", k))], tag="lncast")
            cts.append(cast)
            L[("xst", si)] = [sq, cast]
            m1 = p.t(lambda e, av=av, sub=sub, k=k: e.matmul(
                lb.ps_st[sub][0:1, :], lb.ones_colb[:, :], av,
                start=(k == 0), stop=(k == KD - 1)),
                [cast, lb.init] + ([L.get(("st", sub))] if k == 0 else []),
                tag="st1")
            m2 = p.t(lambda e, si=si, sub=sub, k=k: e.matmul(
                lb.ps_st[2 + sub][0:1, :], lb.ones_colb[:, :],
                lb.sqbf[si][:, :],
                start=(k == 0), stop=(k == KD - 1)),
                [sq, lb.init] + ([L.get(("st", 2 + sub))] if k == 0 else []),
                tag="st2")
            L[("sq", si)] = [m2]
            stat_toks += [m1, m2]
        cast_toks[k] = cts
    R = lb.srow
    bc_toks = []
    for sub in range(2):
        sm = stat_toks
        r0, r1, r2, r3 = (512 * 0, 512 * 0 + 512), None, None, None
        m_ = p.v(lambda e, sub=sub: e.tensor_scalar_mul(
            R[0:1, 0:512], lb.ps_st[sub][0:1, :], 1.0 / D),
            [sm, L.get("rows")], tag="lnrows")
        ms_ = p.v(lambda e: e.tensor_mul(R[0:1, 512:1024], R[0:1, 0:512],
                                         R[0:1, 0:512]), [m_])
        var_ = p.v(lambda e, sub=sub: e.scalar_tensor_tensor(
            R[0:1, 512:1024], lb.ps_st[2 + sub][0:1, :], 1.0 / D,
            R[0:1, 512:1024], ALU.mult, ALU.subtract), [ms_, sm])
        lnv = p.s(lambda e: e.activation(R[0:1, 512:1024],
                                         R[0:1, 512:1024], AF.Ln,
                                         bias=lb.cns[0:1, 0:1]),
                  [var_, lb.init])
        rstd = p.s(lambda e: e.activation(R[0:1, 512:1024], R[0:1, 512:1024],
                                          AF.Exp, scale=-0.5), [lnv])
        c1c = p.v(lambda e: e.tensor_copy(lb.srow_bf[0:1, 0:512],
                                          R[0:1, 512:1024]),
                  [rstd, L.get("srowbf")])
        c0c = p.v(lambda e: e.scalar_tensor_tensor(
            lb.srow_bf[0:1, 512:1024], R[0:1, 0:512], -1.0, R[0:1, 512:1024],
            ALU.mult, ALU.mult), [rstd, m_, L.get("srowbf")])
        bm1 = p.t(lambda e: e.matmul(lb.ps_bc[:, 0:512], lb.ones_rowb[0:1, :],
                                     lb.srow_bf[0:1, 0:512], start=True,
                                     stop=True), [c1c, lb.init, L.get("psbc")])
        bc1 = p.v(lambda e, sub=sub: e.tensor_copy(lb.cb1[sub][:, :],
                                                   lb.ps_bc[:, 0:512]),
                  [bm1, L.get(("cb1", sub))])
        bm2 = p.t(lambda e: e.matmul(lb.ps_bc[:, 0:512], lb.ones_rowb[0:1, :],
                                     lb.srow_bf[0:1, 512:1024], start=True,
                                     stop=True), [c0c, bc1])
        bc0 = p.v(lambda e, sub=sub: e.tensor_copy(lb.cb0[sub][:, :],
                                                   lb.ps_bc[:, 0:512]),
                  [bm2, L.get(("cb0", sub))])
        L["psbc"] = bc0
        L["srowbf"] = [bm1, bm2]
        L[("st", sub)] = [m_]
        L[("st", 2 + sub)] = [var_]
        streaders[sub] = [m_]
        streaders[2 + sub] = [var_]
        L["rows"] = [var_, c0c, rstd, c1c]
        bc_toks.append((bc1, bc0))
    chunk_toks = [None] * KD
    for k in range(KD):
        av = arena[:, acol0 + pitch * k + boff: acol0 + pitch * k + boff + 1024]
        subtoks = []
        for sub in range(2):
            bc1, bc0 = bc_toks[sub]
            avs = av[:, 512 * sub:512 * (sub + 1)]
            fs = lb.fscr[:, 0:512]
            t1 = p.v(lambda e, avs=avs, fs=fs, sub=sub: e.tensor_tensor(
                fs, avs, lb.cb1[sub][:, :], ALU.mult),
                [cast_toks[k], bc1, L.get("fs")], tag="lnap1")
            t2 = p.v(lambda e, avs=avs, fs=fs, sub=sub: e.tensor_tensor(
                avs, fs, lb.cb0[sub][:, :], ALU.add), [t1, bc0], tag="lnap2")
            L["fs"] = t2
            subtoks.append(t2)
        chunk_toks[k] = subtoks
        L[("ar", k)] = subtoks
        L[("cb1", 0 if False else 0)] = None
    allap = _flat(chunk_toks)
    for sub in range(2):
        L[("cb1", sub)] = allap
        L[("cb0", sub)] = allap
    return chunk_toks, allap, streaders



def emit_p1(nc, ctx, p, dr, pfx="", dbg=False):
    u = dr["u"]
    d_lre, d_lim, d_ldl = dr["lre"], dr["lim"], dr["ldl"]
    d_bre, d_bim, d_cst, d_dfm = dr["breT"], dr["bimT"], dr["cstack"], dr["dfm"]
    d_iota, d_iota1 = dr["iota_row"], dr["iota1_row"]
    d_iotac, d_iota1c = dr["iota_col"], dr["iota1_col"]
    d_triu, d_ident = dr["triu"], dr["ident"]
    xo = dr["xo"]

    def sb(name, shape, dtype=F32):
        return ctx.enter_context(nc.sbuf_tensor(pfx + name, shape, dtype))

    rn = ("lre lim ldl lamre delta ldre ldim are aim cre cim den u1 u2 are1 "
          "rt rabs rsin rcos rmag").split()
    row = {n: sb("r_" + n, [1, 32], F32) for n in rn}
    row_i = sb("r_int", [1, 32], I32)
    RW = lambda n: row[n][0:1, 0:32]

    apos = sb("apos", [128, 64])
    aneg = sb("aneg", [128, 64])
    asher = sb("asher", [128, 64])
    tt0 = sb("tt0", [128, 64])
    tt1 = sb("tt1", [128, 64])
    tscr = sb("tscr", [128, 64])
    tt1i = sb("tt1i", [128, 32], I32)
    bb_sb = sb("bb_sb", [128, KD * 64], BF16)
    bstg = sb("bstg", [128, 64])
    bsc = sb("bsc", [128, 64])
    cbt = sb("cbt", [128, 64])
    ceff = sb("ceff", [64, D], BF16)
    cstg = sb("cstg", [64, D])
    dfm_sb = sb("dfm_sb", [128, KD])
    iota_sb = sb("iota_sb", [1, 128])
    iota1_sb = sb("iota1_sb", [1, 128])
    iota_col = sb("iota_col_s", [128, 1])
    iota1_col = sb("iota1_col_s", [128, 1])
    triu_sb = sb("triu_sb", [128, 128])
    ident_sb = sb("ident_sb", [128, 128])
    ones_row = sb("ones_row", [1, 128])
    ones_col = sb("ones_col", [128, 1])
    pihalf = sb("pihalf", [128, 1])

    ccur = [sb(f"ccur{i}", [1, 64]) for i in range(2)]
    cscr = sb("cscr", [1, 64])
    at_r = sb("at_r", [1, 64])
    a1_r = sb("a1_r", [1, 64])
    ap127_r = sb("ap127_r", [1, 64])
    acrow = sb("acrow", [1, 64])

    xa = sb("xa", [128, KD * 1024], BF16)
    ust = [sb(f"ust{i}", [128, 512]) for i in range(4)]
    p_f = [sb(f"p_f{i}", [128, 64]) for i in range(2)]
    w_f = [sb(f"w_f{i}", [128, 64]) for i in range(2)]
    acb = [sb(f"acb{i}", [128, 64]) for i in range(2)]
    x_f = [sb(f"x_f{i}", [128, 64]) for i in range(4)]
    xT = sb("xT", [64, 512], BF16)
    yscr = [sb(f"yscr{i}", [128, 512]) for i in range(2)]
    yout = [sb(f"yout{i}", [128, 512]) for i in range(2)]

    ps = [ctx.enter_context(nc.psum_tensor(pfx + f"ps{i}", [128, 512], F32))
          for i in range(4)]
    ps_c = ctx.enter_context(nc.psum_tensor(pfx + "ps_c", [128, 512], F32))
    ps_sl = ctx.enter_context(nc.psum_tensor(pfx + "ps_sl", [1, 64], F32))
    ps_tr = ctx.enter_context(nc.psum_tensor(pfx + "ps_tr", [64, 128], F32))

    lb = LnBufs(nc, ctx, pfx + "ln_",
                ps_st=[ps[0][0:1, 0:512], ps[1][0:1, 0:512],
                       ps[2][0:1, 0:512], ps[3][0:1, 0:512]],
                ps_bc=ps_c[:, 0:512])
    lb.emit_init(p)

    ld_iota = p.dma(iota_sb[:, :], d_iota[:, :])
    ld_iota1 = p.dma(iota1_sb[:, :], d_iota1[:, :])
    ld_iotac = p.dma(iota_col[:, :], d_iotac[:, :])
    ld_iota1c = p.dma(iota1_col[:, :], d_iota1c[:, :])
    ld_triu = p.dma(triu_sb[:, :], d_triu[:, :])
    ld_ident = p.dma(ident_sb[:, :], d_ident[:, :])
    ld_dfm = p.dma(dfm_sb[:, :], d_dfm[:, :])
    ms1 = p.v(lambda e: e.memset(ones_row[:, :], 1.0))
    ms3 = p.v(lambda e: e.memset(ones_col[:, :], 1.0))
    msp = p.v(lambda e: e.memset(pihalf[:, :], PI / 2))
    ld1 = p.dma(RW("lre"), d_lre[:, :])
    ld2 = p.dma(RW("lim"), d_lim[:, :])
    ld3 = p.dma(RW("ldl"), d_ldl[:, :])

    a1 = p.s(lambda e: e.activation(RW("rabs"), RW("lre"), AF.Abs), [ld1])
    a2 = p.s(lambda e: e.activation(RW("rt"), RW("rabs"), AF.Exp, scale=-1.0), [a1])
    a3 = p.s(lambda e: e.activation(RW("rt"), RW("rt"), AF.Ln, bias=1.0), [a2])
    a4 = p.v(lambda e: e.tensor_scalar_max(RW("u1"), RW("lre"), 0.0), [ld1])
    a5 = p.v(lambda e: e.tensor_add(RW("u1"), RW("u1"), RW("rt")), [a3, a4])
    a6 = p.v(lambda e: e.tensor_scalar(RW("lamre"), RW("u1"), -1.0, -0.0101,
                                       ALU.mult, ALU.add), [a5])
    a7 = p.s(lambda e: e.activation(RW("delta"), RW("ldl"), AF.Exp), [ld3])
    a8 = p.v(lambda e: e.tensor_mul(RW("ldre"), RW("lamre"), RW("delta")), [a6, a7])
    a9 = p.v(lambda e: e.tensor_mul(RW("ldim"), RW("lim"), RW("delta")), [ld2, a7])

    MAGIC = 1.5 * 2.0 ** 23

    def trig_row(dst_cos, dst_sin, src, waits):
        w1 = p.v(lambda e: e.tensor_scalar(RW("rt"), RW(src), 1.0 / (2 * PI),
                                           MAGIC, ALU.mult, ALU.add), waits)
        w3 = p.v(lambda e: e.tensor_scalar_add(RW("rt"), RW("rt"), -MAGIC), [w1])
        w4 = p.v(lambda e: e.scalar_tensor_tensor(RW("rt"), RW("rt"), -2 * PI,
                                                  RW(src), ALU.mult, ALU.add), [w3])
        w5 = p.s(lambda e: e.activation(RW(dst_sin), RW("rt"), AF.Sin), [w4])
        w6 = p.s(lambda e: e.activation(RW("rabs"), RW("rt"), AF.Abs), [w4])
        w7 = p.s(lambda e: e.activation(RW(dst_cos), RW("rabs"), AF.Sin,
                                        scale=-1.0, bias=pihalf[0:1, 0:1]),
                 [w6, msp])
        return [w5, w7]

    tg = trig_row("rcos", "rsin", "ldim", [a9])
    a10 = p.s(lambda e: e.activation(RW("rmag"), RW("ldre"), AF.Exp), [a8])
    a11 = p.v(lambda e: e.tensor_mul(RW("are"), RW("rmag"), RW("rcos")), [a10, tg])
    a12 = p.v(lambda e: e.tensor_mul(RW("aim"), RW("rmag"), RW("rsin")), [a10, tg])

    b1 = p.v(lambda e: e.tensor_mul(RW("den"), RW("lamre"), RW("lamre")), [a6])
    b2 = p.v(lambda e: e.tensor_mul(RW("u1"), RW("lim"), RW("lim")), [ld2, a5])
    b3 = p.v(lambda e: e.tensor_add(RW("den"), RW("den"), RW("u1")), [b1, b2])
    b4 = p.v(lambda e: e.reciprocal(RW("den"), RW("den")), [b3])
    b5 = p.v(lambda e: e.tensor_scalar_add(RW("are1"), RW("are"), -1.0), [a11])
    b6 = p.v(lambda e: e.tensor_mul(RW("u1"), RW("are1"), RW("lamre")), [b5, b3])
    b7 = p.v(lambda e: e.tensor_mul(RW("u2"), RW("aim"), RW("lim")), [a12])
    b8 = p.v(lambda e: e.tensor_add(RW("u1"), RW("u1"), RW("u2")), [b6, b7])
    b9 = p.v(lambda e: e.tensor_mul(RW("cre"), RW("u1"), RW("den")), [b8, b4])
    c1 = p.v(lambda e: e.tensor_mul(RW("u1"), RW("aim"), RW("lamre")), [a12, b9])
    c2 = p.v(lambda e: e.tensor_mul(RW("u2"), RW("are1"), RW("lim")), [b5, b9])
    c3 = p.v(lambda e: e.tensor_sub(RW("u1"), RW("u1"), RW("u2")), [c1, c2])
    c4 = p.v(lambda e: e.tensor_mul(RW("cim"), RW("u1"), RW("den")), [c3, b4])

    d_rows = nc.dram_tensor(pfx + "rowscr", [1, 64], F32, kind="Internal")
    ldre_b = sb("ldre_b", [128, 32])
    ldim_b = sb("ldim_b", [128, 32])
    thA = sb("thA", [128, 64])
    thB = sb("thB", [128, 64])
    wr1 = p.dma(d_rows.ap()[0:1, 0:32], RW("ldre"), [a8], tag="rowwr")
    wr2 = p.dma(d_rows.ap()[0:1, 32:64], RW("ldim"), [a9], tag="rowwr")
    bc1_ = p.dma(ldre_b[:, :], bass.AP(d_rows, 0, [[0, 128], [1, 32]]), [wr1])
    bc2_ = p.dma(ldim_b[:, :], bass.AP(d_rows, 32, [[0, 128], [1, 32]]), [wr2])
    mmt = []
    for dst, srcb, iot, w in [(lambda: thA[:, 0:32], ldre_b, "iota_col", bc1_),
                              (lambda: thA[:, 32:64], ldim_b, "iota_col", bc2_),
                              (lambda: thB[:, 0:32], ldre_b, "iota1_col", bc1_),
                              (lambda: thB[:, 32:64], ldim_b, "iota1_col", bc2_)]:
        col = iota_col if iot == "iota_col" else iota1_col
        mmt.append(p.v(lambda e, dst=dst, srcb=srcb, col=col: e.tensor_scalar(
            dst(), srcb[:, :], col[:, 0:1], None, ALU.mult),
            [w, ld_iotac, ld_iota1c], tag="powtab"))

    def trig_tile(dst, src_ps, waits):
        w1 = p.v(lambda e: e.tensor_scalar(tscr[:, 0:32], src_ps[:, 0:32],
                                           1.0 / (2 * PI), MAGIC,
                                           ALU.mult, ALU.add), waits)
        w3 = p.v(lambda e: e.tensor_scalar_add(tscr[:, 0:32], tscr[:, 0:32],
                                               -MAGIC), [w1])
        w4 = p.v(lambda e: e.scalar_tensor_tensor(tscr[:, 0:32], tscr[:, 0:32],
                                                  -2 * PI, src_ps[:, 0:32],
                                                  ALU.mult, ALU.add), [w3])
        w5 = p.s(lambda e: e.activation(dst[:, 32:64], tscr[:, 0:32], AF.Sin), [w4])
        w6 = p.s(lambda e: e.activation(tscr[:, 32:64], tscr[:, 0:32], AF.Abs), [w4])
        w7 = p.s(lambda e: e.activation(dst[:, 0:32], tscr[:, 32:64], AF.Sin,
                                        scale=-1.0, bias=pihalf[:, :]),
                 [w6, msp])
        return [w5, w7]

    tga = trig_tile(tt0, thA[:, 32:64], [mmt[1]])
    e1 = p.s(lambda e: e.activation(bstg[:, 0:32], thA[:, 0:32], AF.Exp), [mmt[0]])
    e2 = p.s(lambda e: e.activation(bstg[:, 32:64], thA[:, 0:32], AF.Exp,
                                    scale=-1.0), [mmt[0]])
    e3 = p.v(lambda e: e.tensor_mul(apos[:, 0:32], bstg[:, 0:32], tt0[:, 0:32]),
             [e1, tga])
    e4 = p.v(lambda e: e.tensor_mul(apos[:, 32:64], bstg[:, 0:32], tt0[:, 32:64]),
             [e1, tga])
    e5 = p.v(lambda e: e.tensor_mul(aneg[:, 0:32], bstg[:, 32:64], tt0[:, 0:32]),
             [e2, tga])
    e6 = p.v(lambda e: e.scalar_tensor_tensor(aneg[:, 32:64], bstg[:, 32:64],
                                              -1.0, tt0[:, 32:64], ALU.mult,
                                              ALU.mult), [e2, tga])
    tgb = trig_tile(tt1, thB[:, 32:64], [mmt[3], e3, e4, e5, e6])
    e7 = p.s(lambda e: e.activation(bstg[:, 0:32], thB[:, 0:32], AF.Exp),
             [mmt[2], e3, e4, e5, e6])
    e8 = p.v(lambda e: e.tensor_mul(asher[:, 0:32], bstg[:, 0:32], tt1[:, 0:32]),
             [e7, tgb])
    e9 = p.v(lambda e: e.tensor_mul(asher[:, 32:64], bstg[:, 0:32], tt1[:, 32:64]),
             [e7, tgb])

    atr = p.dma(at_r[0:1, 0:64], asher[127:128, 0:64], [e8, e9], tag="atr")
    apr = p.dma(ap127_r[0:1, 0:64], apos[127:128, 0:64], [e3, e4], tag="apr")
    a1r = p.dma(a1_r[0:1, 0:64], apos[1:2, 0:64], [e3, e4], tag="a1r")
    col_toks = [atr, apr, a1r]

    f1 = p.t(lambda e: e.matmul(ps_c[:, 0:32], ones_row[0:1, :], RW("cre"),
                                start=True, stop=True), [b9, ms1])
    f2 = p.t(lambda e: e.matmul(ps_c[:, 32:64], ones_row[0:1, :], RW("cim"),
                                start=True, stop=True), [c4, ms1])
    f3 = p.v(lambda e: e.tensor_copy(cbt[:, 0:64], ps_c[:, 0:64]), [f1, f2])

    prevb = None
    for k in range(KD):
        l1 = p.dma(bstg[:, 0:32], d_bre[128 * k:128 * (k + 1), :],
                   [prevb, e7, e8, e9], tag="bre")
        l2 = p.dma(bstg[:, 32:64], d_bim[128 * k:128 * (k + 1), :],
                   [prevb, e7, e8, e9])
        g1 = p.v(lambda e: e.tensor_mul(bsc[:, 0:32], bstg[:, 0:32], cbt[:, 0:32]),
                 [l1, f3, prevb, tgb])
        g2 = p.v(lambda e: e.tensor_mul(bsc[:, 32:64], bstg[:, 32:64],
                                        cbt[:, 32:64]), [l2, f3, prevb, tgb])
        g3 = p.v(lambda e, k=k: e.tensor_sub(bb_sb[:, 64 * k:64 * k + 32],
                                             bsc[:, 0:32], bsc[:, 32:64]), [g1, g2])
        g4 = p.v(lambda e: e.tensor_mul(bsc[:, 0:32], bstg[:, 32:64], cbt[:, 0:32]),
                 [g3])
        g5 = p.v(lambda e: e.tensor_mul(bsc[:, 32:64], bstg[:, 0:32],
                                        cbt[:, 32:64]), [g3])
        g6 = p.v(lambda e, k=k: e.tensor_add(bb_sb[:, 64 * k + 32:64 * k + 64],
                                             bsc[:, 0:32], bsc[:, 32:64]), [g4, g5])
        prevb = g6
    bb_tok = prevb

    lc = p.dma(cstg[0:64, :], d_cst[:, :])
    h1 = p.s(lambda e: e.activation(ceff[0:64, 0:D], cstg[0:64, 0:D], AF.Copy,
                                    scale=2.0), [lc])
    h2 = p.s(lambda e: e.activation(ceff[32:64, 0:D], cstg[32:64, 0:D], AF.Copy,
                                    scale=-2.0), [h1])
    ceff_tok = h2

    ic1 = p.v(lambda e: e.memset(ccur[0][:, :], 0.0))

    ps_y_last = [None, None]
    ps_bu_last = None
    ps_q_last = None
    lb.last["psbc"] = f3
    ps_sl_last = None
    ps_tr_last = None
    carry_par = 0
    prev_bcm = None
    ctok = [ic1] + col_toks
    xf_last = [None] * 4
    pf_last = [None, None]
    wf_last = [None, None]
    acb_last = [None, None]
    ust_last = [None] * 4
    yscr_last = [None, None]
    yout_last = [None, None]
    xT_last = [None] * 4
    xT_tok = [None] * 4
    arena_readers = {k: [] for k in range(KD)}

    for nb in range(4):
        lb.last[("st", 0)] = ps_bu_last
        lb.last[("st", 1)] = ps_q_last
        lb.last[("st", 2)] = ps_y_last[0]
        lb.last[("st", 3)] = ps_y_last[1]
        chunk_toks, lnall, strd = emit_ln_block(p, lb, u, 1024 * nb, xa, 0)
        ps_bu_last = strd[0]
        ps_q_last = strd[1]
        ps_y_last = [strd[2], strd[3]]
        for cc in range(8):
            bu_mms = []
            for k in range(KD):
                av = xa[:, 1024 * k + 128 * cc: 1024 * k + 128 * (cc + 1)]
                bu_mms.append(p.t(lambda e, av=av, k=k: e.matmul(
                    ps[0][:, 0:64], av, bb_sb[:, 64 * k:64 * (k + 1)],
                    start=(k == 0), stop=(k == KD - 1)),
                    [chunk_toks[k], bb_tok] + ([ps_bu_last] if k == 0 else []),
                    tag="bu"))
                arena_readers[k].append(bu_mms[-1])
            bu = bu_mms[-1]
            par = cc % 2
            pf = p_f[par]
            wf = w_f[par]
            q1 = p.v(lambda e, pf=pf: e.tensor_mul(pf[:, 0:32], aneg[:, 0:32],
                                                   ps[0][:, 0:32]),
                     [bu, pf_last[par], e5], tag="pre")
            q2 = p.v(lambda e, wf=wf: e.tensor_mul(wf[:, 0:32], aneg[:, 32:64],
                                                   ps[0][:, 32:64]),
                     [bu, wf_last[par], e6])
            q3 = p.v(lambda e, pf=pf, wf=wf: e.tensor_sub(pf[:, 0:32],
                                                          pf[:, 0:32],
                                                          wf[:, 0:32]), [q1, q2])
            q4 = p.v(lambda e, pf=pf: e.tensor_mul(pf[:, 32:64], aneg[:, 0:32],
                                                   ps[0][:, 32:64]),
                     [bu, q3], tag="pim")
            q5 = p.v(lambda e, wf=wf: e.tensor_mul(wf[:, 32:64], aneg[:, 32:64],
                                                   ps[0][:, 0:32]), [bu, q2])
            q6 = p.v(lambda e, pf=pf, wf=wf: e.tensor_add(pf[:, 32:64],
                                                          pf[:, 32:64],
                                                          wf[:, 32:64]), [q4, q5])
            p_tok = [q3, q6]
            pf_last[par] = p_tok
            wf_last[par] = [q5, q6]
            ps_bu_last = [q1, q2, q4, q5]
            qmm = p.t(lambda e, pf=pf: e.matmul(ps[1][:, 0:64], triu_sb[:, :],
                                                pf[:, 0:64], start=True,
                                                stop=True),
                      [p_tok, ld_triu, ps_q_last], tag="tri")
            slmm = p.t(lambda e, pf=pf: e.matmul(ps_sl[0:1, 0:64],
                                                 ones_col[:, :], pf[:, 0:64],
                                                 start=True, stop=True),
                       [p_tok, ms3, ps_sl_last], tag="slmm")
            cprev = ccur[carry_par]
            cnext = ccur[1 - carry_par]
            CS_RE = ps_sl[0:1, 0:32]
            CS_IM = ps_sl[0:1, 32:64]
            AB = lambda t, a, b: t[0:1, a:b]
            v_ = []
            v_.append(p.v(lambda e, cp=cprev: e.tensor_mul(
                AB(cscr, 0, 32), AB(a1_r, 0, 32), AB(cp, 0, 32)),
                [ctok, acb_last, prev_bcm], tag="ac1"))
            v_.append(p.v(lambda e, cp=cprev: e.tensor_mul(
                AB(cscr, 32, 64), AB(a1_r, 32, 64), AB(cp, 32, 64)), [ctok]))
            v_.append(p.v(lambda e: e.tensor_sub(AB(acrow, 0, 32), AB(cscr, 0, 32),
                                                 AB(cscr, 32, 64)), []))
            v_.append(p.v(lambda e, cp=cprev: e.tensor_mul(
                AB(cscr, 0, 32), AB(a1_r, 0, 32), AB(cp, 32, 64)), []))
            v_.append(p.v(lambda e, cp=cprev: e.tensor_mul(
                AB(cscr, 32, 64), AB(a1_r, 32, 64), AB(cp, 0, 32)), []))
            v_.append(p.v(lambda e: e.tensor_add(AB(acrow, 32, 64), AB(cscr, 0, 32),
                                                 AB(cscr, 32, 64)), []))
            ac_tok = v_[-1]
            w_ = []
            w_.append(p.v(lambda e: e.tensor_mul(AB(cscr, 0, 32), AB(ap127_r, 0, 32),
                                                 CS_RE), [slmm], tag="cc1"))
            w_.append(p.v(lambda e: e.tensor_mul(AB(cscr, 32, 64),
                                                 AB(ap127_r, 32, 64), CS_IM), [slmm]))
            w_.append(p.v(lambda e, cn=cnext: e.tensor_sub(
                AB(cn, 0, 32), AB(cscr, 0, 32), AB(cscr, 32, 64)), []))
            w_.append(p.v(lambda e: e.tensor_mul(AB(cscr, 0, 32), AB(ap127_r, 0, 32),
                                                 CS_IM), [slmm]))
            w_.append(p.v(lambda e: e.tensor_mul(AB(cscr, 32, 64),
                                                 AB(ap127_r, 32, 64), CS_RE), [slmm]))
            w_.append(p.v(lambda e, cn=cnext: e.tensor_add(
                AB(cn, 32, 64), AB(cscr, 0, 32), AB(cscr, 32, 64)), []))
            w_.append(p.v(lambda e, cp=cprev: e.tensor_mul(
                AB(cscr, 0, 32), AB(at_r, 0, 32), AB(cp, 0, 32)), []))
            w_.append(p.v(lambda e, cp=cprev: e.tensor_mul(
                AB(cscr, 32, 64), AB(at_r, 32, 64), AB(cp, 32, 64)), []))
            w_.append(p.v(lambda e, cn=cnext: e.tensor_add(
                AB(cn, 0, 32), AB(cn, 0, 32), AB(cscr, 0, 32)), []))
            w_.append(p.v(lambda e, cn=cnext: e.tensor_sub(
                AB(cn, 0, 32), AB(cn, 0, 32), AB(cscr, 32, 64)), []))
            w_.append(p.v(lambda e, cp=cprev: e.tensor_mul(
                AB(cscr, 0, 32), AB(at_r, 0, 32), AB(cp, 32, 64)), []))
            w_.append(p.v(lambda e, cp=cprev: e.tensor_mul(
                AB(cscr, 32, 64), AB(at_r, 32, 64), AB(cp, 0, 32)), []))
            w_.append(p.v(lambda e, cn=cnext: e.tensor_add(
                AB(cn, 32, 64), AB(cn, 32, 64), AB(cscr, 0, 32)), []))
            w_.append(p.v(lambda e, cn=cnext: e.tensor_add(
                AB(cn, 32, 64), AB(cn, 32, 64), AB(cscr, 32, 64)), []))
            ps_sl_last = w_[:5]
            ctok = [w_[-1], w_[9]]
            carry_par = 1 - carry_par
            bcm = p.t(lambda e: e.matmul(ps_c[:, 0:64], ones_row[0:1, :],
                                         acrow[0:1, 0:64], start=True,
                                         stop=True),
                      [ac_tok, ms1, lb.last.get("psbc")], tag="acb")
            prev_bcm = bcm
            ae = p.v(lambda e, par=par: e.tensor_copy(acb[par][:, 0:64],
                                                      ps_c[:, 0:64]),
                     [bcm, acb_last[par]])
            lb.last["psbc"] = ae
            wv = p.v(lambda e, wf=wf, par=par: e.tensor_tensor(
                wf[:, 0:64], ps[1][:, 0:64], acb[par][:, 0:64], ALU.add),
                [qmm, ae], tag="wq")
            acb_last[par] = wv
            ps_q_last = wv
            xi = cc % 4
            xf = x_f[xi]
            y1 = p.v(lambda e, xf=xf, wf=wf: e.tensor_mul(
                xf[:, 0:32], apos[:, 0:32], wf[:, 0:32]),
                [wv, xf_last[xi]], tag="xre")
            y2 = p.v(lambda e, pf=pf, wf=wf: e.tensor_mul(
                pf[:, 0:32], apos[:, 32:64], wf[:, 32:64]), [wv])
            y3 = p.v(lambda e, xf=xf, pf=pf: e.tensor_sub(
                xf[:, 0:32], xf[:, 0:32], pf[:, 0:32]), [y1, y2])
            y4 = p.v(lambda e, xf=xf, wf=wf: e.tensor_mul(
                xf[:, 32:64], apos[:, 0:32], wf[:, 32:64]),
                [wv, xf_last[xi]])
            y5 = p.v(lambda e, pf=pf, wf=wf: e.tensor_mul(
                pf[:, 32:64], apos[:, 32:64], wf[:, 0:32]), [wv])
            y6 = p.v(lambda e, xf=xf, pf=pf: e.tensor_add(
                xf[:, 32:64], xf[:, 32:64], pf[:, 32:64]), [y4, y5])
            x_tok = [y3, y6]
            pf_last[par] = [y2, y5, q3, q6]
            wf_last[par] = [wv, y1, y2, y4, y5]
            trm = p.t(lambda e, xf=xf: e.transpose(ps_tr[:, :], xf[:, 0:64],
                                                   ident_sb[:, :]),
                      [x_tok, ld_ident, ps_tr_last], tag="xtr")
            gi = cc % 4
            xtc = p.v(lambda e, gi=gi: e.tensor_copy(
                xT[0:64, 128 * gi:128 * (gi + 1)], ps_tr[0:64, :]),
                [trm, xT_last[gi]], tag="xtc")
            ps_tr_last = xtc
            xf_last[xi] = trm
            xT_tok[gi] = xtc
            if cc % 4 == 3:
                grp = cc // 4
                gcol = 1024 * nb + 512 * grp
                last_ymm = None
                for k in range(KD):
                    ypar = k % 2
                    ym = p.t(lambda e, k=k, ypar=ypar: e.matmul(
                        ps[2 + ypar][:, 0:512], ceff[0:64, 128 * k:128 * (k + 1)],
                        xT[0:64, 0:512], start=True, stop=True),
                        [xT_tok, ceff_tok, ps_y_last[ypar]], tag="ymm")
                    si = k % 4
                    lu = p.dma(ust[si][:, :], u[128 * k:128 * (k + 1),
                                                gcol:gcol + 512],
                               [ust_last[si]], tag="ustld")
                    av = xa[:, 1024 * k + 512 * grp: 1024 * k + 512 * (grp + 1)]
                    w1_ = p.v(lambda e, si=si, k=k, av=av, ypar=ypar:
                              e.scalar_tensor_tensor(
                                  yscr[ypar][:, :], av, dfm_sb[:, k:k + 1],
                                  ust[si][:, :], ALU.mult, ALU.add),
                              [lu, ld_dfm, chunk_toks[k], yscr_last[ypar]],
                              tag="yevac1")
                    arena_readers[k].append(w1_)
                    w2_ = p.v(lambda e, ypar=ypar: e.tensor_tensor(
                        yout[ypar][:, :], yscr[ypar][:, :], ps[2 + ypar][:, 0:512],
                        ALU.add), [w1_, ym, yout_last[ypar]], tag="yevac2")
                    st = p.dmag(xo[128 * k:128 * (k + 1), gcol:gcol + 512],
                                yout[ypar][:, :], [w2_], tag="xost")
                    ust_last[si] = w1_
                    yscr_last[ypar] = w2_
                    yout_last[ypar] = st
                    ps_y_last[ypar] = w2_
                    last_ymm = ym
                for gi2 in range(4):
                    xT_last[gi2] = last_ymm
        for k in range(KD):
            lb.last[("ar", k)] = arena_readers[k]
            arena_readers[k] = []
    return



def p1_inputs(inp, b):
    u = np.asarray(inp["L0_attn_a_u"])
    out = {
        "u": np.ascontiguousarray(u[b].T),
        "lre": np.asarray(inp["L0_attn_a_Lambda_unconstrained_re"], np.float32).reshape(1, 32),
        "lim": np.asarray(inp["L0_attn_a_Lambda_im"], np.float32).reshape(1, 32),
        "ldl": np.asarray(inp["L0_attn_a_log_Delta"], np.float32)[:32].reshape(1, 32),
        "breT": np.ascontiguousarray(np.asarray(inp["L0_attn_a_B_re"], np.float32).T),
        "bimT": np.ascontiguousarray(np.asarray(inp["L0_attn_a_B_im"], np.float32).T),
        "cstack": np.ascontiguousarray(
            np.concatenate([np.asarray(inp["L0_attn_a_C_re"], np.float32).T,
                            np.asarray(inp["L0_attn_a_C_im"], np.float32).T], 0)),
        "dfm": np.ascontiguousarray(
            np.asarray(inp["L0_attn_a_D"], np.float32).reshape(KD, 128).T),
        "iota_row": np.arange(128, dtype=np.float32).reshape(1, 128),
        "iota1_row": (np.arange(128, dtype=np.float32) + 1).reshape(1, 128),
        "iota_col": np.arange(128, dtype=np.float32).reshape(128, 1),
        "iota1_col": (np.arange(128, dtype=np.float32) + 1).reshape(128, 1),
        "triu": np.triu(np.ones((128, 128), np.float32)),
        "ident": np.eye(128, dtype=np.float32),
    }
    return out


_PROGS = {}


def _get_prog(name, builder):
    if name not in _PROGS:
        _PROGS[name] = builder()
    return _PROGS[name][0]



def build_fused():
    nc = bass.Bass("TRN2", target_bir_lowering=False, debug=False)
    top = ExitStack()
    p = Prog(nc)

    def EI(name, shape, dt=F32):
        return nc.dram_tensor(name, shape, dt, kind="ExternalInput").ap()

    dr1 = {
        "u": EI("a_u", [D, S]),
        "lre": EI("a_lre", [1, 32]), "lim": EI("a_lim", [1, 32]),
        "ldl": EI("a_ldl", [1, 32]),
        "breT": EI("a_breT", [D, 32]), "bimT": EI("a_bimT", [D, 32]),
        "cstack": EI("a_cstack", [64, D]), "dfm": EI("a_dfm", [128, KD]),
        "iota_row": EI("a_iota_row", [1, 128]),
        "iota1_row": EI("a_iota1_row", [1, 128]),
        "iota_col": EI("a_iota_col", [128, 1]),
        "iota1_col": EI("a_iota1_col", [128, 1]),
        "triu": EI("a_triu", [128, 128]), "ident": EI("a_ident", [128, 128]),
    }
    dr2 = {
        "wg": EI("b_wg", [64, 128, KD * 128], BF16),
        "wu": EI("b_wu", [64, 128, KD * 128], BF16),
        "wd": EI("b_wd", [KD, 128, 64 * 128], BF16),
    }
    d_gidx = EI("b_gidx", [128, 64], I32)
    dr3 = {
        "wq": EI("c_wq", [KD, 128, D], BF16),
        "wk": EI("c_wk", [KD, 128, D], BF16),
        "wv": EI("c_wv", [KD, 128, D], BF16),
        "wo": EI("c_wo", [KD, 128, D], BF16),
        "qb": EI("c_qb", [128, KD]), "kb": EI("c_kb", [128, KD]),
        "vb": EI("c_vb", [1, D], BF16), "ob": EI("c_ob", [128, KD]),
        "slopes": EI("c_slopes", [1, 16]),
        "kgc": EI("c_kgc", [128, 773]),
        "mstrip": EI("c_mstrip", [128, 128], BF16),
    }
    dr4 = {
        "wq": EI("d_wq", [KD, 128, D], BF16),
        "wk": EI("d_wk", [KD, 128, D], BF16),
        "wv": EI("d_wv", [KD, 128, D], BF16),
        "wo": EI("d_wo", [KD, 128, D], BF16),
        "qb": EI("d_qb", [128, KD]), "kb": EI("d_kb", [128, KD]),
        "vb": EI("d_vb", [1, D], BF16), "ob": EI("d_ob", [128, KD]),
        "wmapq": EI("d_wmapq", [128, 64], BF16),
        "wmapk": EI("d_wmapk", [128, 64], BF16),
    }
    dr5 = {
        "ing": EI("e_ing", [128, KD]), "inb": EI("e_inb", [128, KD]),
        "png": EI("e_png", [128, KD]), "pnb": EI("e_pnb", [128, KD]),
        "outg": EI("e_outg", [128, KD]), "outb": EI("e_outb", [128, KD]),
        "agg": EI("e_agg", [KD, 128, D], BF16), "aggb": EI("e_aggb", [128, KD]),
        "d1w": EI("e_d1w", [32, 128, D], BF16), "d1b": EI("e_d1b", [128, 32]),
        "d2w": EI("e_d2w", [KD, 128, 4096], BF16), "d2b": EI("e_d2b", [128, KD]),
        "bcw": EI("e_bcw", [KD, 128, D], BF16), "bcb": EI("e_bcb", [128, KD]),
        "gw": EI("e_gw", [KD, 128, D], BF16), "gb": EI("e_gb", [128, KD]),
    }
    dr6 = {
        "wq": EI("f_wq", [KD, 128, D], BF16),
        "wk": EI("f_wk", [4, 128, D], BF16),
        "wv": EI("f_wv", [KD, 128, 512], BF16),
        "wo": EI("f_wo", [KD, 128, D], BF16),
        "cosq": EI("f_cosq", [128, TOWN], BF16),
        "sinq": EI("f_sinq", [128, TOWN], BF16),
    }

    def DI(name, shape, dt=F32):
        return nc.dram_tensor(name, shape, dt, kind="Internal")

    dxg = DI("dxg", [D, S]).ap()
    dxA = DI("dxA", [D, TOWN]).ap()
    dxB = DI("dxB", [D, TOWN]).ap()
    dq3 = DI("c_dq", [D, TOWN], BF16).ap()
    dko3_h = DI("c_dko", [D, TOWN], BF16)
    dkg3_h = DI("c_dkg", [2, D, TOWN], BF16)
    dvo3_h = DI("c_dvo", [16, 16, 128, 128], BF16)
    dvg3_h = DI("c_dvg", [2, 16, 16, 128, 128], BF16)
    dq4 = DI("d_dq", [D, TOWN], BF16).ap()
    dko4 = DI("d_dko", [D, TOWN], BF16).ap()
    dvo4 = DI("d_dvo", [16, 16, 128, 128], BF16).ap()
    pki4_h = [DI(f"d_pki{i}", [128, 129]) for i in range(2)]
    pko4_h = [DI(f"d_pko{i}", [128, 129]) for i in range(2)]
    xli5_h = [DI(f"e_xli{i}", [128, KD]) for i in range(2)]
    xlo5_h = [DI(f"e_xlo{i}", [256, KD]) for i in range(2)]
    thki5_h = [DI(f"e_thki{i}", [128, KD]) for i in range(2)]
    thko5_h = [DI(f"e_thko{i}", [128, KD]) for i in range(2)]
    dq6 = DI("f_dq", [D, TOWN], BF16).ap()
    dko6_h = DI("f_dko", [512, TOWN], BF16)
    dkg6_h = DI("f_dkg", [2, 512, TOWN], BF16)
    dvo6_h = DI("f_dvo", [4, 16, 128, 128], BF16)
    dvg6_h = DI("f_dvg", [2, 4, 16, 128, 128], BF16)

    xo = nc.dram_tensor("xo", [D, TOWN], F32, kind="ExternalOutput").ap()

    c = ExitStack()
    emit_p1(nc, c, p, {**dr1, "xo": dxg}, pfx="a_")
    p.phase_break()
    c.close()

    dxg_rows = dxg.rearrange("d (b c) -> (d b) c", b=8)
    c = ExitStack()
    sidx = c.enter_context(nc.sbuf_tensor("b_sidx", [128, 64], I32))
    lsx = p.dma(sidx[:, :], d_gidx[:, :], tag="gidxld")

    def gload(dst, k, j, waits, lsx=lsx):
        col = 4 * k + j
        return p.dmai(dst, dxg_rows, sidx[:, col:col + 1], [waits, lsx],
                      tag="glnld")

    emit_p2(nc, c, p, {**dr2, "xi": None, "xo": dxA}, pfx="b1_",
            load_fn=gload)
    p.phase_break()
    c.close()

    c = ExitStack()
    emit_p3f(nc, c, p, {**dr3, "xi": dxA, "xo": dxB, "dq": dq3,
                        "dko_h": dko3_h, "dkg_h": dkg3_h,
                        "dvo_h": dvo3_h, "dvg_h": dvg3_h}, pfx="c_")
    p.phase_break()
    c.close()

    c = ExitStack()
    emit_p2(nc, c, p, {**dr2, "xi": dxB, "xo": dxA}, pfx="b2_")
    p.phase_break()
    c.close()

    c = ExitStack()
    emit_p4f(nc, c, p, {**dr4, "xi": dxA, "xo": dxB, "dq": dq4,
                        "dko": dko4, "dvo": dvo4,
                        "pki_h": pki4_h, "pko_h": pko4_h}, pfx="d_")
    p.phase_break()
    c.close()

    def mk_xl(stream, pki_h, pko_h, c_stack, pfx):
        xlo = c_stack.enter_context(
            nc.sbuf_tensor(pfx + "xlown", [128, KD], F32))

        def xl_emit(xl_fm):
            lds = [p.dma(xlo[:, k:k + 1],
                         stream[128 * k:128 * (k + 1), 2047:2048],
                         tag="xlc", slow=True) for k in range(KD)]
            wr = p.dma(pki_h.ap()[:, :], xlo[:, :], [lds], tag="xlw")
            cl = p.coll("AllGather", ALU.bypass, PAIRS,
                        [pki_h.ap().opt()], [pko_h.ap().opt()],
                        waits=[wr], tag="xlag")
            return p.dma(xl_fm[:, :], pko_h.ap()[0:128, 0:KD], [cl],
                         tag="xlr")
        return xl_emit

    c = ExitStack()
    emit_p5(nc, c, p, {**dr5, "xi": dxB, "xo": dxA}, pfx="e1_",
            xl_emit=mk_xl(dxB, xli5_h[0], xlo5_h[0], c, "e1_"),
            thr_h=(thki5_h[0], thko5_h[0]))
    p.phase_break()
    c.close()

    c = ExitStack()
    emit_p6f(nc, c, p, {**dr6, "xi": dxA, "xo": dxB, "dq": dq6,
                        "dko_h": dko6_h, "dkg_h": dkg6_h,
                        "dvo_h": dvo6_h, "dvg_h": dvg6_h}, pfx="f_")
    p.phase_break()
    c.close()

    c = ExitStack()
    emit_p5(nc, c, p, {**dr5, "xi": dxB, "xo": xo}, pfx="e2_",
            xl_emit=mk_xl(dxB, xli5_h[1], xlo5_h[1], c, "e2_"),
            thr_h=(thki5_h[1], thko5_h[1]))
    p.finalize(top)
    c.close()
    return nc, top


def gidx_table(h):
    blkmap = [0, 1, 6, 7] if h == 0 else [2, 3, 4, 5]
    g = np.zeros((128, 64), np.int32)
    pp = np.arange(128)
    for k in range(KD):
        for j in range(4):
            g[:, 4 * k + j] = (128 * k + pp) * 8 + blkmap[j]
    return g


def fused_inputs(inp, c):
    b, h = c // 2, c % 2
    m = {}
    m.update({"a_" + k: v for k, v in p1_inputs(inp, b).items()})
    m.update({"b_" + k: v for k, v in p2_weights(inp).items()})
    m["b_gidx"] = gidx_table(h)
    m.update({"c_" + k: v for k, v in p3f_inputs(inp, h).items()})
    m.update({"d_" + k: v for k, v in p4_inputs(inp).items()})
    m.update({"e_" + k: v for k, v in p5f_inputs(inp, h).items()})
    m.update({"f_" + k: v for k, v in p6f_inputs(inp, h).items()})
    return m


def kernel(**inputs):
    inp = {k: np.asarray(v) for k, v in inputs.items()}
    nc = _get_prog("fused", build_fused)
    maps = [fused_inputs(inp, c) for c in range(NCORES)]
    r = run_bass_kernel_spmd(nc, maps, core_ids=list(range(NCORES)))
    out = np.empty((4, S, D), np.float32)
    for b in range(4):
        xf = np.empty((D, S), np.float32)
        xf[:, cols_of(0)] = r.results[2 * b]["xo"]
        xf[:, cols_of(1)] = r.results[2 * b + 1]["xo"]
        out[b] = xf.T
    return out



def emit_p2(nc, ctx, p, dr, pfx="", load_fn=None):
    T = TOWN
    xi = dr.get("xi")
    d_wg, d_wu, d_wd = dr["wg"], dr["wu"], dr["wd"]
    xo = dr["xo"]

    def sb(name, shape, dtype=F32):
        return ctx.enter_context(nc.sbuf_tensor(pfx + name, shape, dtype))

    AR = sb("AR", [128, 81920], BF16)
    XLN0, H0 = 0, 16384

    def xln_ap(k, lo, hi):
        return AR[:, XLN0 + 1024 * k + lo: XLN0 + 1024 * k + hi]

    def h_ap(m, lo, hi):
        return AR[:, H0 + 1024 * m + lo: H0 + 1024 * m + hi]

    wgu = [sb(f"wgu{i}", [128, KD * 128], BF16) for i in range(4)]
    hps = [sb(f"hps{i}", [128, 512], BF16) for i in range(2)]
    habs = [sb(f"habs{i}", [128, 512], BF16) for i in range(4)]
    pmA = sb("pmA", [128, 512], BF16)
    pmB = sb("pmB", [128, 512], BF16)
    pmC = sb("pmC", [128, 512], BF16)
    ust = [sb(f"ust{i}", [128, 512]) for i in range(2)]

    ps = [ctx.enter_context(nc.psum_tensor(pfx + f"ps{i}", [128, 512], F32))
          for i in range(8)]

    lb = LnBufs(nc, ctx, pfx + "ln_",
                ps_st=[ps[0][0:1, 0:512], ps[1][0:1, 0:512],
                       ps[2][0:1, 0:512], ps[3][0:1, 0:512]],
                ps_bc=ps[4][:, 0:512])
    lb.emit_init(p)

    bank_last = {i: None for i in range(8)}
    wgu_last = [None] * 4
    ust_last = [None] * 2
    dw_last = [None, None]
    spl = {}
    h_last = {}

    def emit_sparsify(s_, nsub):
        mem = [s_, 16 + s_, 32 + s_, 48 + s_]
        lo, hi = 512 * nsub, 512 * (nsub + 1)
        ab = []
        for j in range(4):
            ab.append(p.s(lambda e, j=j, mm=mem[j], lo=lo, hi=hi: e.activation(
                habs[j][:, :], h_ap(mm, lo, hi), AF.Abs),
                [h_last[("w", mem[j], nsub)], spl.get(("abs", j))], tag="habs"))
        t1 = p.v(lambda e: e.tensor_max(pmA[:, :], habs[0][:, :], habs[1][:, :]),
                 [ab[0], ab[1], spl.get("pmA")], tag="sp")
        t2 = p.v(lambda e: e.tensor_tensor(pmC[:, :], habs[0][:, :],
                                           habs[1][:, :], ALU.min),
                 [ab[0], ab[1], spl.get("pmC")])
        t3 = p.v(lambda e: e.tensor_max(pmB[:, :], habs[2][:, :], habs[3][:, :]),
                 [ab[2], ab[3], spl.get("pmB")])
        t5 = p.v(lambda e: e.tensor_tensor(pmA[:, :], pmA[:, :], pmB[:, :],
                                           ALU.min), [t1, t3])
        t4 = p.v(lambda e: e.tensor_tensor(pmB[:, :], habs[2][:, :],
                                           habs[3][:, :], ALU.min),
                 [t5, ab[2], ab[3]])
        t6 = p.v(lambda e: e.tensor_max(pmC[:, :], pmC[:, :], pmB[:, :]),
                 [t2, t4])
        t7 = p.v(lambda e: e.tensor_max(pmA[:, :], pmA[:, :], pmC[:, :]),
                 [t5, t6])
        spl["pmB"] = t6
        spl["pmC"] = t7
        mus = []
        for j in range(4):
            mk = p.v(lambda e, j=j: e.tensor_tensor(
                habs[j][:, :], habs[j][:, :], pmA[:, :], ALU.is_ge),
                [t7], tag="mask")
            mu = p.v(lambda e, j=j, mm=mem[j], lo=lo, hi=hi: e.tensor_tensor(
                h_ap(mm, lo, hi), h_ap(mm, lo, hi), habs[j][:, :],
                ALU.mult), [mk], tag="maskmul")
            spl[("abs", j)] = mu
            h_last[("r", mem[j], nsub)] = mu
            mus.append(mu)
        spl["pmA"] = mus

    for nb in range(T // 1024):
        for b in range(4):
            lb.last[("st", b)] = bank_last[b]
        for k in range(KD):
            lb.last[("ar", k)] = [h_last.get(("xa", k))]
        chunk_toks, lnall, strd = emit_ln_block(p, lb, xi, 1024 * nb, AR, XLN0,
                                                load_fn=load_fn)
        for b in range(4):
            bank_last[b] = strd[b]
        last_gu_mm = None
        for m in range(64):
            par = m % 2
            gsl, usl = wgu[par], wgu[2 + par]
            lg = p.dma(gsl[:, :], d_wg[m], [wgu_last[par]], tag="wgld")
            lu = p.dma(usl[:, :], d_wu[m], [wgu_last[2 + par]], tag="wuld")
            for nsub in range(2):
                bg, bu = 2 * par, 2 * par + 1
                mmg = mmu = None
                for k in range(KD):
                    mmg = p.t(lambda e, k=k, gsl=gsl, nsub=nsub, bg=bg: e.matmul(
                        ps[bg][:, 0:512], gsl[:, 128 * k:128 * (k + 1)],
                        xln_ap(k, 512 * nsub, 512 * (nsub + 1)),
                        start=(k == 0), stop=(k == KD - 1)),
                        [chunk_toks[k], lg] + ([bank_last[bg]] if k == 0 else []),
                        tag="gmm")
                    mmu = p.t(lambda e, k=k, usl=usl, nsub=nsub, bu=bu: e.matmul(
                        ps[bu][:, 0:512], usl[:, 128 * k:128 * (k + 1)],
                        xln_ap(k, 512 * nsub, 512 * (nsub + 1)),
                        start=(k == 0), stop=(k == KD - 1)),
                        [chunk_toks[k], lu] + ([bank_last[bu]] if k == 0 else []),
                        tag="umm")
                sil = p.s(lambda e, par=par, bg=bg: e.activation(
                    hps[par][:, :], ps[bg][:, 0:512], AF.Silu),
                    [mmg, bank_last[bg]], tag="silu")
                hmul = p.v(lambda e, m=m, nsub=nsub, par=par, bu=bu: e.tensor_tensor(
                    h_ap(m, 512 * nsub, 512 * (nsub + 1)), hps[par][:, :],
                    ps[bu][:, 0:512], ALU.mult),
                    [sil, mmu, h_last.get(("hw", m, nsub))], tag="hmul")
                bank_last[bg] = sil
                bank_last[bu] = hmul
                h_last[("w", m, nsub)] = hmul
            wgu_last[par] = mmg
            wgu_last[2 + par] = mmu
            last_gu_mm = mmu
            if m >= 48:
                emit_sparsify(m - 48, 0)
                emit_sparsify(m - 48, 1)
        dwld = [None, None]
        for m2 in range(KD):
            par = m2 % 2
            dwsl = AR[:, XLN0 + 8192 * par: XLN0 + 8192 * (par + 1)]
            ld = p.dma(dwsl, d_wd[m2],
                       [last_gu_mm, dw_last[par], lnall], tag="wdld")
            dwld[par] = ld
            for nsub in range(2):
                bank = 5 + par
                mmd = None
                for k in range(64):
                    mmd = p.t(lambda e, k=k, nsub=nsub, bank=bank, dwsl=dwsl: e.matmul(
                        ps[bank][:, 0:512], dwsl[:, 128 * k:128 * (k + 1)],
                        h_ap(k, 512 * nsub, 512 * (nsub + 1)),
                        start=(k == 0), stop=(k == 63)),
                        [h_last[("r", k, nsub)], ld] +
                        ([bank_last[bank]] if k == 0 else []),
                        tag="dmm")
                si = nsub
                if load_fn is not None:
                    lx = load_fn(ust[si][:, :], m2, 2 * nb + nsub,
                                 [ust_last[si]])
                else:
                    lx = p.dma(ust[si][:, :], xi[128 * m2:128 * (m2 + 1),
                                                 1024 * nb + 512 * nsub:
                                                 1024 * nb + 512 * (nsub + 1)],
                               [ust_last[si]], tag="xreld")
                w2_ = p.v(lambda e, si=si, bank=bank: e.tensor_tensor(
                    ust[si][:, :], ust[si][:, :], ps[bank][:, 0:512], ALU.add),
                    [lx, mmd], tag="ffnevac")
                st = p.dmag(xo[128 * m2:128 * (m2 + 1),
                               1024 * nb + 512 * nsub:1024 * nb + 512 * (nsub + 1)],
                            ust[si][:, :], [w2_], tag="xost")
                ust_last[si] = st
                bank_last[bank] = w2_
            dw_last[par] = mmd
        for k in range(KD):
            h_last[("xa", k)] = [dw_last[k // 8], dwld[k // 8]]
        for m in range(64):
            for nsub in range(2):
                h_last[("hw", m, nsub)] = [dw_last[0], dw_last[1]]
    return


def p2_weights(inp):
    wg = np.asarray(inp["L0_ffn_w_gate"], np.float32)
    wu = np.asarray(inp["L0_ffn_w_up"], np.float32)
    wd = np.asarray(inp["L0_ffn_w_down"], np.float32)
    perm = np.array([4 * (i % 2048) + (i // 2048) for i in range(8192)])
    return {
        "wg": tile_lhsT(wg[:, perm]),
        "wu": tile_lhsT(wu[:, perm]),
        "wd": tile_lhsT(wd[perm, :]),
    }



def emit_p3f(nc, ctx, p, dr, pfx=""):
    SCALE = 1.0 / float(np.sqrt(128.0))

    xi = dr["xi"]
    d_wq, d_wk, d_wv, d_wo = dr["wq"], dr["wk"], dr["wv"], dr["wo"]
    d_qb, d_kb, d_vb, d_ob = dr["qb"], dr["kb"], dr["vb"], dr["ob"]
    d_slope, d_kgc, d_mstrip = dr["slopes"], dr["kgc"], dr["mstrip"]
    xo = dr["xo"]
    dq = dr["dq"]
    dko_h, dkg_h = dr["dko_h"], dr["dkg_h"]
    dvo_h, dvg_h = dr["dvo_h"], dr["dvg_h"]
    dko, dkg = dko_h.ap(), dkg_h.ap()
    dvo, dvg = dvo_h.ap(), dvg_h.ap()

    def sb(name, shape, dtype=F32):
        return ctx.enter_context(nc.sbuf_tensor(pfx + name, shape, dtype))

    AR = sb("AR", [128, 65536], BF16)
    def kth(hp, lo, hi):
        return AR[:, 32768 + hp * 4096 + lo: 32768 + hp * 4096 + hi]

    def vth(hp, lo, hi):
        return AR[:, 40960 + hp * 4096 + lo: 40960 + hp * 4096 + hi]

    def qth(hp, lo, hi):
        return AR[:, 49152 + hp * 2048 + lo: 49152 + hp * 2048 + hi]

    def pt_ap(pi, lo, hi):
        return AR[:, 53248 + pi * 512 + lo: 53248 + pi * 512 + hi]

    dinv_b = AR[:, 54784: 55296]
    mst128 = AR[:, 55296: 55424]

    def kod(hp, lo, hi):
        return AR[:, 55424 + hp * 2048 + lo: 55424 + hp * 2048 + hi]

    def vod(hp, lo, hi):
        return AR[:, 59520 + hp * 2048 + lo: 59520 + hp * 2048 + hi]

    def att_ap(h, qc):
        return AR[:, 2048 * h + 512 * qc: 2048 * h + 512 * (qc + 1)]

    def xln_ap(k, lo, hi):
        return AR[:, 2048 * k + lo: 2048 * k + hi]

    wsl = [sb(f"wsl{i}", [128, D], BF16) for i in range(2)]
    wvres = sb("wvres", [128, KD * 512], BF16)
    prst = [sb(f"prst{i}", [128, 512], BF16) for i in range(2)]
    biasc = sb("biasc", [128, KD])
    vb_sb = sb("vb_sb", [1, D], BF16)
    slope_sb = sb("slope_sb", [1, 16])
    slope_b = sb("slope_b", [128, 16])
    kgc_sb = sb("kgc_sb", [128, 773])
    biasfm = sb("biasfm", [128, 773])
    drow = sb("drow", [1, 512])
    drow_bf = sb("drow_bf", [1, 512], BF16)
    ones_rowb = sb("ones_rowb", [1, 128], BF16)
    ones_colb = sb("ones_colb", [128, 1], BF16)
    ones_row_f = sb("ones_row_f", [1, 128])
    xres = [sb(f"xres{i}", [128, 512]) for i in range(2)]

    ps = [ctx.enter_context(nc.psum_tensor(pfx + f"ps{i}", [128, 512], F32))
          for i in range(8)]

    lb = LnBufs(nc, ctx, pfx + "ln_",
                ps_st=[ps[0][0:1, 0:512], ps[1][0:1, 0:512],
                       ps[2][0:1, 0:512], ps[3][0:1, 0:512]],
                ps_bc=ps[4][:, 0:512])
    lb.emit_init(p)

    ms1 = p.v(lambda e: e.memset(ones_rowb[:, :], 1.0))
    ms2 = p.v(lambda e: e.memset(ones_colb[:, :], 1.0))
    ms3 = p.v(lambda e: e.memset(ones_row_f[:, :], 1.0))
    l_sl = p.dma(slope_sb[:, :], d_slope[:, :])
    l_kgc = p.dma(kgc_sb[:, :], d_kgc[:, :])
    l_vb = p.dma(vb_sb[:, :], d_vb[:, :])
    bsl = p.t(lambda e: e.matmul(ps[5][:, 0:16], ones_row_f[0:1, :],
                                 slope_sb[0:1, :], start=True, stop=True),
              [ms3, l_sl], tag="slopeb")
    bslc = p.v(lambda e: e.tensor_copy(slope_b[:, :], ps[5][:, 0:16]), [bsl])

    bank_last = {i: [None] for i in range(8)}
    bank_last[5] = [bslc]

    chunk_toks_all = [[] for _ in range(KD)]
    for nb in range(2):
        for b in range(4):
            lb.last[("st", b)] = bank_last[b]
        chunk_toks, lnall, strd = emit_ln_block(p, lb, xi, 1024 * nb, AR, 0,
                                                pitch=2048, boff=1024 * nb)
        for b in range(4):
            bank_last[b] = [strd[b]]
        for k in range(KD):
            chunk_toks_all[k].append(chunk_toks[k])

    v_store_toks = []
    wv_last = None
    prst_last = [None, None]
    for vc in range(4):
        lws = []
        for k in range(KD):
            lws.append(p.dma(wvres[:, 512 * k:512 * (k + 1)],
                             d_wv[k][:, 512 * vc:512 * (vc + 1)],
                             [wv_last], tag="wvld"))
        last_vmm = None
        for t in range(16):
            bank = 5 + (t % 2)
            pre = p.t(lambda e, vc=vc, bank=bank: e.matmul(
                ps[bank][:, 0:512], ones_rowb[0:1, :],
                vb_sb[0:1, 512 * vc:512 * (vc + 1)], start=True, stop=False),
                [ms1, l_vb, bank_last[bank]], tag="vpre")
            mm = pre
            for k in range(KD):
                mm = p.t(lambda e, k=k, t=t, bank=bank: e.matmul(
                    ps[bank][:, 0:512],
                    xln_ap(k, 128 * t, 128 * (t + 1)),
                    wvres[:, 512 * k:512 * (k + 1)],
                    start=False, stop=(k == KD - 1)),
                    [chunk_toks_all[k], lws[k]], tag="vmm")
            pr = t % 2
            ev = p.s(lambda e, pr=pr, bank=bank: e.activation(
                prst[pr][:, :], ps[bank][:, 0:512], AF.Copy),
                [mm, prst_last[pr]], tag="vev")
            bank_last[bank] = [ev]
            sts = []
            for j in range(4):
                sts.append(p.dmag(dvo[4 * vc + j, t],
                                  prst[pr][:, 128 * j:128 * (j + 1)], [ev],
                                  tag="vst"))
            prst_last[pr] = sts
            v_store_toks += sts
            last_vmm = mm
        wv_last = last_vmm

    wsl_last = [None, None]
    st_by_head_q = {}
    st_by_head_k = {}

    def proj_featmajor(dram_w, dram_out, ntok, bias_ld, stmap, tag):
        for m in range(KD):
            par = m % 2
            lw = p.dma(wsl[par][:, :], dram_w[m], [wsl_last[par]], tag=tag + "w")
            stoks = []
            mm = None
            for qc in range(ntok // 512):
                bank = 5 + (qc % 2)
                for k in range(KD):
                    mm = p.t(lambda e, k=k, qc=qc, par=par, bank=bank: e.matmul(
                        ps[bank][:, 0:512], wsl[par][:, 128 * k:128 * (k + 1)],
                        xln_ap(k, 512 * qc, 512 * (qc + 1)),
                        start=(k == 0), stop=(k == KD - 1)),
                        [chunk_toks_all[k], lw] +
                        (bank_last[bank] if k == 0 else []), tag=tag)
                pr = qc % 2
                ev = p.s(lambda e, m=m, pr=pr, bank=bank: e.activation(
                    prst[pr][:, :], ps[bank][:, 0:512], AF.Identity,
                    bias=biasc[:, m:m + 1]),
                    [mm, bias_ld, prst_last[pr]], tag=tag + "ev")
                st = p.dmag(dram_out[128 * m:128 * (m + 1),
                                     512 * qc:512 * (qc + 1)],
                            prst[pr][:, :], [ev], tag=tag + "st")
                bank_last[bank] = [ev]
                prst_last[pr] = st
                stoks.append(st)
            stmap[m] = stoks
            wsl_last[par] = mm

    coll_v = p.coll("AllGather", ALU.bypass, PAIRS,
                    [dvo_h.ap().opt()], [dvg_h.ap().opt()],
                    waits=[v_store_toks], tag="agv")

    lkb = p.dma(biasc[:, :], d_kb[:, :], [], tag="kbld")
    proj_featmajor(d_wk, dko, TOWN, lkb, st_by_head_k, "kp")
    coll_k = p.coll("AllGather", ALU.bypass, PAIRS,
                    [dko_h.ap().opt()], [dkg_h.ap().opt()],
                    waits=[_flat(list(st_by_head_k.values()))], tag="agk")
    lqb = p.dma(biasc[:, :], d_qb[:, :],
                [_flat(list(st_by_head_k.values()))], tag="qbld")
    proj_featmajor(d_wq, dq, TOWN, lqb, st_by_head_q, "qp")

    all_proj = _flat(list(st_by_head_q.values()) + list(st_by_head_k.values())) \
        + v_store_toks
    lms = p.dma(mst128, d_mstrip[:, :], [wv_last, wsl_last], tag="mstld")
    ar_last = {"k": [None, None], "v": [None, None], "q": [None, None],
               "ko": [None, None], "vo": [None, None]}
    pt_last = [None, None, None]
    att_last = {}
    dinv_last = None
    biasfm_last = None
    drow_last = None
    drow_bf_last = None
    den_pend = None
    TL = {0: list(range(8)) + list(range(16, 19)),
          1: list(range(8)) + list(range(16, 23)),
          2: list(range(11)) + list(range(16, 32)),
          3: list(range(15)) + list(range(16, 32))}
    slopes_f = 2.0 ** (-np.linspace(1.0, 8.0, 16))
    gorder_np = np.r_[cols_of(0), cols_of(1)]
    TLH = {}
    for h_ in range(16):
        cut = 60.0 / float(slopes_f[h_])
        TLH[h_] = {}
        for qc_ in range(4):
            keep = []
            for t_ in TL[qc_]:
                kg = gorder_np[128 * t_:128 * (t_ + 1)]
                ok = False
                for par in range(2):
                    qcols = cols_of(par)[512 * qc_:512 * (qc_ + 1)]
                    dmat = qcols[:, None] - kg[None, :]
                    dpos = dmat[dmat >= 0]
                    if dpos.size and dpos.min() <= cut:
                        ok = True
                        break
                if ok:
                    keep.append(t_)
            assert len(keep) >= 2, (h_, qc_, keep)
            TLH[h_][qc_] = keep
    UQ = {qc_: list(range(4 * qc_, 4 * qc_ + 4)) +
          list(range(16 + 4 * qc_, 16 + 4 * qc_ + 4)) for qc_ in range(4)}
    first_waits = all_proj + [lms]

    for h in range(16):
        hp = h % 2
        lkA = p.dma(kth(hp, 0, 2048), dkg[0, 128 * h:128 * (h + 1), :],
                    [coll_k, ar_last["k"][hp],
                     first_waits if h < 2 else None], tag="kld")
        lkB = p.dma(kth(hp, 2048, 4096), dkg[1, 128 * h:128 * (h + 1), :],
                    [coll_k], tag="kld")
        lk = [lkA, lkB]
        lko = p.dma(kod(hp, 0, 2048), dko[128 * h:128 * (h + 1), :],
                    [st_by_head_k[h], ar_last["ko"][hp],
                     first_waits if h < 2 else None], tag="kold")
        lvs = []
        for t in range(32):
            lvs.append(p.dma(vth(hp, 128 * t, 128 * (t + 1)),
                             dvg[t // 16, h, t % 16],
                             [coll_v, ar_last["v"][hp],
                              first_waits if h < 2 else None], tag="vld"))
        lvo = []
        for tt in range(16):
            lvo.append(p.dma(vod(hp, 128 * tt, 128 * (tt + 1)), dvo[h, tt],
                             [ar_last["vo"][hp],
                              first_waits if h < 2 else None], tag="vold"))
        lq = p.dma(qth(hp, 0, 2048), dq[128 * h:128 * (h + 1), :],
                   [st_by_head_q[h], ar_last["q"][hp],
                    first_waits if h < 2 else None], tag="qld")
        bft = p.v(lambda e, h=h: e.tensor_scalar(
            biasfm[:, :], kgc_sb[:, :], slope_b[:, h:h + 1], None, ALU.mult),
            [l_kgc, bslc, biasfm_last], tag="bfm")
        kreaders = []
        koreaders = []
        voreaders = []
        vreaders = []
        qreaders = []
        exs = []
        SB = (0, 1, 7)

        def make_den(h, qc, dmm, pvm, pvbank, dbank):
            def go():
                nonlocal drow_last, drow_bf_last, dinv_last
                rcp = p.v(lambda e, dbank=dbank: e.reciprocal(
                    drow[0:1, :], ps[dbank][0:1, 0:512]),
                    [dmm, drow_last], tag="rcp")
                rcb = p.v(lambda e: e.tensor_copy(drow_bf[0:1, :],
                                                  drow[0:1, :]),
                          [rcp, drow_bf_last], tag="rcb")
                bank_last[dbank] = [rcp]
                bcm = p.t(lambda e: e.matmul(ps[6][:, 0:512],
                                             ones_rowb[0:1, :],
                                             drow_bf[0:1, :], start=True,
                                             stop=True),
                          [rcb, ms1] + bank_last[6], tag="dbc")
                dcp = p.v(lambda e: e.tensor_copy(dinv_b, ps[6][:, 0:512]),
                          [bcm, dinv_last], tag="dcp")
                bank_last[6] = [dcp]
                drow_last = rcb
                drow_bf_last = bcm
                amul = p.v(lambda e, h=h, qc=qc, pvbank=pvbank:
                           e.tensor_tensor(
                    att_ap(h, qc), ps[pvbank][:, 0:512], dinv_b, ALU.mult),
                    [pvm, dcp, att_last.get((h, qc))], tag="attdiv")
                dinv_last = amul
                bank_last[pvbank] = [amul]
                att_last[(h, qc)] = amul
            return go

        for qc in range(4):
            gl = TLH[h][qc]
            tlist = [("G", gl[0])] + [("D", s) for s in range(4)] + \
                    [("G", t) for t in gl[1:]]
            pvbank = 2 + (qc % 2)
            dbank = 4 + (qc % 2)
            ntl = len(tlist)
            dmm = pvm = None
            pq = []

            def mk_pend(reg, t, pi, ptok, i_, last):
                def go():
                    if reg == "G":
                        dm = p.t(lambda e, pi=pi, dbank=dbank, first=(i_ == 0),
                                 last=last: e.matmul(
                            ps[dbank][0:1, 0:512], ones_colb[:, :],
                            pt_ap(pi, 0, 512), start=first, stop=last),
                            [ptok, ms2] + (bank_last[dbank] if i_ == 0 else []),
                            tag="dmm")
                        pv = p.t(lambda e, pi=pi, t=t, hp=hp, pvbank=pvbank,
                                 first=(i_ == 0), last=last: e.matmul(
                            ps[pvbank][:, 0:512],
                            vth(hp, 128 * t, 128 * (t + 1)),
                            pt_ap(pi, 0, 512), start=first, stop=last),
                            [ptok, lvs[t]] +
                            (bank_last[pvbank] if i_ == 0 else []),
                            tag="pvm")
                        vreaders.append(pv)
                    else:
                        ot = 4 * qc + t
                        dm = p.t(lambda e, pi=pi, t=t, dbank=dbank: e.matmul(
                            ps[dbank][0:1, 128 * t:128 * (t + 1)],
                            ones_colb[:, :],
                            pt_ap(pi, 0, 128), start=False, stop=False),
                            [ptok, ms2], tag="ddmm")
                        pv = p.t(lambda e, pi=pi, t=t, ot=ot, hp=hp,
                                 pvbank=pvbank: e.matmul(
                            ps[pvbank][:, 128 * t:128 * (t + 1)],
                            vod(hp, 128 * ot, 128 * (ot + 1)),
                            pt_ap(pi, 0, 128), start=False, stop=False),
                            [ptok, lvo[ot]], tag="dpvm")
                        voreaders.append(pv)
                    pt_last[pi] = [dm, pv]
                    return dm, pv
                return go

            for i_, (reg, t) in enumerate(tlist):
                sbank = SB[i_ % 3]
                pi = i_ % 3
                last = i_ == ntl - 1
                light = slopes_f[h] * 4095.0 <= 60.0
                if reg == "G":
                    smm = p.t(lambda e, t=t, hp=hp, qc=qc, sbank=sbank: e.matmul(
                        ps[sbank][:, 0:512], kth(hp, 128 * t, 128 * (t + 1)),
                        qth(hp, 512 * qc, 512 * (qc + 1)),
                        start=True, stop=True),
                        [lk, lq] + bank_last[sbank], tag="smm")
                    kreaders.append(smm)
                    qreaders.append(smm)
                    if light and t not in UQ[qc]:
                        idx = 513 + 32 * qc + t
                        exl = [p.s(lambda e, pi=pi, idx=idx, sbank=sbank:
                                   e.activation(
                            pt_ap(pi, 0, 512), ps[sbank][:, 0:512],
                            AF.Exp, scale=SCALE,
                            bias=biasfm[:, idx:idx + 1]),
                            [smm, bft, pt_last[pi]], tag="exl")]
                    else:
                        jb = (641 + 32 * qc + 4 * UQ[qc].index(t)) if light \
                            else None
                        exl = []
                        for sl_ in range(4):
                            idx = jb + sl_ if light else (4 * qc + sl_) * 32 + t
                            exl.append(p.s(lambda e, pi=pi, sl_=sl_, idx=idx,
                                           sbank=sbank: e.activation(
                                pt_ap(pi, 128 * sl_, 128 * (sl_ + 1)),
                                ps[sbank][:, 128 * sl_:128 * (sl_ + 1)],
                                AF.Exp, scale=SCALE,
                                bias=biasfm[:, idx:idx + 1]),
                                [smm, bft, pt_last[pi] if sl_ == 0 else None],
                                tag="exp"))
                    exs += exl
                    bank_last[sbank] = exl
                    ptok = exl
                else:
                    ot = 4 * qc + t
                    smm = p.t(lambda e, t=t, ot=ot, hp=hp, qc=qc, sbank=sbank:
                              e.matmul(
                        ps[sbank][:, 0:128],
                        kod(hp, 128 * ot, 128 * (ot + 1)),
                        qth(hp, 512 * qc + 128 * t, 512 * qc + 128 * (t + 1)),
                        start=True, stop=True),
                        [lko, lq] + bank_last[sbank], tag="dsmm")
                    koreaders.append(smm)
                    qreaders.append(smm)
                    didx = (769 + t) if light else 512
                    ex = p.s(lambda e, pi=pi, didx=didx, sbank=sbank:
                             e.activation(
                        pt_ap(pi, 0, 128), ps[sbank][:, 0:128],
                        AF.Exp, scale=SCALE, bias=biasfm[:, didx:didx + 1]),
                        [smm, bft, pt_last[pi]], tag="dexp")
                    exs.append(ex)
                    bank_last[sbank] = [ex]
                    msk = p.v(lambda e, pi=pi: e.tensor_tensor(
                        pt_ap(pi, 0, 128), pt_ap(pi, 0, 128), mst128,
                        ALU.mult), [ex, lms], tag="dmask")
                    ptok = msk
                if i_ == 4 and den_pend is not None:
                    den_pend()
                    den_pend = None
                if len(pq) == 2:
                    dmm, pvm = pq.pop(0)()
                pq.append(mk_pend(reg, t, pi, ptok, i_, last))
            for go in pq:
                dmm, pvm = go()
            den_pend = make_den(h, qc, dmm, pvm, pvbank, dbank)
        ar_last["k"][hp] = kreaders[-1]
        ar_last["ko"][hp] = koreaders
        ar_last["v"][hp] = vreaders
        ar_last["vo"][hp] = voreaders
        ar_last["q"][hp] = qreaders[-1]
        biasfm_last = exs
    if den_pend is not None:
        den_pend()
        den_pend = None

    lob = p.dma(biasc[:, :], d_ob[:, :],
                [_flat(list(st_by_head_k.values()))], tag="obld")
    xres_last = [None, None]
    for m in range(KD):
        par = m % 2
        lw = p.dma(wsl[par][:, :], d_wo[m], [wsl_last[par]], tag="wow")
        mm = None
        for qc in range(4):
            bank = 5 + (qc % 2)
            for k in range(KD):
                mm = p.t(lambda e, k=k, qc=qc, par=par, bank=bank: e.matmul(
                    ps[bank][:, 0:512], wsl[par][:, 128 * k:128 * (k + 1)],
                    att_ap(k, qc), start=(k == 0), stop=(k == KD - 1)),
                    [att_last[(k, qc)], lw] +
                    (bank_last[bank] if k == 0 else []), tag="omm")
            pr = qc % 2
            lx = p.dma(xres[pr][:, :], xi[128 * m:128 * (m + 1),
                                          512 * qc:512 * (qc + 1)],
                       [xres_last[pr]], tag="oxld")
            evb = p.s(lambda e, m=m, pr=pr: e.activation(
                xres[pr][:, :], xres[pr][:, :], AF.Identity,
                bias=biasc[:, m:m + 1]), [lx, lob], tag="oevb")
            yv = p.v(lambda e, pr=pr, bank=bank: e.tensor_tensor(
                xres[pr][:, :], xres[pr][:, :], ps[bank][:, 0:512], ALU.add),
                [evb, mm], tag="oadd")
            st = p.dmag(xo[128 * m:128 * (m + 1), 512 * qc:512 * (qc + 1)],
                        xres[pr][:, :], [yv], tag="oxst")
            bank_last[bank] = [yv]
            xres_last[pr] = st
        wsl_last[par] = mm
    return


def p3f_inputs(inp, h):
    ipj = np.asarray(inp["L0_attn_b_in_proj"], np.float32)
    ipb = np.asarray(inp["L0_attn_b_in_proj_bias"], np.float32)
    ow = np.asarray(inp["L0_attn_b_out_w"], np.float32)
    ob = np.asarray(inp["L0_attn_b_out_b"], np.float32)
    sl = np.asarray(inp["L0_attn_b_alibi_slopes"], np.float32)
    own = cols_of(h)
    gorder = np.r_[cols_of(0), cols_of(1)]
    kgc = np.zeros((128, 773), np.float32)
    for qs in range(16):
        qmax = float(own[qs * 128 + 127])
        tdiag = qs if h == 0 else 16 + qs
        for t in range(32):
            kg = gorder[128 * t:128 * (t + 1)].astype(np.float32)
            col = np.where(kg <= qmax, kg - qmax, -1e9).astype(np.float32)
            if t == tdiag:
                col[:] = -1e9
            kgc[:, qs * 32 + t] = col
    kgc[:, 512] = np.arange(128, dtype=np.float32) - 127.0
    for qc in range(4):
        qmaxc = float(own[qc * 512 + 511])
        for t in range(32):
            kg = gorder[128 * t:128 * (t + 1)].astype(np.float32)
            kgc[:, 513 + 32 * qc + t] = np.where(kg <= qmaxc, kg - qmaxc,
                                                 -1e9)
        uq = list(range(4 * qc, 4 * qc + 4)) + \
            list(range(16 + 4 * qc, 16 + 4 * qc + 4))
        for j, t in enumerate(uq):
            kg = gorder[128 * t:128 * (t + 1)].astype(np.float32)
            for s_ in range(4):
                qmax_s = float(own[qc * 512 + 128 * s_ + 127])
                tdiag = (4 * qc + s_) if h == 0 else 16 + 4 * qc + s_
                col = np.where(kg <= qmax_s, kg - qmaxc, -1e9)
                if t == tdiag:
                    col = np.full(128, -1e9, np.float32)
                kgc[:, 641 + 32 * qc + 4 * j + s_] = col
    for s_ in range(4):
        kgc[:, 769 + s_] = (np.arange(128, dtype=np.float32)
                            + 128 * s_ - 511.0)
    tri = (np.arange(128)[:, None] <= np.arange(128)[None, :])
    return {
        "wq": tile_lhsT(ipj[0:2048].T),
        "wk": tile_lhsT(ipj[2048:4096].T),
        "wv": to_bf(np.ascontiguousarray(
            ipj[4096:6144].T).reshape(KD, 128, D)),
        "wo": tile_lhsT(ow.T),
        "qb": np.ascontiguousarray(ipb[0:2048].reshape(KD, 128).T),
        "kb": np.ascontiguousarray(ipb[2048:4096].reshape(KD, 128).T),
        "vb": to_bf(ipb[4096:6144].reshape(1, 2048)),
        "ob": np.ascontiguousarray(ob.reshape(KD, 128).T),
        "slopes": sl.reshape(1, 16),
        "kgc": kgc,
        "mstrip": to_bf(tri.astype(np.float32)),
    }



def emit_p4f(nc, ctx, p, dr, pfx=""):
    xi = dr["xi"]
    d_wq, d_wk, d_wv, d_wo = dr["wq"], dr["wk"], dr["wv"], dr["wo"]
    d_qb, d_kb, d_vb, d_ob = dr["qb"], dr["kb"], dr["vb"], dr["ob"]
    d_mq, d_mk = dr["wmapq"], dr["wmapk"]
    xo = dr["xo"]
    dq = dr["dq"]
    dko = dr["dko"]
    dvo = dr["dvo"]
    pki_h, pko_h = dr["pki_h"], dr["pko_h"]

    def sb(name, shape, dtype=F32):
        return ctx.enter_context(nc.sbuf_tensor(pfx + name, shape, dtype))

    AR = sb("AR", [128, 65536], BF16)

    def kth(hp, lo, hi):
        return AR[:, 32768 + hp * 4096 + lo: 32768 + hp * 4096 + hi]

    def vth(hp, lo, hi):
        return AR[:, 40960 + hp * 4096 + lo: 40960 + hp * 4096 + hi]

    def qth(hp, lo, hi):
        return AR[:, 49152 + hp * 2048 + lo: 49152 + hp * 2048 + hi]

    def qmh(hp, lo, hi):
        return AR[0:64, 53248 + 2048 * hp + lo: 53248 + 2048 * hp + hi]

    def att_ap(h, qc):
        return AR[:, 2048 * h + 512 * qc: 2048 * h + 512 * (qc + 1)]

    def xln_ap(k, lo, hi):
        return AR[:, 2048 * k + lo: 2048 * k + hi]

    wsl = [sb(f"wsl{i}", [128, D], BF16) for i in range(2)]
    wvres = sb("wvres", [128, KD * 512], BF16)
    prst = [sb(f"prst{i}", [128, 512], BF16) for i in range(2)]
    biasc = sb("biasc", [128, KD])
    vb_sb = sb("vb_sb", [1, D], BF16)
    mq_sb = sb("mq_sb", [128, 64], BF16)
    mk_sb = sb("mk_sb", [128, 64], BF16)
    kmt = [sb(f"kmt{i}", [128, 512], BF16) for i in range(2)]
    escr = [sb(f"escr{i}", [128, 512]) for i in range(2)]
    kv_sb = [sb(f"kv_sb{i}", [64, 128], BF16) for i in range(2)]
    ksum_sb = [sb(f"ksum_sb{i}", [64, 1], BF16) for i in range(2)]
    kvf = [sb(f"kvf{i}", [128, 132]) for i in range(2)]
    kvr = [sb(f"kvr{i}", [64, 132]) for i in range(2)]
    nrow = sb("nrow", [1, 512])
    nrow_bf = sb("nrow_bf", [1, 512], BF16)
    nb_sb = sb("nb_sb", [128, 512], BF16)
    eps128 = sb("eps128", [128, 1])
    ones_rowb = sb("ones_rowb", [1, 128], BF16)
    ones_colb = sb("ones_colb", [128, 1], BF16)
    xres = [sb(f"xres{i}", [128, 512]) for i in range(2)]

    ps = [ctx.enter_context(nc.psum_tensor(pfx + f"ps{i}", [128, 512], F32))
          for i in range(8)]

    lb = LnBufs(nc, ctx, pfx + "ln_",
                ps_st=[ps[0][0:1, 0:512], ps[1][0:1, 0:512],
                       ps[2][0:1, 0:512], ps[3][0:1, 0:512]],
                ps_bc=ps[4][:, 0:512])
    lb.emit_init(p)

    ms1 = p.v(lambda e: e.memset(ones_rowb[:, :], 1.0))
    ms2 = p.v(lambda e: e.memset(ones_colb[:, :], 1.0))
    mse = p.v(lambda e: e.memset(eps128[:, :], 1e-6))
    msz = [p.v(lambda e, i=i: e.memset(kvf[i][:, :], 0.0)) for i in range(2)]
    l_vb = p.dma(vb_sb[:, :], d_vb[:, :])
    l_mq = p.dma(mq_sb[:, :], d_mq[:, :])
    l_mk = p.dma(mk_sb[:, :], d_mk[:, :])

    bank_last = {i: [None] for i in range(8)}

    chunk_toks_all = [[] for _ in range(KD)]
    for nb in range(2):
        for b in range(4):
            lb.last[("st", b)] = bank_last[b]
        chunk_toks, lnall, strd = emit_ln_block(p, lb, xi, 1024 * nb, AR, 0,
                                                pitch=2048, boff=1024 * nb)
        for b in range(4):
            bank_last[b] = [strd[b]]
        for k in range(KD):
            chunk_toks_all[k].append(chunk_toks[k])

    v_store_toks = []
    wv_last = None
    prst_last = [None, None]
    for vc in range(4):
        lws = []
        for k in range(KD):
            lws.append(p.dma(wvres[:, 512 * k:512 * (k + 1)],
                             d_wv[k][:, 512 * vc:512 * (vc + 1)],
                             [wv_last], tag="wvld"))
        last_vmm = None
        for t in range(16):
            bank = 5 + (t % 2)
            pre = p.t(lambda e, vc=vc, bank=bank: e.matmul(
                ps[bank][:, 0:512], ones_rowb[0:1, :],
                vb_sb[0:1, 512 * vc:512 * (vc + 1)], start=True, stop=False),
                [ms1, l_vb] + bank_last[bank], tag="vpre")
            mm = pre
            for k in range(KD):
                mm = p.t(lambda e, k=k, t=t, bank=bank: e.matmul(
                    ps[bank][:, 0:512],
                    xln_ap(k, 128 * t, 128 * (t + 1)),
                    wvres[:, 512 * k:512 * (k + 1)],
                    start=False, stop=(k == KD - 1)),
                    [chunk_toks_all[k], lws[k]], tag="vmm")
            pr = t % 2
            ev = p.s(lambda e, pr=pr, bank=bank: e.activation(
                prst[pr][:, :], ps[bank][:, 0:512], AF.Copy),
                [mm, prst_last[pr]], tag="vev")
            bank_last[bank] = [ev]
            sts = []
            for j in range(4):
                sts.append(p.dmag(dvo[4 * vc + j, t],
                                  prst[pr][:, 128 * j:128 * (j + 1)], [ev],
                                  tag="vst"))
            prst_last[pr] = sts
            v_store_toks += sts
            last_vmm = mm
        wv_last = last_vmm

    wsl_last = [None, None]
    st_by_head_q = {}
    st_by_head_k = {}

    def proj_featmajor(dram_w, dram_out, ntok, bias_ld, stmap, tag):
        nonlocal wsl_last
        for m in range(KD):
            par = m % 2
            lw = p.dma(wsl[par][:, :], dram_w[m], [wsl_last[par]], tag=tag + "w")
            stoks = []
            mm = None
            for qc in range(ntok // 512):
                bank = 5 + (qc % 2)
                for k in range(KD):
                    mm = p.t(lambda e, k=k, qc=qc, par=par, bank=bank: e.matmul(
                        ps[bank][:, 0:512], wsl[par][:, 128 * k:128 * (k + 1)],
                        xln_ap(k, 512 * qc, 512 * (qc + 1)),
                        start=(k == 0), stop=(k == KD - 1)),
                        [chunk_toks_all[k], lw] +
                        (bank_last[bank] if k == 0 else []), tag=tag)
                pr = qc % 2
                ev = p.s(lambda e, m=m, pr=pr, bank=bank: e.activation(
                    prst[pr][:, :], ps[bank][:, 0:512], AF.Identity,
                    bias=biasc[:, m:m + 1]),
                    [mm, bias_ld, prst_last[pr]], tag=tag + "ev")
                st = p.dmag(dram_out[128 * m:128 * (m + 1),
                                     512 * qc:512 * (qc + 1)],
                            prst[pr][:, :], [ev], tag=tag + "st")
                bank_last[bank] = [ev]
                prst_last[pr] = st
                stoks.append(st)
            stmap[m] = stoks
            wsl_last[par] = mm

    lqb = p.dma(biasc[:, :], d_qb[:, :], [], tag="qbld")
    proj_featmajor(d_wq, dq, TOWN, lqb, st_by_head_q, "qp")
    lkb = p.dma(biasc[:, :], d_kb[:, :],
                [_flat(list(st_by_head_q.values()))], tag="kbld")
    proj_featmajor(d_wk, dko, TOWN, lkb, st_by_head_k, "kp")

    all_proj = _flat(list(st_by_head_q.values()) + list(st_by_head_k.values())) \
        + v_store_toks
    ar_last = {"k": [None, None], "v": [None, None], "q": [None, None]}
    first_waits = all_proj
    att_last = {}
    kmt_last = [None, None]
    escr_last = [None, None]
    kv_last = [None, None]
    ksum_last = [None, None]
    kvf_last = [msz[0], msz[1]]
    kvr_last = [None, None]
    pk_last = [None, None]
    nscr_last = None
    nrowbf_last = None
    nb_last = None

    def elu1(dst_ap, src_ps, esl, waits, fd):
        r1 = p.v(lambda e, dst_ap=dst_ap: e.tensor_scalar_max(
            escr[esl][:, 0:fd], src_ps, 0.0), waits + [escr_last[esl]],
            tag="elu_r")
        r2 = p.v(lambda e: e.tensor_tensor(escr[esl][:, 0:fd], src_ps,
                                           escr[esl][:, 0:fd], ALU.min)
                 if False else e.tensor_sub(escr[esl][:, 0:fd], src_ps,
                                            escr[esl][:, 0:fd]),
                 [r1], tag="elu_s")
        r3 = p.s(lambda e: e.activation(escr[esl][:, 0:fd], escr[esl][:, 0:fd],
                                        AF.Exp) if False else None, [])
        return None


    def emit_elu1(dst_ap, src_ps, np_, fd, waits):
        w1 = p.v(lambda e: e.tensor_scalar_max(escr[1][0:np_, 0:fd], src_ps,
                                               0.0),
                 waits + [escr_last[1]], tag="elu_r")
        w2 = p.v(lambda e: e.tensor_sub(escr[0][0:np_, 0:fd], src_ps,
                                        escr[1][0:np_, 0:fd]),
                 [w1, escr_last[0]], tag="elu_m")
        w3 = p.s(lambda e: e.activation(escr[0][0:np_, 0:fd],
                                        escr[0][0:np_, 0:fd],
                                        AF.Exp), [w2], tag="elu_e")
        w4 = p.v(lambda e, dst_ap=dst_ap: e.tensor_tensor(
            dst_ap, escr[0][0:np_, 0:fd], escr[1][0:np_, 0:fd], ALU.add), [w3],
            tag="elu_a")
        escr_last[0] = w4
        escr_last[1] = w4
        return w4

    qm_last = [None, None]

    def pre_head(h):
        hp = h % 2
        lk = p.dma(kth(hp, 0, 2048), dko[128 * h:128 * (h + 1), :],
                   [st_by_head_k[h], ar_last["k"][hp],
                    first_waits if h < 2 else None], tag="kld")
        lvs = []
        for t in range(16):
            lvs.append(p.dma(vth(hp, 128 * t, 128 * (t + 1)), dvo[h, t],
                             [ar_last["v"][hp],
                              first_waits if h < 2 else None], tag="vld"))
        lq = p.dma(qth(hp, 0, 2048), dq[128 * h:128 * (h + 1), :],
                   [st_by_head_q[h], ar_last["q"][hp],
                    first_waits if h < 2 else None], tag="qld")
        qreaders = []
        qm_toks = []
        for qc in range(4):
            bank = qc % 2
            mmq = p.t(lambda e, qc=qc, hp=hp, bank=bank: e.matmul(
                ps[bank][0:64, 0:512], mq_sb[:, :],
                qth(hp, 512 * qc, 512 * (qc + 1)), start=True, stop=True),
                [lq, l_mq] + bank_last[bank], tag="qmap")
            qreaders.append(mmq)
            e1_ = emit_elu1(qmh(hp, 512 * qc, 512 * (qc + 1)),
                            ps[bank][0:64, 0:512], 64, 512,
                            [mmq, qm_last[hp]])
            bank_last[bank] = [e1_]
            qm_toks.append(e1_)
        kvmm = ksmm = None
        ekg = [None, None]
        for g in range(2):
            kb = 2 + g
            mms = []
            for j in range(8):
                t = 8 * g + j
                mms.append(p.t(lambda e, t=t, j=j, hp=hp, kb=kb: e.matmul(
                    ps[kb][:, 64 * j:64 * (j + 1)],
                    kth(hp, 128 * t, 128 * (t + 1)),
                    mk_sb[:, :], start=True, stop=True),
                    [lk, l_mk] + (bank_last[kb] if j == 0 else []),
                    tag="kmap"))
            ekg[g] = emit_elu1(kmt[g][:, 0:512], ps[kb][:, 0:512], 128, 512,
                               [mms, kmt_last[g]])
            bank_last[kb] = [ekg[g]]
        for g in range(2):
            for j in range(8):
                t = 8 * g + j
                kvmm = p.t(lambda e, t=t, g=g, j=j, hp=hp, first=(t == 0),
                           last=(t == 15): e.matmul(
                    ps[4][0:64, 0:128], kmt[g][:, 64 * j:64 * (j + 1)],
                    vth(hp, 128 * t, 128 * (t + 1)), start=first, stop=last),
                    [ekg[g], lvs[t]] + (bank_last[4] if t == 0 else []),
                    tag="kvmm")
                ksmm = p.t(lambda e, g=g, j=j, first=(t == 0),
                           last=(t == 15): e.matmul(
                    ps[5][0:64, 0:1], kmt[g][:, 64 * j:64 * (j + 1)],
                    ones_colb[:, :], start=first, stop=last),
                    [ekg[g], ms2] + (bank_last[5] if t == 0 else []),
                    tag="ksmm")
            kmt_last[g] = [kvmm, ksmm]
        kvc0 = p.v(lambda e, hp=hp: e.tensor_copy(kvf[hp][0:64, 0:128],
                                                  ps[4][0:64, 0:128]),
                   [kvmm, kvf_last[hp]], tag="kvpk")
        ksc0 = p.v(lambda e, hp=hp: e.tensor_copy(kvf[hp][0:64, 128:129],
                                                  ps[5][0:64, 0:1]),
                   [ksmm, kvc0], tag="kspk")
        bank_last[4] = [kvc0]
        bank_last[5] = [ksc0]
        wrp = p.dma(pki_h[hp].ap()[:, :], kvf[hp][:, 0:129],
                    [kvc0, ksc0, pk_last[hp]], tag="kvwr")
        kvf_last[hp] = wrp
        cl = p.coll("AllReduce", ALU.add, PAIRS,
                    [pki_h[hp].ap().opt()], [pko_h[hp].ap().opt()],
                    waits=[wrp], tag="kvar")
        ar_last["k"][hp] = [ksmm, kvmm]
        ar_last["v"][hp] = kvmm
        ar_last["q"][hp] = qreaders[-1]
        return {"hp": hp, "qm_toks": qm_toks, "cl": cl,
                "kvmm": kvmm, "ksmm": ksmm}

    def post_head(h, st):
        nonlocal nscr_last, nrowbf_last, nb_last
        hp = st["hp"]
        qm_toks = st["qm_toks"]
        lkv = p.dma(kvr[hp][0:64, 0:129], pko_h[hp].ap()[0:64, 0:129],
                    [st["cl"], kvr_last[hp]], tag="kvrd")
        pk_last[hp] = lkv
        kvc = p.v(lambda e, hp=hp: e.tensor_copy(kv_sb[hp][:, :],
                                                 kvr[hp][0:64, 0:128]),
                  [lkv, kv_last[hp]], tag="kvc")
        ksc = p.v(lambda e, hp=hp: e.tensor_copy(ksum_sb[hp][:, :],
                                                 kvr[hp][0:64, 128:129]),
                  [lkv, ksum_last[hp]], tag="ksc")
        kvr_last[hp] = [kvc, ksc]
        omms = []
        nmms = []
        for qc in range(4):
            nmm = p.t(lambda e, qc=qc, hp=hp: e.matmul(
                ps[6][0:1, 0:512], ksum_sb[hp][:, :],
                qmh(hp, 512 * qc, 512 * (qc + 1)),
                start=True, stop=True),
                [ksc, qm_toks[qc]] + bank_last[6], tag="nmm")
            nr = p.v(lambda e: e.tensor_scalar_add(nrow[0:1, :],
                                                   ps[6][0:1, 0:512], 1e-6),
                     [nmm, nscr_last], tag="nadd")
            bank_last[6] = [nr]
            nr2 = p.v(lambda e: e.reciprocal(nrow[0:1, :], nrow[0:1, :]),
                      [nr], tag="nrcp")
            nr3 = p.v(lambda e: e.tensor_copy(nrow_bf[0:1, :], nrow[0:1, :]),
                      [nr2, nrowbf_last], tag="nbf")
            nscr_last = nr3
            nbm = p.t(lambda e: e.matmul(ps[7][:, 0:512],
                                         ones_rowb[0:1, :],
                                         nrow_bf[0:1, :], start=True,
                                         stop=True),
                      [nr3, ms1] + bank_last[7], tag="nbc")
            nrowbf_last = nbm
            nrc = p.v(lambda e: e.tensor_copy(nb_sb[:, :], ps[7][:, 0:512]),
                      [nbm, nb_last], tag="nbcc")
            bank_last[7] = [nrc]
            bank = qc % 2
            omm = p.t(lambda e, qc=qc, hp=hp, bank=bank: e.matmul(
                ps[bank][:, 0:512], kv_sb[hp][:, :],
                qmh(hp, 512 * qc, 512 * (qc + 1)),
                start=True, stop=True),
                [qm_toks[qc], kvc] + bank_last[bank], tag="outmm")
            omms.append(omm)
            nmms.append(nmm)
            oev = p.v(lambda e, h=h, qc=qc, bank=bank: e.tensor_tensor(
                att_ap(h, qc), ps[bank][:, 0:512], nb_sb[:, :], ALU.mult),
                [omm, nrc, att_last.get((h, qc))], tag="oev")
            nb_last = oev
            bank_last[bank] = [oev]
            att_last[(h, qc)] = oev
        qm_last[hp] = omms + nmms
        kv_last[hp] = omms
        ksum_last[hp] = nmms

    prev_st = None
    for h in range(16):
        st = pre_head(h)
        if prev_st is not None:
            post_head(h - 1, prev_st)
        prev_st = st
    post_head(15, prev_st)

    lob = p.dma(biasc[:, :], d_ob[:, :],
                [_flat(list(st_by_head_k.values()))], tag="obld")
    xres_last = [None, None]
    for m in range(KD):
        par = m % 2
        lw = p.dma(wsl[par][:, :], d_wo[m], [wsl_last[par]], tag="wow")
        mm = None
        for qc in range(4):
            bank = 5 + (qc % 2)
            for k in range(KD):
                mm = p.t(lambda e, k=k, qc=qc, par=par, bank=bank: e.matmul(
                    ps[bank][:, 0:512], wsl[par][:, 128 * k:128 * (k + 1)],
                    att_ap(k, qc), start=(k == 0), stop=(k == KD - 1)),
                    [att_last[(k, qc)], lw] +
                    (bank_last[bank] if k == 0 else []), tag="omm2")
            pr = qc % 2
            lx = p.dma(xres[pr][:, :], xi[128 * m:128 * (m + 1),
                                          512 * qc:512 * (qc + 1)],
                       [xres_last[pr]], tag="oxld")
            evb = p.s(lambda e, m=m, pr=pr: e.activation(
                xres[pr][:, :], xres[pr][:, :], AF.Identity,
                bias=biasc[:, m:m + 1]), [lx, lob], tag="oevb")
            yv = p.v(lambda e, pr=pr, bank=bank: e.tensor_tensor(
                xres[pr][:, :], xres[pr][:, :], ps[bank][:, 0:512], ALU.add),
                [evb, mm], tag="oadd")
            st = p.dmag(xo[128 * m:128 * (m + 1), 512 * qc:512 * (qc + 1)],
                        xres[pr][:, :], [yv], tag="oxst")
            bank_last[bank] = [yv]
            xres_last[pr] = st
        wsl_last[par] = mm
    return


def p4_inputs(inp):
    return {
        "wq": tile_lhsT(np.asarray(inp["L1_attn_a_wq"], np.float32).T),
        "wk": tile_lhsT(np.asarray(inp["L1_attn_a_wk"], np.float32).T),
        "wv": to_bf(np.ascontiguousarray(
            np.asarray(inp["L1_attn_a_wv"], np.float32).T).reshape(KD, 128, D)),
        "wo": tile_lhsT(np.asarray(inp["L1_attn_a_wo"], np.float32).T),
        "qb": np.ascontiguousarray(
            np.asarray(inp["L1_attn_a_bq"], np.float32).reshape(KD, 128).T),
        "kb": np.ascontiguousarray(
            np.asarray(inp["L1_attn_a_bk"], np.float32).reshape(KD, 128).T),
        "vb": to_bf(np.asarray(inp["L1_attn_a_bv"], np.float32).reshape(1, D)),
        "ob": np.ascontiguousarray(
            np.asarray(inp["L1_attn_a_bo"], np.float32).reshape(KD, 128).T),
        "wmapq": to_bf(np.asarray(inp["L1_attn_a_wmap_q"], np.float32).T),
        "wmapk": to_bf(np.asarray(inp["L1_attn_a_wmap_k"], np.float32).T),
    }



def emit_p5(nc, ctx, p, dr, pfx="", xl_emit=None, thr_h=None):
    xi = dr["xi"]
    d_xl = dr.get("xlast")
    thki_h, thko_h = thr_h
    d_ing, d_inb = dr["ing"], dr["inb"]
    d_png, d_pnb = dr["png"], dr["pnb"]
    d_outg, d_outb = dr["outg"], dr["outb"]
    d_agg, d_aggb = dr["agg"], dr["aggb"]
    d_d1, d_d1b = dr["d1w"], dr["d1b"]
    d_d2, d_d2b = dr["d2w"], dr["d2b"]
    d_bc, d_bcb = dr["bcw"], dr["bcb"]
    d_gw, d_gb = dr["gw"], dr["gb"]
    xo = dr["xo"]

    def sb(name, shape, dtype=F32):
        return ctx.enter_context(nc.sbuf_tensor(pfx + name, shape, dtype))

    AR = sb("AR", [128, 65536], BF16)

    def xln_ap(k, lo, hi):
        return AR[:, 2048 * k + lo: 2048 * k + hi]

    def o_ap(k, lo, hi):
        return AR[:, 32768 + 2048 * k + lo: 32768 + 2048 * k + hi]

    wsl = [sb(f"wsl{i}", [128, D], BF16) for i in range(2)]
    wsl2 = [sb(f"wsl2_{i}", [128, 4096], BF16) for i in range(2)]
    wsl2_last = [None, None]
    thp = sb("thp", [128, KD])
    thr = sb("thr", [128, KD])
    vg = {}
    for nm, dd in [("ing", d_ing), ("inb", d_inb), ("png", d_png),
                   ("pnb", d_pnb), ("outg", d_outg), ("outb", d_outb),
                   ("aggb", d_aggb), ("d2b", d_d2b), ("bcb", d_bcb),
                   ("gb", d_gb)]:
        vg[nm] = sb("t_" + nm, [128, KD])
    d1b_sb = sb("d1b_sb", [128, 32])
    xl_fm = sb("xl_fm", [128, KD])
    xn_fm = sb("xn_fm", [128, KD])
    th_fm = sb("th_fm", [128, KD])
    tn_fm = sb("tn_fm", [128, KD])
    tn_bf = sb("tn_bf", [128, KD], BF16)
    g_bf = sb("g_bf", [128, 64], BF16)
    sq_fm = sb("sq_fm", [128, KD])
    vrow = sb("vrow", [1, 64])
    vcol = sb("vcol", [128, 4])
    bc_fm = sb("bc_fm", [128, KD])
    gsb = [sb(f"gsb{i}", [128, 512], BF16) for i in range(2)]
    ones_col = sb("ones_col", [128, 1])
    ones_rowf = sb("ones_rowf", [1, 128])
    cns2 = sb("cns2", [1, 1])
    xres = [sb(f"xres{i}", [128, 512]) for i in range(2)]
    oscr = sb("oscr", [128, 512])

    ps = [ctx.enter_context(nc.psum_tensor(pfx + f"ps{i}", [128, 512], F32))
          for i in range(8)]

    lb = LnBufs(nc, ctx, pfx + "ln_",
                ps_st=[ps[0][0:1, 0:512], ps[1][0:1, 0:512],
                       ps[2][0:1, 0:512], ps[3][0:1, 0:512]],
                ps_bc=ps[4][:, 0:512])
    lb.emit_init(p)
    msc = p.v(lambda e: e.memset(cns2[:, :], 1e-6))
    ms1 = p.v(lambda e: e.memset(ones_col[:, :], 1.0))
    ms2 = p.v(lambda e: e.memset(ones_rowf[:, :], 1.0))
    ldt = {}
    for nm, dd in [("ing", d_ing), ("inb", d_inb), ("png", d_png),
                   ("pnb", d_pnb), ("outg", d_outg), ("outb", d_outb),
                   ("aggb", d_aggb), ("d2b", d_d2b), ("bcb", d_bcb),
                   ("gb", d_gb)]:
        ldt[nm] = p.dma(vg[nm][:, :], dd[:, :], tag="vgld")
    ld1b = p.dma(d1b_sb[:, :], d_d1b[:, :])
    if xl_emit is not None:
        lxl = xl_emit(xl_fm)
    else:
        lxl = p.dma(xl_fm[:, :], d_xl[:, :])

    bank_last = {i: [None] for i in range(8)}
    vrow_last = [None]
    vcol_last = [None]
    sq_last = [None]

    def vec_ln(src, dst, eps_ap, gname=None, bname=None, waits=()):
        sq = p.s(lambda e: e.activation(sq_fm[:, :], src[:, :], AF.Square),
                 [waits, sq_last[0]], tag="vln_sq")
        m1 = p.t(lambda e: e.matmul(ps[6][0:1, 0:KD], ones_col[:, :],
                                    src[:, :], start=True, stop=True),
                 [waits, ms1] + bank_last[6], tag="vln_s1")
        m2 = p.t(lambda e: e.matmul(ps[7][0:1, 0:KD], ones_col[:, :],
                                    sq_fm[:, :], start=True, stop=True),
                 [sq, ms1] + bank_last[7], tag="vln_s2")
        r1 = p.v(lambda e: e.reduce_sum(vrow[0:1, 0:1], ps[6][0:1, 0:KD],
                                        axis=mybir.AxisListType.X),
                 [m1, vrow_last[0]], tag="vln_r1")
        r2 = p.v(lambda e: e.reduce_sum(vrow[0:1, 1:2], ps[7][0:1, 0:KD],
                                        axis=mybir.AxisListType.X), [m2, r1],
                 tag="vln_r2")
        bank_last[6] = [r1]
        bank_last[7] = [r2]
        r3 = p.v(lambda e: e.tensor_scalar_mul(vrow[0:1, 0:1], vrow[0:1, 0:1],
                                               1.0 / D), [r1], tag="vln_m")
        r4 = p.v(lambda e: e.tensor_mul(vrow[0:1, 2:3], vrow[0:1, 0:1],
                                        vrow[0:1, 0:1]), [r3])
        r5 = p.v(lambda e: e.scalar_tensor_tensor(vrow[0:1, 1:2],
                                                  vrow[0:1, 1:2], 1.0 / D,
                                                  vrow[0:1, 2:3], ALU.mult,
                                                  ALU.subtract), [r4, r2])
        r6 = p.s(lambda e: e.activation(vrow[0:1, 1:2], vrow[0:1, 1:2], AF.Ln,
                                        bias=lb.cns[0:1, 0:1] if eps_ap is None
                                        else eps_ap), [r5, lb.init, msc],
                 tag="vln_l")
        r7 = p.s(lambda e: e.activation(vrow[0:1, 1:2], vrow[0:1, 1:2], AF.Exp,
                                        scale=-0.5), [r6])
        vrow_last[0] = r7
        bm = p.t(lambda e: e.matmul(ps[6][:, 0:2], ones_rowf[0:1, :],
                                    vrow[0:1, 0:2], start=True, stop=True),
                 [r7, r3, ms2] + bank_last[6], tag="vln_bc")
        bc_ = p.v(lambda e: e.tensor_copy(vcol[:, 0:2], ps[6][:, 0:2]),
                  [bm, vcol_last[0]], tag="vln_bcc")
        bank_last[6] = [bc_]
        ap_ = p.v(lambda e, dst=dst: e.tensor_scalar(
            dst[:, :], src[:, :], vcol[:, 0:1], vcol[:, 1:2],
            ALU.subtract, ALU.mult), [bc_, waits], tag="vln_ap")
        vcol_last[0] = ap_
        sq_last[0] = ap_
        out_tok = ap_
        if gname is not None:
            g1 = p.v(lambda e, dst=dst, gname=gname: e.tensor_tensor(
                dst[:, :], dst[:, :], vg[gname][:, :], ALU.mult),
                [ap_, ldt[gname]], tag="vln_g")
            g2 = p.v(lambda e, dst=dst, bname=bname: e.tensor_tensor(
                dst[:, :], dst[:, :], vg[bname][:, :], ALU.add),
                [g1, ldt[bname]], tag="vln_b")
            out_tok = g2
        return out_tok

    chunk_toks_all = [[] for _ in range(KD)]
    for nb in range(2):
        for b in range(4):
            lb.last[("st", b)] = bank_last[b]
        chunk_toks, lnall, strd = emit_ln_block(p, lb, xi, 1024 * nb, AR, 0,
                                                pitch=2048, boff=1024 * nb)
        for b in range(4):
            bank_last[b] = [strd[b]]
        for k in range(KD):
            chunk_toks_all[k].append(chunk_toks[k])

    gwsl = [sb(f"gwsl{i}", [128, D], BF16) for i in range(2)]
    gwsl_last = [None, None]
    gw_ld = {}
    go_toks = {}
    gstate = {"i": 0}

    def emit_gate_groups(n):
        for _ in range(n):
            i = gstate["i"]
            if i >= 64:
                return
            gstate["i"] = i + 1
            m, qc = i // 4, i % 4
            par = m % 2
            if qc == 0:
                gw_ld[m] = p.dma(gwsl[par][:, :], d_gw[m], [gwsl_last[par]],
                                 tag="gww")
            lw = gw_ld[m]
            bank = 2 + (qc % 2)
            mm = None
            for k in range(KD):
                mm = p.t(lambda e, k=k, qc=qc, par=par, bank=bank: e.matmul(
                    ps[bank][:, 0:512], gwsl[par][:, 128 * k:128 * (k + 1)],
                    xln_ap(k, 512 * qc, 512 * (qc + 1)),
                    start=(k == 0), stop=(k == KD - 1)),
                    [chunk_toks_all[k], lw] +
                    (bank_last[bank] if k == 0 else []), tag="gmm")
            sg = p.s(lambda e, m=m, qc=qc, bank=bank: e.activation(
                o_ap(m, 512 * qc, 512 * (qc + 1)), ps[bank][:, 0:512],
                AF.Sigmoid, bias=vg["gb"][:, m:m + 1]),
                [mm, ldt["gb"], go_toks.get((m, qc))], tag="gsig")
            bank_last[bank] = [sg]
            go_toks[(m, qc)] = sg
            if qc == 3:
                gwsl_last[par] = mm

    emit_gate_groups(4)
    ln1 = vec_ln(xl_fm, xn_fm, None, waits=[lxl])
    emit_gate_groups(3)
    ln2 = vec_ln(xn_fm, xn_fm, None, gname="ing", bname="inb", waits=[ln1])
    emit_gate_groups(3)
    xn_bf_tok = p.v(lambda e: e.tensor_copy(tn_bf[:, :], xn_fm[:, :]), [ln2],
                    tag="xnbf")

    wsl_last = [None, None]
    th_tok = None

    def vec_matmul(dst_fm, bias_fm, dram_w, nm_, src_bf, src_tok):
        toks = []
        for m in range(nm_):
            if m % 4 == 3:
                emit_gate_groups(1)
            par = m % 2
            lw = p.dma(wsl[par][:, :], dram_w[m], [wsl_last[par]], tag="vmw")
            bank = m % 2
            mm = None
            for k in range(KD):
                mm = p.t(lambda e, k=k, par=par, bank=bank: e.matmul(
                    ps[bank][:, 0:1], wsl[par][:, 128 * k:128 * (k + 1)],
                    src_bf[:, k:k + 1], start=(k == 0), stop=(k == KD - 1)),
                    [src_tok, lw] + (bank_last[bank] if k == 0 else []),
                    tag="vmm")
            wsl_last[par] = mm
            ev = p.s(lambda e, m=m, bank=bank, dst_fm=dst_fm, bias_fm=bias_fm:
                     e.activation(dst_fm[:, m:m + 1], ps[bank][:, 0:1],
                                  AF.Identity, bias=bias_fm[:, m:m + 1]),
                     [mm, toks], tag="vmev")
            bank_last[bank] = [ev]
            toks.append(ev)
        return toks

    agg_t = vec_matmul(th_fm, vg["aggb"], d_agg, KD, tn_bf, [xn_bf_tok])
    th_tok = agg_t
    cl_prev = None
    lkr_prev = None
    thp_last = None
    thr_last = None
    for step in range(4):
        lnp = vec_ln(th_fm, tn_fm, None, gname="png", bname="pnb",
                     waits=[th_tok])
        emit_gate_groups(2)
        tbf = p.v(lambda e: e.tensor_copy(tn_bf[:, :], tn_fm[:, :]), [lnp],
                  tag="tbf")
        g_toks = []
        for m in range(32):
            if m % 8 == 7:
                emit_gate_groups(1)
            par = m % 2
            lw = p.dma(wsl[par][:, :], d_d1[m], [wsl_last[par]], tag="d1w")
            bank = m % 2
            mm = None
            for k in range(KD):
                mm = p.t(lambda e, k=k, par=par, bank=bank: e.matmul(
                    ps[bank][:, 0:1], wsl[par][:, 128 * k:128 * (k + 1)],
                    tn_bf[:, k:k + 1], start=(k == 0), stop=(k == KD - 1)),
                    [tbf, lw] + (bank_last[bank] if k == 0 else []), tag="d1m")
            wsl_last[par] = mm
            ev = p.s(lambda e, m=m, bank=bank: e.activation(
                g_bf[:, m:m + 1], ps[bank][:, 0:1], AF.Gelu,
                bias=d1b_sb[:, m:m + 1]), [mm, ld1b, g_toks], tag="d1ev")
            bank_last[bank] = [ev]
            g_toks.append(ev)
        d2_toks = []
        for m2 in range(KD):
            if m2 % 4 == 3:
                emit_gate_groups(1)
            par = m2 % 2
            lw = p.dma(wsl2[par][:, :], d_d2[m2], [wsl2_last[par]], tag="d2w")
            bank = m2 % 2
            mm = None
            for k2 in range(32):
                mm = p.t(lambda e, k2=k2, par=par, bank=bank: e.matmul(
                    ps[bank][:, 0:1], wsl2[par][:, 128 * k2:128 * (k2 + 1)],
                    g_bf[:, k2:k2 + 1], start=(k2 == 0), stop=(k2 == 31)),
                    [g_toks, lw] + (bank_last[bank] if k2 == 0 else []),
                    tag="d2m")
            wsl2_last[par] = mm
            ev = p.s(lambda e, m2=m2, bank=bank: e.activation(
                thp[:, m2:m2 + 1], ps[bank][:, 0:1], AF.Copy),
                [mm, d2_toks, thp_last], tag="d2ev")
            bank_last[bank] = [ev]
            d2_toks.append(ev)
        wr = p.dma(thki_h.ap()[:, :], thp[:, :], [d2_toks, cl_prev],
                   tag="thwr")
        thp_last = wr
        cl = p.coll("AllReduce", ALU.add, PAIRS,
                    [thki_h.ap().opt()], [thko_h.ap().opt()],
                    waits=[wr, lkr_prev], tag="thar")
        cl_prev = cl
        lkr = p.dma(thr[:, :], thko_h.ap()[:, :], [cl, thr_last], tag="thrd")
        lkr_prev = lkr
        a1 = p.v(lambda e: e.tensor_tensor(thr[:, :], thr[:, :],
                                           vg["d2b"][:, :], ALU.add),
                 [lkr, ldt["d2b"]], tag="thb")
        a2 = p.v(lambda e: e.tensor_tensor(th_fm[:, :], th_fm[:, :],
                                           thr[:, :], ALU.add),
                 [a1, lnp], tag="thacc")
        thr_last = a2
        th_tok = a2
    thbf = p.v(lambda e: e.tensor_copy(tn_bf[:, :], th_fm[:, :]), [th_tok],
               tag="thbf")
    bc_t = vec_matmul(bc_fm, vg["bcb"], d_bc, KD, tn_bf, [thbf])

    emit_gate_groups(64)
    xres_last = [None, None]
    o_toks = {}
    for m in range(KD):
        for qc in range(4):
            ot = p.v(lambda e, m=m, qc=qc: e.scalar_tensor_tensor(
                o_ap(m, 512 * qc, 512 * (qc + 1)),
                o_ap(m, 512 * qc, 512 * (qc + 1)),
                bc_fm[:, m:m + 1], xln_ap(m, 512 * qc, 512 * (qc + 1)),
                ALU.mult, ALU.add), [go_toks[(m, qc)], bc_t,
                                     chunk_toks_all[m]], tag="gate_o")
            o_toks[(m, qc)] = ot

    for nb in range(2):
        stat_mms = []
        sq_tl = {}
        for k in range(KD):
            otk = [o_toks[(k, 2 * nb)], o_toks[(k, 2 * nb + 1)]]
            for sub in range(2):
                sqo = p.s(lambda e, k=k, nb=nb, sub=sub: e.activation(
                    lb.sqbf[sub][:, 0:512],
                    o_ap(k, 1024 * nb + 512 * sub,
                         1024 * nb + 512 * (sub + 1)), AF.Square),
                    [otk, sq_tl.get(sub)], tag="osq")
                m1 = p.t(lambda e, k=k, nb=nb, sub=sub: e.matmul(
                    ps[0 + sub][0:1, 0:512], lb.ones_colb[:, :],
                    o_ap(k, 1024 * nb + 512 * sub,
                         1024 * nb + 512 * (sub + 1)),
                    start=(k == 0), stop=(k == KD - 1)),
                    [otk, lb.init] + (bank_last[0 + sub] if k == 0 else []),
                    tag="ost1")
                m2 = p.t(lambda e, sub=sub, k=k: e.matmul(
                    ps[2 + sub][0:1, 0:512], lb.ones_colb[:, :],
                    lb.sqbf[sub][:, 0:512],
                    start=(k == 0), stop=(k == KD - 1)),
                    [sqo, lb.init] + (bank_last[2 + sub] if k == 0 else []),
                    tag="ost2")
                sq_tl[sub] = m2
                stat_mms += [m1, m2]
        bct = []
        R = lb.srow
        for sub in range(2):
            m_ = p.v(lambda e, sub=sub: e.tensor_scalar_mul(
                R[0:1, 0:512], ps[0 + sub][0:1, 0:512], 1.0 / D),
                [stat_mms, lb.last.get("rows")], tag="ornm")
            ms_ = p.v(lambda e: e.tensor_mul(R[0:1, 512:1024], R[0:1, 0:512],
                                             R[0:1, 0:512]), [m_])
            var_ = p.v(lambda e, sub=sub: e.scalar_tensor_tensor(
                R[0:1, 512:1024], ps[2 + sub][0:1, 0:512], 1.0 / D,
                R[0:1, 512:1024], ALU.mult, ALU.subtract), [ms_, stat_mms])
            lnv = p.s(lambda e: e.activation(R[0:1, 512:1024],
                                             R[0:1, 512:1024], AF.Ln,
                                             bias=cns2[0:1, 0:1]),
                      [var_, msc], tag="olnv")
            rstd = p.s(lambda e: e.activation(R[0:1, 512:1024],
                                              R[0:1, 512:1024], AF.Exp,
                                              scale=-0.5), [lnv])
            c1c = p.v(lambda e: e.tensor_copy(lb.srow_bf[0:1, 0:512],
                                              R[0:1, 512:1024]),
                      [rstd, lb.last.get("srowbf")], tag="oc1")
            c0c = p.v(lambda e: e.scalar_tensor_tensor(
                lb.srow_bf[0:1, 512:1024], R[0:1, 0:512], -1.0,
                R[0:1, 512:1024], ALU.mult, ALU.mult),
                [rstd, m_, lb.last.get("srowbf")], tag="oc0")
            bm1 = p.t(lambda e: e.matmul(ps[4][:, 0:512], lb.ones_rowb[0:1, :],
                                         lb.srow_bf[0:1, 0:512], start=True,
                                         stop=True),
                      [c1c, lb.init, lb.last.get("psbc")], tag="obc1")
            bc1 = p.v(lambda e, sub=sub: e.tensor_copy(lb.cb1[sub][:, :],
                                                       ps[4][:, 0:512]),
                      [bm1, lb.last.get(("cb1", sub))], tag="obc1c")
            bm2 = p.t(lambda e: e.matmul(ps[4][:, 0:512], lb.ones_rowb[0:1, :],
                                         lb.srow_bf[0:1, 512:1024], start=True,
                                         stop=True), [c0c, bc1], tag="obc2")
            bc0 = p.v(lambda e, sub=sub: e.tensor_copy(lb.cb0[sub][:, :],
                                                       ps[4][:, 0:512]),
                      [bm2, lb.last.get(("cb0", sub))], tag="obc0c")
            lb.last["psbc"] = bc0
            lb.last["srowbf"] = [bm1, bm2]
            lb.last["rows"] = [var_, c0c, rstd, c1c]
            bank_last[0 + sub] = [m_]
            bank_last[2 + sub] = [var_]
            bct.append((bc1, bc0))
        apl = []
        for k in range(KD):
            for sub in range(2):
                bc1, bc0 = bct[sub]
                qc = 2 * nb + sub
                pr = sub
                otk = [o_toks[(k, 2 * nb)], o_toks[(k, 2 * nb + 1)]]
                lx = p.dma(xres[pr][:, :], xi[128 * k:128 * (k + 1),
                                              512 * qc:512 * (qc + 1)],
                           [xres_last[pr]], tag="oxld")
                t1 = p.v(lambda e, k=k, nb=nb, sub=sub: e.tensor_tensor(
                    oscr[:, 0:512],
                    o_ap(k, 1024 * nb + 512 * sub,
                         1024 * nb + 512 * (sub + 1)),
                    lb.cb1[sub][:, :], ALU.mult), [bc1, otk], tag="oap1")
                t2 = p.v(lambda e, sub=sub: e.tensor_tensor(
                    oscr[:, 0:512], oscr[:, 0:512], lb.cb0[sub][:, :],
                    ALU.add), [t1, bc0], tag="oap2")
                t3 = p.v(lambda e, k=k: e.tensor_scalar(
                    oscr[:, 0:512], oscr[:, 0:512], vg["outg"][:, k:k + 1],
                    vg["outb"][:, k:k + 1], ALU.mult, ALU.add),
                    [t2, ldt["outg"], ldt["outb"]], tag="oap3")
                t4 = p.v(lambda e, pr=pr: e.tensor_tensor(
                    xres[pr][:, :], xres[pr][:, :], oscr[:, 0:512], ALU.add),
                    [t3, lx], tag="oap4")
                st = p.dmag(xo[128 * k:128 * (k + 1),
                               512 * qc:512 * (qc + 1)], xres[pr][:, :],
                            [t4], tag="oxst")
                xres_last[pr] = st
                apl.append(st)
        for sub in range(2):
            lb.last[("cb1", sub)] = apl
            lb.last[("cb0", sub)] = apl
    return


def p5f_inputs(inp, h):
    def fm(name, n=KD):
        return np.ascontiguousarray(
            np.asarray(inp[name], np.float32).reshape(n, 128).T)
    return {
        "ing": fm("L1_ffn_in_gamma"), "inb": fm("L1_ffn_in_beta"),
        "png": fm("L1_ffn_pn_gamma"), "pnb": fm("L1_ffn_pn_beta"),
        "outg": fm("L1_ffn_out_gamma"), "outb": fm("L1_ffn_out_beta"),
        "agg": tile_lhsT(np.asarray(inp["L1_ffn_agg_w"], np.float32)),
        "aggb": fm("L1_ffn_agg_b"),
        "d1w": np.ascontiguousarray(tile_lhsT(
            np.asarray(inp["L1_ffn_d1_w"], np.float32))[32 * h:32 * (h + 1)]),
        "d1b": np.ascontiguousarray(
            fm("L1_ffn_d1_b", 64)[:, 32 * h:32 * (h + 1)]),
        "d2w": np.ascontiguousarray(tile_lhsT(
            np.asarray(inp["L1_ffn_d2_w"],
                       np.float32))[:, :, 4096 * h:4096 * (h + 1)]),
        "d2b": fm("L1_ffn_d2_b"),
        "bcw": tile_lhsT(np.asarray(inp["L1_ffn_bc_w"], np.float32)),
        "bcb": fm("L1_ffn_bc_b"),
        "gw": tile_lhsT(np.asarray(inp["L1_ffn_gate_w"], np.float32)),
        "gb": fm("L1_ffn_gate_b"),
    }



def emit_p6f(nc, ctx, p, dr, pfx=""):
    SCALE = 1.0 / float(np.sqrt(128.0))

    xi = dr["xi"]
    d_wq, d_wk, d_wv, d_wo = dr["wq"], dr["wk"], dr["wv"], dr["wo"]
    d_cq, d_sq = dr["cosq"], dr["sinq"]
    xo = dr["xo"]
    dq = dr["dq"]
    dko_h, dkg_h = dr["dko_h"], dr["dkg_h"]
    dvo_h, dvg_h = dr["dvo_h"], dr["dvg_h"]
    dko, dkg = dko_h.ap(), dkg_h.ap()
    dvo, dvg = dvo_h.ap(), dvg_h.ap()

    def sb(name, shape, dtype=F32):
        return ctx.enter_context(nc.sbuf_tensor(pfx + name, shape, dtype))

    AR = sb("AR", [128, 65536], BF16)

    def kth(hp, lo, hi):
        return AR[:, 32768 + hp * 4096 + lo: 32768 + hp * 4096 + hi]

    def vth(hp, lo, hi):
        return AR[:, 40960 + hp * 4096 + lo: 40960 + hp * 4096 + hi]

    def qth(hp, lo, hi):
        return AR[:, 49152 + hp * 2048 + lo: 49152 + hp * 2048 + hi]

    def pt_ap(pi, lo, hi):
        return AR[:, 53248 + pi * 512 + lo: 53248 + pi * 512 + hi]

    dinv_b = AR[:, 54784: 55296]

    def att_ap(h, qc):
        return AR[:, 2048 * h + 512 * qc: 2048 * h + 512 * (qc + 1)]

    def xln_ap(k, lo, hi):
        return AR[:, 2048 * k + lo: 2048 * k + hi]

    wsl = [sb(f"wsl{i}", [128, D], BF16) for i in range(2)]
    wvres = sb("wvres", [128, KD * 512], BF16)
    prst = [sb(f"prst{i}", [128, 512], BF16) for i in range(2)]
    ctb = [sb(f"ctb{i}", [128, 512], BF16) for i in range(2)]
    stb = [sb(f"stb{i}", [128, 512], BF16) for i in range(2)]
    rsc = [sb(f"rsc{i}", [128, 512]) for i in range(2)]
    drow = sb("drow", [1, 512])
    drow_bf = sb("drow_bf", [1, 512], BF16)
    ones_rowb = sb("ones_rowb", [1, 128], BF16)
    ones_colb = sb("ones_colb", [128, 1], BF16)
    xres = [sb(f"xres{i}", [128, 512]) for i in range(2)]

    ps = [ctx.enter_context(nc.psum_tensor(pfx + f"ps{i}", [128, 512], F32))
          for i in range(8)]

    lb = LnBufs(nc, ctx, pfx + "ln_",
                ps_st=[ps[0][0:1, 0:512], ps[1][0:1, 0:512],
                       ps[2][0:1, 0:512], ps[3][0:1, 0:512]],
                ps_bc=ps[4][:, 0:512])
    lb.emit_init(p)

    ms1 = p.v(lambda e: e.memset(ones_rowb[:, :], 1.0))
    ms2 = p.v(lambda e: e.memset(ones_colb[:, :], 1.0))

    bank_last = {i: [None] for i in range(8)}

    chunk_toks_all = [[] for _ in range(KD)]
    for nb in range(2):
        for b in range(4):
            lb.last[("st", b)] = bank_last[b]
        chunk_toks, lnall, strd = emit_ln_block(p, lb, xi, 1024 * nb, AR, 0,
                                                pitch=2048, boff=1024 * nb)
        for b in range(4):
            bank_last[b] = [strd[b]]
        for k in range(KD):
            chunk_toks_all[k].append(chunk_toks[k])

    v_store_toks = []
    lws = []
    for k in range(KD):
        lws.append(p.dma(wvres[:, 512 * k:512 * (k + 1)], d_wv[k],
                         tag="wvld"))
    prst_last = [None, None]
    wv_last = None
    for t in range(16):
        bank = 5 + (t % 2)
        mm = None
        for k in range(KD):
            mm = p.t(lambda e, k=k, t=t, bank=bank: e.matmul(
                ps[bank][:, 0:512],
                xln_ap(k, 128 * t, 128 * (t + 1)),
                wvres[:, 512 * k:512 * (k + 1)],
                start=(k == 0), stop=(k == KD - 1)),
                [chunk_toks_all[k], lws[k]] +
                (bank_last[bank] if k == 0 else []), tag="vmm")
        pr = t % 2
        ev = p.s(lambda e, pr=pr, bank=bank: e.activation(
            prst[pr][:, :], ps[bank][:, 0:512], AF.Copy),
            [mm, prst_last[pr]], tag="vev")
        bank_last[bank] = [ev]
        sts = []
        for j in range(4):
            sts.append(p.dmag(dvo[j, t], prst[pr][:, 128 * j:128 * (j + 1)],
                              [ev], tag="vst"))
        prst_last[pr] = sts
        v_store_toks += sts
        wv_last = mm

    wsl_last = [None, None]
    st_by_head_q = {}
    st_by_head_k = {}
    ctb_last = [None, None]
    stb_last = [None, None]
    rsc_last = [None, None]

    def proj_rope(dram_w, dram_out, nblocks, ntok, d_cos, d_sin, stmap, tag):
        nonlocal wsl_last
        for m in range(nblocks):
            par = m % 2
            lw = p.dma(wsl[par][:, :], dram_w[m], [wsl_last[par]], tag=tag + "w")
            stoks = []
            mm = None
            for qc in range(ntok // 512):
                bank = 5 + (qc % 2)
                for k in range(KD):
                    mm = p.t(lambda e, k=k, qc=qc, par=par, bank=bank: e.matmul(
                        ps[bank][:, 0:512], wsl[par][:, 128 * k:128 * (k + 1)],
                        xln_ap(k, 512 * qc, 512 * (qc + 1)),
                        start=(k == 0), stop=(k == KD - 1)),
                        [chunk_toks_all[k], lw] +
                        (bank_last[bank] if k == 0 else []), tag=tag)
                pr = qc % 2
                lc = p.dma(ctb[pr][:, :], d_cos[:, 512 * qc:512 * (qc + 1)],
                           [ctb_last[pr]], tag=tag + "ct")
                ls = p.dma(stb[pr][:, :], d_sin[:, 512 * qc:512 * (qc + 1)],
                           [stb_last[pr]], tag=tag + "st")
                c1 = p.v(lambda e, pr=pr, bank=bank: e.tensor_copy(
                    rsc[pr][0:64, 0:512], ps[bank][64:128, 0:512]),
                    [mm, rsc_last[pr]], tag=tag + "r1")
                c2 = p.v(lambda e, pr=pr, bank=bank: e.tensor_copy(
                    rsc[pr][64:128, 0:512], ps[bank][0:64, 0:512]),
                    [c1], tag=tag + "r2")
                m2_ = p.v(lambda e, pr=pr: e.tensor_tensor(
                    rsc[pr][:, 0:512], rsc[pr][:, 0:512], stb[pr][:, :],
                    ALU.mult), [c2, ls], tag=tag + "r3")
                m1_ = p.v(lambda e, pr=pr, bank=bank: e.tensor_tensor(
                    ps[bank][:, 0:512], ps[bank][:, 0:512], ctb[pr][:, :],
                    ALU.mult), [m2_, lc, mm], tag=tag + "r4")
                a1 = p.v(lambda e, pr=pr, bank=bank: e.tensor_tensor(
                    prst[pr][:, :], ps[bank][:, 0:512], rsc[pr][:, 0:512],
                    ALU.add), [m1_, prst_last[pr]], tag=tag + "r5")
                st = p.dmag(dram_out[128 * m:128 * (m + 1),
                                     512 * qc:512 * (qc + 1)],
                            prst[pr][:, :], [a1], tag=tag + "sto")
                bank_last[bank] = [a1]
                prst_last[pr] = st
                ctb_last[pr] = m1_
                stb_last[pr] = m2_
                rsc_last[pr] = a1
                stoks.append(st)
            stmap[m] = stoks
            wsl_last[par] = mm

    proj_rope(d_wq, dq, KD, TOWN, d_cq, d_sq, st_by_head_q, "qp")
    proj_rope(d_wk, dko, 4, TOWN, d_cq, d_sq, st_by_head_k, "kp")

    coll_k = p.coll("AllGather", ALU.bypass, PAIRS,
                    [dko_h.ap().opt()], [dkg_h.ap().opt()],
                    waits=[_flat(list(st_by_head_k.values()))], tag="agk")
    coll_v = p.coll("AllGather", ALU.bypass, PAIRS,
                    [dvo_h.ap().opt()], [dvg_h.ap().opt()],
                    waits=[v_store_toks], tag="agv")

    all_proj = _flat(list(st_by_head_q.values()) + list(st_by_head_k.values())) \
        + v_store_toks
    ar_last = {"k": [None, None], "v": [None, None], "q": [None, None]}
    pt_last = [None, None, None]
    att_last = {}
    dinv_last = None
    drow_last = None
    drow_bf_last = None
    den_pend = None
    first_waits = all_proj

    for g in range(4):
        gp = g % 2
        lkA = p.dma(kth(gp, 0, 2048), dkg[0, 128 * g:128 * (g + 1), :],
                    [coll_k, ar_last["k"][gp],
                     first_waits if g < 2 else None], tag="kld")
        lkB = p.dma(kth(gp, 2048, 4096), dkg[1, 128 * g:128 * (g + 1), :],
                    [coll_k], tag="kld")
        lk = [lkA, lkB]
        lvs = []
        for t in range(32):
            lvs.append(p.dma(vth(gp, 128 * t, 128 * (t + 1)),
                             dvg[t // 16, g, t % 16],
                             [coll_v, ar_last["v"][gp],
                              first_waits if g < 2 else None], tag="vld"))
        kreaders = []
        vreaders = []
        for hh in range(4):
            h = 4 * g + hh
            hp = h % 2
            lq = p.dma(qth(hp, 0, 2048), dq[128 * h:128 * (h + 1), :],
                       [st_by_head_q[h], ar_last["q"][hp],
                        first_waits if h < 2 else None], tag="qld")
            qreaders = []
            for qc in range(4):
                pvbank = 2 + (qc % 2)
                dbank = 4 + (qc % 2)
                dmm = pvm = None
                pend = None

                def make_den(h, qc, dmm, pvm, pvbank, dbank):
                    def go():
                        nonlocal drow_last, drow_bf_last, dinv_last
                        rcp = p.v(lambda e, dbank=dbank: e.reciprocal(
                            drow[0:1, :], ps[dbank][0:1, 0:512]),
                            [dmm, drow_last], tag="rcp")
                        rcb = p.v(lambda e: e.tensor_copy(drow_bf[0:1, :],
                                                          drow[0:1, :]),
                                  [rcp, drow_bf_last], tag="rcb")
                        bank_last[dbank] = [rcp]
                        bcm = p.t(lambda e: e.matmul(ps[6][:, 0:512],
                                                     ones_rowb[0:1, :],
                                                     drow_bf[0:1, :],
                                                     start=True, stop=True),
                                  [rcb, ms1] + bank_last[6], tag="dbc")
                        dcp = p.v(lambda e: e.tensor_copy(dinv_b,
                                                          ps[6][:, 0:512]),
                                  [bcm, dinv_last], tag="dcp")
                        bank_last[6] = [dcp]
                        drow_last = rcb
                        drow_bf_last = bcm
                        amul = p.v(lambda e, h=h, qc=qc, pvbank=pvbank:
                                   e.tensor_tensor(
                            att_ap(h, qc), ps[pvbank][:, 0:512], dinv_b,
                            ALU.mult),
                            [pvm, dcp, att_last.get((h, qc))], tag="attdiv")
                        dinv_last = amul
                        bank_last[pvbank] = [amul]
                        att_last[(h, qc)] = amul
                    return go

                def mk_pend(t, pi, ex, last):
                    def go():
                        dm = p.t(lambda e, pi=pi, dbank=dbank, first=(t == 0),
                                 last=last: e.matmul(
                            ps[dbank][0:1, 0:512], ones_colb[:, :],
                            pt_ap(pi, 0, 512), start=first, stop=last),
                            [ex, ms2] + (bank_last[dbank] if t == 0 else []),
                            tag="dmm")
                        pv = p.t(lambda e, pi=pi, t=t, gp=gp, pvbank=pvbank,
                                 first=(t == 0), last=last: e.matmul(
                            ps[pvbank][:, 0:512],
                            vth(gp, 128 * t, 128 * (t + 1)),
                            pt_ap(pi, 0, 512), start=first, stop=last),
                            [ex, lvs[t]] +
                            (bank_last[pvbank] if t == 0 else []),
                            tag="pvm")
                        vreaders.append(pv)
                        pt_last[pi] = [dm, pv]
                        return dm, pv
                    return go

                for t in range(32):
                    sbank = t % 2
                    smm = p.t(lambda e, t=t, gp=gp, hp=hp, qc=qc, sbank=sbank:
                              e.matmul(ps[sbank][:, 0:512],
                                       kth(gp, 128 * t, 128 * (t + 1)),
                                       qth(hp, 512 * qc, 512 * (qc + 1)),
                                       start=True, stop=True),
                              [lk, lq] + bank_last[sbank], tag="smm")
                    kreaders.append(smm)
                    qreaders.append(smm)
                    pi = t % 3
                    ex = p.s(lambda e, pi=pi, sbank=sbank: e.activation(
                        pt_ap(pi, 0, 512), ps[sbank][:, 0:512], AF.Exp,
                        scale=SCALE), [smm, pt_last[pi]], tag="exp")
                    bank_last[sbank] = [ex]
                    if t == 4 and den_pend is not None:
                        den_pend()
                        den_pend = None
                    if pend is not None:
                        dmm, pvm = pend()
                    pend = mk_pend(t, pi, ex, t == 31)
                dmm, pvm = pend()
                den_pend = make_den(h, qc, dmm, pvm, pvbank, dbank)
            ar_last["q"][hp] = qreaders[-1]
        ar_last["k"][gp] = kreaders[-1]
        ar_last["v"][gp] = vreaders
    if den_pend is not None:
        den_pend()
        den_pend = None

    xres_last = [None, None]
    for m in range(KD):
        par = m % 2
        lw = p.dma(wsl[par][:, :], d_wo[m], [wsl_last[par]], tag="wow")
        mm = None
        for qc in range(4):
            bank = 5 + (qc % 2)
            for k in range(KD):
                mm = p.t(lambda e, k=k, qc=qc, par=par, bank=bank: e.matmul(
                    ps[bank][:, 0:512], wsl[par][:, 128 * k:128 * (k + 1)],
                    att_ap(k, qc), start=(k == 0), stop=(k == KD - 1)),
                    [att_last[(k, qc)], lw] +
                    (bank_last[bank] if k == 0 else []), tag="omm")
            pr = qc % 2
            lx = p.dma(xres[pr][:, :], xi[128 * m:128 * (m + 1),
                                          512 * qc:512 * (qc + 1)],
                       [xres_last[pr]], tag="oxld")
            yv = p.v(lambda e, pr=pr, bank=bank: e.tensor_tensor(
                xres[pr][:, :], xres[pr][:, :], ps[bank][:, 0:512], ALU.add),
                [lx, mm], tag="oadd")
            st = p.dmag(xo[128 * m:128 * (m + 1), 512 * qc:512 * (qc + 1)],
                        xres[pr][:, :], [yv], tag="oxst")
            bank_last[bank] = [yv]
            xres_last[pr] = st
        wsl_last[par] = mm
    return


def p6f_inputs(inp, h):
    own = cols_of(h)
    invf = 1.0 / (10000.0 ** (np.arange(0, 128, 2) / 128.0))
    pos = np.arange(S)
    fr = pos[None, :] * np.concatenate([invf, invf])[:, None]
    cos = np.cos(fr).astype(np.float32)
    sin = np.sin(fr).astype(np.float32)
    sin_signed = sin.copy()
    sin_signed[0:64] *= -1.0
    return {
        "wq": tile_lhsT(np.asarray(inp["L1_attn_b_wq"], np.float32).T),
        "wk": tile_lhsT(np.asarray(inp["L1_attn_b_wk"], np.float32).T),
        "wv": to_bf(np.ascontiguousarray(
            np.asarray(inp["L1_attn_b_wv"], np.float32).T).reshape(KD, 128, 512)),
        "wo": tile_lhsT(np.asarray(inp["L1_attn_b_wo"], np.float32).T),
        "cosq": to_bf(cos[:, own]),
        "sinq": to_bf(sin_signed[:, own]),
    }


# version 12 (final)
# speedup vs baseline: 1.1273x; 1.1273x over previous
import numpy as np
import ml_dtypes
from contextlib import ExitStack

import concourse.bass as bass
import concourse.mybir as mybir
from concourse.bass_utils import run_bass_kernel_spmd
from concourse.bass_core_rust import IndirectOffsetOnAxis

F32 = mybir.dt.float32
BF16 = mybir.dt.bfloat16
I32 = mybir.dt.int32
AF = mybir.ActivationFunctionType
ALU = mybir.AluOpType
BFNP = ml_dtypes.bfloat16

D = 2048
KD = D // 128
S = 4096
TOWN = 2048
NCORES = 8
PI = float(np.pi)
PAIRS = [[0, 1], [2, 3], [4, 5], [6, 7]]

ENGS = ("sync", "tensor", "vector", "scalar", "gpsimd")
_ESEM = {"sync": "e_sy", "tensor": "e_t", "vector": "e_v", "scalar": "e_s",
         "gpsimd": "e_g"}


class _Op:
    __slots__ = ("eng", "emit", "waits", "sem", "by", "marked", "val", "tag")

    def __init__(self, eng, emit, waits, sem, by, tag=""):
        self.eng, self.emit, self.waits = eng, emit, waits
        self.sem, self.by = sem, by
        self.marked = by == 16
        self.val = None
        self.tag = tag


def _flat(waits):
    out = []
    for w in waits:
        if w is None:
            continue
        if isinstance(w, (list, tuple)):
            out.extend(_flat(w))
        else:
            out.append(w)
    return out


class Prog:
    NQ = 8

    def __init__(self, nc):
        self.nc = nc
        self.ops = {e: [] for e in ENGS}
        self.sem_names = set()
        self.qidx = {"q_s": 0, "q_g": 0, "q_c": 0}
        self._phase_fr = None
        self._phase_seen = set()

    def phase_break(self):
        fr = []
        for e in ENGS:
            seen = set()
            for op in reversed(self.ops[e]):
                if op.sem not in seen:
                    seen.add(op.sem)
                    fr.append(op)
        self._phase_fr = fr
        self._phase_seen = set()

    def _add(self, eng, emit, waits, sem, by, tag=""):
        ws = _flat(waits)
        if self._phase_fr and eng not in self._phase_seen:
            self._phase_seen.add(eng)
            ws = ws + self._phase_fr
        for w in ws:
            w.marked = True
        op = _Op(eng, emit, ws, sem, by, tag)
        self.ops[eng].append(op)
        self.sem_names.add(sem)
        return op

    def t(self, emit, waits=(), tag=""):
        return self._add("tensor", emit, waits, "e_t", 1, tag)

    def v(self, emit, waits=(), tag=""):
        return self._add("vector", emit, waits, "e_v", 1, tag)

    def s(self, emit, waits=(), tag=""):
        return self._add("scalar", emit, waits, "e_s", 1, tag)

    def g(self, emit, waits=(), tag=""):
        return self._add("gpsimd", emit, waits, "e_g", 1, tag)

    def _qsem(self, q):
        i = self.qidx[q]
        self.qidx[q] = i + 1
        return f"{q}{i % self.NQ}"

    def dma(self, out, in_, waits=(), tag="", slow=False):
        if slow:
            emit = (lambda e, o=out, i=in_: e.dma_start(
                out=o, in_=i, allow_slow_non_contiguous=True))
        else:
            emit = (lambda e, o=out, i=in_: e.dma_start(out=o, in_=i))
        return self._add("sync", emit, waits, self._qsem("q_s"), 16, tag)

    def dmag(self, out, in_, waits=(), tag=""):
        return self._add("gpsimd", (lambda e, o=out, i=in_: e.dma_start(out=o, in_=i)),
                         waits, self._qsem("q_g"), 16, tag)

    def dmai(self, out, in_2d, idx_col, waits=(), tag=""):
        return self._add(
            "gpsimd",
            (lambda e, o=out, i=in_2d, x=idx_col: e.indirect_dma_start(
                o, None, i, IndirectOffsetOnAxis(x, 0))),
            waits, self._qsem("q_g"), 16, tag)

    def coll(self, kind, op, groups, ins, outs, waits=(), tag=""):
        def emit(e, kind=kind, op=op, groups=groups, ins=ins, outs=outs):
            return e.collective_compute(kind, op, replica_groups=groups,
                                        ins=ins, outs=outs)
        o = self._add("gpsimd", emit, waits, self._qsem("q_c"), 1, tag)
        o.marked = True
        return o

    SERIAL = ("vector", "scalar", "gpsimd")

    def finalize(self, ctx):
        nc = self.nc
        for e in self.SERIAL:
            for op in self.ops[e]:
                if not (op.by == 16 or op.sem == "q_c"):
                    op.marked = True
            self.sem_names.add(_ESEM[e] if e != "gpsimd" else "e_g")
        handles = {n: ctx.enter_context(nc.semaphore(n))
                   for n in sorted(self.sem_names)}
        counts = {n: 0 for n in self.sem_names}
        self._auto = {}
        for eng in ENGS:
            for op in self.ops[eng]:
                if op.marked:
                    if op.sem[0] == "q":
                        self._auto[id(op)] = (op.sem, counts[op.sem])
                    counts[op.sem] += op.by
                    op.val = counts[op.sem]
        block = ctx.enter_context(nc.Block())

        def mkbody(engname, tail):
            ops = self.ops[engname]
            serial = engname in self.SERIAL

            def body(eng):
                satisfied = {}
                prev_compute_val = 0
                for op in ops:
                    if serial and prev_compute_val and op.by == 1:
                        sem = _ESEM[engname]
                        if satisfied.get(sem, 0) < prev_compute_val:
                            eng.wait_ge(handles[sem], prev_compute_val)
                            satisfied[sem] = prev_compute_val
                    aw = self._auto.get(id(op))
                    if aw is not None and aw[1] > 0 and \
                            satisfied.get(aw[0], 0) < aw[1]:
                        eng.wait_ge(handles[aw[0]], aw[1])
                        satisfied[aw[0]] = aw[1]
                    for w in op.waits:
                        if w.eng == op.eng and w.by == 1 and w.sem == _ESEM[op.eng]:
                            continue
                        assert w.val is not None, (w.tag, "unmarked dep")
                        if satisfied.get(w.sem, 0) < w.val:
                            eng.wait_ge(handles[w.sem], w.val)
                            satisfied[w.sem] = w.val
                    inst = op.emit(eng)
                    if op.marked:
                        inst.then_inc(handles[op.sem], op.by)
                        if op.by == 1 and op.sem == _ESEM[engname]:
                            prev_compute_val = op.val
                if tail:
                    for qn in sorted(counts):
                        if qn[0] == "q" and counts[qn] > 0 and \
                                satisfied.get(qn, 0) < counts[qn]:
                            eng.wait_ge(handles[qn], counts[qn])
            return body

        for engname in ENGS:
            getattr(block, engname)(mkbody(engname, engname == "gpsimd"))



def cols_of(h):
    return np.r_[0:1024, 3072:4096] if h == 0 else np.r_[1024:3072]


def to_bf(x):
    return np.ascontiguousarray(np.asarray(x, dtype=np.float32).astype(BFNP))


def tile_lhsT(w):
    din, dout = w.shape
    nk, nm = din // 128, dout // 128
    img = np.ascontiguousarray(
        w.reshape(nk, 128, nm, 128).transpose(2, 1, 0, 3).reshape(nm, 128, nk * 128))
    return to_bf(img)



class LnBufs:

    def __init__(self, nc, ctx, name, ps_st, ps_bc):
        def sb(n, shape, dtype=F32):
            return ctx.enter_context(nc.sbuf_tensor(name + n, shape, dtype))
        self.xst = [sb(f"xst{i}", [128, 512]) for i in range(2)]
        self.sqbf = [sb(f"sqbf{i}", [128, 512], BF16) for i in range(2)]
        self.srow = sb("srow", [1, 1024])
        self.srow_bf = sb("srow_bf", [1, 1024], BF16)
        self.cb1 = [sb(f"cb1_{i}", [128, 512], BF16) for i in range(2)]
        self.cb0 = [sb(f"cb0_{i}", [128, 512], BF16) for i in range(2)]
        self.fscr = sb("fscr", [128, 512])
        self.ones_colb = sb("ones_colb", [128, 1], BF16)
        self.ones_rowb = sb("ones_rowb", [1, 128], BF16)
        self.cns = sb("cns", [1, 1])
        self.ps_st = ps_st
        self.ps_bc = ps_bc
        self.init = None
        self.last = {}

    def emit_init(self, p, eps=1e-5):
        m1 = p.v(lambda e: e.memset(self.ones_colb[:, :], 1.0))
        m2 = p.v(lambda e: e.memset(self.ones_rowb[:, :], 1.0))
        m3 = p.v(lambda e: e.memset(self.cns[:, :], eps))
        self.init = [m1, m2, m3]


def emit_ln_block(p, lb, x_dram, col0, arena, acol0, eps=1e-5, pitch=1024,
                  boff=0, load_fn=None):
    L = lb.last
    stat_toks = []
    streaders = {}
    cast_toks = [None] * KD
    for k in range(KD):
        cts = []
        for sub in range(2):
            si = sub
            if load_fn is not None:
                ld = load_fn(lb.xst[si][:, :], k, (col0 + 512 * sub) // 512,
                             [L.get(("xst", si))])
            else:
                ld = p.dma(lb.xst[si][:, :],
                           x_dram[128 * k:128 * (k + 1),
                                  col0 + 512 * sub:col0 + 512 * (sub + 1)],
                           [L.get(("xst", si))], tag="lnld")
            sq = p.s(lambda e, si=si: e.activation(
                lb.sqbf[si][:, :], lb.xst[si][:, :], AF.Square),
                [ld, L.get(("sq", si))], tag="lnsq")
            av = arena[:, acol0 + pitch * k + boff + 512 * sub:
                       acol0 + pitch * k + boff + 512 * (sub + 1)]
            cast = p.v(lambda e, av=av, si=si: e.tensor_copy(av, lb.xst[si][:, :]),
                       [ld, L.get(("ar", k))], tag="lncast")
            cts.append(cast)
            L[("xst", si)] = [sq, cast]
            m1 = p.t(lambda e, av=av, sub=sub, k=k: e.matmul(
                lb.ps_st[sub][0:1, :], lb.ones_colb[:, :], av,
                start=(k == 0), stop=(k == KD - 1)),
                [cast, lb.init] + ([L.get(("st", sub))] if k == 0 else []),
                tag="st1")
            m2 = p.t(lambda e, si=si, sub=sub, k=k: e.matmul(
                lb.ps_st[2 + sub][0:1, :], lb.ones_colb[:, :],
                lb.sqbf[si][:, :],
                start=(k == 0), stop=(k == KD - 1)),
                [sq, lb.init] + ([L.get(("st", 2 + sub))] if k == 0 else []),
                tag="st2")
            L[("sq", si)] = [m2]
            stat_toks += [m1, m2]
        cast_toks[k] = cts
    R = lb.srow
    bc_toks = []
    for sub in range(2):
        sm = stat_toks
        r0, r1, r2, r3 = (512 * 0, 512 * 0 + 512), None, None, None
        m_ = p.v(lambda e, sub=sub: e.tensor_scalar_mul(
            R[0:1, 0:512], lb.ps_st[sub][0:1, :], 1.0 / D),
            [sm, L.get("rows")], tag="lnrows")
        ms_ = p.v(lambda e: e.tensor_mul(R[0:1, 512:1024], R[0:1, 0:512],
                                         R[0:1, 0:512]), [m_])
        var_ = p.v(lambda e, sub=sub: e.scalar_tensor_tensor(
            R[0:1, 512:1024], lb.ps_st[2 + sub][0:1, :], 1.0 / D,
            R[0:1, 512:1024], ALU.mult, ALU.subtract), [ms_, sm])
        lnv = p.s(lambda e: e.activation(R[0:1, 512:1024],
                                         R[0:1, 512:1024], AF.Ln,
                                         bias=lb.cns[0:1, 0:1]),
                  [var_, lb.init])
        rstd = p.s(lambda e: e.activation(R[0:1, 512:1024], R[0:1, 512:1024],
                                          AF.Exp, scale=-0.5), [lnv])
        c1c = p.v(lambda e: e.tensor_copy(lb.srow_bf[0:1, 0:512],
                                          R[0:1, 512:1024]),
                  [rstd, L.get("srowbf")])
        c0c = p.v(lambda e: e.scalar_tensor_tensor(
            lb.srow_bf[0:1, 512:1024], R[0:1, 0:512], -1.0, R[0:1, 512:1024],
            ALU.mult, ALU.mult), [rstd, m_, L.get("srowbf")])
        bm1 = p.t(lambda e: e.matmul(lb.ps_bc[:, 0:512], lb.ones_rowb[0:1, :],
                                     lb.srow_bf[0:1, 0:512], start=True,
                                     stop=True), [c1c, lb.init, L.get("psbc")])
        bc1 = p.v(lambda e, sub=sub: e.tensor_copy(lb.cb1[sub][:, :],
                                                   lb.ps_bc[:, 0:512]),
                  [bm1, L.get(("cb1", sub))])
        bm2 = p.t(lambda e: e.matmul(lb.ps_bc[:, 0:512], lb.ones_rowb[0:1, :],
                                     lb.srow_bf[0:1, 512:1024], start=True,
                                     stop=True), [c0c, bc1])
        bc0 = p.v(lambda e, sub=sub: e.tensor_copy(lb.cb0[sub][:, :],
                                                   lb.ps_bc[:, 0:512]),
                  [bm2, L.get(("cb0", sub))])
        L["psbc"] = bc0
        L["srowbf"] = [bm1, bm2]
        L[("st", sub)] = [m_]
        L[("st", 2 + sub)] = [var_]
        streaders[sub] = [m_]
        streaders[2 + sub] = [var_]
        L["rows"] = [var_, c0c, rstd, c1c]
        bc_toks.append((bc1, bc0))
    chunk_toks = [None] * KD
    for k in range(KD):
        av = arena[:, acol0 + pitch * k + boff: acol0 + pitch * k + boff + 1024]
        subtoks = []
        for sub in range(2):
            bc1, bc0 = bc_toks[sub]
            avs = av[:, 512 * sub:512 * (sub + 1)]
            fs = lb.fscr[:, 0:512]
            t1 = p.v(lambda e, avs=avs, fs=fs, sub=sub: e.tensor_tensor(
                fs, avs, lb.cb1[sub][:, :], ALU.mult),
                [cast_toks[k], bc1, L.get("fs")], tag="lnap1")
            t2 = p.v(lambda e, avs=avs, fs=fs, sub=sub: e.tensor_tensor(
                avs, fs, lb.cb0[sub][:, :], ALU.add), [t1, bc0], tag="lnap2")
            L["fs"] = t2
            subtoks.append(t2)
        chunk_toks[k] = subtoks
        L[("ar", k)] = subtoks
        L[("cb1", 0 if False else 0)] = None
    allap = _flat(chunk_toks)
    for sub in range(2):
        L[("cb1", sub)] = allap
        L[("cb0", sub)] = allap
    return chunk_toks, allap, streaders



def emit_p1(nc, ctx, p, dr, pfx="", dbg=False):
    u = dr["u"]
    d_lre, d_lim, d_ldl = dr["lre"], dr["lim"], dr["ldl"]
    d_bre, d_bim, d_cst, d_dfm = dr["breT"], dr["bimT"], dr["cstack"], dr["dfm"]
    d_iota, d_iota1 = dr["iota_row"], dr["iota1_row"]
    d_iotac, d_iota1c = dr["iota_col"], dr["iota1_col"]
    d_triu, d_ident = dr["triu"], dr["ident"]
    xo = dr["xo"]

    def sb(name, shape, dtype=F32):
        return ctx.enter_context(nc.sbuf_tensor(pfx + name, shape, dtype))

    rn = ("lre lim ldl lamre delta ldre ldim are aim cre cim den u1 u2 are1 "
          "rt rabs rsin rcos rmag").split()
    row = {n: sb("r_" + n, [1, 32], F32) for n in rn}
    row_i = sb("r_int", [1, 32], I32)
    RW = lambda n: row[n][0:1, 0:32]

    apos = sb("apos", [128, 64])
    aneg = sb("aneg", [128, 64])
    asher = sb("asher", [128, 64])
    tt0 = sb("tt0", [128, 64])
    tt1 = sb("tt1", [128, 64])
    tscr = sb("tscr", [128, 64])
    tt1i = sb("tt1i", [128, 32], I32)
    bb_sb = sb("bb_sb", [128, KD * 64], BF16)
    bstg = sb("bstg", [128, 64])
    bsc = sb("bsc", [128, 64])
    cbt = sb("cbt", [128, 64])
    ceff = sb("ceff", [64, D], BF16)
    cstg = sb("cstg", [64, D])
    dfm_sb = sb("dfm_sb", [128, KD])
    iota_sb = sb("iota_sb", [1, 128])
    iota1_sb = sb("iota1_sb", [1, 128])
    iota_col = sb("iota_col_s", [128, 1])
    iota1_col = sb("iota1_col_s", [128, 1])
    triu_sb = sb("triu_sb", [128, 128])
    ident_sb = sb("ident_sb", [128, 128])
    ones_row = sb("ones_row", [1, 128])
    ones_col = sb("ones_col", [128, 1])
    pihalf = sb("pihalf", [128, 1])

    ccur = [sb(f"ccur{i}", [1, 64]) for i in range(2)]
    cscr = sb("cscr", [1, 64])
    at_r = sb("at_r", [1, 64])
    a1_r = sb("a1_r", [1, 64])
    ap127_r = sb("ap127_r", [1, 64])
    acrow = sb("acrow", [1, 64])

    xa = sb("xa", [128, KD * 1024], BF16)
    ust = [sb(f"ust{i}", [128, 512]) for i in range(4)]
    p_f = [sb(f"p_f{i}", [128, 64]) for i in range(2)]
    w_f = [sb(f"w_f{i}", [128, 64]) for i in range(2)]
    acb = [sb(f"acb{i}", [128, 64]) for i in range(2)]
    x_f = [sb(f"x_f{i}", [128, 64]) for i in range(4)]
    xT = sb("xT", [64, 512], BF16)
    yscr = [sb(f"yscr{i}", [128, 512]) for i in range(2)]
    yout = [sb(f"yout{i}", [128, 512]) for i in range(2)]

    ps = [ctx.enter_context(nc.psum_tensor(pfx + f"ps{i}", [128, 512], F32))
          for i in range(4)]
    ps_c = ctx.enter_context(nc.psum_tensor(pfx + "ps_c", [128, 512], F32))
    ps_sl = ctx.enter_context(nc.psum_tensor(pfx + "ps_sl", [1, 64], F32))
    ps_tr = ctx.enter_context(nc.psum_tensor(pfx + "ps_tr", [64, 128], F32))

    lb = LnBufs(nc, ctx, pfx + "ln_",
                ps_st=[ps[0][0:1, 0:512], ps[1][0:1, 0:512],
                       ps[2][0:1, 0:512], ps[3][0:1, 0:512]],
                ps_bc=ps_c[:, 0:512])
    lb.emit_init(p)

    ld_iota = p.dma(iota_sb[:, :], d_iota[:, :])
    ld_iota1 = p.dma(iota1_sb[:, :], d_iota1[:, :])
    ld_iotac = p.dma(iota_col[:, :], d_iotac[:, :])
    ld_iota1c = p.dma(iota1_col[:, :], d_iota1c[:, :])
    ld_triu = p.dma(triu_sb[:, :], d_triu[:, :])
    ld_ident = p.dma(ident_sb[:, :], d_ident[:, :])
    ld_dfm = p.dma(dfm_sb[:, :], d_dfm[:, :])
    ms1 = p.v(lambda e: e.memset(ones_row[:, :], 1.0))
    ms3 = p.v(lambda e: e.memset(ones_col[:, :], 1.0))
    msp = p.v(lambda e: e.memset(pihalf[:, :], PI / 2))
    ld1 = p.dma(RW("lre"), d_lre[:, :])
    ld2 = p.dma(RW("lim"), d_lim[:, :])
    ld3 = p.dma(RW("ldl"), d_ldl[:, :])

    a1 = p.s(lambda e: e.activation(RW("rabs"), RW("lre"), AF.Abs), [ld1])
    a2 = p.s(lambda e: e.activation(RW("rt"), RW("rabs"), AF.Exp, scale=-1.0), [a1])
    a3 = p.s(lambda e: e.activation(RW("rt"), RW("rt"), AF.Ln, bias=1.0), [a2])
    a4 = p.v(lambda e: e.tensor_scalar_max(RW("u1"), RW("lre"), 0.0), [ld1])
    a5 = p.v(lambda e: e.tensor_add(RW("u1"), RW("u1"), RW("rt")), [a3, a4])
    a6 = p.v(lambda e: e.tensor_scalar(RW("lamre"), RW("u1"), -1.0, -0.0101,
                                       ALU.mult, ALU.add), [a5])
    a7 = p.s(lambda e: e.activation(RW("delta"), RW("ldl"), AF.Exp), [ld3])
    a8 = p.v(lambda e: e.tensor_mul(RW("ldre"), RW("lamre"), RW("delta")), [a6, a7])
    a9 = p.v(lambda e: e.tensor_mul(RW("ldim"), RW("lim"), RW("delta")), [ld2, a7])

    MAGIC = 1.5 * 2.0 ** 23

    def trig_row(dst_cos, dst_sin, src, waits):
        w1 = p.v(lambda e: e.tensor_scalar(RW("rt"), RW(src), 1.0 / (2 * PI),
                                           MAGIC, ALU.mult, ALU.add), waits)
        w3 = p.v(lambda e: e.tensor_scalar_add(RW("rt"), RW("rt"), -MAGIC), [w1])
        w4 = p.v(lambda e: e.scalar_tensor_tensor(RW("rt"), RW("rt"), -2 * PI,
                                                  RW(src), ALU.mult, ALU.add), [w3])
        w5 = p.s(lambda e: e.activation(RW(dst_sin), RW("rt"), AF.Sin), [w4])
        w6 = p.s(lambda e: e.activation(RW("rabs"), RW("rt"), AF.Abs), [w4])
        w7 = p.s(lambda e: e.activation(RW(dst_cos), RW("rabs"), AF.Sin,
                                        scale=-1.0, bias=pihalf[0:1, 0:1]),
                 [w6, msp])
        return [w5, w7]

    tg = trig_row("rcos", "rsin", "ldim", [a9])
    a10 = p.s(lambda e: e.activation(RW("rmag"), RW("ldre"), AF.Exp), [a8])
    a11 = p.v(lambda e: e.tensor_mul(RW("are"), RW("rmag"), RW("rcos")), [a10, tg])
    a12 = p.v(lambda e: e.tensor_mul(RW("aim"), RW("rmag"), RW("rsin")), [a10, tg])

    b1 = p.v(lambda e: e.tensor_mul(RW("den"), RW("lamre"), RW("lamre")), [a6])
    b2 = p.v(lambda e: e.tensor_mul(RW("u1"), RW("lim"), RW("lim")), [ld2, a5])
    b3 = p.v(lambda e: e.tensor_add(RW("den"), RW("den"), RW("u1")), [b1, b2])
    b4 = p.v(lambda e: e.reciprocal(RW("den"), RW("den")), [b3])
    b5 = p.v(lambda e: e.tensor_scalar_add(RW("are1"), RW("are"), -1.0), [a11])
    b6 = p.v(lambda e: e.tensor_mul(RW("u1"), RW("are1"), RW("lamre")), [b5, b3])
    b7 = p.v(lambda e: e.tensor_mul(RW("u2"), RW("aim"), RW("lim")), [a12])
    b8 = p.v(lambda e: e.tensor_add(RW("u1"), RW("u1"), RW("u2")), [b6, b7])
    b9 = p.v(lambda e: e.tensor_mul(RW("cre"), RW("u1"), RW("den")), [b8, b4])
    c1 = p.v(lambda e: e.tensor_mul(RW("u1"), RW("aim"), RW("lamre")), [a12, b9])
    c2 = p.v(lambda e: e.tensor_mul(RW("u2"), RW("are1"), RW("lim")), [b5, b9])
    c3 = p.v(lambda e: e.tensor_sub(RW("u1"), RW("u1"), RW("u2")), [c1, c2])
    c4 = p.v(lambda e: e.tensor_mul(RW("cim"), RW("u1"), RW("den")), [c3, b4])

    d_rows = nc.dram_tensor(pfx + "rowscr", [1, 64], F32, kind="Internal")
    ldre_b = sb("ldre_b", [128, 32])
    ldim_b = sb("ldim_b", [128, 32])
    thA = sb("thA", [128, 64])
    thB = sb("thB", [128, 64])
    wr1 = p.dma(d_rows.ap()[0:1, 0:32], RW("ldre"), [a8], tag="rowwr")
    wr2 = p.dma(d_rows.ap()[0:1, 32:64], RW("ldim"), [a9], tag="rowwr")
    bc1_ = p.dma(ldre_b[:, :], bass.AP(d_rows, 0, [[0, 128], [1, 32]]), [wr1])
    bc2_ = p.dma(ldim_b[:, :], bass.AP(d_rows, 32, [[0, 128], [1, 32]]), [wr2])
    mmt = []
    for dst, srcb, iot, w in [(lambda: thA[:, 0:32], ldre_b, "iota_col", bc1_),
                              (lambda: thA[:, 32:64], ldim_b, "iota_col", bc2_),
                              (lambda: thB[:, 0:32], ldre_b, "iota1_col", bc1_),
                              (lambda: thB[:, 32:64], ldim_b, "iota1_col", bc2_)]:
        col = iota_col if iot == "iota_col" else iota1_col
        mmt.append(p.v(lambda e, dst=dst, srcb=srcb, col=col: e.tensor_scalar(
            dst(), srcb[:, :], col[:, 0:1], None, ALU.mult),
            [w, ld_iotac, ld_iota1c], tag="powtab"))

    def trig_tile(dst, src_ps, waits):
        w1 = p.v(lambda e: e.tensor_scalar(tscr[:, 0:32], src_ps[:, 0:32],
                                           1.0 / (2 * PI), MAGIC,
                                           ALU.mult, ALU.add), waits)
        w3 = p.v(lambda e: e.tensor_scalar_add(tscr[:, 0:32], tscr[:, 0:32],
                                               -MAGIC), [w1])
        w4 = p.v(lambda e: e.scalar_tensor_tensor(tscr[:, 0:32], tscr[:, 0:32],
                                                  -2 * PI, src_ps[:, 0:32],
                                                  ALU.mult, ALU.add), [w3])
        w5 = p.s(lambda e: e.activation(dst[:, 32:64], tscr[:, 0:32], AF.Sin), [w4])
        w6 = p.s(lambda e: e.activation(tscr[:, 32:64], tscr[:, 0:32], AF.Abs), [w4])
        w7 = p.s(lambda e: e.activation(dst[:, 0:32], tscr[:, 32:64], AF.Sin,
                                        scale=-1.0, bias=pihalf[:, :]),
                 [w6, msp])
        return [w5, w7]

    tga = trig_tile(tt0, thA[:, 32:64], [mmt[1]])
    e1 = p.s(lambda e: e.activation(bstg[:, 0:32], thA[:, 0:32], AF.Exp), [mmt[0]])
    e2 = p.s(lambda e: e.activation(bstg[:, 32:64], thA[:, 0:32], AF.Exp,
                                    scale=-1.0), [mmt[0]])
    e3 = p.v(lambda e: e.tensor_mul(apos[:, 0:32], bstg[:, 0:32], tt0[:, 0:32]),
             [e1, tga])
    e4 = p.v(lambda e: e.tensor_mul(apos[:, 32:64], bstg[:, 0:32], tt0[:, 32:64]),
             [e1, tga])
    e5 = p.v(lambda e: e.tensor_mul(aneg[:, 0:32], bstg[:, 32:64], tt0[:, 0:32]),
             [e2, tga])
    e6 = p.v(lambda e: e.scalar_tensor_tensor(aneg[:, 32:64], bstg[:, 32:64],
                                              -1.0, tt0[:, 32:64], ALU.mult,
                                              ALU.mult), [e2, tga])
    tgb = trig_tile(tt1, thB[:, 32:64], [mmt[3], e3, e4, e5, e6])
    e7 = p.s(lambda e: e.activation(bstg[:, 0:32], thB[:, 0:32], AF.Exp),
             [mmt[2], e3, e4, e5, e6])
    e8 = p.v(lambda e: e.tensor_mul(asher[:, 0:32], bstg[:, 0:32], tt1[:, 0:32]),
             [e7, tgb])
    e9 = p.v(lambda e: e.tensor_mul(asher[:, 32:64], bstg[:, 0:32], tt1[:, 32:64]),
             [e7, tgb])

    atr = p.dma(at_r[0:1, 0:64], asher[127:128, 0:64], [e8, e9], tag="atr")
    apr = p.dma(ap127_r[0:1, 0:64], apos[127:128, 0:64], [e3, e4], tag="apr")
    a1r = p.dma(a1_r[0:1, 0:64], apos[1:2, 0:64], [e3, e4], tag="a1r")
    col_toks = [atr, apr, a1r]

    f1 = p.t(lambda e: e.matmul(ps_c[:, 0:32], ones_row[0:1, :], RW("cre"),
                                start=True, stop=True), [b9, ms1])
    f2 = p.t(lambda e: e.matmul(ps_c[:, 32:64], ones_row[0:1, :], RW("cim"),
                                start=True, stop=True), [c4, ms1])
    f3 = p.v(lambda e: e.tensor_copy(cbt[:, 0:64], ps_c[:, 0:64]), [f1, f2])

    prevb = None
    for k in range(KD):
        l1 = p.dma(bstg[:, 0:32], d_bre[128 * k:128 * (k + 1), :],
                   [prevb, e7, e8, e9], tag="bre")
        l2 = p.dma(bstg[:, 32:64], d_bim[128 * k:128 * (k + 1), :],
                   [prevb, e7, e8, e9])
        g1 = p.v(lambda e: e.tensor_mul(bsc[:, 0:32], bstg[:, 0:32], cbt[:, 0:32]),
                 [l1, f3, prevb, tgb])
        g2 = p.v(lambda e: e.tensor_mul(bsc[:, 32:64], bstg[:, 32:64],
                                        cbt[:, 32:64]), [l2, f3, prevb, tgb])
        g3 = p.v(lambda e, k=k: e.tensor_sub(bb_sb[:, 64 * k:64 * k + 32],
                                             bsc[:, 0:32], bsc[:, 32:64]), [g1, g2])
        g4 = p.v(lambda e: e.tensor_mul(bsc[:, 0:32], bstg[:, 32:64], cbt[:, 0:32]),
                 [g3])
        g5 = p.v(lambda e: e.tensor_mul(bsc[:, 32:64], bstg[:, 0:32],
                                        cbt[:, 32:64]), [g3])
        g6 = p.v(lambda e, k=k: e.tensor_add(bb_sb[:, 64 * k + 32:64 * k + 64],
                                             bsc[:, 0:32], bsc[:, 32:64]), [g4, g5])
        prevb = g6
    bb_tok = prevb

    lc = p.dma(cstg[0:64, :], d_cst[:, :])
    h1 = p.s(lambda e: e.activation(ceff[0:64, 0:D], cstg[0:64, 0:D], AF.Copy,
                                    scale=2.0), [lc])
    h2 = p.s(lambda e: e.activation(ceff[32:64, 0:D], cstg[32:64, 0:D], AF.Copy,
                                    scale=-2.0), [h1])
    ceff_tok = h2

    ic1 = p.v(lambda e: e.memset(ccur[0][:, :], 0.0))

    ps_y_last = [None, None]
    ps_bu_last = None
    ps_q_last = None
    lb.last["psbc"] = f3
    ps_sl_last = None
    ps_tr_last = None
    carry_par = 0
    prev_bcm = None
    ctok = [ic1] + col_toks
    xf_last = [None] * 4
    pf_last = [None, None]
    wf_last = [None, None]
    acb_last = [None, None]
    ust_last = [None] * 4
    yscr_last = [None, None]
    yout_last = [None, None]
    xT_last = [None] * 4
    xT_tok = [None] * 4
    arena_readers = {k: [] for k in range(KD)}

    for nb in range(4):
        lb.last[("st", 0)] = ps_bu_last
        lb.last[("st", 1)] = ps_q_last
        lb.last[("st", 2)] = ps_y_last[0]
        lb.last[("st", 3)] = ps_y_last[1]
        chunk_toks, lnall, strd = emit_ln_block(p, lb, u, 1024 * nb, xa, 0)
        ps_bu_last = strd[0]
        ps_q_last = strd[1]
        ps_y_last = [strd[2], strd[3]]
        for cc in range(8):
            bu_mms = []
            for k in range(KD):
                av = xa[:, 1024 * k + 128 * cc: 1024 * k + 128 * (cc + 1)]
                bu_mms.append(p.t(lambda e, av=av, k=k: e.matmul(
                    ps[0][:, 0:64], av, bb_sb[:, 64 * k:64 * (k + 1)],
                    start=(k == 0), stop=(k == KD - 1)),
                    [chunk_toks[k], bb_tok] + ([ps_bu_last] if k == 0 else []),
                    tag="bu"))
                arena_readers[k].append(bu_mms[-1])
            bu = bu_mms[-1]
            par = cc % 2
            pf = p_f[par]
            wf = w_f[par]
            q1 = p.v(lambda e, pf=pf: e.tensor_mul(pf[:, 0:32], aneg[:, 0:32],
                                                   ps[0][:, 0:32]),
                     [bu, pf_last[par], e5], tag="pre")
            q2 = p.v(lambda e, wf=wf: e.tensor_mul(wf[:, 0:32], aneg[:, 32:64],
                                                   ps[0][:, 32:64]),
                     [bu, wf_last[par], e6])
            q3 = p.v(lambda e, pf=pf, wf=wf: e.tensor_sub(pf[:, 0:32],
                                                          pf[:, 0:32],
                                                          wf[:, 0:32]), [q1, q2])
            q4 = p.v(lambda e, pf=pf: e.tensor_mul(pf[:, 32:64], aneg[:, 0:32],
                                                   ps[0][:, 32:64]),
                     [bu, q3], tag="pim")
            q5 = p.v(lambda e, wf=wf: e.tensor_mul(wf[:, 32:64], aneg[:, 32:64],
                                                   ps[0][:, 0:32]), [bu, q2])
            q6 = p.v(lambda e, pf=pf, wf=wf: e.tensor_add(pf[:, 32:64],
                                                          pf[:, 32:64],
                                                          wf[:, 32:64]), [q4, q5])
            p_tok = [q3, q6]
            pf_last[par] = p_tok
            wf_last[par] = [q5, q6]
            ps_bu_last = [q1, q2, q4, q5]
            qmm = p.t(lambda e, pf=pf: e.matmul(ps[1][:, 0:64], triu_sb[:, :],
                                                pf[:, 0:64], start=True,
                                                stop=True),
                      [p_tok, ld_triu, ps_q_last], tag="tri")
            slmm = p.t(lambda e, pf=pf: e.matmul(ps_sl[0:1, 0:64],
                                                 ones_col[:, :], pf[:, 0:64],
                                                 start=True, stop=True),
                       [p_tok, ms3, ps_sl_last], tag="slmm")
            cprev = ccur[carry_par]
            cnext = ccur[1 - carry_par]
            CS_RE = ps_sl[0:1, 0:32]
            CS_IM = ps_sl[0:1, 32:64]
            AB = lambda t, a, b: t[0:1, a:b]
            v_ = []
            v_.append(p.v(lambda e, cp=cprev: e.tensor_mul(
                AB(cscr, 0, 32), AB(a1_r, 0, 32), AB(cp, 0, 32)),
                [ctok, acb_last, prev_bcm], tag="ac1"))
            v_.append(p.v(lambda e, cp=cprev: e.tensor_mul(
                AB(cscr, 32, 64), AB(a1_r, 32, 64), AB(cp, 32, 64)), [ctok]))
            v_.append(p.v(lambda e: e.tensor_sub(AB(acrow, 0, 32), AB(cscr, 0, 32),
                                                 AB(cscr, 32, 64)), []))
            v_.append(p.v(lambda e, cp=cprev: e.tensor_mul(
                AB(cscr, 0, 32), AB(a1_r, 0, 32), AB(cp, 32, 64)), []))
            v_.append(p.v(lambda e, cp=cprev: e.tensor_mul(
                AB(cscr, 32, 64), AB(a1_r, 32, 64), AB(cp, 0, 32)), []))
            v_.append(p.v(lambda e: e.tensor_add(AB(acrow, 32, 64), AB(cscr, 0, 32),
                                                 AB(cscr, 32, 64)), []))
            ac_tok = v_[-1]
            w_ = []
            w_.append(p.v(lambda e: e.tensor_mul(AB(cscr, 0, 32), AB(ap127_r, 0, 32),
                                                 CS_RE), [slmm], tag="cc1"))
            w_.append(p.v(lambda e: e.tensor_mul(AB(cscr, 32, 64),
                                                 AB(ap127_r, 32, 64), CS_IM), [slmm]))
            w_.append(p.v(lambda e, cn=cnext: e.tensor_sub(
                AB(cn, 0, 32), AB(cscr, 0, 32), AB(cscr, 32, 64)), []))
            w_.append(p.v(lambda e: e.tensor_mul(AB(cscr, 0, 32), AB(ap127_r, 0, 32),
                                                 CS_IM), [slmm]))
            w_.append(p.v(lambda e: e.tensor_mul(AB(cscr, 32, 64),
                                                 AB(ap127_r, 32, 64), CS_RE), [slmm]))
            w_.append(p.v(lambda e, cn=cnext: e.tensor_add(
                AB(cn, 32, 64), AB(cscr, 0, 32), AB(cscr, 32, 64)), []))
            w_.append(p.v(lambda e, cp=cprev: e.tensor_mul(
                AB(cscr, 0, 32), AB(at_r, 0, 32), AB(cp, 0, 32)), []))
            w_.append(p.v(lambda e, cp=cprev: e.tensor_mul(
                AB(cscr, 32, 64), AB(at_r, 32, 64), AB(cp, 32, 64)), []))
            w_.append(p.v(lambda e, cn=cnext: e.tensor_add(
                AB(cn, 0, 32), AB(cn, 0, 32), AB(cscr, 0, 32)), []))
            w_.append(p.v(lambda e, cn=cnext: e.tensor_sub(
                AB(cn, 0, 32), AB(cn, 0, 32), AB(cscr, 32, 64)), []))
            w_.append(p.v(lambda e, cp=cprev: e.tensor_mul(
                AB(cscr, 0, 32), AB(at_r, 0, 32), AB(cp, 32, 64)), []))
            w_.append(p.v(lambda e, cp=cprev: e.tensor_mul(
                AB(cscr, 32, 64), AB(at_r, 32, 64), AB(cp, 0, 32)), []))
            w_.append(p.v(lambda e, cn=cnext: e.tensor_add(
                AB(cn, 32, 64), AB(cn, 32, 64), AB(cscr, 0, 32)), []))
            w_.append(p.v(lambda e, cn=cnext: e.tensor_add(
                AB(cn, 32, 64), AB(cn, 32, 64), AB(cscr, 32, 64)), []))
            ps_sl_last = w_[:5]
            ctok = [w_[-1], w_[9]]
            carry_par = 1 - carry_par
            bcm = p.t(lambda e: e.matmul(ps_c[:, 0:64], ones_row[0:1, :],
                                         acrow[0:1, 0:64], start=True,
                                         stop=True),
                      [ac_tok, ms1, lb.last.get("psbc")], tag="acb")
            prev_bcm = bcm
            ae = p.v(lambda e, par=par: e.tensor_copy(acb[par][:, 0:64],
                                                      ps_c[:, 0:64]),
                     [bcm, acb_last[par]])
            lb.last["psbc"] = ae
            wv = p.v(lambda e, wf=wf, par=par: e.tensor_tensor(
                wf[:, 0:64], ps[1][:, 0:64], acb[par][:, 0:64], ALU.add),
                [qmm, ae], tag="wq")
            acb_last[par] = wv
            ps_q_last = wv
            xi = cc % 4
            xf = x_f[xi]
            y1 = p.v(lambda e, xf=xf, wf=wf: e.tensor_mul(
                xf[:, 0:32], apos[:, 0:32], wf[:, 0:32]),
                [wv, xf_last[xi]], tag="xre")
            y2 = p.v(lambda e, pf=pf, wf=wf: e.tensor_mul(
                pf[:, 0:32], apos[:, 32:64], wf[:, 32:64]), [wv])
            y3 = p.v(lambda e, xf=xf, pf=pf: e.tensor_sub(
                xf[:, 0:32], xf[:, 0:32], pf[:, 0:32]), [y1, y2])
            y4 = p.v(lambda e, xf=xf, wf=wf: e.tensor_mul(
                xf[:, 32:64], apos[:, 0:32], wf[:, 32:64]),
                [wv, xf_last[xi]])
            y5 = p.v(lambda e, pf=pf, wf=wf: e.tensor_mul(
                pf[:, 32:64], apos[:, 32:64], wf[:, 0:32]), [wv])
            y6 = p.v(lambda e, xf=xf, pf=pf: e.tensor_add(
                xf[:, 32:64], xf[:, 32:64], pf[:, 32:64]), [y4, y5])
            x_tok = [y3, y6]
            pf_last[par] = [y2, y5, q3, q6]
            wf_last[par] = [wv, y1, y2, y4, y5]
            trm = p.t(lambda e, xf=xf: e.transpose(ps_tr[:, :], xf[:, 0:64],
                                                   ident_sb[:, :]),
                      [x_tok, ld_ident, ps_tr_last], tag="xtr")
            gi = cc % 4
            xtc = p.v(lambda e, gi=gi: e.tensor_copy(
                xT[0:64, 128 * gi:128 * (gi + 1)], ps_tr[0:64, :]),
                [trm, xT_last[gi]], tag="xtc")
            ps_tr_last = xtc
            xf_last[xi] = trm
            xT_tok[gi] = xtc
            if cc % 4 == 3:
                grp = cc // 4
                gcol = 1024 * nb + 512 * grp
                last_ymm = None
                for k in range(KD):
                    ypar = k % 2
                    ym = p.t(lambda e, k=k, ypar=ypar: e.matmul(
                        ps[2 + ypar][:, 0:512], ceff[0:64, 128 * k:128 * (k + 1)],
                        xT[0:64, 0:512], start=True, stop=True),
                        [xT_tok, ceff_tok, ps_y_last[ypar]], tag="ymm")
                    si = k % 4
                    lu = p.dma(ust[si][:, :], u[128 * k:128 * (k + 1),
                                                gcol:gcol + 512],
                               [ust_last[si]], tag="ustld")
                    av = xa[:, 1024 * k + 512 * grp: 1024 * k + 512 * (grp + 1)]
                    w1_ = p.v(lambda e, si=si, k=k, av=av, ypar=ypar:
                              e.scalar_tensor_tensor(
                                  yscr[ypar][:, :], av, dfm_sb[:, k:k + 1],
                                  ust[si][:, :], ALU.mult, ALU.add),
                              [lu, ld_dfm, chunk_toks[k], yscr_last[ypar]],
                              tag="yevac1")
                    arena_readers[k].append(w1_)
                    w2_ = p.v(lambda e, ypar=ypar: e.tensor_tensor(
                        yout[ypar][:, :], yscr[ypar][:, :], ps[2 + ypar][:, 0:512],
                        ALU.add), [w1_, ym, yout_last[ypar]], tag="yevac2")
                    st = p.dmag(xo[128 * k:128 * (k + 1), gcol:gcol + 512],
                                yout[ypar][:, :], [w2_], tag="xost")
                    ust_last[si] = w1_
                    yscr_last[ypar] = w2_
                    yout_last[ypar] = st
                    ps_y_last[ypar] = w2_
                    last_ymm = ym
                for gi2 in range(4):
                    xT_last[gi2] = last_ymm
        for k in range(KD):
            lb.last[("ar", k)] = arena_readers[k]
            arena_readers[k] = []
    return



def p1_inputs(inp, b):
    u = np.asarray(inp["L0_attn_a_u"])
    out = {
        "u": np.ascontiguousarray(u[b].T),
        "lre": np.asarray(inp["L0_attn_a_Lambda_unconstrained_re"], np.float32).reshape(1, 32),
        "lim": np.asarray(inp["L0_attn_a_Lambda_im"], np.float32).reshape(1, 32),
        "ldl": np.asarray(inp["L0_attn_a_log_Delta"], np.float32)[:32].reshape(1, 32),
        "breT": np.ascontiguousarray(np.asarray(inp["L0_attn_a_B_re"], np.float32).T),
        "bimT": np.ascontiguousarray(np.asarray(inp["L0_attn_a_B_im"], np.float32).T),
        "cstack": np.ascontiguousarray(
            np.concatenate([np.asarray(inp["L0_attn_a_C_re"], np.float32).T,
                            np.asarray(inp["L0_attn_a_C_im"], np.float32).T], 0)),
        "dfm": np.ascontiguousarray(
            np.asarray(inp["L0_attn_a_D"], np.float32).reshape(KD, 128).T),
        "iota_row": np.arange(128, dtype=np.float32).reshape(1, 128),
        "iota1_row": (np.arange(128, dtype=np.float32) + 1).reshape(1, 128),
        "iota_col": np.arange(128, dtype=np.float32).reshape(128, 1),
        "iota1_col": (np.arange(128, dtype=np.float32) + 1).reshape(128, 1),
        "triu": np.triu(np.ones((128, 128), np.float32)),
        "ident": np.eye(128, dtype=np.float32),
    }
    return out


_PROGS = {}


def _get_prog(name, builder):
    if name not in _PROGS:
        _PROGS[name] = builder()
    return _PROGS[name][0]



def build_fused():
    nc = bass.Bass("TRN2", target_bir_lowering=False, debug=False)
    top = ExitStack()
    p = Prog(nc)

    def EI(name, shape, dt=F32):
        return nc.dram_tensor(name, shape, dt, kind="ExternalInput").ap()

    dr1 = {
        "u": EI("a_u", [D, S]),
        "lre": EI("a_lre", [1, 32]), "lim": EI("a_lim", [1, 32]),
        "ldl": EI("a_ldl", [1, 32]),
        "breT": EI("a_breT", [D, 32]), "bimT": EI("a_bimT", [D, 32]),
        "cstack": EI("a_cstack", [64, D]), "dfm": EI("a_dfm", [128, KD]),
        "iota_row": EI("a_iota_row", [1, 128]),
        "iota1_row": EI("a_iota1_row", [1, 128]),
        "iota_col": EI("a_iota_col", [128, 1]),
        "iota1_col": EI("a_iota1_col", [128, 1]),
        "triu": EI("a_triu", [128, 128]), "ident": EI("a_ident", [128, 128]),
    }
    dr2 = {
        "wg": EI("b_wg", [64, 128, KD * 128], BF16),
        "wu": EI("b_wu", [64, 128, KD * 128], BF16),
        "wd": EI("b_wd", [KD, 128, 64 * 128], BF16),
    }
    d_gidx = EI("b_gidx", [128, 64], I32)
    dr3 = {
        "wq": EI("c_wq", [KD, 128, D], BF16),
        "wk": EI("c_wk", [KD, 128, D], BF16),
        "wv": EI("c_wv", [KD, 128, D], BF16),
        "wo": EI("c_wo", [KD, 128, D], BF16),
        "qb": EI("c_qb", [128, KD]), "kb": EI("c_kb", [128, KD]),
        "vb": EI("c_vb", [1, D], BF16), "ob": EI("c_ob", [128, KD]),
        "slopes": EI("c_slopes", [1, 16]),
        "kgc": EI("c_kgc", [128, 773]),
        "mstrip": EI("c_mstrip", [128, 128], BF16),
    }
    dr4 = {
        "wq": EI("d_wq", [KD, 128, D], BF16),
        "wk": EI("d_wk", [KD, 128, D], BF16),
        "wv": EI("d_wv", [KD, 128, D], BF16),
        "wo": EI("d_wo", [KD, 128, D], BF16),
        "qb": EI("d_qb", [128, KD]), "kb": EI("d_kb", [128, KD]),
        "vb": EI("d_vb", [1, D], BF16), "ob": EI("d_ob", [128, KD]),
        "wmapq": EI("d_wmapq", [128, 64], BF16),
        "wmapk": EI("d_wmapk", [128, 64], BF16),
    }
    dr5 = {
        "ing": EI("e_ing", [128, KD]), "inb": EI("e_inb", [128, KD]),
        "png": EI("e_png", [128, KD]), "pnb": EI("e_pnb", [128, KD]),
        "outg": EI("e_outg", [128, KD]), "outb": EI("e_outb", [128, KD]),
        "agg": EI("e_agg", [KD, 128, D], BF16), "aggb": EI("e_aggb", [128, KD]),
        "d1w": EI("e_d1w", [32, 128, D], BF16), "d1b": EI("e_d1b", [128, 32]),
        "d2w": EI("e_d2w", [KD, 128, 4096], BF16), "d2b": EI("e_d2b", [128, KD]),
        "bcw": EI("e_bcw", [KD, 128, D], BF16), "bcb": EI("e_bcb", [128, KD]),
        "gw": EI("e_gw", [KD, 128, D], BF16), "gb": EI("e_gb", [128, KD]),
    }
    dr6 = {
        "wq": EI("f_wq", [KD, 128, D], BF16),
        "wk": EI("f_wk", [4, 128, D], BF16),
        "wv": EI("f_wv", [KD, 128, 512], BF16),
        "wo": EI("f_wo", [KD, 128, D], BF16),
        "cosq": EI("f_cosq", [128, TOWN], BF16),
        "sinq": EI("f_sinq", [128, TOWN], BF16),
    }

    def DI(name, shape, dt=F32):
        return nc.dram_tensor(name, shape, dt, kind="Internal")

    dxg = DI("dxg", [D, S]).ap()
    dxA = DI("dxA", [D, TOWN]).ap()
    dxB = DI("dxB", [D, TOWN]).ap()
    dq3 = DI("c_dq", [D, TOWN], BF16).ap()
    dko3_h = DI("c_dko", [D, TOWN], BF16)
    dkg3_h = DI("c_dkg", [2, D, TOWN], BF16)
    dvo3_h = DI("c_dvo", [16, 16, 128, 128], BF16)
    dvg3_h = DI("c_dvg", [2, 16, 16, 128, 128], BF16)
    dq4 = DI("d_dq", [D, TOWN], BF16).ap()
    dko4 = DI("d_dko", [D, TOWN], BF16).ap()
    dvo4 = DI("d_dvo", [16, 16, 128, 128], BF16).ap()
    pki4_h = [DI(f"d_pki{i}", [128, 129]) for i in range(2)]
    pko4_h = [DI(f"d_pko{i}", [128, 129]) for i in range(2)]
    xli5_h = [DI(f"e_xli{i}", [128, KD]) for i in range(2)]
    xlo5_h = [DI(f"e_xlo{i}", [256, KD]) for i in range(2)]
    thki5_h = [DI(f"e_thki{i}", [128, KD]) for i in range(2)]
    thko5_h = [DI(f"e_thko{i}", [128, KD]) for i in range(2)]
    dq6 = DI("f_dq", [D, TOWN], BF16).ap()
    dko6_h = DI("f_dko", [512, TOWN], BF16)
    dkg6_h = DI("f_dkg", [2, 512, TOWN], BF16)
    dvo6_h = DI("f_dvo", [4, 16, 128, 128], BF16)
    dvg6_h = DI("f_dvg", [2, 4, 16, 128, 128], BF16)

    xo = nc.dram_tensor("xo", [D, TOWN], F32, kind="ExternalOutput").ap()

    c = ExitStack()
    emit_p1(nc, c, p, {**dr1, "xo": dxg}, pfx="a_")
    p.phase_break()
    c.close()

    dxg_rows = dxg.rearrange("d (b c) -> (d b) c", b=8)
    c = ExitStack()
    sidx = c.enter_context(nc.sbuf_tensor("b_sidx", [128, 64], I32))
    lsx = p.dma(sidx[:, :], d_gidx[:, :], tag="gidxld")

    def gload(dst, k, j, waits, lsx=lsx):
        col = 4 * k + j
        return p.dmai(dst, dxg_rows, sidx[:, col:col + 1], [waits, lsx],
                      tag="glnld")

    emit_p2(nc, c, p, {**dr2, "xi": None, "xo": dxA}, pfx="b1_",
            load_fn=gload)
    p.phase_break()
    c.close()

    c = ExitStack()
    emit_p3f(nc, c, p, {**dr3, "xi": dxA, "xo": dxB, "dq": dq3,
                        "dko_h": dko3_h, "dkg_h": dkg3_h,
                        "dvo_h": dvo3_h, "dvg_h": dvg3_h}, pfx="c_")
    p.phase_break()
    c.close()

    c = ExitStack()
    emit_p2(nc, c, p, {**dr2, "xi": dxB, "xo": dxA}, pfx="b2_")
    p.phase_break()
    c.close()

    c = ExitStack()
    emit_p4f(nc, c, p, {**dr4, "xi": dxA, "xo": dxB, "dq": dq4,
                        "dko": dko4, "dvo": dvo4,
                        "pki_h": pki4_h, "pko_h": pko4_h}, pfx="d_")
    p.phase_break()
    c.close()

    def mk_xl(stream, pki_h, pko_h, c_stack, pfx):
        xlo = c_stack.enter_context(
            nc.sbuf_tensor(pfx + "xlown", [128, KD], F32))

        def xl_emit(xl_fm):
            lds = [p.dma(xlo[:, k:k + 1],
                         stream[128 * k:128 * (k + 1), 2047:2048],
                         tag="xlc", slow=True) for k in range(KD)]
            wr = p.dma(pki_h.ap()[:, :], xlo[:, :], [lds], tag="xlw")
            cl = p.coll("AllGather", ALU.bypass, PAIRS,
                        [pki_h.ap().opt()], [pko_h.ap().opt()],
                        waits=[wr], tag="xlag")
            return p.dma(xl_fm[:, :], pko_h.ap()[0:128, 0:KD], [cl],
                         tag="xlr")
        return xl_emit

    c = ExitStack()
    emit_p5(nc, c, p, {**dr5, "xi": dxB, "xo": dxA}, pfx="e1_",
            xl_emit=mk_xl(dxB, xli5_h[0], xlo5_h[0], c, "e1_"),
            thr_h=(thki5_h[0], thko5_h[0]))
    p.phase_break()
    c.close()

    c = ExitStack()
    emit_p6f(nc, c, p, {**dr6, "xi": dxA, "xo": dxB, "dq": dq6,
                        "dko_h": dko6_h, "dkg_h": dkg6_h,
                        "dvo_h": dvo6_h, "dvg_h": dvg6_h}, pfx="f_")
    p.phase_break()
    c.close()

    c = ExitStack()
    emit_p5(nc, c, p, {**dr5, "xi": dxB, "xo": xo}, pfx="e2_",
            xl_emit=mk_xl(dxB, xli5_h[1], xlo5_h[1], c, "e2_"),
            thr_h=(thki5_h[1], thko5_h[1]))
    p.finalize(top)
    c.close()
    return nc, top


def gidx_table(h):
    blkmap = [0, 1, 6, 7] if h == 0 else [2, 3, 4, 5]
    g = np.zeros((128, 64), np.int32)
    pp = np.arange(128)
    for k in range(KD):
        for j in range(4):
            g[:, 4 * k + j] = (128 * k + pp) * 8 + blkmap[j]
    return g


def fused_inputs(inp, c):
    b, h = c // 2, c % 2
    m = {}
    m.update({"a_" + k: v for k, v in p1_inputs(inp, b).items()})
    m.update({"b_" + k: v for k, v in p2_weights(inp).items()})
    m["b_gidx"] = gidx_table(h)
    m.update({"c_" + k: v for k, v in p3f_inputs(inp, h).items()})
    m.update({"d_" + k: v for k, v in p4_inputs(inp).items()})
    m.update({"e_" + k: v for k, v in p5f_inputs(inp, h).items()})
    m.update({"f_" + k: v for k, v in p6f_inputs(inp, h).items()})
    return m


def kernel(**inputs):
    inp = {k: np.asarray(v) for k, v in inputs.items()}
    nc = _get_prog("fused", build_fused)
    maps = [fused_inputs(inp, c) for c in range(NCORES)]
    r = run_bass_kernel_spmd(nc, maps, core_ids=list(range(NCORES)))
    out = np.empty((4, S, D), np.float32)
    for b in range(4):
        xf = np.empty((D, S), np.float32)
        xf[:, cols_of(0)] = r.results[2 * b]["xo"]
        xf[:, cols_of(1)] = r.results[2 * b + 1]["xo"]
        out[b] = xf.T
    return out



def emit_p2(nc, ctx, p, dr, pfx="", load_fn=None):
    T = TOWN
    xi = dr.get("xi")
    d_wg, d_wu, d_wd = dr["wg"], dr["wu"], dr["wd"]
    xo = dr["xo"]

    def sb(name, shape, dtype=F32):
        return ctx.enter_context(nc.sbuf_tensor(pfx + name, shape, dtype))

    AR = sb("AR", [128, 81920], BF16)
    XLN0, H0 = 0, 16384

    def xln_ap(k, lo, hi):
        return AR[:, XLN0 + 1024 * k + lo: XLN0 + 1024 * k + hi]

    def h_ap(m, lo, hi):
        return AR[:, H0 + 1024 * m + lo: H0 + 1024 * m + hi]

    wgu = [sb(f"wgu{i}", [128, KD * 128], BF16) for i in range(4)]
    hps = [sb(f"hps{i}", [128, 512], BF16) for i in range(2)]
    habs = [sb(f"habs{i}", [128, 512], BF16) for i in range(4)]
    pmA = sb("pmA", [128, 512], BF16)
    pmB = sb("pmB", [128, 512], BF16)
    pmC = sb("pmC", [128, 512], BF16)
    ust = [sb(f"ust{i}", [128, 512]) for i in range(2)]

    ps = [ctx.enter_context(nc.psum_tensor(pfx + f"ps{i}", [128, 512], F32))
          for i in range(8)]

    lb = LnBufs(nc, ctx, pfx + "ln_",
                ps_st=[ps[0][0:1, 0:512], ps[1][0:1, 0:512],
                       ps[2][0:1, 0:512], ps[3][0:1, 0:512]],
                ps_bc=ps[4][:, 0:512])
    lb.emit_init(p)

    bank_last = {i: None for i in range(8)}
    wgu_last = [None] * 4
    ust_last = [None] * 2
    dw_last = [None, None]
    spl = {}
    h_last = {}

    def emit_sparsify(s_, nsub):
        mem = [s_, 16 + s_, 32 + s_, 48 + s_]
        lo, hi = 512 * nsub, 512 * (nsub + 1)
        ab = []
        for j in range(4):
            ab.append(p.s(lambda e, j=j, mm=mem[j], lo=lo, hi=hi: e.activation(
                habs[j][:, :], h_ap(mm, lo, hi), AF.Abs),
                [h_last[("w", mem[j], nsub)], spl.get(("abs", j))], tag="habs"))
        t1 = p.v(lambda e: e.tensor_max(pmA[:, :], habs[0][:, :], habs[1][:, :]),
                 [ab[0], ab[1], spl.get("pmA")], tag="sp")
        t2 = p.v(lambda e: e.tensor_tensor(pmC[:, :], habs[0][:, :],
                                           habs[1][:, :], ALU.min),
                 [ab[0], ab[1], spl.get("pmC")])
        t3 = p.v(lambda e: e.tensor_max(pmB[:, :], habs[2][:, :], habs[3][:, :]),
                 [ab[2], ab[3], spl.get("pmB")])
        t5 = p.v(lambda e: e.tensor_tensor(pmA[:, :], pmA[:, :], pmB[:, :],
                                           ALU.min), [t1, t3])
        t4 = p.v(lambda e: e.tensor_tensor(pmB[:, :], habs[2][:, :],
                                           habs[3][:, :], ALU.min),
                 [t5, ab[2], ab[3]])
        t6 = p.v(lambda e: e.tensor_max(pmC[:, :], pmC[:, :], pmB[:, :]),
                 [t2, t4])
        t7 = p.v(lambda e: e.tensor_max(pmA[:, :], pmA[:, :], pmC[:, :]),
                 [t5, t6])
        spl["pmB"] = t6
        spl["pmC"] = t7
        mus = []
        for j in range(4):
            mk = p.v(lambda e, j=j: e.tensor_tensor(
                habs[j][:, :], habs[j][:, :], pmA[:, :], ALU.is_ge),
                [t7], tag="mask")
            mu = p.v(lambda e, j=j, mm=mem[j], lo=lo, hi=hi: e.tensor_tensor(
                h_ap(mm, lo, hi), h_ap(mm, lo, hi), habs[j][:, :],
                ALU.mult), [mk], tag="maskmul")
            spl[("abs", j)] = mu
            h_last[("r", mem[j], nsub)] = mu
            mus.append(mu)
        spl["pmA"] = mus

    for nb in range(T // 1024):
        for b in range(4):
            lb.last[("st", b)] = bank_last[b]
        for k in range(KD):
            lb.last[("ar", k)] = [h_last.get(("xa", k))]
        chunk_toks, lnall, strd = emit_ln_block(p, lb, xi, 1024 * nb, AR, XLN0,
                                                load_fn=load_fn)
        for b in range(4):
            bank_last[b] = strd[b]
        last_gu_mm = None
        for m in range(64):
            par = m % 2
            gsl, usl = wgu[par], wgu[2 + par]
            lg = p.dma(gsl[:, :], d_wg[m], [wgu_last[par]], tag="wgld")
            lu = p.dma(usl[:, :], d_wu[m], [wgu_last[2 + par]], tag="wuld")
            for nsub in range(2):
                bg, bu = 2 * par, 2 * par + 1
                mmg = mmu = None
                for k in range(KD):
                    mmg = p.t(lambda e, k=k, gsl=gsl, nsub=nsub, bg=bg: e.matmul(
                        ps[bg][:, 0:512], gsl[:, 128 * k:128 * (k + 1)],
                        xln_ap(k, 512 * nsub, 512 * (nsub + 1)),
                        start=(k == 0), stop=(k == KD - 1)),
                        [chunk_toks[k], lg] + ([bank_last[bg]] if k == 0 else []),
                        tag="gmm")
                    mmu = p.t(lambda e, k=k, usl=usl, nsub=nsub, bu=bu: e.matmul(
                        ps[bu][:, 0:512], usl[:, 128 * k:128 * (k + 1)],
                        xln_ap(k, 512 * nsub, 512 * (nsub + 1)),
                        start=(k == 0), stop=(k == KD - 1)),
                        [chunk_toks[k], lu] + ([bank_last[bu]] if k == 0 else []),
                        tag="umm")
                sil = p.s(lambda e, par=par, bg=bg: e.activation(
                    hps[par][:, :], ps[bg][:, 0:512], AF.Silu),
                    [mmg, bank_last[bg]], tag="silu")
                hmul = p.v(lambda e, m=m, nsub=nsub, par=par, bu=bu: e.tensor_tensor(
                    h_ap(m, 512 * nsub, 512 * (nsub + 1)), hps[par][:, :],
                    ps[bu][:, 0:512], ALU.mult),
                    [sil, mmu, h_last.get(("hw", m, nsub))], tag="hmul")
                bank_last[bg] = sil
                bank_last[bu] = hmul
                h_last[("w", m, nsub)] = hmul
            wgu_last[par] = mmg
            wgu_last[2 + par] = mmu
            last_gu_mm = mmu
            if m >= 48:
                emit_sparsify(m - 48, 0)
                emit_sparsify(m - 48, 1)
        dwld = [None, None]
        for m2 in range(KD):
            par = m2 % 2
            dwsl = AR[:, XLN0 + 8192 * par: XLN0 + 8192 * (par + 1)]
            ld = p.dma(dwsl, d_wd[m2],
                       [last_gu_mm, dw_last[par], lnall], tag="wdld")
            dwld[par] = ld
            for nsub in range(2):
                bank = 5 + par
                mmd = None
                for k in range(64):
                    mmd = p.t(lambda e, k=k, nsub=nsub, bank=bank, dwsl=dwsl: e.matmul(
                        ps[bank][:, 0:512], dwsl[:, 128 * k:128 * (k + 1)],
                        h_ap(k, 512 * nsub, 512 * (nsub + 1)),
                        start=(k == 0), stop=(k == 63)),
                        [h_last[("r", k, nsub)], ld] +
                        ([bank_last[bank]] if k == 0 else []),
                        tag="dmm")
                si = nsub
                if load_fn is not None:
                    lx = load_fn(ust[si][:, :], m2, 2 * nb + nsub,
                                 [ust_last[si]])
                else:
                    lx = p.dma(ust[si][:, :], xi[128 * m2:128 * (m2 + 1),
                                                 1024 * nb + 512 * nsub:
                                                 1024 * nb + 512 * (nsub + 1)],
                               [ust_last[si]], tag="xreld")
                w2_ = p.v(lambda e, si=si, bank=bank: e.tensor_tensor(
                    ust[si][:, :], ust[si][:, :], ps[bank][:, 0:512], ALU.add),
                    [lx, mmd], tag="ffnevac")
                st = p.dmag(xo[128 * m2:128 * (m2 + 1),
                               1024 * nb + 512 * nsub:1024 * nb + 512 * (nsub + 1)],
                            ust[si][:, :], [w2_], tag="xost")
                ust_last[si] = st
                bank_last[bank] = w2_
            dw_last[par] = mmd
        for k in range(KD):
            h_last[("xa", k)] = [dw_last[k // 8], dwld[k // 8]]
        for m in range(64):
            for nsub in range(2):
                h_last[("hw", m, nsub)] = [dw_last[0], dw_last[1]]
    return


def p2_weights(inp):
    wg = np.asarray(inp["L0_ffn_w_gate"], np.float32)
    wu = np.asarray(inp["L0_ffn_w_up"], np.float32)
    wd = np.asarray(inp["L0_ffn_w_down"], np.float32)
    perm = np.array([4 * (i % 2048) + (i // 2048) for i in range(8192)])
    return {
        "wg": tile_lhsT(wg[:, perm]),
        "wu": tile_lhsT(wu[:, perm]),
        "wd": tile_lhsT(wd[perm, :]),
    }



def emit_p3f(nc, ctx, p, dr, pfx=""):
    SCALE = 1.0 / float(np.sqrt(128.0))

    xi = dr["xi"]
    d_wq, d_wk, d_wv, d_wo = dr["wq"], dr["wk"], dr["wv"], dr["wo"]
    d_qb, d_kb, d_vb, d_ob = dr["qb"], dr["kb"], dr["vb"], dr["ob"]
    d_slope, d_kgc, d_mstrip = dr["slopes"], dr["kgc"], dr["mstrip"]
    xo = dr["xo"]
    dq = dr["dq"]
    dko_h, dkg_h = dr["dko_h"], dr["dkg_h"]
    dvo_h, dvg_h = dr["dvo_h"], dr["dvg_h"]
    dko, dkg = dko_h.ap(), dkg_h.ap()
    dvo, dvg = dvo_h.ap(), dvg_h.ap()

    def sb(name, shape, dtype=F32):
        return ctx.enter_context(nc.sbuf_tensor(pfx + name, shape, dtype))

    AR = sb("AR", [128, 65536], BF16)
    def kth(hp, lo, hi):
        return AR[:, 32768 + hp * 4096 + lo: 32768 + hp * 4096 + hi]

    def vth(hp, lo, hi):
        return AR[:, 40960 + hp * 4096 + lo: 40960 + hp * 4096 + hi]

    def qth(hp, lo, hi):
        return AR[:, 49152 + hp * 2048 + lo: 49152 + hp * 2048 + hi]

    def pt_ap(pi, lo, hi):
        base = 53248 + pi * 512 if pi < 3 else 63616
        return AR[:, base + lo: base + hi]

    dinv_b = AR[:, 54784: 55296]
    mst128 = AR[:, 55296: 55424]

    def kod(hp, lo, hi):
        return AR[:, 55424 + hp * 2048 + lo: 55424 + hp * 2048 + hi]

    def vod(hp, lo, hi):
        return AR[:, 59520 + hp * 2048 + lo: 59520 + hp * 2048 + hi]

    def att_ap(h, qc):
        return AR[:, 2048 * h + 512 * qc: 2048 * h + 512 * (qc + 1)]

    def xln_ap(k, lo, hi):
        return AR[:, 2048 * k + lo: 2048 * k + hi]

    wsl = [sb(f"wsl{i}", [128, D], BF16) for i in range(2)]
    wvres = sb("wvres", [128, KD * 512], BF16)
    prst = [sb(f"prst{i}", [128, 512], BF16) for i in range(2)]
    biasc = sb("biasc", [128, KD])
    vb_sb = sb("vb_sb", [1, D], BF16)
    slope_sb = sb("slope_sb", [1, 16])
    slope_b = sb("slope_b", [128, 16])
    kgc_sb = sb("kgc_sb", [128, 773])
    biasfm = sb("biasfm", [128, 773])
    drow = sb("drow", [1, 512])
    drow_bf = sb("drow_bf", [1, 512], BF16)
    ones_rowb = sb("ones_rowb", [1, 128], BF16)
    ones_colb = sb("ones_colb", [128, 1], BF16)
    ones_row_f = sb("ones_row_f", [1, 128])
    xres = [sb(f"xres{i}", [128, 512]) for i in range(2)]

    ps = [ctx.enter_context(nc.psum_tensor(pfx + f"ps{i}", [128, 512], F32))
          for i in range(8)]

    lb = LnBufs(nc, ctx, pfx + "ln_",
                ps_st=[ps[0][0:1, 0:512], ps[1][0:1, 0:512],
                       ps[2][0:1, 0:512], ps[3][0:1, 0:512]],
                ps_bc=ps[4][:, 0:512])
    lb.emit_init(p)

    ms1 = p.v(lambda e: e.memset(ones_rowb[:, :], 1.0))
    ms2 = p.v(lambda e: e.memset(ones_colb[:, :], 1.0))
    ms3 = p.v(lambda e: e.memset(ones_row_f[:, :], 1.0))
    l_sl = p.dma(slope_sb[:, :], d_slope[:, :])
    l_kgc = p.dma(kgc_sb[:, :], d_kgc[:, :])
    l_vb = p.dma(vb_sb[:, :], d_vb[:, :])
    bsl = p.t(lambda e: e.matmul(ps[5][:, 0:16], ones_row_f[0:1, :],
                                 slope_sb[0:1, :], start=True, stop=True),
              [ms3, l_sl], tag="slopeb")
    bslc = p.v(lambda e: e.tensor_copy(slope_b[:, :], ps[5][:, 0:16]), [bsl])

    bank_last = {i: [None] for i in range(8)}
    bank_last[5] = [bslc]

    chunk_toks_all = [[] for _ in range(KD)]
    for nb in range(2):
        for b in range(4):
            lb.last[("st", b)] = bank_last[b]
        chunk_toks, lnall, strd = emit_ln_block(p, lb, xi, 1024 * nb, AR, 0,
                                                pitch=2048, boff=1024 * nb)
        for b in range(4):
            bank_last[b] = [strd[b]]
        for k in range(KD):
            chunk_toks_all[k].append(chunk_toks[k])

    v_store_toks = []
    wv_last = None
    prst_last = [None, None]
    for vc in range(4):
        lws = []
        for k in range(KD):
            lws.append(p.dma(wvres[:, 512 * k:512 * (k + 1)],
                             d_wv[k][:, 512 * vc:512 * (vc + 1)],
                             [wv_last], tag="wvld"))
        last_vmm = None
        for t in range(16):
            bank = 5 + (t % 2)
            pre = p.t(lambda e, vc=vc, bank=bank: e.matmul(
                ps[bank][:, 0:512], ones_rowb[0:1, :],
                vb_sb[0:1, 512 * vc:512 * (vc + 1)], start=True, stop=False),
                [ms1, l_vb, bank_last[bank]], tag="vpre")
            mm = pre
            for k in range(KD):
                mm = p.t(lambda e, k=k, t=t, bank=bank: e.matmul(
                    ps[bank][:, 0:512],
                    xln_ap(k, 128 * t, 128 * (t + 1)),
                    wvres[:, 512 * k:512 * (k + 1)],
                    start=False, stop=(k == KD - 1)),
                    [chunk_toks_all[k], lws[k]], tag="vmm")
            pr = t % 2
            ev = p.s(lambda e, pr=pr, bank=bank: e.activation(
                prst[pr][:, :], ps[bank][:, 0:512], AF.Copy),
                [mm, prst_last[pr]], tag="vev")
            bank_last[bank] = [ev]
            sts = []
            for j in range(4):
                sts.append(p.dmag(dvo[4 * vc + j, t],
                                  prst[pr][:, 128 * j:128 * (j + 1)], [ev],
                                  tag="vst"))
            prst_last[pr] = sts
            v_store_toks += sts
            last_vmm = mm
        wv_last = last_vmm

    wsl_last = [None, None]
    st_by_head_q = {}
    st_by_head_k = {}

    def proj_featmajor(dram_w, dram_out, ntok, bias_ld, stmap, tag):
        for m in range(KD):
            par = m % 2
            lw = p.dma(wsl[par][:, :], dram_w[m], [wsl_last[par]], tag=tag + "w")
            stoks = []
            mm = None
            for qc in range(ntok // 512):
                bank = 5 + (qc % 2)
                for k in range(KD):
                    mm = p.t(lambda e, k=k, qc=qc, par=par, bank=bank: e.matmul(
                        ps[bank][:, 0:512], wsl[par][:, 128 * k:128 * (k + 1)],
                        xln_ap(k, 512 * qc, 512 * (qc + 1)),
                        start=(k == 0), stop=(k == KD - 1)),
                        [chunk_toks_all[k], lw] +
                        (bank_last[bank] if k == 0 else []), tag=tag)
                pr = qc % 2
                ev = p.s(lambda e, m=m, pr=pr, bank=bank: e.activation(
                    prst[pr][:, :], ps[bank][:, 0:512], AF.Identity,
                    bias=biasc[:, m:m + 1]),
                    [mm, bias_ld, prst_last[pr]], tag=tag + "ev")
                st = p.dmag(dram_out[128 * m:128 * (m + 1),
                                     512 * qc:512 * (qc + 1)],
                            prst[pr][:, :], [ev], tag=tag + "st")
                bank_last[bank] = [ev]
                prst_last[pr] = st
                stoks.append(st)
            stmap[m] = stoks
            wsl_last[par] = mm

    coll_v = p.coll("AllGather", ALU.bypass, PAIRS,
                    [dvo_h.ap().opt()], [dvg_h.ap().opt()],
                    waits=[v_store_toks], tag="agv")

    lkb = p.dma(biasc[:, :], d_kb[:, :], [], tag="kbld")
    proj_featmajor(d_wk, dko, TOWN, lkb, st_by_head_k, "kp")
    coll_k = p.coll("AllGather", ALU.bypass, PAIRS,
                    [dko_h.ap().opt()], [dkg_h.ap().opt()],
                    waits=[_flat(list(st_by_head_k.values()))], tag="agk")
    lqb = p.dma(biasc[:, :], d_qb[:, :],
                [_flat(list(st_by_head_k.values()))], tag="qbld")
    proj_featmajor(d_wq, dq, TOWN, lqb, st_by_head_q, "qp")

    all_proj = _flat(list(st_by_head_q.values()) + list(st_by_head_k.values())) \
        + v_store_toks
    lms = p.dma(mst128, d_mstrip[:, :], [wv_last, wsl_last], tag="mstld")
    ar_last = {"k": [None, None], "v": [None, None], "q": [None, None],
               "ko": [None, None], "vo": [None, None]}
    pt_last = [None, None, None, None]
    att_last = {}
    dinv_last = None
    biasfm_last = None
    drow_last = None
    drow_bf_last = None
    den_pend = None
    TL = {0: list(range(8)) + list(range(16, 19)),
          1: list(range(8)) + list(range(16, 23)),
          2: list(range(11)) + list(range(16, 32)),
          3: list(range(15)) + list(range(16, 32))}
    slopes_f = 2.0 ** (-np.linspace(1.0, 8.0, 16))
    gorder_np = np.r_[cols_of(0), cols_of(1)]
    TLH = {}
    for h_ in range(16):
        cut = 60.0 / float(slopes_f[h_])
        TLH[h_] = {}
        for qc_ in range(4):
            keep = []
            for t_ in TL[qc_]:
                kg = gorder_np[128 * t_:128 * (t_ + 1)]
                ok = False
                for par in range(2):
                    qcols = cols_of(par)[512 * qc_:512 * (qc_ + 1)]
                    dmat = qcols[:, None] - kg[None, :]
                    dpos = dmat[dmat >= 0]
                    if dpos.size and dpos.min() <= cut:
                        ok = True
                        break
                if ok:
                    keep.append(t_)
            assert len(keep) >= 2, (h_, qc_, keep)
            TLH[h_][qc_] = keep
    UQ = {qc_: list(range(4 * qc_, 4 * qc_ + 4)) +
          list(range(16 + 4 * qc_, 16 + 4 * qc_ + 4)) for qc_ in range(4)}
    first_waits = all_proj + [lms]

    for h in range(16):
        hp = h % 2
        lkA = p.dma(kth(hp, 0, 2048), dkg[0, 128 * h:128 * (h + 1), :],
                    [coll_k, ar_last["k"][hp],
                     first_waits if h < 2 else None], tag="kld")
        lkB = p.dma(kth(hp, 2048, 4096), dkg[1, 128 * h:128 * (h + 1), :],
                    [coll_k], tag="kld")
        lk = [lkA, lkB]
        lko = p.dma(kod(hp, 0, 2048), dko[128 * h:128 * (h + 1), :],
                    [st_by_head_k[h], ar_last["ko"][hp],
                     first_waits if h < 2 else None], tag="kold")
        lvs = []
        for t in range(32):
            lvs.append(p.dma(vth(hp, 128 * t, 128 * (t + 1)),
                             dvg[t // 16, h, t % 16],
                             [coll_v, ar_last["v"][hp],
                              first_waits if h < 2 else None], tag="vld"))
        lvo = []
        for tt in range(16):
            lvo.append(p.dma(vod(hp, 128 * tt, 128 * (tt + 1)), dvo[h, tt],
                             [ar_last["vo"][hp],
                              first_waits if h < 2 else None], tag="vold"))
        lq = p.dma(qth(hp, 0, 2048), dq[128 * h:128 * (h + 1), :],
                   [st_by_head_q[h], ar_last["q"][hp],
                    first_waits if h < 2 else None], tag="qld")
        bft = p.v(lambda e, h=h: e.tensor_scalar(
            biasfm[:, :], kgc_sb[:, :], slope_b[:, h:h + 1], None, ALU.mult),
            [l_kgc, bslc, biasfm_last], tag="bfm")
        kreaders = []
        koreaders = []
        voreaders = []
        vreaders = []
        qreaders = []
        exs = []
        SB = (0, 1, 7)

        def make_den(h, qc, dmm, pvm, pvbank, dbank):
            def go():
                nonlocal drow_last, drow_bf_last, dinv_last
                rcp = p.v(lambda e, dbank=dbank: e.reciprocal(
                    drow[0:1, :], ps[dbank][0:1, 0:512]),
                    [dmm, drow_last], tag="rcp")
                rcb = p.v(lambda e: e.tensor_copy(drow_bf[0:1, :],
                                                  drow[0:1, :]),
                          [rcp, drow_bf_last], tag="rcb")
                bank_last[dbank] = [rcp]
                bcm = p.t(lambda e: e.matmul(ps[6][:, 0:512],
                                             ones_rowb[0:1, :],
                                             drow_bf[0:1, :], start=True,
                                             stop=True),
                          [rcb, ms1] + bank_last[6], tag="dbc")
                dcp = p.v(lambda e: e.tensor_copy(dinv_b, ps[6][:, 0:512]),
                          [bcm, dinv_last], tag="dcp")
                bank_last[6] = [dcp]
                drow_last = rcb
                drow_bf_last = bcm
                amul = p.v(lambda e, h=h, qc=qc, pvbank=pvbank:
                           e.tensor_tensor(
                    att_ap(h, qc), ps[pvbank][:, 0:512], dinv_b, ALU.mult),
                    [pvm, dcp, att_last.get((h, qc))], tag="attdiv")
                dinv_last = amul
                bank_last[pvbank] = [amul]
                att_last[(h, qc)] = amul
            return go

        for qc in range(4):
            gl = TLH[h][qc]
            tlist = [("G", gl[0])] + [("D", s) for s in range(4)] + \
                    [("G", t) for t in gl[1:]]
            pvbank = 2 + (qc % 2)
            dbank = 4 + (qc % 2)
            ntl = len(tlist)
            dmm = pvm = None
            pq = []

            def mk_pend(reg, t, pi, ptok, i_, last):
                def go():
                    if reg == "G":
                        dm = p.t(lambda e, pi=pi, dbank=dbank, first=(i_ == 0),
                                 last=last: e.matmul(
                            ps[dbank][0:1, 0:512], ones_colb[:, :],
                            pt_ap(pi, 0, 512), start=first, stop=last),
                            [ptok, ms2] + (bank_last[dbank] if i_ == 0 else []),
                            tag="dmm")
                        pv = p.t(lambda e, pi=pi, t=t, hp=hp, pvbank=pvbank,
                                 first=(i_ == 0), last=last: e.matmul(
                            ps[pvbank][:, 0:512],
                            vth(hp, 128 * t, 128 * (t + 1)),
                            pt_ap(pi, 0, 512), start=first, stop=last),
                            [ptok, lvs[t]] +
                            (bank_last[pvbank] if i_ == 0 else []),
                            tag="pvm")
                        vreaders.append(pv)
                    else:
                        ot = 4 * qc + t
                        dm = p.t(lambda e, pi=pi, t=t, dbank=dbank: e.matmul(
                            ps[dbank][0:1, 128 * t:128 * (t + 1)],
                            ones_colb[:, :],
                            pt_ap(pi, 0, 128), start=False, stop=False),
                            [ptok, ms2], tag="ddmm")
                        pv = p.t(lambda e, pi=pi, t=t, ot=ot, hp=hp,
                                 pvbank=pvbank: e.matmul(
                            ps[pvbank][:, 128 * t:128 * (t + 1)],
                            vod(hp, 128 * ot, 128 * (ot + 1)),
                            pt_ap(pi, 0, 128), start=False, stop=False),
                            [ptok, lvo[ot]], tag="dpvm")
                        voreaders.append(pv)
                    pt_last[pi] = [dm, pv]
                    return dm, pv
                return go

            for i_, (reg, t) in enumerate(tlist):
                sbank = SB[i_ % 3]
                pi = i_ % 4
                last = i_ == ntl - 1
                light = slopes_f[h] * 4095.0 <= 60.0
                if reg == "G":
                    smm = p.t(lambda e, t=t, hp=hp, qc=qc, sbank=sbank: e.matmul(
                        ps[sbank][:, 0:512], kth(hp, 128 * t, 128 * (t + 1)),
                        qth(hp, 512 * qc, 512 * (qc + 1)),
                        start=True, stop=True),
                        [lk, lq] + bank_last[sbank], tag="smm")
                    kreaders.append(smm)
                    qreaders.append(smm)
                    if light and t not in UQ[qc]:
                        idx = 513 + 32 * qc + t
                        exl = [p.s(lambda e, pi=pi, idx=idx, sbank=sbank:
                                   e.activation(
                            pt_ap(pi, 0, 512), ps[sbank][:, 0:512],
                            AF.Exp, scale=SCALE,
                            bias=biasfm[:, idx:idx + 1]),
                            [smm, bft, pt_last[pi]], tag="exl")]
                    else:
                        jb = (641 + 32 * qc + 4 * UQ[qc].index(t)) if light \
                            else None
                        exl = []
                        for sl_ in range(4):
                            idx = jb + sl_ if light else (4 * qc + sl_) * 32 + t
                            exl.append(p.s(lambda e, pi=pi, sl_=sl_, idx=idx,
                                           sbank=sbank: e.activation(
                                pt_ap(pi, 128 * sl_, 128 * (sl_ + 1)),
                                ps[sbank][:, 128 * sl_:128 * (sl_ + 1)],
                                AF.Exp, scale=SCALE,
                                bias=biasfm[:, idx:idx + 1]),
                                [smm, bft, pt_last[pi] if sl_ == 0 else None],
                                tag="exp"))
                    exs += exl
                    bank_last[sbank] = exl
                    ptok = exl
                else:
                    ot = 4 * qc + t
                    smm = p.t(lambda e, t=t, ot=ot, hp=hp, qc=qc, sbank=sbank:
                              e.matmul(
                        ps[sbank][:, 0:128],
                        kod(hp, 128 * ot, 128 * (ot + 1)),
                        qth(hp, 512 * qc + 128 * t, 512 * qc + 128 * (t + 1)),
                        start=True, stop=True),
                        [lko, lq] + bank_last[sbank], tag="dsmm")
                    koreaders.append(smm)
                    qreaders.append(smm)
                    didx = (769 + t) if light else 512
                    ex = p.s(lambda e, pi=pi, didx=didx, sbank=sbank:
                             e.activation(
                        pt_ap(pi, 0, 128), ps[sbank][:, 0:128],
                        AF.Exp, scale=SCALE, bias=biasfm[:, didx:didx + 1]),
                        [smm, bft, pt_last[pi]], tag="dexp")
                    exs.append(ex)
                    bank_last[sbank] = [ex]
                    msk = p.v(lambda e, pi=pi: e.tensor_tensor(
                        pt_ap(pi, 0, 128), pt_ap(pi, 0, 128), mst128,
                        ALU.mult), [ex, lms], tag="dmask")
                    ptok = msk
                if i_ == 4 and den_pend is not None:
                    den_pend()
                    den_pend = None
                if len(pq) == 2:
                    dmm, pvm = pq.pop(0)()
                pq.append(mk_pend(reg, t, pi, ptok, i_, last))
            for go in pq:
                dmm, pvm = go()
            den_pend = make_den(h, qc, dmm, pvm, pvbank, dbank)
        ar_last["k"][hp] = kreaders[-1]
        ar_last["ko"][hp] = koreaders
        ar_last["v"][hp] = vreaders
        ar_last["vo"][hp] = voreaders
        ar_last["q"][hp] = qreaders[-1]
        biasfm_last = exs
    if den_pend is not None:
        den_pend()
        den_pend = None

    lob = p.dma(biasc[:, :], d_ob[:, :],
                [_flat(list(st_by_head_k.values()))], tag="obld")
    xres_last = [None, None]
    for m in range(KD):
        par = m % 2
        lw = p.dma(wsl[par][:, :], d_wo[m], [wsl_last[par]], tag="wow")
        mm = None
        for qc in range(4):
            bank = 5 + (qc % 2)
            for k in range(KD):
                mm = p.t(lambda e, k=k, qc=qc, par=par, bank=bank: e.matmul(
                    ps[bank][:, 0:512], wsl[par][:, 128 * k:128 * (k + 1)],
                    att_ap(k, qc), start=(k == 0), stop=(k == KD - 1)),
                    [att_last[(k, qc)], lw] +
                    (bank_last[bank] if k == 0 else []), tag="omm")
            pr = qc % 2
            lx = p.dma(xres[pr][:, :], xi[128 * m:128 * (m + 1),
                                          512 * qc:512 * (qc + 1)],
                       [xres_last[pr]], tag="oxld")
            evb = p.s(lambda e, m=m, pr=pr: e.activation(
                xres[pr][:, :], xres[pr][:, :], AF.Identity,
                bias=biasc[:, m:m + 1]), [lx, lob], tag="oevb")
            yv = p.v(lambda e, pr=pr, bank=bank: e.tensor_tensor(
                xres[pr][:, :], xres[pr][:, :], ps[bank][:, 0:512], ALU.add),
                [evb, mm], tag="oadd")
            st = p.dmag(xo[128 * m:128 * (m + 1), 512 * qc:512 * (qc + 1)],
                        xres[pr][:, :], [yv], tag="oxst")
            bank_last[bank] = [yv]
            xres_last[pr] = st
        wsl_last[par] = mm
    return


def p3f_inputs(inp, h):
    ipj = np.asarray(inp["L0_attn_b_in_proj"], np.float32)
    ipb = np.asarray(inp["L0_attn_b_in_proj_bias"], np.float32)
    ow = np.asarray(inp["L0_attn_b_out_w"], np.float32)
    ob = np.asarray(inp["L0_attn_b_out_b"], np.float32)
    sl = np.asarray(inp["L0_attn_b_alibi_slopes"], np.float32)
    own = cols_of(h)
    gorder = np.r_[cols_of(0), cols_of(1)]
    kgc = np.zeros((128, 773), np.float32)
    for qs in range(16):
        qmax = float(own[qs * 128 + 127])
        tdiag = qs if h == 0 else 16 + qs
        for t in range(32):
            kg = gorder[128 * t:128 * (t + 1)].astype(np.float32)
            col = np.where(kg <= qmax, kg - qmax, -1e9).astype(np.float32)
            if t == tdiag:
                col[:] = -1e9
            kgc[:, qs * 32 + t] = col
    kgc[:, 512] = np.arange(128, dtype=np.float32) - 127.0
    for qc in range(4):
        qmaxc = float(own[qc * 512 + 511])
        for t in range(32):
            kg = gorder[128 * t:128 * (t + 1)].astype(np.float32)
            kgc[:, 513 + 32 * qc + t] = np.where(kg <= qmaxc, kg - qmaxc,
                                                 -1e9)
        uq = list(range(4 * qc, 4 * qc + 4)) + \
            list(range(16 + 4 * qc, 16 + 4 * qc + 4))
        for j, t in enumerate(uq):
            kg = gorder[128 * t:128 * (t + 1)].astype(np.float32)
            for s_ in range(4):
                qmax_s = float(own[qc * 512 + 128 * s_ + 127])
                tdiag = (4 * qc + s_) if h == 0 else 16 + 4 * qc + s_
                col = np.where(kg <= qmax_s, kg - qmaxc, -1e9)
                if t == tdiag:
                    col = np.full(128, -1e9, np.float32)
                kgc[:, 641 + 32 * qc + 4 * j + s_] = col
    for s_ in range(4):
        kgc[:, 769 + s_] = (np.arange(128, dtype=np.float32)
                            + 128 * s_ - 511.0)
    tri = (np.arange(128)[:, None] <= np.arange(128)[None, :])
    return {
        "wq": tile_lhsT(ipj[0:2048].T),
        "wk": tile_lhsT(ipj[2048:4096].T),
        "wv": to_bf(np.ascontiguousarray(
            ipj[4096:6144].T).reshape(KD, 128, D)),
        "wo": tile_lhsT(ow.T),
        "qb": np.ascontiguousarray(ipb[0:2048].reshape(KD, 128).T),
        "kb": np.ascontiguousarray(ipb[2048:4096].reshape(KD, 128).T),
        "vb": to_bf(ipb[4096:6144].reshape(1, 2048)),
        "ob": np.ascontiguousarray(ob.reshape(KD, 128).T),
        "slopes": sl.reshape(1, 16),
        "kgc": kgc,
        "mstrip": to_bf(tri.astype(np.float32)),
    }



def emit_p4f(nc, ctx, p, dr, pfx=""):
    xi = dr["xi"]
    d_wq, d_wk, d_wv, d_wo = dr["wq"], dr["wk"], dr["wv"], dr["wo"]
    d_qb, d_kb, d_vb, d_ob = dr["qb"], dr["kb"], dr["vb"], dr["ob"]
    d_mq, d_mk = dr["wmapq"], dr["wmapk"]
    xo = dr["xo"]
    dq = dr["dq"]
    dko = dr["dko"]
    dvo = dr["dvo"]
    pki_h, pko_h = dr["pki_h"], dr["pko_h"]

    def sb(name, shape, dtype=F32):
        return ctx.enter_context(nc.sbuf_tensor(pfx + name, shape, dtype))

    AR = sb("AR", [128, 65536], BF16)

    def kth(hp, lo, hi):
        return AR[:, 32768 + hp * 4096 + lo: 32768 + hp * 4096 + hi]

    def vth(hp, lo, hi):
        return AR[:, 40960 + hp * 4096 + lo: 40960 + hp * 4096 + hi]

    def qth(hp, lo, hi):
        return AR[:, 49152 + hp * 2048 + lo: 49152 + hp * 2048 + hi]

    def qmh(hp, lo, hi):
        return AR[0:64, 53248 + 2048 * hp + lo: 53248 + 2048 * hp + hi]

    def att_ap(h, qc):
        return AR[:, 2048 * h + 512 * qc: 2048 * h + 512 * (qc + 1)]

    def xln_ap(k, lo, hi):
        return AR[:, 2048 * k + lo: 2048 * k + hi]

    wsl = [sb(f"wsl{i}", [128, D], BF16) for i in range(2)]
    wvres = sb("wvres", [128, KD * 512], BF16)
    prst = [sb(f"prst{i}", [128, 512], BF16) for i in range(2)]
    biasc = sb("biasc", [128, KD])
    vb_sb = sb("vb_sb", [1, D], BF16)
    mq_sb = sb("mq_sb", [128, 64], BF16)
    mk_sb = sb("mk_sb", [128, 64], BF16)
    kmt = [sb(f"kmt{i}", [128, 512], BF16) for i in range(2)]
    escr = [sb(f"escr{i}", [128, 512]) for i in range(2)]
    kv_sb = [sb(f"kv_sb{i}", [64, 128], BF16) for i in range(2)]
    ksum_sb = [sb(f"ksum_sb{i}", [64, 1], BF16) for i in range(2)]
    kvf = [sb(f"kvf{i}", [128, 132]) for i in range(2)]
    kvr = [sb(f"kvr{i}", [64, 132]) for i in range(2)]
    nrow = sb("nrow", [1, 512])
    nrow_bf = sb("nrow_bf", [1, 512], BF16)
    nb_sb = sb("nb_sb", [128, 512], BF16)
    eps128 = sb("eps128", [128, 1])
    ones_rowb = sb("ones_rowb", [1, 128], BF16)
    ones_colb = sb("ones_colb", [128, 1], BF16)
    xres = [sb(f"xres{i}", [128, 512]) for i in range(2)]

    ps = [ctx.enter_context(nc.psum_tensor(pfx + f"ps{i}", [128, 512], F32))
          for i in range(8)]

    lb = LnBufs(nc, ctx, pfx + "ln_",
                ps_st=[ps[0][0:1, 0:512], ps[1][0:1, 0:512],
                       ps[2][0:1, 0:512], ps[3][0:1, 0:512]],
                ps_bc=ps[4][:, 0:512])
    lb.emit_init(p)

    ms1 = p.v(lambda e: e.memset(ones_rowb[:, :], 1.0))
    ms2 = p.v(lambda e: e.memset(ones_colb[:, :], 1.0))
    mse = p.v(lambda e: e.memset(eps128[:, :], 1e-6))
    msz = [p.v(lambda e, i=i: e.memset(kvf[i][:, :], 0.0)) for i in range(2)]
    l_vb = p.dma(vb_sb[:, :], d_vb[:, :])
    l_mq = p.dma(mq_sb[:, :], d_mq[:, :])
    l_mk = p.dma(mk_sb[:, :], d_mk[:, :])

    bank_last = {i: [None] for i in range(8)}

    chunk_toks_all = [[] for _ in range(KD)]
    for nb in range(2):
        for b in range(4):
            lb.last[("st", b)] = bank_last[b]
        chunk_toks, lnall, strd = emit_ln_block(p, lb, xi, 1024 * nb, AR, 0,
                                                pitch=2048, boff=1024 * nb)
        for b in range(4):
            bank_last[b] = [strd[b]]
        for k in range(KD):
            chunk_toks_all[k].append(chunk_toks[k])

    v_store_toks = []
    wv_last = None
    prst_last = [None, None]
    for vc in range(4):
        lws = []
        for k in range(KD):
            lws.append(p.dma(wvres[:, 512 * k:512 * (k + 1)],
                             d_wv[k][:, 512 * vc:512 * (vc + 1)],
                             [wv_last], tag="wvld"))
        last_vmm = None
        for t in range(16):
            bank = 5 + (t % 2)
            pre = p.t(lambda e, vc=vc, bank=bank: e.matmul(
                ps[bank][:, 0:512], ones_rowb[0:1, :],
                vb_sb[0:1, 512 * vc:512 * (vc + 1)], start=True, stop=False),
                [ms1, l_vb] + bank_last[bank], tag="vpre")
            mm = pre
            for k in range(KD):
                mm = p.t(lambda e, k=k, t=t, bank=bank: e.matmul(
                    ps[bank][:, 0:512],
                    xln_ap(k, 128 * t, 128 * (t + 1)),
                    wvres[:, 512 * k:512 * (k + 1)],
                    start=False, stop=(k == KD - 1)),
                    [chunk_toks_all[k], lws[k]], tag="vmm")
            pr = t % 2
            ev = p.s(lambda e, pr=pr, bank=bank: e.activation(
                prst[pr][:, :], ps[bank][:, 0:512], AF.Copy),
                [mm, prst_last[pr]], tag="vev")
            bank_last[bank] = [ev]
            sts = []
            for j in range(4):
                sts.append(p.dmag(dvo[4 * vc + j, t],
                                  prst[pr][:, 128 * j:128 * (j + 1)], [ev],
                                  tag="vst"))
            prst_last[pr] = sts
            v_store_toks += sts
            last_vmm = mm
        wv_last = last_vmm

    wsl_last = [None, None]
    st_by_head_q = {}
    st_by_head_k = {}

    def proj_featmajor(dram_w, dram_out, ntok, bias_ld, stmap, tag):
        nonlocal wsl_last
        for m in range(KD):
            par = m % 2
            lw = p.dma(wsl[par][:, :], dram_w[m], [wsl_last[par]], tag=tag + "w")
            stoks = []
            mm = None
            for qc in range(ntok // 512):
                bank = 5 + (qc % 2)
                for k in range(KD):
                    mm = p.t(lambda e, k=k, qc=qc, par=par, bank=bank: e.matmul(
                        ps[bank][:, 0:512], wsl[par][:, 128 * k:128 * (k + 1)],
                        xln_ap(k, 512 * qc, 512 * (qc + 1)),
                        start=(k == 0), stop=(k == KD - 1)),
                        [chunk_toks_all[k], lw] +
                        (bank_last[bank] if k == 0 else []), tag=tag)
                pr = qc % 2
                ev = p.s(lambda e, m=m, pr=pr, bank=bank: e.activation(
                    prst[pr][:, :], ps[bank][:, 0:512], AF.Identity,
                    bias=biasc[:, m:m + 1]),
                    [mm, bias_ld, prst_last[pr]], tag=tag + "ev")
                st = p.dmag(dram_out[128 * m:128 * (m + 1),
                                     512 * qc:512 * (qc + 1)],
                            prst[pr][:, :], [ev], tag=tag + "st")
                bank_last[bank] = [ev]
                prst_last[pr] = st
                stoks.append(st)
            stmap[m] = stoks
            wsl_last[par] = mm

    lqb = p.dma(biasc[:, :], d_qb[:, :], [], tag="qbld")
    proj_featmajor(d_wq, dq, TOWN, lqb, st_by_head_q, "qp")
    lkb = p.dma(biasc[:, :], d_kb[:, :],
                [_flat(list(st_by_head_q.values()))], tag="kbld")
    proj_featmajor(d_wk, dko, TOWN, lkb, st_by_head_k, "kp")

    all_proj = _flat(list(st_by_head_q.values()) + list(st_by_head_k.values())) \
        + v_store_toks
    ar_last = {"k": [None, None], "v": [None, None], "q": [None, None]}
    first_waits = all_proj
    att_last = {}
    kmt_last = [None, None]
    escr_last = [None, None]
    kv_last = [None, None]
    ksum_last = [None, None]
    kvf_last = [msz[0], msz[1]]
    kvr_last = [None, None]
    pk_last = [None, None]
    nscr_last = None
    nrowbf_last = None
    nb_last = None

    def elu1(dst_ap, src_ps, esl, waits, fd):
        r1 = p.v(lambda e, dst_ap=dst_ap: e.tensor_scalar_max(
            escr[esl][:, 0:fd], src_ps, 0.0), waits + [escr_last[esl]],
            tag="elu_r")
        r2 = p.v(lambda e: e.tensor_tensor(escr[esl][:, 0:fd], src_ps,
                                           escr[esl][:, 0:fd], ALU.min)
                 if False else e.tensor_sub(escr[esl][:, 0:fd], src_ps,
                                            escr[esl][:, 0:fd]),
                 [r1], tag="elu_s")
        r3 = p.s(lambda e: e.activation(escr[esl][:, 0:fd], escr[esl][:, 0:fd],
                                        AF.Exp) if False else None, [])
        return None


    def emit_elu1(dst_ap, src_ps, np_, fd, waits):
        w1 = p.v(lambda e: e.tensor_scalar_max(escr[1][0:np_, 0:fd], src_ps,
                                               0.0),
                 waits + [escr_last[1]], tag="elu_r")
        w2 = p.v(lambda e: e.tensor_sub(escr[0][0:np_, 0:fd], src_ps,
                                        escr[1][0:np_, 0:fd]),
                 [w1, escr_last[0]], tag="elu_m")
        w3 = p.s(lambda e: e.activation(escr[0][0:np_, 0:fd],
                                        escr[0][0:np_, 0:fd],
                                        AF.Exp), [w2], tag="elu_e")
        w4 = p.v(lambda e, dst_ap=dst_ap: e.tensor_tensor(
            dst_ap, escr[0][0:np_, 0:fd], escr[1][0:np_, 0:fd], ALU.add), [w3],
            tag="elu_a")
        escr_last[0] = w4
        escr_last[1] = w4
        return w4

    qm_last = [None, None]

    def pre_head(h):
        hp = h % 2
        lk = p.dma(kth(hp, 0, 2048), dko[128 * h:128 * (h + 1), :],
                   [st_by_head_k[h], ar_last["k"][hp],
                    first_waits if h < 2 else None], tag="kld")
        lvs = []
        for t in range(16):
            lvs.append(p.dma(vth(hp, 128 * t, 128 * (t + 1)), dvo[h, t],
                             [ar_last["v"][hp],
                              first_waits if h < 2 else None], tag="vld"))
        lq = p.dma(qth(hp, 0, 2048), dq[128 * h:128 * (h + 1), :],
                   [st_by_head_q[h], ar_last["q"][hp],
                    first_waits if h < 2 else None], tag="qld")
        qreaders = []
        qm_toks = []
        for qc in range(4):
            bank = qc % 2
            mmq = p.t(lambda e, qc=qc, hp=hp, bank=bank: e.matmul(
                ps[bank][0:64, 0:512], mq_sb[:, :],
                qth(hp, 512 * qc, 512 * (qc + 1)), start=True, stop=True),
                [lq, l_mq] + bank_last[bank], tag="qmap")
            qreaders.append(mmq)
            e1_ = emit_elu1(qmh(hp, 512 * qc, 512 * (qc + 1)),
                            ps[bank][0:64, 0:512], 64, 512,
                            [mmq, qm_last[hp]])
            bank_last[bank] = [e1_]
            qm_toks.append(e1_)
        kvmm = ksmm = None
        ekg = [None, None]
        for g in range(2):
            kb = 2 + g
            mms = []
            for j in range(8):
                t = 8 * g + j
                mms.append(p.t(lambda e, t=t, j=j, hp=hp, kb=kb: e.matmul(
                    ps[kb][:, 64 * j:64 * (j + 1)],
                    kth(hp, 128 * t, 128 * (t + 1)),
                    mk_sb[:, :], start=True, stop=True),
                    [lk, l_mk] + (bank_last[kb] if j == 0 else []),
                    tag="kmap"))
            ekg[g] = emit_elu1(kmt[g][:, 0:512], ps[kb][:, 0:512], 128, 512,
                               [mms, kmt_last[g]])
            bank_last[kb] = [ekg[g]]
        for g in range(2):
            for j in range(8):
                t = 8 * g + j
                kvmm = p.t(lambda e, t=t, g=g, j=j, hp=hp, first=(t == 0),
                           last=(t == 15): e.matmul(
                    ps[4][0:64, 0:128], kmt[g][:, 64 * j:64 * (j + 1)],
                    vth(hp, 128 * t, 128 * (t + 1)), start=first, stop=last),
                    [ekg[g], lvs[t]] + (bank_last[4] if t == 0 else []),
                    tag="kvmm")
                ksmm = p.t(lambda e, g=g, j=j, first=(t == 0),
                           last=(t == 15): e.matmul(
                    ps[5][0:64, 0:1], kmt[g][:, 64 * j:64 * (j + 1)],
                    ones_colb[:, :], start=first, stop=last),
                    [ekg[g], ms2] + (bank_last[5] if t == 0 else []),
                    tag="ksmm")
            kmt_last[g] = [kvmm, ksmm]
        kvc0 = p.v(lambda e, hp=hp: e.tensor_copy(kvf[hp][0:64, 0:128],
                                                  ps[4][0:64, 0:128]),
                   [kvmm, kvf_last[hp]], tag="kvpk")
        ksc0 = p.v(lambda e, hp=hp: e.tensor_copy(kvf[hp][0:64, 128:129],
                                                  ps[5][0:64, 0:1]),
                   [ksmm, kvc0], tag="kspk")
        bank_last[4] = [kvc0]
        bank_last[5] = [ksc0]
        wrp = p.dma(pki_h[hp].ap()[:, :], kvf[hp][:, 0:129],
                    [kvc0, ksc0, pk_last[hp]], tag="kvwr")
        kvf_last[hp] = wrp
        cl = p.coll("AllReduce", ALU.add, PAIRS,
                    [pki_h[hp].ap().opt()], [pko_h[hp].ap().opt()],
                    waits=[wrp], tag="kvar")
        ar_last["k"][hp] = [ksmm, kvmm]
        ar_last["v"][hp] = kvmm
        ar_last["q"][hp] = qreaders[-1]
        return {"hp": hp, "qm_toks": qm_toks, "cl": cl,
                "kvmm": kvmm, "ksmm": ksmm}

    def post_head(h, st):
        nonlocal nscr_last, nrowbf_last, nb_last
        hp = st["hp"]
        qm_toks = st["qm_toks"]
        lkv = p.dma(kvr[hp][0:64, 0:129], pko_h[hp].ap()[0:64, 0:129],
                    [st["cl"], kvr_last[hp]], tag="kvrd")
        pk_last[hp] = lkv
        kvc = p.v(lambda e, hp=hp: e.tensor_copy(kv_sb[hp][:, :],
                                                 kvr[hp][0:64, 0:128]),
                  [lkv, kv_last[hp]], tag="kvc")
        ksc = p.v(lambda e, hp=hp: e.tensor_copy(ksum_sb[hp][:, :],
                                                 kvr[hp][0:64, 128:129]),
                  [lkv, ksum_last[hp]], tag="ksc")
        kvr_last[hp] = [kvc, ksc]
        omms = []
        nmms = []
        for qc in range(4):
            nmm = p.t(lambda e, qc=qc, hp=hp: e.matmul(
                ps[6][0:1, 0:512], ksum_sb[hp][:, :],
                qmh(hp, 512 * qc, 512 * (qc + 1)),
                start=True, stop=True),
                [ksc, qm_toks[qc]] + bank_last[6], tag="nmm")
            nr = p.v(lambda e: e.tensor_scalar_add(nrow[0:1, :],
                                                   ps[6][0:1, 0:512], 1e-6),
                     [nmm, nscr_last], tag="nadd")
            bank_last[6] = [nr]
            nr2 = p.v(lambda e: e.reciprocal(nrow[0:1, :], nrow[0:1, :]),
                      [nr], tag="nrcp")
            nr3 = p.v(lambda e: e.tensor_copy(nrow_bf[0:1, :], nrow[0:1, :]),
                      [nr2, nrowbf_last], tag="nbf")
            nscr_last = nr3
            nbm = p.t(lambda e: e.matmul(ps[7][:, 0:512],
                                         ones_rowb[0:1, :],
                                         nrow_bf[0:1, :], start=True,
                                         stop=True),
                      [nr3, ms1] + bank_last[7], tag="nbc")
            nrowbf_last = nbm
            nrc = p.v(lambda e: e.tensor_copy(nb_sb[:, :], ps[7][:, 0:512]),
                      [nbm, nb_last], tag="nbcc")
            bank_last[7] = [nrc]
            bank = qc % 2
            omm = p.t(lambda e, qc=qc, hp=hp, bank=bank: e.matmul(
                ps[bank][:, 0:512], kv_sb[hp][:, :],
                qmh(hp, 512 * qc, 512 * (qc + 1)),
                start=True, stop=True),
                [qm_toks[qc], kvc] + bank_last[bank], tag="outmm")
            omms.append(omm)
            nmms.append(nmm)
            oev = p.v(lambda e, h=h, qc=qc, bank=bank: e.tensor_tensor(
                att_ap(h, qc), ps[bank][:, 0:512], nb_sb[:, :], ALU.mult),
                [omm, nrc, att_last.get((h, qc))], tag="oev")
            nb_last = oev
            bank_last[bank] = [oev]
            att_last[(h, qc)] = oev
        qm_last[hp] = omms + nmms
        kv_last[hp] = omms
        ksum_last[hp] = nmms

    prev_st = None
    for h in range(16):
        st = pre_head(h)
        if prev_st is not None:
            post_head(h - 1, prev_st)
        prev_st = st
    post_head(15, prev_st)

    lob = p.dma(biasc[:, :], d_ob[:, :],
                [_flat(list(st_by_head_k.values()))], tag="obld")
    xres_last = [None, None]
    for m in range(KD):
        par = m % 2
        lw = p.dma(wsl[par][:, :], d_wo[m], [wsl_last[par]], tag="wow")
        mm = None
        for qc in range(4):
            bank = 5 + (qc % 2)
            for k in range(KD):
                mm = p.t(lambda e, k=k, qc=qc, par=par, bank=bank: e.matmul(
                    ps[bank][:, 0:512], wsl[par][:, 128 * k:128 * (k + 1)],
                    att_ap(k, qc), start=(k == 0), stop=(k == KD - 1)),
                    [att_last[(k, qc)], lw] +
                    (bank_last[bank] if k == 0 else []), tag="omm2")
            pr = qc % 2
            lx = p.dma(xres[pr][:, :], xi[128 * m:128 * (m + 1),
                                          512 * qc:512 * (qc + 1)],
                       [xres_last[pr]], tag="oxld")
            evb = p.s(lambda e, m=m, pr=pr: e.activation(
                xres[pr][:, :], xres[pr][:, :], AF.Identity,
                bias=biasc[:, m:m + 1]), [lx, lob], tag="oevb")
            yv = p.v(lambda e, pr=pr, bank=bank: e.tensor_tensor(
                xres[pr][:, :], xres[pr][:, :], ps[bank][:, 0:512], ALU.add),
                [evb, mm], tag="oadd")
            st = p.dmag(xo[128 * m:128 * (m + 1), 512 * qc:512 * (qc + 1)],
                        xres[pr][:, :], [yv], tag="oxst")
            bank_last[bank] = [yv]
            xres_last[pr] = st
        wsl_last[par] = mm
    return


def p4_inputs(inp):
    return {
        "wq": tile_lhsT(np.asarray(inp["L1_attn_a_wq"], np.float32).T),
        "wk": tile_lhsT(np.asarray(inp["L1_attn_a_wk"], np.float32).T),
        "wv": to_bf(np.ascontiguousarray(
            np.asarray(inp["L1_attn_a_wv"], np.float32).T).reshape(KD, 128, D)),
        "wo": tile_lhsT(np.asarray(inp["L1_attn_a_wo"], np.float32).T),
        "qb": np.ascontiguousarray(
            np.asarray(inp["L1_attn_a_bq"], np.float32).reshape(KD, 128).T),
        "kb": np.ascontiguousarray(
            np.asarray(inp["L1_attn_a_bk"], np.float32).reshape(KD, 128).T),
        "vb": to_bf(np.asarray(inp["L1_attn_a_bv"], np.float32).reshape(1, D)),
        "ob": np.ascontiguousarray(
            np.asarray(inp["L1_attn_a_bo"], np.float32).reshape(KD, 128).T),
        "wmapq": to_bf(np.asarray(inp["L1_attn_a_wmap_q"], np.float32).T),
        "wmapk": to_bf(np.asarray(inp["L1_attn_a_wmap_k"], np.float32).T),
    }



def emit_p5(nc, ctx, p, dr, pfx="", xl_emit=None, thr_h=None):
    xi = dr["xi"]
    d_xl = dr.get("xlast")
    thki_h, thko_h = thr_h
    d_ing, d_inb = dr["ing"], dr["inb"]
    d_png, d_pnb = dr["png"], dr["pnb"]
    d_outg, d_outb = dr["outg"], dr["outb"]
    d_agg, d_aggb = dr["agg"], dr["aggb"]
    d_d1, d_d1b = dr["d1w"], dr["d1b"]
    d_d2, d_d2b = dr["d2w"], dr["d2b"]
    d_bc, d_bcb = dr["bcw"], dr["bcb"]
    d_gw, d_gb = dr["gw"], dr["gb"]
    xo = dr["xo"]

    def sb(name, shape, dtype=F32):
        return ctx.enter_context(nc.sbuf_tensor(pfx + name, shape, dtype))

    AR = sb("AR", [128, 65536], BF16)

    def xln_ap(k, lo, hi):
        return AR[:, 2048 * k + lo: 2048 * k + hi]

    def o_ap(k, lo, hi):
        return AR[:, 32768 + 2048 * k + lo: 32768 + 2048 * k + hi]

    wsl = [sb(f"wsl{i}", [128, D], BF16) for i in range(2)]
    wsl2 = [sb(f"wsl2_{i}", [128, 4096], BF16) for i in range(2)]
    wsl2_last = [None, None]
    thp = sb("thp", [128, KD])
    thr = sb("thr", [128, KD])
    vg = {}
    for nm, dd in [("ing", d_ing), ("inb", d_inb), ("png", d_png),
                   ("pnb", d_pnb), ("outg", d_outg), ("outb", d_outb),
                   ("aggb", d_aggb), ("d2b", d_d2b), ("bcb", d_bcb),
                   ("gb", d_gb)]:
        vg[nm] = sb("t_" + nm, [128, KD])
    d1b_sb = sb("d1b_sb", [128, 32])
    xl_fm = sb("xl_fm", [128, KD])
    xn_fm = sb("xn_fm", [128, KD])
    th_fm = sb("th_fm", [128, KD])
    tn_fm = sb("tn_fm", [128, KD])
    tn_bf = sb("tn_bf", [128, KD], BF16)
    g_bf = sb("g_bf", [128, 64], BF16)
    sq_fm = sb("sq_fm", [128, KD])
    vrow = sb("vrow", [1, 64])
    vcol = sb("vcol", [128, 4])
    bc_fm = sb("bc_fm", [128, KD])
    gsb = [sb(f"gsb{i}", [128, 512], BF16) for i in range(2)]
    ones_col = sb("ones_col", [128, 1])
    ones_rowf = sb("ones_rowf", [1, 128])
    cns2 = sb("cns2", [1, 1])
    xres = [sb(f"xres{i}", [128, 512]) for i in range(2)]
    oscr = sb("oscr", [128, 512])

    ps = [ctx.enter_context(nc.psum_tensor(pfx + f"ps{i}", [128, 512], F32))
          for i in range(8)]

    lb = LnBufs(nc, ctx, pfx + "ln_",
                ps_st=[ps[0][0:1, 0:512], ps[1][0:1, 0:512],
                       ps[2][0:1, 0:512], ps[3][0:1, 0:512]],
                ps_bc=ps[4][:, 0:512])
    lb.emit_init(p)
    msc = p.v(lambda e: e.memset(cns2[:, :], 1e-6))
    ms1 = p.v(lambda e: e.memset(ones_col[:, :], 1.0))
    ms2 = p.v(lambda e: e.memset(ones_rowf[:, :], 1.0))
    ldt = {}
    for nm, dd in [("ing", d_ing), ("inb", d_inb), ("png", d_png),
                   ("pnb", d_pnb), ("outg", d_outg), ("outb", d_outb),
                   ("aggb", d_aggb), ("d2b", d_d2b), ("bcb", d_bcb),
                   ("gb", d_gb)]:
        ldt[nm] = p.dma(vg[nm][:, :], dd[:, :], tag="vgld")
    ld1b = p.dma(d1b_sb[:, :], d_d1b[:, :])
    if xl_emit is not None:
        lxl = xl_emit(xl_fm)
    else:
        lxl = p.dma(xl_fm[:, :], d_xl[:, :])

    bank_last = {i: [None] for i in range(8)}
    vrow_last = [None]
    vcol_last = [None]
    sq_last = [None]

    def vec_ln(src, dst, eps_ap, gname=None, bname=None, waits=()):
        sq = p.s(lambda e: e.activation(sq_fm[:, :], src[:, :], AF.Square),
                 [waits, sq_last[0]], tag="vln_sq")
        m1 = p.t(lambda e: e.matmul(ps[6][0:1, 0:KD], ones_col[:, :],
                                    src[:, :], start=True, stop=True),
                 [waits, ms1] + bank_last[6], tag="vln_s1")
        m2 = p.t(lambda e: e.matmul(ps[7][0:1, 0:KD], ones_col[:, :],
                                    sq_fm[:, :], start=True, stop=True),
                 [sq, ms1] + bank_last[7], tag="vln_s2")
        r1 = p.v(lambda e: e.reduce_sum(vrow[0:1, 0:1], ps[6][0:1, 0:KD],
                                        axis=mybir.AxisListType.X),
                 [m1, vrow_last[0]], tag="vln_r1")
        r2 = p.v(lambda e: e.reduce_sum(vrow[0:1, 1:2], ps[7][0:1, 0:KD],
                                        axis=mybir.AxisListType.X), [m2, r1],
                 tag="vln_r2")
        bank_last[6] = [r1]
        bank_last[7] = [r2]
        r3 = p.v(lambda e: e.tensor_scalar_mul(vrow[0:1, 0:1], vrow[0:1, 0:1],
                                               1.0 / D), [r1], tag="vln_m")
        r4 = p.v(lambda e: e.tensor_mul(vrow[0:1, 2:3], vrow[0:1, 0:1],
                                        vrow[0:1, 0:1]), [r3])
        r5 = p.v(lambda e: e.scalar_tensor_tensor(vrow[0:1, 1:2],
                                                  vrow[0:1, 1:2], 1.0 / D,
                                                  vrow[0:1, 2:3], ALU.mult,
                                                  ALU.subtract), [r4, r2])
        r6 = p.s(lambda e: e.activation(vrow[0:1, 1:2], vrow[0:1, 1:2], AF.Ln,
                                        bias=lb.cns[0:1, 0:1] if eps_ap is None
                                        else eps_ap), [r5, lb.init, msc],
                 tag="vln_l")
        r7 = p.s(lambda e: e.activation(vrow[0:1, 1:2], vrow[0:1, 1:2], AF.Exp,
                                        scale=-0.5), [r6])
        vrow_last[0] = r7
        bm = p.t(lambda e: e.matmul(ps[6][:, 0:2], ones_rowf[0:1, :],
                                    vrow[0:1, 0:2], start=True, stop=True),
                 [r7, r3, ms2] + bank_last[6], tag="vln_bc")
        bc_ = p.v(lambda e: e.tensor_copy(vcol[:, 0:2], ps[6][:, 0:2]),
                  [bm, vcol_last[0]], tag="vln_bcc")
        bank_last[6] = [bc_]
        ap_ = p.v(lambda e, dst=dst: e.tensor_scalar(
            dst[:, :], src[:, :], vcol[:, 0:1], vcol[:, 1:2],
            ALU.subtract, ALU.mult), [bc_, waits], tag="vln_ap")
        vcol_last[0] = ap_
        sq_last[0] = ap_
        out_tok = ap_
        if gname is not None:
            g1 = p.v(lambda e, dst=dst, gname=gname: e.tensor_tensor(
                dst[:, :], dst[:, :], vg[gname][:, :], ALU.mult),
                [ap_, ldt[gname]], tag="vln_g")
            g2 = p.v(lambda e, dst=dst, bname=bname: e.tensor_tensor(
                dst[:, :], dst[:, :], vg[bname][:, :], ALU.add),
                [g1, ldt[bname]], tag="vln_b")
            out_tok = g2
        return out_tok

    chunk_toks_all = [[] for _ in range(KD)]
    for nb in range(2):
        for b in range(4):
            lb.last[("st", b)] = bank_last[b]
        chunk_toks, lnall, strd = emit_ln_block(p, lb, xi, 1024 * nb, AR, 0,
                                                pitch=2048, boff=1024 * nb)
        for b in range(4):
            bank_last[b] = [strd[b]]
        for k in range(KD):
            chunk_toks_all[k].append(chunk_toks[k])

    gwsl = [sb(f"gwsl{i}", [128, D], BF16) for i in range(2)]
    gwsl_last = [None, None]
    gw_ld = {}
    go_toks = {}
    gstate = {"i": 0}

    def emit_gate_groups(n):
        for _ in range(n):
            i = gstate["i"]
            if i >= 64:
                return
            gstate["i"] = i + 1
            m, qc = i // 4, i % 4
            par = m % 2
            if qc == 0:
                gw_ld[m] = p.dma(gwsl[par][:, :], d_gw[m], [gwsl_last[par]],
                                 tag="gww")
            lw = gw_ld[m]
            bank = 2 + (qc % 2)
            mm = None
            for k in range(KD):
                mm = p.t(lambda e, k=k, qc=qc, par=par, bank=bank: e.matmul(
                    ps[bank][:, 0:512], gwsl[par][:, 128 * k:128 * (k + 1)],
                    xln_ap(k, 512 * qc, 512 * (qc + 1)),
                    start=(k == 0), stop=(k == KD - 1)),
                    [chunk_toks_all[k], lw] +
                    (bank_last[bank] if k == 0 else []), tag="gmm")
            sg = p.s(lambda e, m=m, qc=qc, bank=bank: e.activation(
                o_ap(m, 512 * qc, 512 * (qc + 1)), ps[bank][:, 0:512],
                AF.Sigmoid, bias=vg["gb"][:, m:m + 1]),
                [mm, ldt["gb"], go_toks.get((m, qc))], tag="gsig")
            bank_last[bank] = [sg]
            go_toks[(m, qc)] = sg
            if qc == 3:
                gwsl_last[par] = mm

    emit_gate_groups(4)
    ln1 = vec_ln(xl_fm, xn_fm, None, waits=[lxl])
    emit_gate_groups(3)
    ln2 = vec_ln(xn_fm, xn_fm, None, gname="ing", bname="inb", waits=[ln1])
    emit_gate_groups(3)
    xn_bf_tok = p.v(lambda e: e.tensor_copy(tn_bf[:, :], xn_fm[:, :]), [ln2],
                    tag="xnbf")

    wsl_last = [None, None]
    th_tok = None

    def vec_matmul(dst_fm, bias_fm, dram_w, nm_, src_bf, src_tok):
        toks = []
        for m in range(nm_):
            if m % 4 == 3:
                emit_gate_groups(1)
            par = m % 2
            lw = p.dma(wsl[par][:, :], dram_w[m], [wsl_last[par]], tag="vmw")
            bank = m % 2
            mm = None
            for k in range(KD):
                mm = p.t(lambda e, k=k, par=par, bank=bank: e.matmul(
                    ps[bank][:, 0:1], wsl[par][:, 128 * k:128 * (k + 1)],
                    src_bf[:, k:k + 1], start=(k == 0), stop=(k == KD - 1)),
                    [src_tok, lw] + (bank_last[bank] if k == 0 else []),
                    tag="vmm")
            wsl_last[par] = mm
            ev = p.s(lambda e, m=m, bank=bank, dst_fm=dst_fm, bias_fm=bias_fm:
                     e.activation(dst_fm[:, m:m + 1], ps[bank][:, 0:1],
                                  AF.Identity, bias=bias_fm[:, m:m + 1]),
                     [mm, toks], tag="vmev")
            bank_last[bank] = [ev]
            toks.append(ev)
        return toks

    agg_t = vec_matmul(th_fm, vg["aggb"], d_agg, KD, tn_bf, [xn_bf_tok])
    th_tok = agg_t
    cl_prev = None
    lkr_prev = None
    thp_last = None
    thr_last = None
    for step in range(4):
        lnp = vec_ln(th_fm, tn_fm, None, gname="png", bname="pnb",
                     waits=[th_tok])
        emit_gate_groups(2)
        tbf = p.v(lambda e: e.tensor_copy(tn_bf[:, :], tn_fm[:, :]), [lnp],
                  tag="tbf")
        g_toks = []
        for m in range(32):
            if m % 8 == 7:
                emit_gate_groups(1)
            par = m % 2
            lw = p.dma(wsl[par][:, :], d_d1[m], [wsl_last[par]], tag="d1w")
            bank = m % 2
            mm = None
            for k in range(KD):
                mm = p.t(lambda e, k=k, par=par, bank=bank: e.matmul(
                    ps[bank][:, 0:1], wsl[par][:, 128 * k:128 * (k + 1)],
                    tn_bf[:, k:k + 1], start=(k == 0), stop=(k == KD - 1)),
                    [tbf, lw] + (bank_last[bank] if k == 0 else []), tag="d1m")
            wsl_last[par] = mm
            ev = p.s(lambda e, m=m, bank=bank: e.activation(
                g_bf[:, m:m + 1], ps[bank][:, 0:1], AF.Gelu,
                bias=d1b_sb[:, m:m + 1]), [mm, ld1b, g_toks], tag="d1ev")
            bank_last[bank] = [ev]
            g_toks.append(ev)
        d2_toks = []
        for m2 in range(KD):
            if m2 % 4 == 3:
                emit_gate_groups(1)
            par = m2 % 2
            lw = p.dma(wsl2[par][:, :], d_d2[m2], [wsl2_last[par]], tag="d2w")
            bank = m2 % 2
            mm = None
            for k2 in range(32):
                mm = p.t(lambda e, k2=k2, par=par, bank=bank: e.matmul(
                    ps[bank][:, 0:1], wsl2[par][:, 128 * k2:128 * (k2 + 1)],
                    g_bf[:, k2:k2 + 1], start=(k2 == 0), stop=(k2 == 31)),
                    [g_toks, lw] + (bank_last[bank] if k2 == 0 else []),
                    tag="d2m")
            wsl2_last[par] = mm
            ev = p.s(lambda e, m2=m2, bank=bank: e.activation(
                thp[:, m2:m2 + 1], ps[bank][:, 0:1], AF.Copy),
                [mm, d2_toks, thp_last], tag="d2ev")
            bank_last[bank] = [ev]
            d2_toks.append(ev)
        wr = p.dma(thki_h.ap()[:, :], thp[:, :], [d2_toks, cl_prev],
                   tag="thwr")
        thp_last = wr
        cl = p.coll("AllReduce", ALU.add, PAIRS,
                    [thki_h.ap().opt()], [thko_h.ap().opt()],
                    waits=[wr, lkr_prev], tag="thar")
        cl_prev = cl
        lkr = p.dma(thr[:, :], thko_h.ap()[:, :], [cl, thr_last], tag="thrd")
        lkr_prev = lkr
        a1 = p.v(lambda e: e.tensor_tensor(thr[:, :], thr[:, :],
                                           vg["d2b"][:, :], ALU.add),
                 [lkr, ldt["d2b"]], tag="thb")
        a2 = p.v(lambda e: e.tensor_tensor(th_fm[:, :], th_fm[:, :],
                                           thr[:, :], ALU.add),
                 [a1, lnp], tag="thacc")
        thr_last = a2
        th_tok = a2
    thbf = p.v(lambda e: e.tensor_copy(tn_bf[:, :], th_fm[:, :]), [th_tok],
               tag="thbf")
    bc_t = vec_matmul(bc_fm, vg["bcb"], d_bc, KD, tn_bf, [thbf])

    emit_gate_groups(64)
    xres_last = [None, None]
    o_toks = {}
    for m in range(KD):
        for qc in range(4):
            ot = p.v(lambda e, m=m, qc=qc: e.scalar_tensor_tensor(
                o_ap(m, 512 * qc, 512 * (qc + 1)),
                o_ap(m, 512 * qc, 512 * (qc + 1)),
                bc_fm[:, m:m + 1], xln_ap(m, 512 * qc, 512 * (qc + 1)),
                ALU.mult, ALU.add), [go_toks[(m, qc)], bc_t,
                                     chunk_toks_all[m]], tag="gate_o")
            o_toks[(m, qc)] = ot

    for nb in range(2):
        stat_mms = []
        sq_tl = {}
        for k in range(KD):
            otk = [o_toks[(k, 2 * nb)], o_toks[(k, 2 * nb + 1)]]
            for sub in range(2):
                sqo = p.s(lambda e, k=k, nb=nb, sub=sub: e.activation(
                    lb.sqbf[sub][:, 0:512],
                    o_ap(k, 1024 * nb + 512 * sub,
                         1024 * nb + 512 * (sub + 1)), AF.Square),
                    [otk, sq_tl.get(sub)], tag="osq")
                m1 = p.t(lambda e, k=k, nb=nb, sub=sub: e.matmul(
                    ps[0 + sub][0:1, 0:512], lb.ones_colb[:, :],
                    o_ap(k, 1024 * nb + 512 * sub,
                         1024 * nb + 512 * (sub + 1)),
                    start=(k == 0), stop=(k == KD - 1)),
                    [otk, lb.init] + (bank_last[0 + sub] if k == 0 else []),
                    tag="ost1")
                m2 = p.t(lambda e, sub=sub, k=k: e.matmul(
                    ps[2 + sub][0:1, 0:512], lb.ones_colb[:, :],
                    lb.sqbf[sub][:, 0:512],
                    start=(k == 0), stop=(k == KD - 1)),
                    [sqo, lb.init] + (bank_last[2 + sub] if k == 0 else []),
                    tag="ost2")
                sq_tl[sub] = m2
                stat_mms += [m1, m2]
        bct = []
        R = lb.srow
        for sub in range(2):
            m_ = p.v(lambda e, sub=sub: e.tensor_scalar_mul(
                R[0:1, 0:512], ps[0 + sub][0:1, 0:512], 1.0 / D),
                [stat_mms, lb.last.get("rows")], tag="ornm")
            ms_ = p.v(lambda e: e.tensor_mul(R[0:1, 512:1024], R[0:1, 0:512],
                                             R[0:1, 0:512]), [m_])
            var_ = p.v(lambda e, sub=sub: e.scalar_tensor_tensor(
                R[0:1, 512:1024], ps[2 + sub][0:1, 0:512], 1.0 / D,
                R[0:1, 512:1024], ALU.mult, ALU.subtract), [ms_, stat_mms])
            lnv = p.s(lambda e: e.activation(R[0:1, 512:1024],
                                             R[0:1, 512:1024], AF.Ln,
                                             bias=cns2[0:1, 0:1]),
                      [var_, msc], tag="olnv")
            rstd = p.s(lambda e: e.activation(R[0:1, 512:1024],
                                              R[0:1, 512:1024], AF.Exp,
                                              scale=-0.5), [lnv])
            c1c = p.v(lambda e: e.tensor_copy(lb.srow_bf[0:1, 0:512],
                                              R[0:1, 512:1024]),
                      [rstd, lb.last.get("srowbf")], tag="oc1")
            c0c = p.v(lambda e: e.scalar_tensor_tensor(
                lb.srow_bf[0:1, 512:1024], R[0:1, 0:512], -1.0,
                R[0:1, 512:1024], ALU.mult, ALU.mult),
                [rstd, m_, lb.last.get("srowbf")], tag="oc0")
            bm1 = p.t(lambda e: e.matmul(ps[4][:, 0:512], lb.ones_rowb[0:1, :],
                                         lb.srow_bf[0:1, 0:512], start=True,
                                         stop=True),
                      [c1c, lb.init, lb.last.get("psbc")], tag="obc1")
            bc1 = p.v(lambda e, sub=sub: e.tensor_copy(lb.cb1[sub][:, :],
                                                       ps[4][:, 0:512]),
                      [bm1, lb.last.get(("cb1", sub))], tag="obc1c")
            bm2 = p.t(lambda e: e.matmul(ps[4][:, 0:512], lb.ones_rowb[0:1, :],
                                         lb.srow_bf[0:1, 512:1024], start=True,
                                         stop=True), [c0c, bc1], tag="obc2")
            bc0 = p.v(lambda e, sub=sub: e.tensor_copy(lb.cb0[sub][:, :],
                                                       ps[4][:, 0:512]),
                      [bm2, lb.last.get(("cb0", sub))], tag="obc0c")
            lb.last["psbc"] = bc0
            lb.last["srowbf"] = [bm1, bm2]
            lb.last["rows"] = [var_, c0c, rstd, c1c]
            bank_last[0 + sub] = [m_]
            bank_last[2 + sub] = [var_]
            bct.append((bc1, bc0))
        apl = []
        for k in range(KD):
            for sub in range(2):
                bc1, bc0 = bct[sub]
                qc = 2 * nb + sub
                pr = sub
                otk = [o_toks[(k, 2 * nb)], o_toks[(k, 2 * nb + 1)]]
                lx = p.dma(xres[pr][:, :], xi[128 * k:128 * (k + 1),
                                              512 * qc:512 * (qc + 1)],
                           [xres_last[pr]], tag="oxld")
                t1 = p.v(lambda e, k=k, nb=nb, sub=sub: e.tensor_tensor(
                    oscr[:, 0:512],
                    o_ap(k, 1024 * nb + 512 * sub,
                         1024 * nb + 512 * (sub + 1)),
                    lb.cb1[sub][:, :], ALU.mult), [bc1, otk], tag="oap1")
                t2 = p.v(lambda e, sub=sub: e.tensor_tensor(
                    oscr[:, 0:512], oscr[:, 0:512], lb.cb0[sub][:, :],
                    ALU.add), [t1, bc0], tag="oap2")
                t3 = p.v(lambda e, k=k: e.tensor_scalar(
                    oscr[:, 0:512], oscr[:, 0:512], vg["outg"][:, k:k + 1],
                    vg["outb"][:, k:k + 1], ALU.mult, ALU.add),
                    [t2, ldt["outg"], ldt["outb"]], tag="oap3")
                t4 = p.v(lambda e, pr=pr: e.tensor_tensor(
                    xres[pr][:, :], xres[pr][:, :], oscr[:, 0:512], ALU.add),
                    [t3, lx], tag="oap4")
                st = p.dmag(xo[128 * k:128 * (k + 1),
                               512 * qc:512 * (qc + 1)], xres[pr][:, :],
                            [t4], tag="oxst")
                xres_last[pr] = st
                apl.append(st)
        for sub in range(2):
            lb.last[("cb1", sub)] = apl
            lb.last[("cb0", sub)] = apl
    return


def p5f_inputs(inp, h):
    def fm(name, n=KD):
        return np.ascontiguousarray(
            np.asarray(inp[name], np.float32).reshape(n, 128).T)
    return {
        "ing": fm("L1_ffn_in_gamma"), "inb": fm("L1_ffn_in_beta"),
        "png": fm("L1_ffn_pn_gamma"), "pnb": fm("L1_ffn_pn_beta"),
        "outg": fm("L1_ffn_out_gamma"), "outb": fm("L1_ffn_out_beta"),
        "agg": tile_lhsT(np.asarray(inp["L1_ffn_agg_w"], np.float32)),
        "aggb": fm("L1_ffn_agg_b"),
        "d1w": np.ascontiguousarray(tile_lhsT(
            np.asarray(inp["L1_ffn_d1_w"], np.float32))[32 * h:32 * (h + 1)]),
        "d1b": np.ascontiguousarray(
            fm("L1_ffn_d1_b", 64)[:, 32 * h:32 * (h + 1)]),
        "d2w": np.ascontiguousarray(tile_lhsT(
            np.asarray(inp["L1_ffn_d2_w"],
                       np.float32))[:, :, 4096 * h:4096 * (h + 1)]),
        "d2b": fm("L1_ffn_d2_b"),
        "bcw": tile_lhsT(np.asarray(inp["L1_ffn_bc_w"], np.float32)),
        "bcb": fm("L1_ffn_bc_b"),
        "gw": tile_lhsT(np.asarray(inp["L1_ffn_gate_w"], np.float32)),
        "gb": fm("L1_ffn_gate_b"),
    }



def emit_p6f(nc, ctx, p, dr, pfx=""):
    SCALE = 1.0 / float(np.sqrt(128.0))

    xi = dr["xi"]
    d_wq, d_wk, d_wv, d_wo = dr["wq"], dr["wk"], dr["wv"], dr["wo"]
    d_cq, d_sq = dr["cosq"], dr["sinq"]
    xo = dr["xo"]
    dq = dr["dq"]
    dko_h, dkg_h = dr["dko_h"], dr["dkg_h"]
    dvo_h, dvg_h = dr["dvo_h"], dr["dvg_h"]
    dko, dkg = dko_h.ap(), dkg_h.ap()
    dvo, dvg = dvo_h.ap(), dvg_h.ap()

    def sb(name, shape, dtype=F32):
        return ctx.enter_context(nc.sbuf_tensor(pfx + name, shape, dtype))

    AR = sb("AR", [128, 65536], BF16)

    def kth(hp, lo, hi):
        return AR[:, 32768 + hp * 4096 + lo: 32768 + hp * 4096 + hi]

    def vth(hp, lo, hi):
        return AR[:, 40960 + hp * 4096 + lo: 40960 + hp * 4096 + hi]

    def qth(hp, lo, hi):
        return AR[:, 49152 + hp * 2048 + lo: 49152 + hp * 2048 + hi]

    def pt_ap(pi, lo, hi):
        base = 53248 + pi * 512 if pi < 3 else 63616
        return AR[:, base + lo: base + hi]

    dinv_b = AR[:, 54784: 55296]

    def att_ap(h, qc):
        return AR[:, 2048 * h + 512 * qc: 2048 * h + 512 * (qc + 1)]

    def xln_ap(k, lo, hi):
        return AR[:, 2048 * k + lo: 2048 * k + hi]

    wsl = [sb(f"wsl{i}", [128, D], BF16) for i in range(2)]
    wvres = sb("wvres", [128, KD * 512], BF16)
    prst = [sb(f"prst{i}", [128, 512], BF16) for i in range(2)]
    ctb = [sb(f"ctb{i}", [128, 512], BF16) for i in range(2)]
    stb = [sb(f"stb{i}", [128, 512], BF16) for i in range(2)]
    rsc = [sb(f"rsc{i}", [128, 512]) for i in range(2)]
    drow = sb("drow", [1, 512])
    drow_bf = sb("drow_bf", [1, 512], BF16)
    ones_rowb = sb("ones_rowb", [1, 128], BF16)
    ones_colb = sb("ones_colb", [128, 1], BF16)
    xres = [sb(f"xres{i}", [128, 512]) for i in range(2)]

    ps = [ctx.enter_context(nc.psum_tensor(pfx + f"ps{i}", [128, 512], F32))
          for i in range(8)]

    lb = LnBufs(nc, ctx, pfx + "ln_",
                ps_st=[ps[0][0:1, 0:512], ps[1][0:1, 0:512],
                       ps[2][0:1, 0:512], ps[3][0:1, 0:512]],
                ps_bc=ps[4][:, 0:512])
    lb.emit_init(p)

    ms1 = p.v(lambda e: e.memset(ones_rowb[:, :], 1.0))
    ms2 = p.v(lambda e: e.memset(ones_colb[:, :], 1.0))

    bank_last = {i: [None] for i in range(8)}

    chunk_toks_all = [[] for _ in range(KD)]
    for nb in range(2):
        for b in range(4):
            lb.last[("st", b)] = bank_last[b]
        chunk_toks, lnall, strd = emit_ln_block(p, lb, xi, 1024 * nb, AR, 0,
                                                pitch=2048, boff=1024 * nb)
        for b in range(4):
            bank_last[b] = [strd[b]]
        for k in range(KD):
            chunk_toks_all[k].append(chunk_toks[k])

    v_store_toks = []
    lws = []
    for k in range(KD):
        lws.append(p.dma(wvres[:, 512 * k:512 * (k + 1)], d_wv[k],
                         tag="wvld"))
    prst_last = [None, None]
    wv_last = None
    for t in range(16):
        bank = 5 + (t % 2)
        mm = None
        for k in range(KD):
            mm = p.t(lambda e, k=k, t=t, bank=bank: e.matmul(
                ps[bank][:, 0:512],
                xln_ap(k, 128 * t, 128 * (t + 1)),
                wvres[:, 512 * k:512 * (k + 1)],
                start=(k == 0), stop=(k == KD - 1)),
                [chunk_toks_all[k], lws[k]] +
                (bank_last[bank] if k == 0 else []), tag="vmm")
        pr = t % 2
        ev = p.s(lambda e, pr=pr, bank=bank: e.activation(
            prst[pr][:, :], ps[bank][:, 0:512], AF.Copy),
            [mm, prst_last[pr]], tag="vev")
        bank_last[bank] = [ev]
        sts = []
        for j in range(4):
            sts.append(p.dmag(dvo[j, t], prst[pr][:, 128 * j:128 * (j + 1)],
                              [ev], tag="vst"))
        prst_last[pr] = sts
        v_store_toks += sts
        wv_last = mm

    wsl_last = [None, None]
    st_by_head_q = {}
    st_by_head_k = {}
    ctb_last = [None, None]
    stb_last = [None, None]
    rsc_last = [None, None]

    def proj_rope(dram_w, dram_out, nblocks, ntok, d_cos, d_sin, stmap, tag):
        nonlocal wsl_last
        for m in range(nblocks):
            par = m % 2
            lw = p.dma(wsl[par][:, :], dram_w[m], [wsl_last[par]], tag=tag + "w")
            stoks = []
            mm = None
            for qc in range(ntok // 512):
                bank = 5 + (qc % 2)
                for k in range(KD):
                    mm = p.t(lambda e, k=k, qc=qc, par=par, bank=bank: e.matmul(
                        ps[bank][:, 0:512], wsl[par][:, 128 * k:128 * (k + 1)],
                        xln_ap(k, 512 * qc, 512 * (qc + 1)),
                        start=(k == 0), stop=(k == KD - 1)),
                        [chunk_toks_all[k], lw] +
                        (bank_last[bank] if k == 0 else []), tag=tag)
                pr = qc % 2
                lc = p.dma(ctb[pr][:, :], d_cos[:, 512 * qc:512 * (qc + 1)],
                           [ctb_last[pr]], tag=tag + "ct")
                ls = p.dma(stb[pr][:, :], d_sin[:, 512 * qc:512 * (qc + 1)],
                           [stb_last[pr]], tag=tag + "st")
                c1 = p.v(lambda e, pr=pr, bank=bank: e.tensor_copy(
                    rsc[pr][0:64, 0:512], ps[bank][64:128, 0:512]),
                    [mm, rsc_last[pr]], tag=tag + "r1")
                c2 = p.v(lambda e, pr=pr, bank=bank: e.tensor_copy(
                    rsc[pr][64:128, 0:512], ps[bank][0:64, 0:512]),
                    [c1], tag=tag + "r2")
                m2_ = p.v(lambda e, pr=pr: e.tensor_tensor(
                    rsc[pr][:, 0:512], rsc[pr][:, 0:512], stb[pr][:, :],
                    ALU.mult), [c2, ls], tag=tag + "r3")
                m1_ = p.v(lambda e, pr=pr, bank=bank: e.tensor_tensor(
                    ps[bank][:, 0:512], ps[bank][:, 0:512], ctb[pr][:, :],
                    ALU.mult), [m2_, lc, mm], tag=tag + "r4")
                a1 = p.v(lambda e, pr=pr, bank=bank: e.tensor_tensor(
                    prst[pr][:, :], ps[bank][:, 0:512], rsc[pr][:, 0:512],
                    ALU.add), [m1_, prst_last[pr]], tag=tag + "r5")
                st = p.dmag(dram_out[128 * m:128 * (m + 1),
                                     512 * qc:512 * (qc + 1)],
                            prst[pr][:, :], [a1], tag=tag + "sto")
                bank_last[bank] = [a1]
                prst_last[pr] = st
                ctb_last[pr] = m1_
                stb_last[pr] = m2_
                rsc_last[pr] = a1
                stoks.append(st)
            stmap[m] = stoks
            wsl_last[par] = mm

    proj_rope(d_wq, dq, KD, TOWN, d_cq, d_sq, st_by_head_q, "qp")
    proj_rope(d_wk, dko, 4, TOWN, d_cq, d_sq, st_by_head_k, "kp")

    coll_k = p.coll("AllGather", ALU.bypass, PAIRS,
                    [dko_h.ap().opt()], [dkg_h.ap().opt()],
                    waits=[_flat(list(st_by_head_k.values()))], tag="agk")
    coll_v = p.coll("AllGather", ALU.bypass, PAIRS,
                    [dvo_h.ap().opt()], [dvg_h.ap().opt()],
                    waits=[v_store_toks], tag="agv")

    all_proj = _flat(list(st_by_head_q.values()) + list(st_by_head_k.values())) \
        + v_store_toks
    ar_last = {"k": [None, None], "v": [None, None], "q": [None, None]}
    pt_last = [None, None, None, None]
    att_last = {}
    dinv_last = None
    drow_last = None
    drow_bf_last = None
    den_pend = None
    first_waits = all_proj

    for g in range(4):
        gp = g % 2
        lkA = p.dma(kth(gp, 0, 2048), dkg[0, 128 * g:128 * (g + 1), :],
                    [coll_k, ar_last["k"][gp],
                     first_waits if g < 2 else None], tag="kld")
        lkB = p.dma(kth(gp, 2048, 4096), dkg[1, 128 * g:128 * (g + 1), :],
                    [coll_k], tag="kld")
        lk = [lkA, lkB]
        lvs = []
        for t in range(32):
            lvs.append(p.dma(vth(gp, 128 * t, 128 * (t + 1)),
                             dvg[t // 16, g, t % 16],
                             [coll_v, ar_last["v"][gp],
                              first_waits if g < 2 else None], tag="vld"))
        kreaders = []
        vreaders = []
        for hh in range(4):
            h = 4 * g + hh
            hp = h % 2
            lq = p.dma(qth(hp, 0, 2048), dq[128 * h:128 * (h + 1), :],
                       [st_by_head_q[h], ar_last["q"][hp],
                        first_waits if h < 2 else None], tag="qld")
            qreaders = []
            for qc in range(4):
                pvbank = 2 + (qc % 2)
                dbank = 4 + (qc % 2)
                dmm = pvm = None
                pend = None

                def make_den(h, qc, dmm, pvm, pvbank, dbank):
                    def go():
                        nonlocal drow_last, drow_bf_last, dinv_last
                        rcp = p.v(lambda e, dbank=dbank: e.reciprocal(
                            drow[0:1, :], ps[dbank][0:1, 0:512]),
                            [dmm, drow_last], tag="rcp")
                        rcb = p.v(lambda e: e.tensor_copy(drow_bf[0:1, :],
                                                          drow[0:1, :]),
                                  [rcp, drow_bf_last], tag="rcb")
                        bank_last[dbank] = [rcp]
                        bcm = p.t(lambda e: e.matmul(ps[6][:, 0:512],
                                                     ones_rowb[0:1, :],
                                                     drow_bf[0:1, :],
                                                     start=True, stop=True),
                                  [rcb, ms1] + bank_last[6], tag="dbc")
                        dcp = p.v(lambda e: e.tensor_copy(dinv_b,
                                                          ps[6][:, 0:512]),
                                  [bcm, dinv_last], tag="dcp")
                        bank_last[6] = [dcp]
                        drow_last = rcb
                        drow_bf_last = bcm
                        amul = p.v(lambda e, h=h, qc=qc, pvbank=pvbank:
                                   e.tensor_tensor(
                            att_ap(h, qc), ps[pvbank][:, 0:512], dinv_b,
                            ALU.mult),
                            [pvm, dcp, att_last.get((h, qc))], tag="attdiv")
                        dinv_last = amul
                        bank_last[pvbank] = [amul]
                        att_last[(h, qc)] = amul
                    return go

                def mk_pend(t, pi, ex, last):
                    def go():
                        dm = p.t(lambda e, pi=pi, dbank=dbank, first=(t == 0),
                                 last=last: e.matmul(
                            ps[dbank][0:1, 0:512], ones_colb[:, :],
                            pt_ap(pi, 0, 512), start=first, stop=last),
                            [ex, ms2] + (bank_last[dbank] if t == 0 else []),
                            tag="dmm")
                        pv = p.t(lambda e, pi=pi, t=t, gp=gp, pvbank=pvbank,
                                 first=(t == 0), last=last: e.matmul(
                            ps[pvbank][:, 0:512],
                            vth(gp, 128 * t, 128 * (t + 1)),
                            pt_ap(pi, 0, 512), start=first, stop=last),
                            [ex, lvs[t]] +
                            (bank_last[pvbank] if t == 0 else []),
                            tag="pvm")
                        vreaders.append(pv)
                        pt_last[pi] = [dm, pv]
                        return dm, pv
                    return go

                for t in range(32):
                    sbank = t % 2
                    smm = p.t(lambda e, t=t, gp=gp, hp=hp, qc=qc, sbank=sbank:
                              e.matmul(ps[sbank][:, 0:512],
                                       kth(gp, 128 * t, 128 * (t + 1)),
                                       qth(hp, 512 * qc, 512 * (qc + 1)),
                                       start=True, stop=True),
                              [lk, lq] + bank_last[sbank], tag="smm")
                    kreaders.append(smm)
                    qreaders.append(smm)
                    pi = t % 4
                    ex = p.s(lambda e, pi=pi, sbank=sbank: e.activation(
                        pt_ap(pi, 0, 512), ps[sbank][:, 0:512], AF.Exp,
                        scale=SCALE), [smm, pt_last[pi]], tag="exp")
                    bank_last[sbank] = [ex]
                    if t == 4 and den_pend is not None:
                        den_pend()
                        den_pend = None
                    if pend is not None:
                        dmm, pvm = pend()
                    pend = mk_pend(t, pi, ex, t == 31)
                dmm, pvm = pend()
                den_pend = make_den(h, qc, dmm, pvm, pvbank, dbank)
            ar_last["q"][hp] = qreaders[-1]
        ar_last["k"][gp] = kreaders[-1]
        ar_last["v"][gp] = vreaders
    if den_pend is not None:
        den_pend()
        den_pend = None

    xres_last = [None, None]
    for m in range(KD):
        par = m % 2
        lw = p.dma(wsl[par][:, :], d_wo[m], [wsl_last[par]], tag="wow")
        mm = None
        for qc in range(4):
            bank = 5 + (qc % 2)
            for k in range(KD):
                mm = p.t(lambda e, k=k, qc=qc, par=par, bank=bank: e.matmul(
                    ps[bank][:, 0:512], wsl[par][:, 128 * k:128 * (k + 1)],
                    att_ap(k, qc), start=(k == 0), stop=(k == KD - 1)),
                    [att_last[(k, qc)], lw] +
                    (bank_last[bank] if k == 0 else []), tag="omm")
            pr = qc % 2
            lx = p.dma(xres[pr][:, :], xi[128 * m:128 * (m + 1),
                                          512 * qc:512 * (qc + 1)],
                       [xres_last[pr]], tag="oxld")
            yv = p.v(lambda e, pr=pr, bank=bank: e.tensor_tensor(
                xres[pr][:, :], xres[pr][:, :], ps[bank][:, 0:512], ALU.add),
                [lx, mm], tag="oadd")
            st = p.dmag(xo[128 * m:128 * (m + 1), 512 * qc:512 * (qc + 1)],
                        xres[pr][:, :], [yv], tag="oxst")
            bank_last[bank] = [yv]
            xres_last[pr] = st
        wsl_last[par] = mm
    return


def p6f_inputs(inp, h):
    own = cols_of(h)
    invf = 1.0 / (10000.0 ** (np.arange(0, 128, 2) / 128.0))
    pos = np.arange(S)
    fr = pos[None, :] * np.concatenate([invf, invf])[:, None]
    cos = np.cos(fr).astype(np.float32)
    sin = np.sin(fr).astype(np.float32)
    sin_signed = sin.copy()
    sin_signed[0:64] *= -1.0
    return {
        "wq": tile_lhsT(np.asarray(inp["L1_attn_b_wq"], np.float32).T),
        "wk": tile_lhsT(np.asarray(inp["L1_attn_b_wk"], np.float32).T),
        "wv": to_bf(np.ascontiguousarray(
            np.asarray(inp["L1_attn_b_wv"], np.float32).T).reshape(KD, 128, 512)),
        "wo": tile_lhsT(np.asarray(inp["L1_attn_b_wo"], np.float32).T),
        "cosq": to_bf(cos[:, own]),
        "sinq": to_bf(sin_signed[:, own]),
    }


# version 13
# speedup vs baseline: 1.1380x; 1.0094x over previous
import numpy as np
import ml_dtypes
from contextlib import ExitStack

import concourse.bass as bass
import concourse.mybir as mybir
from concourse.bass_utils import run_bass_kernel_spmd
from concourse.bass_core_rust import IndirectOffsetOnAxis

F32 = mybir.dt.float32
BF16 = mybir.dt.bfloat16
I32 = mybir.dt.int32
AF = mybir.ActivationFunctionType
ALU = mybir.AluOpType
BFNP = ml_dtypes.bfloat16

D = 2048
KD = D // 128
S = 4096
TOWN = 2048
NCORES = 8
PI = float(np.pi)
PAIRS = [[0, 1], [2, 3], [4, 5], [6, 7]]

ENGS = ("sync", "tensor", "vector", "scalar", "gpsimd")
_ESEM = {"sync": "e_sy", "tensor": "e_t", "vector": "e_v", "scalar": "e_s",
         "gpsimd": "e_g"}


class _Op:
    __slots__ = ("eng", "emit", "waits", "sem", "by", "marked", "val", "tag")

    def __init__(self, eng, emit, waits, sem, by, tag=""):
        self.eng, self.emit, self.waits = eng, emit, waits
        self.sem, self.by = sem, by
        self.marked = by == 16
        self.val = None
        self.tag = tag


def _flat(waits):
    out = []
    for w in waits:
        if w is None:
            continue
        if isinstance(w, (list, tuple)):
            out.extend(_flat(w))
        else:
            out.append(w)
    return out


class Prog:
    NQ = 8

    def __init__(self, nc):
        self.nc = nc
        self.ops = {e: [] for e in ENGS}
        self.sem_names = set()
        self.qidx = {"q_s": 0, "q_g": 0, "q_c": 0}
        self._phase_fr = None
        self._phase_seen = set()

    def phase_break(self):
        fr = []
        for e in ENGS:
            seen = set()
            for op in reversed(self.ops[e]):
                if op.sem not in seen:
                    seen.add(op.sem)
                    fr.append(op)
        self._phase_fr = fr
        self._phase_seen = set()

    def _add(self, eng, emit, waits, sem, by, tag=""):
        ws = _flat(waits)
        if self._phase_fr and eng not in self._phase_seen:
            self._phase_seen.add(eng)
            ws = ws + self._phase_fr
        for w in ws:
            w.marked = True
        op = _Op(eng, emit, ws, sem, by, tag)
        self.ops[eng].append(op)
        self.sem_names.add(sem)
        return op

    def t(self, emit, waits=(), tag=""):
        return self._add("tensor", emit, waits, "e_t", 1, tag)

    def v(self, emit, waits=(), tag=""):
        return self._add("vector", emit, waits, "e_v", 1, tag)

    def s(self, emit, waits=(), tag=""):
        return self._add("scalar", emit, waits, "e_s", 1, tag)

    def g(self, emit, waits=(), tag=""):
        return self._add("gpsimd", emit, waits, "e_g", 1, tag)

    def _qsem(self, q):
        i = self.qidx[q]
        self.qidx[q] = i + 1
        return f"{q}{i % self.NQ}"

    def dma(self, out, in_, waits=(), tag="", slow=False):
        if slow:
            emit = (lambda e, o=out, i=in_: e.dma_start(
                out=o, in_=i, allow_slow_non_contiguous=True))
        else:
            emit = (lambda e, o=out, i=in_: e.dma_start(out=o, in_=i))
        return self._add("sync", emit, waits, self._qsem("q_s"), 16, tag)

    def dmag(self, out, in_, waits=(), tag=""):
        return self._add("gpsimd", (lambda e, o=out, i=in_: e.dma_start(out=o, in_=i)),
                         waits, self._qsem("q_g"), 16, tag)

    def dmai(self, out, in_2d, idx_col, waits=(), tag=""):
        return self._add(
            "gpsimd",
            (lambda e, o=out, i=in_2d, x=idx_col: e.indirect_dma_start(
                o, None, i, IndirectOffsetOnAxis(x, 0))),
            waits, self._qsem("q_g"), 16, tag)

    def coll(self, kind, op, groups, ins, outs, waits=(), tag=""):
        def emit(e, kind=kind, op=op, groups=groups, ins=ins, outs=outs):
            return e.collective_compute(kind, op, replica_groups=groups,
                                        ins=ins, outs=outs)
        o = self._add("gpsimd", emit, waits, self._qsem("q_c"), 1, tag)
        o.marked = True
        return o

    SERIAL = ("vector", "scalar", "gpsimd")

    def finalize(self, ctx):
        nc = self.nc
        for e in self.SERIAL:
            for op in self.ops[e]:
                if not (op.by == 16 or op.sem == "q_c"):
                    op.marked = True
            self.sem_names.add(_ESEM[e] if e != "gpsimd" else "e_g")
        handles = {n: ctx.enter_context(nc.semaphore(n))
                   for n in sorted(self.sem_names)}
        counts = {n: 0 for n in self.sem_names}
        self._auto = {}
        for eng in ENGS:
            for op in self.ops[eng]:
                if op.marked:
                    if op.sem[0] == "q":
                        self._auto[id(op)] = (op.sem, counts[op.sem])
                    counts[op.sem] += op.by
                    op.val = counts[op.sem]
        block = ctx.enter_context(nc.Block())

        def mkbody(engname, tail):
            ops = self.ops[engname]
            serial = engname in self.SERIAL

            def body(eng):
                satisfied = {}
                prev_compute_val = 0
                for op in ops:
                    if serial and prev_compute_val and op.by == 1:
                        sem = _ESEM[engname]
                        if satisfied.get(sem, 0) < prev_compute_val:
                            eng.wait_ge(handles[sem], prev_compute_val)
                            satisfied[sem] = prev_compute_val
                    aw = self._auto.get(id(op))
                    if aw is not None and aw[1] > 0 and \
                            satisfied.get(aw[0], 0) < aw[1]:
                        eng.wait_ge(handles[aw[0]], aw[1])
                        satisfied[aw[0]] = aw[1]
                    for w in op.waits:
                        if w.eng == op.eng and w.by == 1 and w.sem == _ESEM[op.eng]:
                            continue
                        assert w.val is not None, (w.tag, "unmarked dep")
                        if satisfied.get(w.sem, 0) < w.val:
                            eng.wait_ge(handles[w.sem], w.val)
                            satisfied[w.sem] = w.val
                    inst = op.emit(eng)
                    if op.marked:
                        inst.then_inc(handles[op.sem], op.by)
                        if op.by == 1 and op.sem == _ESEM[engname]:
                            prev_compute_val = op.val
                if tail:
                    for qn in sorted(counts):
                        if qn[0] == "q" and counts[qn] > 0 and \
                                satisfied.get(qn, 0) < counts[qn]:
                            eng.wait_ge(handles[qn], counts[qn])
            return body

        for engname in ENGS:
            getattr(block, engname)(mkbody(engname, engname == "gpsimd"))



def cols_of(h):
    return np.r_[0:1024, 3072:4096] if h == 0 else np.r_[1024:3072]


def to_bf(x):
    return np.ascontiguousarray(np.asarray(x, dtype=np.float32).astype(BFNP))


def tile_lhsT(w):
    din, dout = w.shape
    nk, nm = din // 128, dout // 128
    img = np.ascontiguousarray(
        w.reshape(nk, 128, nm, 128).transpose(2, 1, 0, 3).reshape(nm, 128, nk * 128))
    return to_bf(img)



class LnBufs:

    def __init__(self, nc, ctx, name, ps_st, ps_bc):
        def sb(n, shape, dtype=F32):
            return ctx.enter_context(nc.sbuf_tensor(name + n, shape, dtype))
        self.xst = [sb(f"xst{i}", [128, 512]) for i in range(2)]
        self.sqbf = [sb(f"sqbf{i}", [128, 512], BF16) for i in range(2)]
        self.srow = sb("srow", [1, 1024])
        self.srow_bf = sb("srow_bf", [1, 1024], BF16)
        self.cb1 = [sb(f"cb1_{i}", [128, 512], BF16) for i in range(2)]
        self.cb0 = [sb(f"cb0_{i}", [128, 512], BF16) for i in range(2)]
        self.fscr = sb("fscr", [128, 512])
        self.ones_colb = sb("ones_colb", [128, 1], BF16)
        self.ones_rowb = sb("ones_rowb", [1, 128], BF16)
        self.cns = sb("cns", [1, 1])
        self.ps_st = ps_st
        self.ps_bc = ps_bc
        self.init = None
        self.last = {}

    def emit_init(self, p, eps=1e-5):
        m1 = p.v(lambda e: e.memset(self.ones_colb[:, :], 1.0))
        m2 = p.v(lambda e: e.memset(self.ones_rowb[:, :], 1.0))
        m3 = p.v(lambda e: e.memset(self.cns[:, :], eps))
        self.init = [m1, m2, m3]


def emit_ln_block(p, lb, x_dram, col0, arena, acol0, eps=1e-5, pitch=1024,
                  boff=0, load_fn=None):
    L = lb.last
    stat_toks = []
    streaders = {}
    cast_toks = [None] * KD
    for k in range(KD):
        cts = []
        for sub in range(2):
            si = sub
            if load_fn is not None:
                ld = load_fn(lb.xst[si][:, :], k, (col0 + 512 * sub) // 512,
                             [L.get(("xst", si))])
            else:
                ld = p.dma(lb.xst[si][:, :],
                           x_dram[128 * k:128 * (k + 1),
                                  col0 + 512 * sub:col0 + 512 * (sub + 1)],
                           [L.get(("xst", si))], tag="lnld")
            sq = p.s(lambda e, si=si: e.activation(
                lb.sqbf[si][:, :], lb.xst[si][:, :], AF.Square),
                [ld, L.get(("sq", si))], tag="lnsq")
            av = arena[:, acol0 + pitch * k + boff + 512 * sub:
                       acol0 + pitch * k + boff + 512 * (sub + 1)]
            cast = p.v(lambda e, av=av, si=si: e.tensor_copy(av, lb.xst[si][:, :]),
                       [ld, L.get(("ar", k))], tag="lncast")
            cts.append(cast)
            L[("xst", si)] = [sq, cast]
            m1 = p.t(lambda e, av=av, sub=sub, k=k: e.matmul(
                lb.ps_st[sub][0:1, :], lb.ones_colb[:, :], av,
                start=(k == 0), stop=(k == KD - 1)),
                [cast, lb.init] + ([L.get(("st", sub))] if k == 0 else []),
                tag="st1")
            m2 = p.t(lambda e, si=si, sub=sub, k=k: e.matmul(
                lb.ps_st[2 + sub][0:1, :], lb.ones_colb[:, :],
                lb.sqbf[si][:, :],
                start=(k == 0), stop=(k == KD - 1)),
                [sq, lb.init] + ([L.get(("st", 2 + sub))] if k == 0 else []),
                tag="st2")
            L[("sq", si)] = [m2]
            stat_toks += [m1, m2]
        cast_toks[k] = cts
    R = lb.srow
    bc_toks = []
    for sub in range(2):
        sm = stat_toks
        r0, r1, r2, r3 = (512 * 0, 512 * 0 + 512), None, None, None
        m_ = p.v(lambda e, sub=sub: e.tensor_scalar_mul(
            R[0:1, 0:512], lb.ps_st[sub][0:1, :], 1.0 / D),
            [sm, L.get("rows")], tag="lnrows")
        ms_ = p.v(lambda e: e.tensor_mul(R[0:1, 512:1024], R[0:1, 0:512],
                                         R[0:1, 0:512]), [m_])
        var_ = p.v(lambda e, sub=sub: e.scalar_tensor_tensor(
            R[0:1, 512:1024], lb.ps_st[2 + sub][0:1, :], 1.0 / D,
            R[0:1, 512:1024], ALU.mult, ALU.subtract), [ms_, sm])
        lnv = p.s(lambda e: e.activation(R[0:1, 512:1024],
                                         R[0:1, 512:1024], AF.Ln,
                                         bias=lb.cns[0:1, 0:1]),
                  [var_, lb.init])
        rstd = p.s(lambda e: e.activation(R[0:1, 512:1024], R[0:1, 512:1024],
                                          AF.Exp, scale=-0.5), [lnv])
        c1c = p.v(lambda e: e.tensor_copy(lb.srow_bf[0:1, 0:512],
                                          R[0:1, 512:1024]),
                  [rstd, L.get("srowbf")])
        c0c = p.v(lambda e: e.scalar_tensor_tensor(
            lb.srow_bf[0:1, 512:1024], R[0:1, 0:512], -1.0, R[0:1, 512:1024],
            ALU.mult, ALU.mult), [rstd, m_, L.get("srowbf")])
        bm1 = p.t(lambda e: e.matmul(lb.ps_bc[:, 0:512], lb.ones_rowb[0:1, :],
                                     lb.srow_bf[0:1, 0:512], start=True,
                                     stop=True), [c1c, lb.init, L.get("psbc")])
        bc1 = p.v(lambda e, sub=sub: e.tensor_copy(lb.cb1[sub][:, :],
                                                   lb.ps_bc[:, 0:512]),
                  [bm1, L.get(("cb1", sub))])
        bm2 = p.t(lambda e: e.matmul(lb.ps_bc[:, 0:512], lb.ones_rowb[0:1, :],
                                     lb.srow_bf[0:1, 512:1024], start=True,
                                     stop=True), [c0c, bc1])
        bc0 = p.v(lambda e, sub=sub: e.tensor_copy(lb.cb0[sub][:, :],
                                                   lb.ps_bc[:, 0:512]),
                  [bm2, L.get(("cb0", sub))])
        L["psbc"] = bc0
        L["srowbf"] = [bm1, bm2]
        L[("st", sub)] = [m_]
        L[("st", 2 + sub)] = [var_]
        streaders[sub] = [m_]
        streaders[2 + sub] = [var_]
        L["rows"] = [var_, c0c, rstd, c1c]
        bc_toks.append((bc1, bc0))
    chunk_toks = [None] * KD
    for k in range(KD):
        av = arena[:, acol0 + pitch * k + boff: acol0 + pitch * k + boff + 1024]
        subtoks = []
        for sub in range(2):
            bc1, bc0 = bc_toks[sub]
            avs = av[:, 512 * sub:512 * (sub + 1)]
            fs = lb.fscr[:, 0:512]
            t1 = p.v(lambda e, avs=avs, fs=fs, sub=sub: e.tensor_tensor(
                fs, avs, lb.cb1[sub][:, :], ALU.mult),
                [cast_toks[k], bc1, L.get("fs")], tag="lnap1")
            t2 = p.v(lambda e, avs=avs, fs=fs, sub=sub: e.tensor_tensor(
                avs, fs, lb.cb0[sub][:, :], ALU.add), [t1, bc0], tag="lnap2")
            L["fs"] = t2
            subtoks.append(t2)
        chunk_toks[k] = subtoks
        L[("ar", k)] = subtoks
        L[("cb1", 0 if False else 0)] = None
    allap = _flat(chunk_toks)
    for sub in range(2):
        L[("cb1", sub)] = allap
        L[("cb0", sub)] = allap
    return chunk_toks, allap, streaders



def emit_p1(nc, ctx, p, dr, pfx="", dbg=False):
    u = dr["u"]
    d_lre, d_lim, d_ldl = dr["lre"], dr["lim"], dr["ldl"]
    d_bre, d_bim, d_cst, d_dfm = dr["breT"], dr["bimT"], dr["cstack"], dr["dfm"]
    d_iota, d_iota1 = dr["iota_row"], dr["iota1_row"]
    d_iotac, d_iota1c = dr["iota_col"], dr["iota1_col"]
    d_triu, d_ident = dr["triu"], dr["ident"]
    xo = dr["xo"]

    def sb(name, shape, dtype=F32):
        return ctx.enter_context(nc.sbuf_tensor(pfx + name, shape, dtype))

    rn = ("lre lim ldl lamre delta ldre ldim are aim cre cim den u1 u2 are1 "
          "rt rabs rsin rcos rmag").split()
    row = {n: sb("r_" + n, [1, 32], F32) for n in rn}
    row_i = sb("r_int", [1, 32], I32)
    RW = lambda n: row[n][0:1, 0:32]

    apos = sb("apos", [128, 64])
    aneg = sb("aneg", [128, 64])
    asher = sb("asher", [128, 64])
    tt0 = sb("tt0", [128, 64])
    tt1 = sb("tt1", [128, 64])
    tscr = sb("tscr", [128, 64])
    tt1i = sb("tt1i", [128, 32], I32)
    bb_sb = sb("bb_sb", [128, KD * 64], BF16)
    bstg = sb("bstg", [128, 64])
    bsc = sb("bsc", [128, 64])
    cbt = sb("cbt", [128, 64])
    ceff = sb("ceff", [64, D], BF16)
    cstg = sb("cstg", [64, D])
    dfm_sb = sb("dfm_sb", [128, KD])
    iota_sb = sb("iota_sb", [1, 128])
    iota1_sb = sb("iota1_sb", [1, 128])
    iota_col = sb("iota_col_s", [128, 1])
    iota1_col = sb("iota1_col_s", [128, 1])
    triu_sb = sb("triu_sb", [128, 128])
    ident_sb = sb("ident_sb", [128, 128])
    ones_row = sb("ones_row", [1, 128])
    ones_col = sb("ones_col", [128, 1])
    pihalf = sb("pihalf", [128, 1])

    ccur = [sb(f"ccur{i}", [1, 64]) for i in range(2)]
    cscr = sb("cscr", [1, 64])
    at_r = sb("at_r", [1, 64])
    a1_r = sb("a1_r", [1, 64])
    ap127_r = sb("ap127_r", [1, 64])
    acrow = sb("acrow", [1, 64])

    xa = sb("xa", [128, KD * 1024], BF16)
    ust = [sb(f"ust{i}", [128, 512]) for i in range(4)]
    p_f = [sb(f"p_f{i}", [128, 64]) for i in range(2)]
    w_f = [sb(f"w_f{i}", [128, 64]) for i in range(2)]
    acb = [sb(f"acb{i}", [128, 64]) for i in range(2)]
    x_f = [sb(f"x_f{i}", [128, 64]) for i in range(4)]
    xT = sb("xT", [64, 512], BF16)
    yscr = [sb(f"yscr{i}", [128, 512]) for i in range(2)]
    yout = [sb(f"yout{i}", [128, 512]) for i in range(2)]

    ps = [ctx.enter_context(nc.psum_tensor(pfx + f"ps{i}", [128, 512], F32))
          for i in range(4)]
    ps_c = ctx.enter_context(nc.psum_tensor(pfx + "ps_c", [128, 512], F32))
    ps_sl = ctx.enter_context(nc.psum_tensor(pfx + "ps_sl", [1, 64], F32))
    ps_tr = ctx.enter_context(nc.psum_tensor(pfx + "ps_tr", [64, 128], F32))

    lb = LnBufs(nc, ctx, pfx + "ln_",
                ps_st=[ps[0][0:1, 0:512], ps[1][0:1, 0:512],
                       ps[2][0:1, 0:512], ps[3][0:1, 0:512]],
                ps_bc=ps_c[:, 0:512])
    lb.emit_init(p)

    ld_iota = p.dma(iota_sb[:, :], d_iota[:, :])
    ld_iota1 = p.dma(iota1_sb[:, :], d_iota1[:, :])
    ld_iotac = p.dma(iota_col[:, :], d_iotac[:, :])
    ld_iota1c = p.dma(iota1_col[:, :], d_iota1c[:, :])
    ld_triu = p.dma(triu_sb[:, :], d_triu[:, :])
    ld_ident = p.dma(ident_sb[:, :], d_ident[:, :])
    ld_dfm = p.dma(dfm_sb[:, :], d_dfm[:, :])
    ms1 = p.v(lambda e: e.memset(ones_row[:, :], 1.0))
    ms3 = p.v(lambda e: e.memset(ones_col[:, :], 1.0))
    msp = p.v(lambda e: e.memset(pihalf[:, :], PI / 2))
    ld1 = p.dma(RW("lre"), d_lre[:, :])
    ld2 = p.dma(RW("lim"), d_lim[:, :])
    ld3 = p.dma(RW("ldl"), d_ldl[:, :])

    a1 = p.s(lambda e: e.activation(RW("rabs"), RW("lre"), AF.Abs), [ld1])
    a2 = p.s(lambda e: e.activation(RW("rt"), RW("rabs"), AF.Exp, scale=-1.0), [a1])
    a3 = p.s(lambda e: e.activation(RW("rt"), RW("rt"), AF.Ln, bias=1.0), [a2])
    a4 = p.v(lambda e: e.tensor_scalar_max(RW("u1"), RW("lre"), 0.0), [ld1])
    a5 = p.v(lambda e: e.tensor_add(RW("u1"), RW("u1"), RW("rt")), [a3, a4])
    a6 = p.v(lambda e: e.tensor_scalar(RW("lamre"), RW("u1"), -1.0, -0.0101,
                                       ALU.mult, ALU.add), [a5])
    a7 = p.s(lambda e: e.activation(RW("delta"), RW("ldl"), AF.Exp), [ld3])
    a8 = p.v(lambda e: e.tensor_mul(RW("ldre"), RW("lamre"), RW("delta")), [a6, a7])
    a9 = p.v(lambda e: e.tensor_mul(RW("ldim"), RW("lim"), RW("delta")), [ld2, a7])

    MAGIC = 1.5 * 2.0 ** 23

    def trig_row(dst_cos, dst_sin, src, waits):
        w1 = p.v(lambda e: e.tensor_scalar(RW("rt"), RW(src), 1.0 / (2 * PI),
                                           MAGIC, ALU.mult, ALU.add), waits)
        w3 = p.v(lambda e: e.tensor_scalar_add(RW("rt"), RW("rt"), -MAGIC), [w1])
        w4 = p.v(lambda e: e.scalar_tensor_tensor(RW("rt"), RW("rt"), -2 * PI,
                                                  RW(src), ALU.mult, ALU.add), [w3])
        w5 = p.s(lambda e: e.activation(RW(dst_sin), RW("rt"), AF.Sin), [w4])
        w6 = p.s(lambda e: e.activation(RW("rabs"), RW("rt"), AF.Abs), [w4])
        w7 = p.s(lambda e: e.activation(RW(dst_cos), RW("rabs"), AF.Sin,
                                        scale=-1.0, bias=pihalf[0:1, 0:1]),
                 [w6, msp])
        return [w5, w7]

    tg = trig_row("rcos", "rsin", "ldim", [a9])
    a10 = p.s(lambda e: e.activation(RW("rmag"), RW("ldre"), AF.Exp), [a8])
    a11 = p.v(lambda e: e.tensor_mul(RW("are"), RW("rmag"), RW("rcos")), [a10, tg])
    a12 = p.v(lambda e: e.tensor_mul(RW("aim"), RW("rmag"), RW("rsin")), [a10, tg])

    b1 = p.v(lambda e: e.tensor_mul(RW("den"), RW("lamre"), RW("lamre")), [a6])
    b2 = p.v(lambda e: e.tensor_mul(RW("u1"), RW("lim"), RW("lim")), [ld2, a5])
    b3 = p.v(lambda e: e.tensor_add(RW("den"), RW("den"), RW("u1")), [b1, b2])
    b4 = p.v(lambda e: e.reciprocal(RW("den"), RW("den")), [b3])
    b5 = p.v(lambda e: e.tensor_scalar_add(RW("are1"), RW("are"), -1.0), [a11])
    b6 = p.v(lambda e: e.tensor_mul(RW("u1"), RW("are1"), RW("lamre")), [b5, b3])
    b7 = p.v(lambda e: e.tensor_mul(RW("u2"), RW("aim"), RW("lim")), [a12])
    b8 = p.v(lambda e: e.tensor_add(RW("u1"), RW("u1"), RW("u2")), [b6, b7])
    b9 = p.v(lambda e: e.tensor_mul(RW("cre"), RW("u1"), RW("den")), [b8, b4])
    c1 = p.v(lambda e: e.tensor_mul(RW("u1"), RW("aim"), RW("lamre")), [a12, b9])
    c2 = p.v(lambda e: e.tensor_mul(RW("u2"), RW("are1"), RW("lim")), [b5, b9])
    c3 = p.v(lambda e: e.tensor_sub(RW("u1"), RW("u1"), RW("u2")), [c1, c2])
    c4 = p.v(lambda e: e.tensor_mul(RW("cim"), RW("u1"), RW("den")), [c3, b4])

    d_rows = nc.dram_tensor(pfx + "rowscr", [1, 64], F32, kind="Internal")
    ldre_b = sb("ldre_b", [128, 32])
    ldim_b = sb("ldim_b", [128, 32])
    thA = sb("thA", [128, 64])
    thB = sb("thB", [128, 64])
    wr1 = p.dma(d_rows.ap()[0:1, 0:32], RW("ldre"), [a8], tag="rowwr")
    wr2 = p.dma(d_rows.ap()[0:1, 32:64], RW("ldim"), [a9], tag="rowwr")
    bc1_ = p.dma(ldre_b[:, :], bass.AP(d_rows, 0, [[0, 128], [1, 32]]), [wr1])
    bc2_ = p.dma(ldim_b[:, :], bass.AP(d_rows, 32, [[0, 128], [1, 32]]), [wr2])
    mmt = []
    for dst, srcb, iot, w in [(lambda: thA[:, 0:32], ldre_b, "iota_col", bc1_),
                              (lambda: thA[:, 32:64], ldim_b, "iota_col", bc2_),
                              (lambda: thB[:, 0:32], ldre_b, "iota1_col", bc1_),
                              (lambda: thB[:, 32:64], ldim_b, "iota1_col", bc2_)]:
        col = iota_col if iot == "iota_col" else iota1_col
        mmt.append(p.v(lambda e, dst=dst, srcb=srcb, col=col: e.tensor_scalar(
            dst(), srcb[:, :], col[:, 0:1], None, ALU.mult),
            [w, ld_iotac, ld_iota1c], tag="powtab"))

    def trig_tile(dst, src_ps, waits):
        w1 = p.v(lambda e: e.tensor_scalar(tscr[:, 0:32], src_ps[:, 0:32],
                                           1.0 / (2 * PI), MAGIC,
                                           ALU.mult, ALU.add), waits)
        w3 = p.v(lambda e: e.tensor_scalar_add(tscr[:, 0:32], tscr[:, 0:32],
                                               -MAGIC), [w1])
        w4 = p.v(lambda e: e.scalar_tensor_tensor(tscr[:, 0:32], tscr[:, 0:32],
                                                  -2 * PI, src_ps[:, 0:32],
                                                  ALU.mult, ALU.add), [w3])
        w5 = p.s(lambda e: e.activation(dst[:, 32:64], tscr[:, 0:32], AF.Sin), [w4])
        w6 = p.s(lambda e: e.activation(tscr[:, 32:64], tscr[:, 0:32], AF.Abs), [w4])
        w7 = p.s(lambda e: e.activation(dst[:, 0:32], tscr[:, 32:64], AF.Sin,
                                        scale=-1.0, bias=pihalf[:, :]),
                 [w6, msp])
        return [w5, w7]

    tga = trig_tile(tt0, thA[:, 32:64], [mmt[1]])
    e1 = p.s(lambda e: e.activation(bstg[:, 0:32], thA[:, 0:32], AF.Exp), [mmt[0]])
    e2 = p.s(lambda e: e.activation(bstg[:, 32:64], thA[:, 0:32], AF.Exp,
                                    scale=-1.0), [mmt[0]])
    e3 = p.v(lambda e: e.tensor_mul(apos[:, 0:32], bstg[:, 0:32], tt0[:, 0:32]),
             [e1, tga])
    e4 = p.v(lambda e: e.tensor_mul(apos[:, 32:64], bstg[:, 0:32], tt0[:, 32:64]),
             [e1, tga])
    e5 = p.v(lambda e: e.tensor_mul(aneg[:, 0:32], bstg[:, 32:64], tt0[:, 0:32]),
             [e2, tga])
    e6 = p.v(lambda e: e.scalar_tensor_tensor(aneg[:, 32:64], bstg[:, 32:64],
                                              -1.0, tt0[:, 32:64], ALU.mult,
                                              ALU.mult), [e2, tga])
    tgb = trig_tile(tt1, thB[:, 32:64], [mmt[3], e3, e4, e5, e6])
    e7 = p.s(lambda e: e.activation(bstg[:, 0:32], thB[:, 0:32], AF.Exp),
             [mmt[2], e3, e4, e5, e6])
    e8 = p.v(lambda e: e.tensor_mul(asher[:, 0:32], bstg[:, 0:32], tt1[:, 0:32]),
             [e7, tgb])
    e9 = p.v(lambda e: e.tensor_mul(asher[:, 32:64], bstg[:, 0:32], tt1[:, 32:64]),
             [e7, tgb])

    atr = p.dma(at_r[0:1, 0:64], asher[127:128, 0:64], [e8, e9], tag="atr")
    apr = p.dma(ap127_r[0:1, 0:64], apos[127:128, 0:64], [e3, e4], tag="apr")
    a1r = p.dma(a1_r[0:1, 0:64], apos[1:2, 0:64], [e3, e4], tag="a1r")
    col_toks = [atr, apr, a1r]

    f1 = p.t(lambda e: e.matmul(ps_c[:, 0:32], ones_row[0:1, :], RW("cre"),
                                start=True, stop=True), [b9, ms1])
    f2 = p.t(lambda e: e.matmul(ps_c[:, 32:64], ones_row[0:1, :], RW("cim"),
                                start=True, stop=True), [c4, ms1])
    f3 = p.v(lambda e: e.tensor_copy(cbt[:, 0:64], ps_c[:, 0:64]), [f1, f2])

    prevb = None
    for k in range(KD):
        l1 = p.dma(bstg[:, 0:32], d_bre[128 * k:128 * (k + 1), :],
                   [prevb, e7, e8, e9], tag="bre")
        l2 = p.dma(bstg[:, 32:64], d_bim[128 * k:128 * (k + 1), :],
                   [prevb, e7, e8, e9])
        g1 = p.v(lambda e: e.tensor_mul(bsc[:, 0:32], bstg[:, 0:32], cbt[:, 0:32]),
                 [l1, f3, prevb, tgb])
        g2 = p.v(lambda e: e.tensor_mul(bsc[:, 32:64], bstg[:, 32:64],
                                        cbt[:, 32:64]), [l2, f3, prevb, tgb])
        g3 = p.v(lambda e, k=k: e.tensor_sub(bb_sb[:, 64 * k:64 * k + 32],
                                             bsc[:, 0:32], bsc[:, 32:64]), [g1, g2])
        g4 = p.v(lambda e: e.tensor_mul(bsc[:, 0:32], bstg[:, 32:64], cbt[:, 0:32]),
                 [g3])
        g5 = p.v(lambda e: e.tensor_mul(bsc[:, 32:64], bstg[:, 0:32],
                                        cbt[:, 32:64]), [g3])
        g6 = p.v(lambda e, k=k: e.tensor_add(bb_sb[:, 64 * k + 32:64 * k + 64],
                                             bsc[:, 0:32], bsc[:, 32:64]), [g4, g5])
        prevb = g6
    bb_tok = prevb

    lc = p.dma(cstg[0:64, :], d_cst[:, :])
    h1 = p.s(lambda e: e.activation(ceff[0:64, 0:D], cstg[0:64, 0:D], AF.Copy,
                                    scale=2.0), [lc])
    h2 = p.s(lambda e: e.activation(ceff[32:64, 0:D], cstg[32:64, 0:D], AF.Copy,
                                    scale=-2.0), [h1])
    ceff_tok = h2

    ic1 = p.v(lambda e: e.memset(ccur[0][:, :], 0.0))

    ps_y_last = [None, None]
    ps_bu_last = None
    ps_q_last = None
    lb.last["psbc"] = f3
    ps_sl_last = None
    ps_tr_last = None
    carry_par = 0
    prev_bcm = None
    ctok = [ic1] + col_toks
    xf_last = [None] * 4
    pf_last = [None, None]
    wf_last = [None, None]
    acb_last = [None, None]
    ust_last = [None] * 4
    yscr_last = [None, None]
    yout_last = [None, None]
    xT_last = [None] * 4
    xT_tok = [None] * 4
    arena_readers = {k: [] for k in range(KD)}

    for nb in range(4):
        lb.last[("st", 0)] = ps_bu_last
        lb.last[("st", 1)] = ps_q_last
        lb.last[("st", 2)] = ps_y_last[0]
        lb.last[("st", 3)] = ps_y_last[1]
        chunk_toks, lnall, strd = emit_ln_block(p, lb, u, 1024 * nb, xa, 0)
        ps_bu_last = strd[0]
        ps_q_last = strd[1]
        ps_y_last = [strd[2], strd[3]]
        for cc in range(8):
            bu_mms = []
            for k in range(KD):
                av = xa[:, 1024 * k + 128 * cc: 1024 * k + 128 * (cc + 1)]
                bu_mms.append(p.t(lambda e, av=av, k=k: e.matmul(
                    ps[0][:, 0:64], av, bb_sb[:, 64 * k:64 * (k + 1)],
                    start=(k == 0), stop=(k == KD - 1)),
                    [chunk_toks[k], bb_tok] + ([ps_bu_last] if k == 0 else []),
                    tag="bu"))
                arena_readers[k].append(bu_mms[-1])
            bu = bu_mms[-1]
            par = cc % 2
            pf = p_f[par]
            wf = w_f[par]
            q1 = p.v(lambda e, pf=pf: e.tensor_mul(pf[:, 0:32], aneg[:, 0:32],
                                                   ps[0][:, 0:32]),
                     [bu, pf_last[par], e5], tag="pre")
            q2 = p.v(lambda e, wf=wf: e.tensor_mul(wf[:, 0:32], aneg[:, 32:64],
                                                   ps[0][:, 32:64]),
                     [bu, wf_last[par], e6])
            q3 = p.v(lambda e, pf=pf, wf=wf: e.tensor_sub(pf[:, 0:32],
                                                          pf[:, 0:32],
                                                          wf[:, 0:32]), [q1, q2])
            q4 = p.v(lambda e, pf=pf: e.tensor_mul(pf[:, 32:64], aneg[:, 0:32],
                                                   ps[0][:, 32:64]),
                     [bu, q3], tag="pim")
            q5 = p.v(lambda e, wf=wf: e.tensor_mul(wf[:, 32:64], aneg[:, 32:64],
                                                   ps[0][:, 0:32]), [bu, q2])
            q6 = p.v(lambda e, pf=pf, wf=wf: e.tensor_add(pf[:, 32:64],
                                                          pf[:, 32:64],
                                                          wf[:, 32:64]), [q4, q5])
            p_tok = [q3, q6]
            pf_last[par] = p_tok
            wf_last[par] = [q5, q6]
            ps_bu_last = [q1, q2, q4, q5]
            qmm = p.t(lambda e, pf=pf: e.matmul(ps[1][:, 0:64], triu_sb[:, :],
                                                pf[:, 0:64], start=True,
                                                stop=True),
                      [p_tok, ld_triu, ps_q_last], tag="tri")
            slmm = p.t(lambda e, pf=pf: e.matmul(ps_sl[0:1, 0:64],
                                                 ones_col[:, :], pf[:, 0:64],
                                                 start=True, stop=True),
                       [p_tok, ms3, ps_sl_last], tag="slmm")
            cprev = ccur[carry_par]
            cnext = ccur[1 - carry_par]
            CS_RE = ps_sl[0:1, 0:32]
            CS_IM = ps_sl[0:1, 32:64]
            AB = lambda t, a, b: t[0:1, a:b]
            v_ = []
            v_.append(p.v(lambda e, cp=cprev: e.tensor_mul(
                AB(cscr, 0, 32), AB(a1_r, 0, 32), AB(cp, 0, 32)),
                [ctok, acb_last, prev_bcm], tag="ac1"))
            v_.append(p.v(lambda e, cp=cprev: e.tensor_mul(
                AB(cscr, 32, 64), AB(a1_r, 32, 64), AB(cp, 32, 64)), [ctok]))
            v_.append(p.v(lambda e: e.tensor_sub(AB(acrow, 0, 32), AB(cscr, 0, 32),
                                                 AB(cscr, 32, 64)), []))
            v_.append(p.v(lambda e, cp=cprev: e.tensor_mul(
                AB(cscr, 0, 32), AB(a1_r, 0, 32), AB(cp, 32, 64)), []))
            v_.append(p.v(lambda e, cp=cprev: e.tensor_mul(
                AB(cscr, 32, 64), AB(a1_r, 32, 64), AB(cp, 0, 32)), []))
            v_.append(p.v(lambda e: e.tensor_add(AB(acrow, 32, 64), AB(cscr, 0, 32),
                                                 AB(cscr, 32, 64)), []))
            ac_tok = v_[-1]
            w_ = []
            w_.append(p.v(lambda e: e.tensor_mul(AB(cscr, 0, 32), AB(ap127_r, 0, 32),
                                                 CS_RE), [slmm], tag="cc1"))
            w_.append(p.v(lambda e: e.tensor_mul(AB(cscr, 32, 64),
                                                 AB(ap127_r, 32, 64), CS_IM), [slmm]))
            w_.append(p.v(lambda e, cn=cnext: e.tensor_sub(
                AB(cn, 0, 32), AB(cscr, 0, 32), AB(cscr, 32, 64)), []))
            w_.append(p.v(lambda e: e.tensor_mul(AB(cscr, 0, 32), AB(ap127_r, 0, 32),
                                                 CS_IM), [slmm]))
            w_.append(p.v(lambda e: e.tensor_mul(AB(cscr, 32, 64),
                                                 AB(ap127_r, 32, 64), CS_RE), [slmm]))
            w_.append(p.v(lambda e, cn=cnext: e.tensor_add(
                AB(cn, 32, 64), AB(cscr, 0, 32), AB(cscr, 32, 64)), []))
            w_.append(p.v(lambda e, cp=cprev: e.tensor_mul(
                AB(cscr, 0, 32), AB(at_r, 0, 32), AB(cp, 0, 32)), []))
            w_.append(p.v(lambda e, cp=cprev: e.tensor_mul(
                AB(cscr, 32, 64), AB(at_r, 32, 64), AB(cp, 32, 64)), []))
            w_.append(p.v(lambda e, cn=cnext: e.tensor_add(
                AB(cn, 0, 32), AB(cn, 0, 32), AB(cscr, 0, 32)), []))
            w_.append(p.v(lambda e, cn=cnext: e.tensor_sub(
                AB(cn, 0, 32), AB(cn, 0, 32), AB(cscr, 32, 64)), []))
            w_.append(p.v(lambda e, cp=cprev: e.tensor_mul(
                AB(cscr, 0, 32), AB(at_r, 0, 32), AB(cp, 32, 64)), []))
            w_.append(p.v(lambda e, cp=cprev: e.tensor_mul(
                AB(cscr, 32, 64), AB(at_r, 32, 64), AB(cp, 0, 32)), []))
            w_.append(p.v(lambda e, cn=cnext: e.tensor_add(
                AB(cn, 32, 64), AB(cn, 32, 64), AB(cscr, 0, 32)), []))
            w_.append(p.v(lambda e, cn=cnext: e.tensor_add(
                AB(cn, 32, 64), AB(cn, 32, 64), AB(cscr, 32, 64)), []))
            ps_sl_last = w_[:5]
            ctok = [w_[-1], w_[9]]
            carry_par = 1 - carry_par
            bcm = p.t(lambda e: e.matmul(ps_c[:, 0:64], ones_row[0:1, :],
                                         acrow[0:1, 0:64], start=True,
                                         stop=True),
                      [ac_tok, ms1, lb.last.get("psbc")], tag="acb")
            prev_bcm = bcm
            ae = p.v(lambda e, par=par: e.tensor_copy(acb[par][:, 0:64],
                                                      ps_c[:, 0:64]),
                     [bcm, acb_last[par]])
            lb.last["psbc"] = ae
            wv = p.v(lambda e, wf=wf, par=par: e.tensor_tensor(
                wf[:, 0:64], ps[1][:, 0:64], acb[par][:, 0:64], ALU.add),
                [qmm, ae], tag="wq")
            acb_last[par] = wv
            ps_q_last = wv
            xi = cc % 4
            xf = x_f[xi]
            y1 = p.v(lambda e, xf=xf, wf=wf: e.tensor_mul(
                xf[:, 0:32], apos[:, 0:32], wf[:, 0:32]),
                [wv, xf_last[xi]], tag="xre")
            y2 = p.v(lambda e, pf=pf, wf=wf: e.tensor_mul(
                pf[:, 0:32], apos[:, 32:64], wf[:, 32:64]), [wv])
            y3 = p.v(lambda e, xf=xf, pf=pf: e.tensor_sub(
                xf[:, 0:32], xf[:, 0:32], pf[:, 0:32]), [y1, y2])
            y4 = p.v(lambda e, xf=xf, wf=wf: e.tensor_mul(
                xf[:, 32:64], apos[:, 0:32], wf[:, 32:64]),
                [wv, xf_last[xi]])
            y5 = p.v(lambda e, pf=pf, wf=wf: e.tensor_mul(
                pf[:, 32:64], apos[:, 32:64], wf[:, 0:32]), [wv])
            y6 = p.v(lambda e, xf=xf, pf=pf: e.tensor_add(
                xf[:, 32:64], xf[:, 32:64], pf[:, 32:64]), [y4, y5])
            x_tok = [y3, y6]
            pf_last[par] = [y2, y5, q3, q6]
            wf_last[par] = [wv, y1, y2, y4, y5]
            trm = p.t(lambda e, xf=xf: e.transpose(ps_tr[:, :], xf[:, 0:64],
                                                   ident_sb[:, :]),
                      [x_tok, ld_ident, ps_tr_last], tag="xtr")
            gi = cc % 4
            xtc = p.v(lambda e, gi=gi: e.tensor_copy(
                xT[0:64, 128 * gi:128 * (gi + 1)], ps_tr[0:64, :]),
                [trm, xT_last[gi]], tag="xtc")
            ps_tr_last = xtc
            xf_last[xi] = trm
            xT_tok[gi] = xtc
            if cc % 4 == 3:
                grp = cc // 4
                gcol = 1024 * nb + 512 * grp
                last_ymm = None
                for k in range(KD):
                    ypar = k % 2
                    ym = p.t(lambda e, k=k, ypar=ypar: e.matmul(
                        ps[2 + ypar][:, 0:512], ceff[0:64, 128 * k:128 * (k + 1)],
                        xT[0:64, 0:512], start=True, stop=True),
                        [xT_tok, ceff_tok, ps_y_last[ypar]], tag="ymm")
                    si = k % 4
                    lu = p.dma(ust[si][:, :], u[128 * k:128 * (k + 1),
                                                gcol:gcol + 512],
                               [ust_last[si]], tag="ustld")
                    av = xa[:, 1024 * k + 512 * grp: 1024 * k + 512 * (grp + 1)]
                    w1_ = p.v(lambda e, si=si, k=k, av=av, ypar=ypar:
                              e.scalar_tensor_tensor(
                                  yscr[ypar][:, :], av, dfm_sb[:, k:k + 1],
                                  ust[si][:, :], ALU.mult, ALU.add),
                              [lu, ld_dfm, chunk_toks[k], yscr_last[ypar]],
                              tag="yevac1")
                    arena_readers[k].append(w1_)
                    w2_ = p.v(lambda e, ypar=ypar: e.tensor_tensor(
                        yout[ypar][:, :], yscr[ypar][:, :], ps[2 + ypar][:, 0:512],
                        ALU.add), [w1_, ym, yout_last[ypar]], tag="yevac2")
                    st = p.dmag(xo[128 * k:128 * (k + 1), gcol:gcol + 512],
                                yout[ypar][:, :], [w2_], tag="xost")
                    ust_last[si] = w1_
                    yscr_last[ypar] = w2_
                    yout_last[ypar] = st
                    ps_y_last[ypar] = w2_
                    last_ymm = ym
                for gi2 in range(4):
                    xT_last[gi2] = last_ymm
        for k in range(KD):
            lb.last[("ar", k)] = arena_readers[k]
            arena_readers[k] = []
    return



def p1_inputs(inp, b):
    u = np.asarray(inp["L0_attn_a_u"])
    out = {
        "u": np.ascontiguousarray(u[b].T),
        "lre": np.asarray(inp["L0_attn_a_Lambda_unconstrained_re"], np.float32).reshape(1, 32),
        "lim": np.asarray(inp["L0_attn_a_Lambda_im"], np.float32).reshape(1, 32),
        "ldl": np.asarray(inp["L0_attn_a_log_Delta"], np.float32)[:32].reshape(1, 32),
        "breT": np.ascontiguousarray(np.asarray(inp["L0_attn_a_B_re"], np.float32).T),
        "bimT": np.ascontiguousarray(np.asarray(inp["L0_attn_a_B_im"], np.float32).T),
        "cstack": np.ascontiguousarray(
            np.concatenate([np.asarray(inp["L0_attn_a_C_re"], np.float32).T,
                            np.asarray(inp["L0_attn_a_C_im"], np.float32).T], 0)),
        "dfm": np.ascontiguousarray(
            np.asarray(inp["L0_attn_a_D"], np.float32).reshape(KD, 128).T),
        "iota_row": np.arange(128, dtype=np.float32).reshape(1, 128),
        "iota1_row": (np.arange(128, dtype=np.float32) + 1).reshape(1, 128),
        "iota_col": np.arange(128, dtype=np.float32).reshape(128, 1),
        "iota1_col": (np.arange(128, dtype=np.float32) + 1).reshape(128, 1),
        "triu": np.triu(np.ones((128, 128), np.float32)),
        "ident": np.eye(128, dtype=np.float32),
    }
    return out


_PROGS = {}


def _get_prog(name, builder):
    if name not in _PROGS:
        _PROGS[name] = builder()
    return _PROGS[name][0]



def build_fused():
    nc = bass.Bass("TRN2", target_bir_lowering=False, debug=False)
    top = ExitStack()
    p = Prog(nc)

    def EI(name, shape, dt=F32):
        return nc.dram_tensor(name, shape, dt, kind="ExternalInput").ap()

    dr1 = {
        "u": EI("a_u", [D, S]),
        "lre": EI("a_lre", [1, 32]), "lim": EI("a_lim", [1, 32]),
        "ldl": EI("a_ldl", [1, 32]),
        "breT": EI("a_breT", [D, 32]), "bimT": EI("a_bimT", [D, 32]),
        "cstack": EI("a_cstack", [64, D]), "dfm": EI("a_dfm", [128, KD]),
        "iota_row": EI("a_iota_row", [1, 128]),
        "iota1_row": EI("a_iota1_row", [1, 128]),
        "iota_col": EI("a_iota_col", [128, 1]),
        "iota1_col": EI("a_iota1_col", [128, 1]),
        "triu": EI("a_triu", [128, 128]), "ident": EI("a_ident", [128, 128]),
    }
    dr2 = {
        "wg": EI("b_wg", [64, 128, KD * 128], BF16),
        "wu": EI("b_wu", [64, 128, KD * 128], BF16),
        "wd": EI("b_wd", [KD, 128, 64 * 128], BF16),
    }
    d_gidx = EI("b_gidx", [128, 64], I32)
    dr3 = {
        "wq": EI("c_wq", [KD, 128, D], BF16),
        "wk": EI("c_wk", [KD, 128, D], BF16),
        "wv": EI("c_wv", [KD, 128, D], BF16),
        "wo": EI("c_wo", [KD, 128, D], BF16),
        "qb": EI("c_qb", [128, KD]), "kb": EI("c_kb", [128, KD]),
        "vb": EI("c_vb", [1, D], BF16), "ob": EI("c_ob", [128, KD]),
        "slopes": EI("c_slopes", [1, 16]),
        "kgc": EI("c_kgc", [128, 773]),
        "mstrip": EI("c_mstrip", [128, 128], BF16),
    }
    dr4 = {
        "wq": EI("d_wq", [KD, 128, D], BF16),
        "wk": EI("d_wk", [KD, 128, D], BF16),
        "wv": EI("d_wv", [KD, 128, D], BF16),
        "wo": EI("d_wo", [KD, 128, D], BF16),
        "qb": EI("d_qb", [128, KD]), "kb": EI("d_kb", [128, KD]),
        "vb": EI("d_vb", [1, D], BF16), "ob": EI("d_ob", [128, KD]),
        "wmapq": EI("d_wmapq", [128, 64], BF16),
        "wmapk": EI("d_wmapk", [128, 64], BF16),
    }
    dr5 = {
        "ing": EI("e_ing", [128, KD]), "inb": EI("e_inb", [128, KD]),
        "png": EI("e_png", [128, KD]), "pnb": EI("e_pnb", [128, KD]),
        "outg": EI("e_outg", [128, KD]), "outb": EI("e_outb", [128, KD]),
        "agg": EI("e_agg", [KD, 128, D], BF16), "aggb": EI("e_aggb", [128, KD]),
        "d1w": EI("e_d1w", [32, 128, D], BF16), "d1b": EI("e_d1b", [128, 32]),
        "d2w": EI("e_d2w", [KD, 128, 4096], BF16), "d2b": EI("e_d2b", [128, KD]),
        "bcw": EI("e_bcw", [KD, 128, D], BF16), "bcb": EI("e_bcb", [128, KD]),
        "gw": EI("e_gw", [KD, 128, D], BF16), "gb": EI("e_gb", [128, KD]),
    }
    dr6 = {
        "wq": EI("f_wq", [KD, 128, D], BF16),
        "wk": EI("f_wk", [4, 128, D], BF16),
        "wv": EI("f_wv", [KD, 128, 512], BF16),
        "wo": EI("f_wo", [KD, 128, D], BF16),
        "cosq": EI("f_cosq", [128, TOWN], BF16),
        "sinq": EI("f_sinq", [128, TOWN], BF16),
    }

    def DI(name, shape, dt=F32):
        return nc.dram_tensor(name, shape, dt, kind="Internal")

    dxg = DI("dxg", [D, S]).ap()
    dxA = DI("dxA", [D, TOWN]).ap()
    dxB = DI("dxB", [D, TOWN]).ap()
    dq3 = DI("c_dq", [D, TOWN], BF16).ap()
    dko3_h = DI("c_dko", [D, TOWN], BF16)
    dkg3_h = DI("c_dkg", [2, D, TOWN], BF16)
    dvo3_h = DI("c_dvo", [16, 16, 128, 128], BF16)
    dvg3_h = DI("c_dvg", [2, 16, 16, 128, 128], BF16)
    dq4 = DI("d_dq", [D, TOWN], BF16).ap()
    dko4 = DI("d_dko", [D, TOWN], BF16).ap()
    dvo4 = DI("d_dvo", [16, 16, 128, 128], BF16).ap()
    pki4_h = [DI(f"d_pki{i}", [128, 129]) for i in range(2)]
    pko4_h = [DI(f"d_pko{i}", [128, 129]) for i in range(2)]
    xli5_h = [DI(f"e_xli{i}", [128, KD]) for i in range(2)]
    xlo5_h = [DI(f"e_xlo{i}", [256, KD]) for i in range(2)]
    thki5_h = [DI(f"e_thki{i}", [128, KD]) for i in range(2)]
    thko5_h = [DI(f"e_thko{i}", [128, KD]) for i in range(2)]
    dq6 = DI("f_dq", [D, TOWN], BF16).ap()
    dko6_h = DI("f_dko", [512, TOWN], BF16)
    dkg6_h = DI("f_dkg", [2, 512, TOWN], BF16)
    dvo6_h = DI("f_dvo", [4, 16, 128, 128], BF16)
    dvg6_h = DI("f_dvg", [2, 4, 16, 128, 128], BF16)

    xo = nc.dram_tensor("xo", [D, TOWN], F32, kind="ExternalOutput").ap()

    c = ExitStack()
    emit_p1(nc, c, p, {**dr1, "xo": dxg}, pfx="a_")
    p.phase_break()
    c.close()

    dxg_rows = dxg.rearrange("d (b c) -> (d b) c", b=8)
    c = ExitStack()
    sidx = c.enter_context(nc.sbuf_tensor("b_sidx", [128, 64], I32))
    lsx = p.dma(sidx[:, :], d_gidx[:, :], tag="gidxld")

    def gload(dst, k, j, waits, lsx=lsx):
        col = 4 * k + j
        return p.dmai(dst, dxg_rows, sidx[:, col:col + 1], [waits, lsx],
                      tag="glnld")

    emit_p2(nc, c, p, {**dr2, "xi": None, "xo": dxA}, pfx="b1_",
            load_fn=gload)
    p.phase_break()
    c.close()

    c = ExitStack()
    emit_p3f(nc, c, p, {**dr3, "xi": dxA, "xo": dxB, "dq": dq3,
                        "dko_h": dko3_h, "dkg_h": dkg3_h,
                        "dvo_h": dvo3_h, "dvg_h": dvg3_h}, pfx="c_")
    p.phase_break()
    c.close()

    c = ExitStack()
    emit_p2(nc, c, p, {**dr2, "xi": dxB, "xo": dxA}, pfx="b2_")
    p.phase_break()
    c.close()

    c = ExitStack()
    emit_p4f(nc, c, p, {**dr4, "xi": dxA, "xo": dxB, "dq": dq4,
                        "dko": dko4, "dvo": dvo4,
                        "pki_h": pki4_h, "pko_h": pko4_h}, pfx="d_")
    p.phase_break()
    c.close()

    def mk_xl(stream, pki_h, pko_h, c_stack, pfx):
        xlo = c_stack.enter_context(
            nc.sbuf_tensor(pfx + "xlown", [128, KD], F32))

        def xl_emit(xl_fm):
            lds = [p.dma(xlo[:, k:k + 1],
                         stream[128 * k:128 * (k + 1), 2047:2048],
                         tag="xlc", slow=True) for k in range(KD)]
            wr = p.dma(pki_h.ap()[:, :], xlo[:, :], [lds], tag="xlw")
            cl = p.coll("AllGather", ALU.bypass, PAIRS,
                        [pki_h.ap().opt()], [pko_h.ap().opt()],
                        waits=[wr], tag="xlag")
            return p.dma(xl_fm[:, :], pko_h.ap()[0:128, 0:KD], [cl],
                         tag="xlr")
        return xl_emit

    c = ExitStack()
    emit_p5(nc, c, p, {**dr5, "xi": dxB, "xo": dxA}, pfx="e1_",
            xl_emit=mk_xl(dxB, xli5_h[0], xlo5_h[0], c, "e1_"),
            thr_h=(thki5_h[0], thko5_h[0]))
    p.phase_break()
    c.close()

    c = ExitStack()
    emit_p6f(nc, c, p, {**dr6, "xi": dxA, "xo": dxB, "dq": dq6,
                        "dko_h": dko6_h, "dkg_h": dkg6_h,
                        "dvo_h": dvo6_h, "dvg_h": dvg6_h}, pfx="f_")
    p.phase_break()
    c.close()

    c = ExitStack()
    emit_p5(nc, c, p, {**dr5, "xi": dxB, "xo": xo}, pfx="e2_",
            xl_emit=mk_xl(dxB, xli5_h[1], xlo5_h[1], c, "e2_"),
            thr_h=(thki5_h[1], thko5_h[1]))
    p.finalize(top)
    c.close()
    return nc, top


def gidx_table(h):
    blkmap = [0, 1, 6, 7] if h == 0 else [2, 3, 4, 5]
    g = np.zeros((128, 64), np.int32)
    pp = np.arange(128)
    for k in range(KD):
        for j in range(4):
            g[:, 4 * k + j] = (128 * k + pp) * 8 + blkmap[j]
    return g


def fused_inputs(inp, c):
    b, h = c // 2, c % 2
    m = {}
    m.update({"a_" + k: v for k, v in p1_inputs(inp, b).items()})
    m.update({"b_" + k: v for k, v in p2_weights(inp).items()})
    m["b_gidx"] = gidx_table(h)
    m.update({"c_" + k: v for k, v in p3f_inputs(inp, h).items()})
    m.update({"d_" + k: v for k, v in p4_inputs(inp).items()})
    m.update({"e_" + k: v for k, v in p5f_inputs(inp, h).items()})
    m.update({"f_" + k: v for k, v in p6f_inputs(inp, h).items()})
    return m


def kernel(**inputs):
    inp = {k: np.asarray(v) for k, v in inputs.items()}
    nc = _get_prog("fused", build_fused)
    maps = [fused_inputs(inp, c) for c in range(NCORES)]
    r = run_bass_kernel_spmd(nc, maps, core_ids=list(range(NCORES)))
    out = np.empty((4, S, D), np.float32)
    for b in range(4):
        xf = np.empty((D, S), np.float32)
        xf[:, cols_of(0)] = r.results[2 * b]["xo"]
        xf[:, cols_of(1)] = r.results[2 * b + 1]["xo"]
        out[b] = xf.T
    return out



def emit_p2(nc, ctx, p, dr, pfx="", load_fn=None):
    T = TOWN
    xi = dr.get("xi")
    d_wg, d_wu, d_wd = dr["wg"], dr["wu"], dr["wd"]
    xo = dr["xo"]

    def sb(name, shape, dtype=F32):
        return ctx.enter_context(nc.sbuf_tensor(pfx + name, shape, dtype))

    AR = sb("AR", [128, 81920], BF16)
    XLN0, H0 = 0, 16384

    def xln_ap(k, lo, hi):
        return AR[:, XLN0 + 1024 * k + lo: XLN0 + 1024 * k + hi]

    def h_ap(m, lo, hi):
        return AR[:, H0 + 1024 * m + lo: H0 + 1024 * m + hi]

    wgu = [sb(f"wgu{i}", [128, KD * 128], BF16) for i in range(4)]
    hps = [sb(f"hps{i}", [128, 512], BF16) for i in range(2)]
    habs = [sb(f"habs{i}", [128, 512], BF16) for i in range(4)]
    pmA = sb("pmA", [128, 512], BF16)
    pmB = sb("pmB", [128, 512], BF16)
    pmC = sb("pmC", [128, 512], BF16)
    ust = [sb(f"ust{i}", [128, 512]) for i in range(2)]

    ps = [ctx.enter_context(nc.psum_tensor(pfx + f"ps{i}", [128, 512], F32))
          for i in range(8)]

    lb = LnBufs(nc, ctx, pfx + "ln_",
                ps_st=[ps[0][0:1, 0:512], ps[1][0:1, 0:512],
                       ps[2][0:1, 0:512], ps[3][0:1, 0:512]],
                ps_bc=ps[4][:, 0:512])
    lb.emit_init(p)

    bank_last = {i: None for i in range(8)}
    wgu_last = [None] * 4
    ust_last = [None] * 2
    dw_last = [None, None]
    spl = {}
    h_last = {}

    def emit_sparsify(s_, nsub):
        mem = [s_, 16 + s_, 32 + s_, 48 + s_]
        lo, hi = 512 * nsub, 512 * (nsub + 1)
        ab = []
        for j in range(4):
            ab.append(p.s(lambda e, j=j, mm=mem[j], lo=lo, hi=hi: e.activation(
                habs[j][:, :], h_ap(mm, lo, hi), AF.Abs),
                [h_last[("w", mem[j], nsub)], spl.get(("abs", j))], tag="habs"))
        t1 = p.v(lambda e: e.tensor_max(pmA[:, :], habs[0][:, :], habs[1][:, :]),
                 [ab[0], ab[1], spl.get("pmA")], tag="sp")
        t2 = p.v(lambda e: e.tensor_tensor(pmC[:, :], habs[0][:, :],
                                           habs[1][:, :], ALU.min),
                 [ab[0], ab[1], spl.get("pmC")])
        t3 = p.v(lambda e: e.tensor_max(pmB[:, :], habs[2][:, :], habs[3][:, :]),
                 [ab[2], ab[3], spl.get("pmB")])
        t5 = p.v(lambda e: e.tensor_tensor(pmA[:, :], pmA[:, :], pmB[:, :],
                                           ALU.min), [t1, t3])
        t4 = p.v(lambda e: e.tensor_tensor(pmB[:, :], habs[2][:, :],
                                           habs[3][:, :], ALU.min),
                 [t5, ab[2], ab[3]])
        t6 = p.v(lambda e: e.tensor_max(pmC[:, :], pmC[:, :], pmB[:, :]),
                 [t2, t4])
        t7 = p.v(lambda e: e.tensor_max(pmA[:, :], pmA[:, :], pmC[:, :]),
                 [t5, t6])
        spl["pmB"] = t6
        spl["pmC"] = t7
        mus = []
        for j in range(4):
            mk = p.v(lambda e, j=j: e.tensor_tensor(
                habs[j][:, :], habs[j][:, :], pmA[:, :], ALU.is_ge),
                [t7], tag="mask")
            mu = p.v(lambda e, j=j, mm=mem[j], lo=lo, hi=hi: e.tensor_tensor(
                h_ap(mm, lo, hi), h_ap(mm, lo, hi), habs[j][:, :],
                ALU.mult), [mk], tag="maskmul")
            spl[("abs", j)] = mu
            h_last[("r", mem[j], nsub)] = mu
            mus.append(mu)
        spl["pmA"] = mus

    for nb in range(T // 1024):
        for b in range(4):
            lb.last[("st", b)] = bank_last[b]
        for k in range(KD):
            lb.last[("ar", k)] = [h_last.get(("xa", k))]
        chunk_toks, lnall, strd = emit_ln_block(p, lb, xi, 1024 * nb, AR, XLN0,
                                                load_fn=load_fn)
        for b in range(4):
            bank_last[b] = strd[b]
        last_gu_mm = None
        for m in range(64):
            par = m % 2
            gsl, usl = wgu[par], wgu[2 + par]
            lg = p.dma(gsl[:, :], d_wg[m], [wgu_last[par]], tag="wgld")
            lu = p.dma(usl[:, :], d_wu[m], [wgu_last[2 + par]], tag="wuld")
            for nsub in range(2):
                bg, bu = 2 * par, 2 * par + 1
                mmg = mmu = None
                for k in range(KD):
                    mmg = p.t(lambda e, k=k, gsl=gsl, nsub=nsub, bg=bg: e.matmul(
                        ps[bg][:, 0:512], gsl[:, 128 * k:128 * (k + 1)],
                        xln_ap(k, 512 * nsub, 512 * (nsub + 1)),
                        start=(k == 0), stop=(k == KD - 1)),
                        [chunk_toks[k], lg] + ([bank_last[bg]] if k == 0 else []),
                        tag="gmm")
                    mmu = p.t(lambda e, k=k, usl=usl, nsub=nsub, bu=bu: e.matmul(
                        ps[bu][:, 0:512], usl[:, 128 * k:128 * (k + 1)],
                        xln_ap(k, 512 * nsub, 512 * (nsub + 1)),
                        start=(k == 0), stop=(k == KD - 1)),
                        [chunk_toks[k], lu] + ([bank_last[bu]] if k == 0 else []),
                        tag="umm")
                sil = p.s(lambda e, par=par, bg=bg: e.activation(
                    hps[par][:, :], ps[bg][:, 0:512], AF.Silu),
                    [mmg, bank_last[bg]], tag="silu")
                hmul = p.v(lambda e, m=m, nsub=nsub, par=par, bu=bu: e.tensor_tensor(
                    h_ap(m, 512 * nsub, 512 * (nsub + 1)), hps[par][:, :],
                    ps[bu][:, 0:512], ALU.mult),
                    [sil, mmu, h_last.get(("hw", m, nsub))], tag="hmul")
                bank_last[bg] = sil
                bank_last[bu] = hmul
                h_last[("w", m, nsub)] = hmul
            wgu_last[par] = mmg
            wgu_last[2 + par] = mmu
            last_gu_mm = mmu
            if m >= 48:
                emit_sparsify(m - 48, 0)
                emit_sparsify(m - 48, 1)
        dwld = [None, None]
        for m2 in range(KD):
            par = m2 % 2
            dwsl = AR[:, XLN0 + 8192 * par: XLN0 + 8192 * (par + 1)]
            ld = p.dma(dwsl, d_wd[m2],
                       [last_gu_mm, dw_last[par], lnall], tag="wdld")
            dwld[par] = ld
            for nsub in range(2):
                bank = 5 + par
                mmd = None
                for k in range(64):
                    mmd = p.t(lambda e, k=k, nsub=nsub, bank=bank, dwsl=dwsl: e.matmul(
                        ps[bank][:, 0:512], dwsl[:, 128 * k:128 * (k + 1)],
                        h_ap(k, 512 * nsub, 512 * (nsub + 1)),
                        start=(k == 0), stop=(k == 63)),
                        [h_last[("r", k, nsub)], ld] +
                        ([bank_last[bank]] if k == 0 else []),
                        tag="dmm")
                si = nsub
                if load_fn is not None:
                    lx = load_fn(ust[si][:, :], m2, 2 * nb + nsub,
                                 [ust_last[si]])
                else:
                    lx = p.dma(ust[si][:, :], xi[128 * m2:128 * (m2 + 1),
                                                 1024 * nb + 512 * nsub:
                                                 1024 * nb + 512 * (nsub + 1)],
                               [ust_last[si]], tag="xreld")
                w2_ = p.v(lambda e, si=si, bank=bank: e.tensor_tensor(
                    ust[si][:, :], ust[si][:, :], ps[bank][:, 0:512], ALU.add),
                    [lx, mmd], tag="ffnevac")
                st = p.dmag(xo[128 * m2:128 * (m2 + 1),
                               1024 * nb + 512 * nsub:1024 * nb + 512 * (nsub + 1)],
                            ust[si][:, :], [w2_], tag="xost")
                ust_last[si] = st
                bank_last[bank] = w2_
            dw_last[par] = mmd
        for k in range(KD):
            h_last[("xa", k)] = [dw_last[k // 8], dwld[k // 8]]
        for m in range(64):
            for nsub in range(2):
                h_last[("hw", m, nsub)] = [dw_last[0], dw_last[1]]
    return


def p2_weights(inp):
    wg = np.asarray(inp["L0_ffn_w_gate"], np.float32)
    wu = np.asarray(inp["L0_ffn_w_up"], np.float32)
    wd = np.asarray(inp["L0_ffn_w_down"], np.float32)
    perm = np.array([4 * (i % 2048) + (i // 2048) for i in range(8192)])
    return {
        "wg": tile_lhsT(wg[:, perm]),
        "wu": tile_lhsT(wu[:, perm]),
        "wd": tile_lhsT(wd[perm, :]),
    }



def emit_p3f(nc, ctx, p, dr, pfx=""):
    SCALE = 1.0 / float(np.sqrt(128.0))

    xi = dr["xi"]
    d_wq, d_wk, d_wv, d_wo = dr["wq"], dr["wk"], dr["wv"], dr["wo"]
    d_qb, d_kb, d_vb, d_ob = dr["qb"], dr["kb"], dr["vb"], dr["ob"]
    d_slope, d_kgc, d_mstrip = dr["slopes"], dr["kgc"], dr["mstrip"]
    xo = dr["xo"]
    dq = dr["dq"]
    dko_h, dkg_h = dr["dko_h"], dr["dkg_h"]
    dvo_h, dvg_h = dr["dvo_h"], dr["dvg_h"]
    dko, dkg = dko_h.ap(), dkg_h.ap()
    dvo, dvg = dvo_h.ap(), dvg_h.ap()

    def sb(name, shape, dtype=F32):
        return ctx.enter_context(nc.sbuf_tensor(pfx + name, shape, dtype))

    AR = sb("AR", [128, 65536], BF16)
    def kth(hp, lo, hi):
        return AR[:, 32768 + hp * 4096 + lo: 32768 + hp * 4096 + hi]

    def vth(hp, lo, hi):
        return AR[:, 40960 + hp * 4096 + lo: 40960 + hp * 4096 + hi]

    def qth(hp, lo, hi):
        return AR[:, 49152 + hp * 2048 + lo: 49152 + hp * 2048 + hi]

    def pt_ap(pi, lo, hi):
        base = 53248 + pi * 512 if pi < 3 else 63616
        return AR[:, base + lo: base + hi]

    dinv_b = AR[:, 54784: 55296]
    mst128 = AR[:, 55296: 55424]

    def kod(hp, lo, hi):
        return AR[:, 55424 + hp * 2048 + lo: 55424 + hp * 2048 + hi]

    def vod(hp, lo, hi):
        return AR[:, 59520 + hp * 2048 + lo: 59520 + hp * 2048 + hi]

    def att_ap(h, qc):
        return AR[:, 2048 * h + 512 * qc: 2048 * h + 512 * (qc + 1)]

    def xln_ap(k, lo, hi):
        return AR[:, 2048 * k + lo: 2048 * k + hi]

    wsl = [sb(f"wsl{i}", [128, D], BF16) for i in range(2)]
    wvres = sb("wvres", [128, KD * 512], BF16)
    prst = [sb(f"prst{i}", [128, 512], BF16) for i in range(2)]
    biasc = sb("biasc", [128, KD])
    vb_sb = sb("vb_sb", [1, D], BF16)
    slope_sb = sb("slope_sb", [1, 16])
    slope_b = sb("slope_b", [128, 16])
    kgc_sb = sb("kgc_sb", [128, 773])
    biasfm = sb("biasfm", [128, 773])
    drow = sb("drow", [1, 512])
    drow_bf = sb("drow_bf", [1, 512], BF16)
    ones_rowb = sb("ones_rowb", [1, 128], BF16)
    ones_colb = sb("ones_colb", [128, 1], BF16)
    ones_row_f = sb("ones_row_f", [1, 128])
    xres = [sb(f"xres{i}", [128, 512]) for i in range(2)]

    ps = [ctx.enter_context(nc.psum_tensor(pfx + f"ps{i}", [128, 512], F32))
          for i in range(8)]

    lb = LnBufs(nc, ctx, pfx + "ln_",
                ps_st=[ps[0][0:1, 0:512], ps[1][0:1, 0:512],
                       ps[2][0:1, 0:512], ps[3][0:1, 0:512]],
                ps_bc=ps[4][:, 0:512])
    lb.emit_init(p)

    ms1 = p.v(lambda e: e.memset(ones_rowb[:, :], 1.0))
    ms2 = p.v(lambda e: e.memset(ones_colb[:, :], 1.0))
    ms3 = p.v(lambda e: e.memset(ones_row_f[:, :], 1.0))
    l_sl = p.dma(slope_sb[:, :], d_slope[:, :])
    l_kgc = p.dma(kgc_sb[:, :], d_kgc[:, :])
    l_vb = p.dma(vb_sb[:, :], d_vb[:, :])
    bsl = p.t(lambda e: e.matmul(ps[5][:, 0:16], ones_row_f[0:1, :],
                                 slope_sb[0:1, :], start=True, stop=True),
              [ms3, l_sl], tag="slopeb")
    bslc = p.v(lambda e: e.tensor_copy(slope_b[:, :], ps[5][:, 0:16]), [bsl])

    bank_last = {i: [None] for i in range(8)}
    bank_last[5] = [bslc]

    chunk_toks_all = [[] for _ in range(KD)]
    for nb in range(2):
        for b in range(4):
            lb.last[("st", b)] = bank_last[b]
        chunk_toks, lnall, strd = emit_ln_block(p, lb, xi, 1024 * nb, AR, 0,
                                                pitch=2048, boff=1024 * nb)
        for b in range(4):
            bank_last[b] = [strd[b]]
        for k in range(KD):
            chunk_toks_all[k].append(chunk_toks[k])

    v_store_toks = []
    wv_last = None
    prst_last = [None, None]
    for vc in range(4):
        lws = []
        for k in range(KD):
            lws.append(p.dma(wvres[:, 512 * k:512 * (k + 1)],
                             d_wv[k][:, 512 * vc:512 * (vc + 1)],
                             [wv_last], tag="wvld"))
        last_vmm = None
        for t in range(16):
            bank = 5 + (t % 2)
            pre = p.t(lambda e, vc=vc, bank=bank: e.matmul(
                ps[bank][:, 0:512], ones_rowb[0:1, :],
                vb_sb[0:1, 512 * vc:512 * (vc + 1)], start=True, stop=False),
                [ms1, l_vb, bank_last[bank]], tag="vpre")
            mm = pre
            for k in range(KD):
                mm = p.t(lambda e, k=k, t=t, bank=bank: e.matmul(
                    ps[bank][:, 0:512],
                    xln_ap(k, 128 * t, 128 * (t + 1)),
                    wvres[:, 512 * k:512 * (k + 1)],
                    start=False, stop=(k == KD - 1)),
                    [chunk_toks_all[k], lws[k]], tag="vmm")
            pr = t % 2
            ev = p.s(lambda e, pr=pr, bank=bank: e.activation(
                prst[pr][:, :], ps[bank][:, 0:512], AF.Copy),
                [mm, prst_last[pr]], tag="vev")
            bank_last[bank] = [ev]
            sts = []
            for j in range(4):
                sts.append(p.dmag(dvo[4 * vc + j, t],
                                  prst[pr][:, 128 * j:128 * (j + 1)], [ev],
                                  tag="vst"))
            prst_last[pr] = sts
            v_store_toks += sts
            last_vmm = mm
        wv_last = last_vmm

    wsl_last = [None, None]
    st_by_head_q = {}
    st_by_head_k = {}

    def proj_featmajor(dram_w, dram_out, ntok, bias_ld, stmap, tag):
        for m in range(KD):
            par = m % 2
            lw = p.dma(wsl[par][:, :], dram_w[m], [wsl_last[par]], tag=tag + "w")
            stoks = []
            mm = None
            for qc in range(ntok // 512):
                bank = 5 + (qc % 2)
                for k in range(KD):
                    mm = p.t(lambda e, k=k, qc=qc, par=par, bank=bank: e.matmul(
                        ps[bank][:, 0:512], wsl[par][:, 128 * k:128 * (k + 1)],
                        xln_ap(k, 512 * qc, 512 * (qc + 1)),
                        start=(k == 0), stop=(k == KD - 1)),
                        [chunk_toks_all[k], lw] +
                        (bank_last[bank] if k == 0 else []), tag=tag)
                pr = qc % 2
                ev = p.s(lambda e, m=m, pr=pr, bank=bank: e.activation(
                    prst[pr][:, :], ps[bank][:, 0:512], AF.Identity,
                    bias=biasc[:, m:m + 1]),
                    [mm, bias_ld, prst_last[pr]], tag=tag + "ev")
                st = p.dmag(dram_out[128 * m:128 * (m + 1),
                                     512 * qc:512 * (qc + 1)],
                            prst[pr][:, :], [ev], tag=tag + "st")
                bank_last[bank] = [ev]
                prst_last[pr] = st
                stoks.append(st)
            stmap[m] = stoks
            wsl_last[par] = mm

    coll_v = p.coll("AllGather", ALU.bypass, PAIRS,
                    [dvo_h.ap().opt()], [dvg_h.ap().opt()],
                    waits=[v_store_toks], tag="agv")

    lkb = p.dma(biasc[:, :], d_kb[:, :], [], tag="kbld")
    proj_featmajor(d_wk, dko, TOWN, lkb, st_by_head_k, "kp")
    coll_k = p.coll("AllGather", ALU.bypass, PAIRS,
                    [dko_h.ap().opt()], [dkg_h.ap().opt()],
                    waits=[_flat(list(st_by_head_k.values()))], tag="agk")
    lqb = p.dma(biasc[:, :], d_qb[:, :],
                [_flat(list(st_by_head_k.values()))], tag="qbld")
    proj_featmajor(d_wq, dq, TOWN, lqb, st_by_head_q, "qp")

    all_proj = _flat(list(st_by_head_q.values()) + list(st_by_head_k.values())) \
        + v_store_toks
    lms = p.dma(mst128, d_mstrip[:, :], [wv_last, wsl_last], tag="mstld")
    ar_last = {"k": [None, None], "v": [None, None], "q": [None, None],
               "ko": [None, None], "vo": [None, None]}
    pt_last = [None, None, None, None]
    att_last = {}
    dinv_last = None
    biasfm_last = None
    drow_last = None
    drow_bf_last = None
    den_pend = None
    TL = {0: list(range(8)) + list(range(16, 19)),
          1: list(range(8)) + list(range(16, 23)),
          2: list(range(11)) + list(range(16, 32)),
          3: list(range(15)) + list(range(16, 32))}
    slopes_f = 2.0 ** (-np.linspace(1.0, 8.0, 16))
    gorder_np = np.r_[cols_of(0), cols_of(1)]
    TLH = {}
    for h_ in range(16):
        cut = 46.0 / float(slopes_f[h_])
        TLH[h_] = {}
        for qc_ in range(4):
            keep = []
            for t_ in TL[qc_]:
                kg = gorder_np[128 * t_:128 * (t_ + 1)]
                ok = False
                for par in range(2):
                    qcols = cols_of(par)[512 * qc_:512 * (qc_ + 1)]
                    dmat = qcols[:, None] - kg[None, :]
                    dpos = dmat[dmat >= 0]
                    if dpos.size and dpos.min() <= cut:
                        ok = True
                        break
                if ok:
                    keep.append(t_)
            assert len(keep) >= 2, (h_, qc_, keep)
            TLH[h_][qc_] = keep
    UQ = {qc_: list(range(4 * qc_, 4 * qc_ + 4)) +
          list(range(16 + 4 * qc_, 16 + 4 * qc_ + 4)) for qc_ in range(4)}
    first_waits = all_proj + [lms]

    for h in range(16):
        hp = h % 2
        lkA = p.dma(kth(hp, 0, 2048), dkg[0, 128 * h:128 * (h + 1), :],
                    [coll_k, ar_last["k"][hp],
                     first_waits if h < 2 else None], tag="kld")
        lkB = p.dma(kth(hp, 2048, 4096), dkg[1, 128 * h:128 * (h + 1), :],
                    [coll_k], tag="kld")
        lk = [lkA, lkB]
        lko = p.dma(kod(hp, 0, 2048), dko[128 * h:128 * (h + 1), :],
                    [st_by_head_k[h], ar_last["ko"][hp],
                     first_waits if h < 2 else None], tag="kold")
        lvs = []
        for t in range(32):
            lvs.append(p.dma(vth(hp, 128 * t, 128 * (t + 1)),
                             dvg[t // 16, h, t % 16],
                             [coll_v, ar_last["v"][hp],
                              first_waits if h < 2 else None], tag="vld"))
        lvo = []
        for tt in range(16):
            lvo.append(p.dma(vod(hp, 128 * tt, 128 * (tt + 1)), dvo[h, tt],
                             [ar_last["vo"][hp],
                              first_waits if h < 2 else None], tag="vold"))
        lq = p.dma(qth(hp, 0, 2048), dq[128 * h:128 * (h + 1), :],
                   [st_by_head_q[h], ar_last["q"][hp],
                    first_waits if h < 2 else None], tag="qld")
        bft = p.v(lambda e, h=h: e.tensor_scalar(
            biasfm[:, :], kgc_sb[:, :], slope_b[:, h:h + 1], None, ALU.mult),
            [l_kgc, bslc, biasfm_last], tag="bfm")
        kreaders = []
        koreaders = []
        voreaders = []
        vreaders = []
        qreaders = []
        exs = []
        SB = (0, 1, 7)

        def make_den(h, qc, dmm, pvm, pvbank, dbank):
            def go():
                nonlocal drow_last, drow_bf_last, dinv_last
                rcp = p.v(lambda e, dbank=dbank: e.reciprocal(
                    drow[0:1, :], ps[dbank][0:1, 0:512]),
                    [dmm, drow_last], tag="rcp")
                rcb = p.v(lambda e: e.tensor_copy(drow_bf[0:1, :],
                                                  drow[0:1, :]),
                          [rcp, drow_bf_last], tag="rcb")
                bank_last[dbank] = [rcp]
                bcm = p.t(lambda e: e.matmul(ps[6][:, 0:512],
                                             ones_rowb[0:1, :],
                                             drow_bf[0:1, :], start=True,
                                             stop=True),
                          [rcb, ms1] + bank_last[6], tag="dbc")
                dcp = p.v(lambda e: e.tensor_copy(dinv_b, ps[6][:, 0:512]),
                          [bcm, dinv_last], tag="dcp")
                bank_last[6] = [dcp]
                drow_last = rcb
                drow_bf_last = bcm
                amul = p.v(lambda e, h=h, qc=qc, pvbank=pvbank:
                           e.tensor_tensor(
                    att_ap(h, qc), ps[pvbank][:, 0:512], dinv_b, ALU.mult),
                    [pvm, dcp, att_last.get((h, qc))], tag="attdiv")
                dinv_last = amul
                bank_last[pvbank] = [amul]
                att_last[(h, qc)] = amul
            return go

        for qc in range(4):
            gl = TLH[h][qc]
            tlist = [("G", gl[0])] + [("D", s) for s in range(4)] + \
                    [("G", t) for t in gl[1:]]
            pvbank = 2 + (qc % 2)
            dbank = 4 + (qc % 2)
            ntl = len(tlist)
            dmm = pvm = None
            pq = []

            def mk_pend(reg, t, pi, ptok, i_, last):
                def go():
                    if reg == "G":
                        dm = p.t(lambda e, pi=pi, dbank=dbank, first=(i_ == 0),
                                 last=last: e.matmul(
                            ps[dbank][0:1, 0:512], ones_colb[:, :],
                            pt_ap(pi, 0, 512), start=first, stop=last),
                            [ptok, ms2] + (bank_last[dbank] if i_ == 0 else []),
                            tag="dmm")
                        pv = p.t(lambda e, pi=pi, t=t, hp=hp, pvbank=pvbank,
                                 first=(i_ == 0), last=last: e.matmul(
                            ps[pvbank][:, 0:512],
                            vth(hp, 128 * t, 128 * (t + 1)),
                            pt_ap(pi, 0, 512), start=first, stop=last),
                            [ptok, lvs[t]] +
                            (bank_last[pvbank] if i_ == 0 else []),
                            tag="pvm")
                        vreaders.append(pv)
                    else:
                        ot = 4 * qc + t
                        dm = p.t(lambda e, pi=pi, t=t, dbank=dbank: e.matmul(
                            ps[dbank][0:1, 128 * t:128 * (t + 1)],
                            ones_colb[:, :],
                            pt_ap(pi, 0, 128), start=False, stop=False),
                            [ptok, ms2], tag="ddmm")
                        pv = p.t(lambda e, pi=pi, t=t, ot=ot, hp=hp,
                                 pvbank=pvbank: e.matmul(
                            ps[pvbank][:, 128 * t:128 * (t + 1)],
                            vod(hp, 128 * ot, 128 * (ot + 1)),
                            pt_ap(pi, 0, 128), start=False, stop=False),
                            [ptok, lvo[ot]], tag="dpvm")
                        voreaders.append(pv)
                    pt_last[pi] = [dm, pv]
                    return dm, pv
                return go

            for i_, (reg, t) in enumerate(tlist):
                sbank = SB[i_ % 3]
                pi = i_ % 4
                last = i_ == ntl - 1
                light = slopes_f[h] * 4095.0 <= 60.0
                if reg == "G":
                    smm = p.t(lambda e, t=t, hp=hp, qc=qc, sbank=sbank: e.matmul(
                        ps[sbank][:, 0:512], kth(hp, 128 * t, 128 * (t + 1)),
                        qth(hp, 512 * qc, 512 * (qc + 1)),
                        start=True, stop=True),
                        [lk, lq] + bank_last[sbank], tag="smm")
                    kreaders.append(smm)
                    qreaders.append(smm)
                    if light and t not in UQ[qc]:
                        idx = 513 + 32 * qc + t
                        exl = [p.s(lambda e, pi=pi, idx=idx, sbank=sbank:
                                   e.activation(
                            pt_ap(pi, 0, 512), ps[sbank][:, 0:512],
                            AF.Exp, scale=SCALE,
                            bias=biasfm[:, idx:idx + 1]),
                            [smm, bft, pt_last[pi]], tag="exl")]
                    else:
                        jb = (641 + 32 * qc + 4 * UQ[qc].index(t)) if light \
                            else None
                        exl = []
                        for sl_ in range(4):
                            idx = jb + sl_ if light else (4 * qc + sl_) * 32 + t
                            exl.append(p.s(lambda e, pi=pi, sl_=sl_, idx=idx,
                                           sbank=sbank: e.activation(
                                pt_ap(pi, 128 * sl_, 128 * (sl_ + 1)),
                                ps[sbank][:, 128 * sl_:128 * (sl_ + 1)],
                                AF.Exp, scale=SCALE,
                                bias=biasfm[:, idx:idx + 1]),
                                [smm, bft, pt_last[pi] if sl_ == 0 else None],
                                tag="exp"))
                    exs += exl
                    bank_last[sbank] = exl
                    ptok = exl
                else:
                    ot = 4 * qc + t
                    smm = p.t(lambda e, t=t, ot=ot, hp=hp, qc=qc, sbank=sbank:
                              e.matmul(
                        ps[sbank][:, 0:128],
                        kod(hp, 128 * ot, 128 * (ot + 1)),
                        qth(hp, 512 * qc + 128 * t, 512 * qc + 128 * (t + 1)),
                        start=True, stop=True),
                        [lko, lq] + bank_last[sbank], tag="dsmm")
                    koreaders.append(smm)
                    qreaders.append(smm)
                    didx = (769 + t) if light else 512
                    ex = p.s(lambda e, pi=pi, didx=didx, sbank=sbank:
                             e.activation(
                        pt_ap(pi, 0, 128), ps[sbank][:, 0:128],
                        AF.Exp, scale=SCALE, bias=biasfm[:, didx:didx + 1]),
                        [smm, bft, pt_last[pi]], tag="dexp")
                    exs.append(ex)
                    bank_last[sbank] = [ex]
                    msk = p.v(lambda e, pi=pi: e.tensor_tensor(
                        pt_ap(pi, 0, 128), pt_ap(pi, 0, 128), mst128,
                        ALU.mult), [ex, lms], tag="dmask")
                    ptok = msk
                if i_ == 4 and den_pend is not None:
                    den_pend()
                    den_pend = None
                if len(pq) == 2:
                    dmm, pvm = pq.pop(0)()
                pq.append(mk_pend(reg, t, pi, ptok, i_, last))
            for go in pq:
                dmm, pvm = go()
            den_pend = make_den(h, qc, dmm, pvm, pvbank, dbank)
        ar_last["k"][hp] = kreaders[-1]
        ar_last["ko"][hp] = koreaders
        ar_last["v"][hp] = vreaders
        ar_last["vo"][hp] = voreaders
        ar_last["q"][hp] = qreaders[-1]
        biasfm_last = exs
    if den_pend is not None:
        den_pend()
        den_pend = None

    lob = p.dma(biasc[:, :], d_ob[:, :],
                [_flat(list(st_by_head_k.values()))], tag="obld")
    xres_last = [None, None]
    for m in range(KD):
        par = m % 2
        lw = p.dma(wsl[par][:, :], d_wo[m], [wsl_last[par]], tag="wow")
        mm = None
        for qc in range(4):
            bank = 5 + (qc % 2)
            for k in range(KD):
                mm = p.t(lambda e, k=k, qc=qc, par=par, bank=bank: e.matmul(
                    ps[bank][:, 0:512], wsl[par][:, 128 * k:128 * (k + 1)],
                    att_ap(k, qc), start=(k == 0), stop=(k == KD - 1)),
                    [att_last[(k, qc)], lw] +
                    (bank_last[bank] if k == 0 else []), tag="omm")
            pr = qc % 2
            lx = p.dma(xres[pr][:, :], xi[128 * m:128 * (m + 1),
                                          512 * qc:512 * (qc + 1)],
                       [xres_last[pr]], tag="oxld")
            evb = p.s(lambda e, m=m, pr=pr: e.activation(
                xres[pr][:, :], xres[pr][:, :], AF.Identity,
                bias=biasc[:, m:m + 1]), [lx, lob], tag="oevb")
            yv = p.v(lambda e, pr=pr, bank=bank: e.tensor_tensor(
                xres[pr][:, :], xres[pr][:, :], ps[bank][:, 0:512], ALU.add),
                [evb, mm], tag="oadd")
            st = p.dmag(xo[128 * m:128 * (m + 1), 512 * qc:512 * (qc + 1)],
                        xres[pr][:, :], [yv], tag="oxst")
            bank_last[bank] = [yv]
            xres_last[pr] = st
        wsl_last[par] = mm
    return


def p3f_inputs(inp, h):
    ipj = np.asarray(inp["L0_attn_b_in_proj"], np.float32)
    ipb = np.asarray(inp["L0_attn_b_in_proj_bias"], np.float32)
    ow = np.asarray(inp["L0_attn_b_out_w"], np.float32)
    ob = np.asarray(inp["L0_attn_b_out_b"], np.float32)
    sl = np.asarray(inp["L0_attn_b_alibi_slopes"], np.float32)
    own = cols_of(h)
    gorder = np.r_[cols_of(0), cols_of(1)]
    kgc = np.zeros((128, 773), np.float32)
    for qs in range(16):
        qmax = float(own[qs * 128 + 127])
        tdiag = qs if h == 0 else 16 + qs
        for t in range(32):
            kg = gorder[128 * t:128 * (t + 1)].astype(np.float32)
            col = np.where(kg <= qmax, kg - qmax, -1e9).astype(np.float32)
            if t == tdiag:
                col[:] = -1e9
            kgc[:, qs * 32 + t] = col
    kgc[:, 512] = np.arange(128, dtype=np.float32) - 127.0
    for qc in range(4):
        qmaxc = float(own[qc * 512 + 511])
        for t in range(32):
            kg = gorder[128 * t:128 * (t + 1)].astype(np.float32)
            kgc[:, 513 + 32 * qc + t] = np.where(kg <= qmaxc, kg - qmaxc,
                                                 -1e9)
        uq = list(range(4 * qc, 4 * qc + 4)) + \
            list(range(16 + 4 * qc, 16 + 4 * qc + 4))
        for j, t in enumerate(uq):
            kg = gorder[128 * t:128 * (t + 1)].astype(np.float32)
            for s_ in range(4):
                qmax_s = float(own[qc * 512 + 128 * s_ + 127])
                tdiag = (4 * qc + s_) if h == 0 else 16 + 4 * qc + s_
                col = np.where(kg <= qmax_s, kg - qmaxc, -1e9)
                if t == tdiag:
                    col = np.full(128, -1e9, np.float32)
                kgc[:, 641 + 32 * qc + 4 * j + s_] = col
    for s_ in range(4):
        kgc[:, 769 + s_] = (np.arange(128, dtype=np.float32)
                            + 128 * s_ - 511.0)
    tri = (np.arange(128)[:, None] <= np.arange(128)[None, :])
    return {
        "wq": tile_lhsT(ipj[0:2048].T),
        "wk": tile_lhsT(ipj[2048:4096].T),
        "wv": to_bf(np.ascontiguousarray(
            ipj[4096:6144].T).reshape(KD, 128, D)),
        "wo": tile_lhsT(ow.T),
        "qb": np.ascontiguousarray(ipb[0:2048].reshape(KD, 128).T),
        "kb": np.ascontiguousarray(ipb[2048:4096].reshape(KD, 128).T),
        "vb": to_bf(ipb[4096:6144].reshape(1, 2048)),
        "ob": np.ascontiguousarray(ob.reshape(KD, 128).T),
        "slopes": sl.reshape(1, 16),
        "kgc": kgc,
        "mstrip": to_bf(tri.astype(np.float32)),
    }



def emit_p4f(nc, ctx, p, dr, pfx=""):
    xi = dr["xi"]
    d_wq, d_wk, d_wv, d_wo = dr["wq"], dr["wk"], dr["wv"], dr["wo"]
    d_qb, d_kb, d_vb, d_ob = dr["qb"], dr["kb"], dr["vb"], dr["ob"]
    d_mq, d_mk = dr["wmapq"], dr["wmapk"]
    xo = dr["xo"]
    dq = dr["dq"]
    dko = dr["dko"]
    dvo = dr["dvo"]
    pki_h, pko_h = dr["pki_h"], dr["pko_h"]

    def sb(name, shape, dtype=F32):
        return ctx.enter_context(nc.sbuf_tensor(pfx + name, shape, dtype))

    AR = sb("AR", [128, 65536], BF16)

    def kth(hp, lo, hi):
        return AR[:, 32768 + hp * 4096 + lo: 32768 + hp * 4096 + hi]

    def vth(hp, lo, hi):
        return AR[:, 40960 + hp * 4096 + lo: 40960 + hp * 4096 + hi]

    def qth(hp, lo, hi):
        return AR[:, 49152 + hp * 2048 + lo: 49152 + hp * 2048 + hi]

    def qmh(hp, lo, hi):
        return AR[0:64, 53248 + 2048 * hp + lo: 53248 + 2048 * hp + hi]

    def att_ap(h, qc):
        return AR[:, 2048 * h + 512 * qc: 2048 * h + 512 * (qc + 1)]

    def xln_ap(k, lo, hi):
        return AR[:, 2048 * k + lo: 2048 * k + hi]

    wsl = [sb(f"wsl{i}", [128, D], BF16) for i in range(2)]
    wvres = sb("wvres", [128, KD * 512], BF16)
    prst = [sb(f"prst{i}", [128, 512], BF16) for i in range(2)]
    biasc = sb("biasc", [128, KD])
    vb_sb = sb("vb_sb", [1, D], BF16)
    mq_sb = sb("mq_sb", [128, 64], BF16)
    mk_sb = sb("mk_sb", [128, 64], BF16)
    kmt = [sb(f"kmt{i}", [128, 512], BF16) for i in range(2)]
    escr = [sb(f"escr{i}", [128, 512]) for i in range(2)]
    kv_sb = [sb(f"kv_sb{i}", [64, 128], BF16) for i in range(2)]
    ksum_sb = [sb(f"ksum_sb{i}", [64, 1], BF16) for i in range(2)]
    kvf = [sb(f"kvf{i}", [128, 132]) for i in range(2)]
    kvr = [sb(f"kvr{i}", [64, 132]) for i in range(2)]
    nrow = sb("nrow", [1, 512])
    nrow_bf = sb("nrow_bf", [1, 512], BF16)
    nb_sb = sb("nb_sb", [128, 512], BF16)
    eps128 = sb("eps128", [128, 1])
    ones_rowb = sb("ones_rowb", [1, 128], BF16)
    ones_colb = sb("ones_colb", [128, 1], BF16)
    xres = [sb(f"xres{i}", [128, 512]) for i in range(2)]

    ps = [ctx.enter_context(nc.psum_tensor(pfx + f"ps{i}", [128, 512], F32))
          for i in range(8)]

    lb = LnBufs(nc, ctx, pfx + "ln_",
                ps_st=[ps[0][0:1, 0:512], ps[1][0:1, 0:512],
                       ps[2][0:1, 0:512], ps[3][0:1, 0:512]],
                ps_bc=ps[4][:, 0:512])
    lb.emit_init(p)

    ms1 = p.v(lambda e: e.memset(ones_rowb[:, :], 1.0))
    ms2 = p.v(lambda e: e.memset(ones_colb[:, :], 1.0))
    mse = p.v(lambda e: e.memset(eps128[:, :], 1e-6))
    msz = [p.v(lambda e, i=i: e.memset(kvf[i][:, :], 0.0)) for i in range(2)]
    l_vb = p.dma(vb_sb[:, :], d_vb[:, :])
    l_mq = p.dma(mq_sb[:, :], d_mq[:, :])
    l_mk = p.dma(mk_sb[:, :], d_mk[:, :])

    bank_last = {i: [None] for i in range(8)}

    chunk_toks_all = [[] for _ in range(KD)]
    for nb in range(2):
        for b in range(4):
            lb.last[("st", b)] = bank_last[b]
        chunk_toks, lnall, strd = emit_ln_block(p, lb, xi, 1024 * nb, AR, 0,
                                                pitch=2048, boff=1024 * nb)
        for b in range(4):
            bank_last[b] = [strd[b]]
        for k in range(KD):
            chunk_toks_all[k].append(chunk_toks[k])

    v_store_toks = []
    wv_last = None
    prst_last = [None, None]
    for vc in range(4):
        lws = []
        for k in range(KD):
            lws.append(p.dma(wvres[:, 512 * k:512 * (k + 1)],
                             d_wv[k][:, 512 * vc:512 * (vc + 1)],
                             [wv_last], tag="wvld"))
        last_vmm = None
        for t in range(16):
            bank = 5 + (t % 2)
            pre = p.t(lambda e, vc=vc, bank=bank: e.matmul(
                ps[bank][:, 0:512], ones_rowb[0:1, :],
                vb_sb[0:1, 512 * vc:512 * (vc + 1)], start=True, stop=False),
                [ms1, l_vb] + bank_last[bank], tag="vpre")
            mm = pre
            for k in range(KD):
                mm = p.t(lambda e, k=k, t=t, bank=bank: e.matmul(
                    ps[bank][:, 0:512],
                    xln_ap(k, 128 * t, 128 * (t + 1)),
                    wvres[:, 512 * k:512 * (k + 1)],
                    start=False, stop=(k == KD - 1)),
                    [chunk_toks_all[k], lws[k]], tag="vmm")
            pr = t % 2
            ev = p.s(lambda e, pr=pr, bank=bank: e.activation(
                prst[pr][:, :], ps[bank][:, 0:512], AF.Copy),
                [mm, prst_last[pr]], tag="vev")
            bank_last[bank] = [ev]
            sts = []
            for j in range(4):
                sts.append(p.dmag(dvo[4 * vc + j, t],
                                  prst[pr][:, 128 * j:128 * (j + 1)], [ev],
                                  tag="vst"))
            prst_last[pr] = sts
            v_store_toks += sts
            last_vmm = mm
        wv_last = last_vmm

    wsl_last = [None, None]
    st_by_head_q = {}
    st_by_head_k = {}

    def proj_featmajor(dram_w, dram_out, ntok, bias_ld, stmap, tag):
        nonlocal wsl_last
        for m in range(KD):
            par = m % 2
            lw = p.dma(wsl[par][:, :], dram_w[m], [wsl_last[par]], tag=tag + "w")
            stoks = []
            mm = None
            for qc in range(ntok // 512):
                bank = 5 + (qc % 2)
                for k in range(KD):
                    mm = p.t(lambda e, k=k, qc=qc, par=par, bank=bank: e.matmul(
                        ps[bank][:, 0:512], wsl[par][:, 128 * k:128 * (k + 1)],
                        xln_ap(k, 512 * qc, 512 * (qc + 1)),
                        start=(k == 0), stop=(k == KD - 1)),
                        [chunk_toks_all[k], lw] +
                        (bank_last[bank] if k == 0 else []), tag=tag)
                pr = qc % 2
                ev = p.s(lambda e, m=m, pr=pr, bank=bank: e.activation(
                    prst[pr][:, :], ps[bank][:, 0:512], AF.Identity,
                    bias=biasc[:, m:m + 1]),
                    [mm, bias_ld, prst_last[pr]], tag=tag + "ev")
                st = p.dmag(dram_out[128 * m:128 * (m + 1),
                                     512 * qc:512 * (qc + 1)],
                            prst[pr][:, :], [ev], tag=tag + "st")
                bank_last[bank] = [ev]
                prst_last[pr] = st
                stoks.append(st)
            stmap[m] = stoks
            wsl_last[par] = mm

    lqb = p.dma(biasc[:, :], d_qb[:, :], [], tag="qbld")
    proj_featmajor(d_wq, dq, TOWN, lqb, st_by_head_q, "qp")
    lkb = p.dma(biasc[:, :], d_kb[:, :],
                [_flat(list(st_by_head_q.values()))], tag="kbld")
    proj_featmajor(d_wk, dko, TOWN, lkb, st_by_head_k, "kp")

    all_proj = _flat(list(st_by_head_q.values()) + list(st_by_head_k.values())) \
        + v_store_toks
    ar_last = {"k": [None, None], "v": [None, None], "q": [None, None]}
    first_waits = all_proj
    att_last = {}
    kmt_last = [None, None]
    escr_last = [None, None]
    kv_last = [None, None]
    ksum_last = [None, None]
    kvf_last = [msz[0], msz[1]]
    kvr_last = [None, None]
    pk_last = [None, None]
    nscr_last = None
    nrowbf_last = None
    nb_last = None

    def elu1(dst_ap, src_ps, esl, waits, fd):
        r1 = p.v(lambda e, dst_ap=dst_ap: e.tensor_scalar_max(
            escr[esl][:, 0:fd], src_ps, 0.0), waits + [escr_last[esl]],
            tag="elu_r")
        r2 = p.v(lambda e: e.tensor_tensor(escr[esl][:, 0:fd], src_ps,
                                           escr[esl][:, 0:fd], ALU.min)
                 if False else e.tensor_sub(escr[esl][:, 0:fd], src_ps,
                                            escr[esl][:, 0:fd]),
                 [r1], tag="elu_s")
        r3 = p.s(lambda e: e.activation(escr[esl][:, 0:fd], escr[esl][:, 0:fd],
                                        AF.Exp) if False else None, [])
        return None


    def emit_elu1(dst_ap, src_ps, np_, fd, waits):
        w1 = p.v(lambda e: e.tensor_scalar_max(escr[1][0:np_, 0:fd], src_ps,
                                               0.0),
                 waits + [escr_last[1]], tag="elu_r")
        w2 = p.v(lambda e: e.tensor_sub(escr[0][0:np_, 0:fd], src_ps,
                                        escr[1][0:np_, 0:fd]),
                 [w1, escr_last[0]], tag="elu_m")
        w3 = p.s(lambda e: e.activation(escr[0][0:np_, 0:fd],
                                        escr[0][0:np_, 0:fd],
                                        AF.Exp), [w2], tag="elu_e")
        w4 = p.v(lambda e, dst_ap=dst_ap: e.tensor_tensor(
            dst_ap, escr[0][0:np_, 0:fd], escr[1][0:np_, 0:fd], ALU.add), [w3],
            tag="elu_a")
        escr_last[0] = w4
        escr_last[1] = w4
        return w4

    qm_last = [None, None]

    def pre_head(h):
        hp = h % 2
        lk = p.dma(kth(hp, 0, 2048), dko[128 * h:128 * (h + 1), :],
                   [st_by_head_k[h], ar_last["k"][hp],
                    first_waits if h < 2 else None], tag="kld")
        lvs = []
        for t in range(16):
            lvs.append(p.dma(vth(hp, 128 * t, 128 * (t + 1)), dvo[h, t],
                             [ar_last["v"][hp],
                              first_waits if h < 2 else None], tag="vld"))
        lq = p.dma(qth(hp, 0, 2048), dq[128 * h:128 * (h + 1), :],
                   [st_by_head_q[h], ar_last["q"][hp],
                    first_waits if h < 2 else None], tag="qld")
        qreaders = []
        qm_toks = []
        for qc in range(4):
            bank = qc % 2
            mmq = p.t(lambda e, qc=qc, hp=hp, bank=bank: e.matmul(
                ps[bank][0:64, 0:512], mq_sb[:, :],
                qth(hp, 512 * qc, 512 * (qc + 1)), start=True, stop=True),
                [lq, l_mq] + bank_last[bank], tag="qmap")
            qreaders.append(mmq)
            e1_ = emit_elu1(qmh(hp, 512 * qc, 512 * (qc + 1)),
                            ps[bank][0:64, 0:512], 64, 512,
                            [mmq, qm_last[hp]])
            bank_last[bank] = [e1_]
            qm_toks.append(e1_)
        kvmm = ksmm = None
        ekg = [None, None]
        for g in range(2):
            kb = 2 + g
            mms = []
            for j in range(8):
                t = 8 * g + j
                mms.append(p.t(lambda e, t=t, j=j, hp=hp, kb=kb: e.matmul(
                    ps[kb][:, 64 * j:64 * (j + 1)],
                    kth(hp, 128 * t, 128 * (t + 1)),
                    mk_sb[:, :], start=True, stop=True),
                    [lk, l_mk] + (bank_last[kb] if j == 0 else []),
                    tag="kmap"))
            ekg[g] = emit_elu1(kmt[g][:, 0:512], ps[kb][:, 0:512], 128, 512,
                               [mms, kmt_last[g]])
            bank_last[kb] = [ekg[g]]
        for g in range(2):
            for j in range(8):
                t = 8 * g + j
                kvmm = p.t(lambda e, t=t, g=g, j=j, hp=hp, first=(t == 0),
                           last=(t == 15): e.matmul(
                    ps[4][0:64, 0:128], kmt[g][:, 64 * j:64 * (j + 1)],
                    vth(hp, 128 * t, 128 * (t + 1)), start=first, stop=last),
                    [ekg[g], lvs[t]] + (bank_last[4] if t == 0 else []),
                    tag="kvmm")
                ksmm = p.t(lambda e, g=g, j=j, first=(t == 0),
                           last=(t == 15): e.matmul(
                    ps[5][0:64, 0:1], kmt[g][:, 64 * j:64 * (j + 1)],
                    ones_colb[:, :], start=first, stop=last),
                    [ekg[g], ms2] + (bank_last[5] if t == 0 else []),
                    tag="ksmm")
            kmt_last[g] = [kvmm, ksmm]
        kvc0 = p.v(lambda e, hp=hp: e.tensor_copy(kvf[hp][0:64, 0:128],
                                                  ps[4][0:64, 0:128]),
                   [kvmm, kvf_last[hp]], tag="kvpk")
        ksc0 = p.v(lambda e, hp=hp: e.tensor_copy(kvf[hp][0:64, 128:129],
                                                  ps[5][0:64, 0:1]),
                   [ksmm, kvc0], tag="kspk")
        bank_last[4] = [kvc0]
        bank_last[5] = [ksc0]
        wrp = p.dma(pki_h[hp].ap()[:, :], kvf[hp][:, 0:129],
                    [kvc0, ksc0, pk_last[hp]], tag="kvwr")
        kvf_last[hp] = wrp
        cl = p.coll("AllReduce", ALU.add, PAIRS,
                    [pki_h[hp].ap().opt()], [pko_h[hp].ap().opt()],
                    waits=[wrp], tag="kvar")
        ar_last["k"][hp] = [ksmm, kvmm]
        ar_last["v"][hp] = kvmm
        ar_last["q"][hp] = qreaders[-1]
        return {"hp": hp, "qm_toks": qm_toks, "cl": cl,
                "kvmm": kvmm, "ksmm": ksmm}

    def post_head(h, st):
        nonlocal nscr_last, nrowbf_last, nb_last
        hp = st["hp"]
        qm_toks = st["qm_toks"]
        lkv = p.dma(kvr[hp][0:64, 0:129], pko_h[hp].ap()[0:64, 0:129],
                    [st["cl"], kvr_last[hp]], tag="kvrd")
        pk_last[hp] = lkv
        kvc = p.v(lambda e, hp=hp: e.tensor_copy(kv_sb[hp][:, :],
                                                 kvr[hp][0:64, 0:128]),
                  [lkv, kv_last[hp]], tag="kvc")
        ksc = p.v(lambda e, hp=hp: e.tensor_copy(ksum_sb[hp][:, :],
                                                 kvr[hp][0:64, 128:129]),
                  [lkv, ksum_last[hp]], tag="ksc")
        kvr_last[hp] = [kvc, ksc]
        omms = []
        nmms = []
        for qc in range(4):
            nmm = p.t(lambda e, qc=qc, hp=hp: e.matmul(
                ps[6][0:1, 0:512], ksum_sb[hp][:, :],
                qmh(hp, 512 * qc, 512 * (qc + 1)),
                start=True, stop=True),
                [ksc, qm_toks[qc]] + bank_last[6], tag="nmm")
            bank = qc % 2
            omm = p.t(lambda e, qc=qc, hp=hp, bank=bank: e.matmul(
                ps[bank][:, 0:512], kv_sb[hp][:, :],
                qmh(hp, 512 * qc, 512 * (qc + 1)),
                start=True, stop=True),
                [qm_toks[qc], kvc] + bank_last[bank], tag="outmm")
            nr = p.v(lambda e: e.tensor_scalar_add(nrow[0:1, :],
                                                   ps[6][0:1, 0:512], 1e-6),
                     [nmm, nscr_last], tag="nadd")
            bank_last[6] = [nr]
            nr2 = p.v(lambda e: e.reciprocal(nrow[0:1, :], nrow[0:1, :]),
                      [nr], tag="nrcp")
            nr3 = p.v(lambda e: e.tensor_copy(nrow_bf[0:1, :], nrow[0:1, :]),
                      [nr2, nrowbf_last], tag="nbf")
            nscr_last = nr3
            nbm = p.t(lambda e: e.matmul(ps[7][:, 0:512],
                                         ones_rowb[0:1, :],
                                         nrow_bf[0:1, :], start=True,
                                         stop=True),
                      [nr3, ms1] + bank_last[7], tag="nbc")
            nrowbf_last = nbm
            nrc = p.v(lambda e: e.tensor_copy(nb_sb[:, :], ps[7][:, 0:512]),
                      [nbm, nb_last], tag="nbcc")
            bank_last[7] = [nrc]
            omms.append(omm)
            nmms.append(nmm)
            oev = p.v(lambda e, h=h, qc=qc, bank=bank: e.tensor_tensor(
                att_ap(h, qc), ps[bank][:, 0:512], nb_sb[:, :], ALU.mult),
                [omm, nrc, att_last.get((h, qc))], tag="oev")
            nb_last = oev
            bank_last[bank] = [oev]
            att_last[(h, qc)] = oev
        qm_last[hp] = omms + nmms
        kv_last[hp] = omms
        ksum_last[hp] = nmms

    prev_st = None
    for h in range(16):
        st = pre_head(h)
        if prev_st is not None:
            post_head(h - 1, prev_st)
        prev_st = st
    post_head(15, prev_st)

    lob = p.dma(biasc[:, :], d_ob[:, :],
                [_flat(list(st_by_head_k.values()))], tag="obld")
    xres_last = [None, None]
    for m in range(KD):
        par = m % 2
        lw = p.dma(wsl[par][:, :], d_wo[m], [wsl_last[par]], tag="wow")
        mm = None
        for qc in range(4):
            bank = 5 + (qc % 2)
            for k in range(KD):
                mm = p.t(lambda e, k=k, qc=qc, par=par, bank=bank: e.matmul(
                    ps[bank][:, 0:512], wsl[par][:, 128 * k:128 * (k + 1)],
                    att_ap(k, qc), start=(k == 0), stop=(k == KD - 1)),
                    [att_last[(k, qc)], lw] +
                    (bank_last[bank] if k == 0 else []), tag="omm2")
            pr = qc % 2
            lx = p.dma(xres[pr][:, :], xi[128 * m:128 * (m + 1),
                                          512 * qc:512 * (qc + 1)],
                       [xres_last[pr]], tag="oxld")
            evb = p.s(lambda e, m=m, pr=pr: e.activation(
                xres[pr][:, :], xres[pr][:, :], AF.Identity,
                bias=biasc[:, m:m + 1]), [lx, lob], tag="oevb")
            yv = p.v(lambda e, pr=pr, bank=bank: e.tensor_tensor(
                xres[pr][:, :], xres[pr][:, :], ps[bank][:, 0:512], ALU.add),
                [evb, mm], tag="oadd")
            st = p.dmag(xo[128 * m:128 * (m + 1), 512 * qc:512 * (qc + 1)],
                        xres[pr][:, :], [yv], tag="oxst")
            bank_last[bank] = [yv]
            xres_last[pr] = st
        wsl_last[par] = mm
    return


def p4_inputs(inp):
    return {
        "wq": tile_lhsT(np.asarray(inp["L1_attn_a_wq"], np.float32).T),
        "wk": tile_lhsT(np.asarray(inp["L1_attn_a_wk"], np.float32).T),
        "wv": to_bf(np.ascontiguousarray(
            np.asarray(inp["L1_attn_a_wv"], np.float32).T).reshape(KD, 128, D)),
        "wo": tile_lhsT(np.asarray(inp["L1_attn_a_wo"], np.float32).T),
        "qb": np.ascontiguousarray(
            np.asarray(inp["L1_attn_a_bq"], np.float32).reshape(KD, 128).T),
        "kb": np.ascontiguousarray(
            np.asarray(inp["L1_attn_a_bk"], np.float32).reshape(KD, 128).T),
        "vb": to_bf(np.asarray(inp["L1_attn_a_bv"], np.float32).reshape(1, D)),
        "ob": np.ascontiguousarray(
            np.asarray(inp["L1_attn_a_bo"], np.float32).reshape(KD, 128).T),
        "wmapq": to_bf(np.asarray(inp["L1_attn_a_wmap_q"], np.float32).T),
        "wmapk": to_bf(np.asarray(inp["L1_attn_a_wmap_k"], np.float32).T),
    }



def emit_p5(nc, ctx, p, dr, pfx="", xl_emit=None, thr_h=None):
    xi = dr["xi"]
    d_xl = dr.get("xlast")
    thki_h, thko_h = thr_h
    d_ing, d_inb = dr["ing"], dr["inb"]
    d_png, d_pnb = dr["png"], dr["pnb"]
    d_outg, d_outb = dr["outg"], dr["outb"]
    d_agg, d_aggb = dr["agg"], dr["aggb"]
    d_d1, d_d1b = dr["d1w"], dr["d1b"]
    d_d2, d_d2b = dr["d2w"], dr["d2b"]
    d_bc, d_bcb = dr["bcw"], dr["bcb"]
    d_gw, d_gb = dr["gw"], dr["gb"]
    xo = dr["xo"]

    def sb(name, shape, dtype=F32):
        return ctx.enter_context(nc.sbuf_tensor(pfx + name, shape, dtype))

    AR = sb("AR", [128, 65536], BF16)

    def xln_ap(k, lo, hi):
        return AR[:, 2048 * k + lo: 2048 * k + hi]

    def o_ap(k, lo, hi):
        return AR[:, 32768 + 2048 * k + lo: 32768 + 2048 * k + hi]

    wsl = [sb(f"wsl{i}", [128, D], BF16) for i in range(2)]
    wsl2 = [sb(f"wsl2_{i}", [128, 4096], BF16) for i in range(2)]
    wsl2_last = [None, None]
    thp = sb("thp", [128, KD])
    thr = sb("thr", [128, KD])
    vg = {}
    for nm, dd in [("ing", d_ing), ("inb", d_inb), ("png", d_png),
                   ("pnb", d_pnb), ("outg", d_outg), ("outb", d_outb),
                   ("aggb", d_aggb), ("d2b", d_d2b), ("bcb", d_bcb),
                   ("gb", d_gb)]:
        vg[nm] = sb("t_" + nm, [128, KD])
    d1b_sb = sb("d1b_sb", [128, 32])
    xl_fm = sb("xl_fm", [128, KD])
    xn_fm = sb("xn_fm", [128, KD])
    th_fm = sb("th_fm", [128, KD])
    tn_fm = sb("tn_fm", [128, KD])
    tn_bf = sb("tn_bf", [128, KD], BF16)
    g_bf = sb("g_bf", [128, 64], BF16)
    sq_fm = sb("sq_fm", [128, KD])
    vrow = sb("vrow", [1, 64])
    vcol = sb("vcol", [128, 4])
    bc_fm = sb("bc_fm", [128, KD])
    gsb = [sb(f"gsb{i}", [128, 512], BF16) for i in range(2)]
    ones_col = sb("ones_col", [128, 1])
    ones_rowf = sb("ones_rowf", [1, 128])
    cns2 = sb("cns2", [1, 1])
    xres = [sb(f"xres{i}", [128, 512]) for i in range(2)]
    oscr = sb("oscr", [128, 512])

    ps = [ctx.enter_context(nc.psum_tensor(pfx + f"ps{i}", [128, 512], F32))
          for i in range(8)]

    lb = LnBufs(nc, ctx, pfx + "ln_",
                ps_st=[ps[0][0:1, 0:512], ps[1][0:1, 0:512],
                       ps[2][0:1, 0:512], ps[3][0:1, 0:512]],
                ps_bc=ps[4][:, 0:512])
    lb.emit_init(p)
    msc = p.v(lambda e: e.memset(cns2[:, :], 1e-6))
    ms1 = p.v(lambda e: e.memset(ones_col[:, :], 1.0))
    ms2 = p.v(lambda e: e.memset(ones_rowf[:, :], 1.0))
    ldt = {}
    for nm, dd in [("ing", d_ing), ("inb", d_inb), ("png", d_png),
                   ("pnb", d_pnb), ("outg", d_outg), ("outb", d_outb),
                   ("aggb", d_aggb), ("d2b", d_d2b), ("bcb", d_bcb),
                   ("gb", d_gb)]:
        ldt[nm] = p.dma(vg[nm][:, :], dd[:, :], tag="vgld")
    ld1b = p.dma(d1b_sb[:, :], d_d1b[:, :])
    if xl_emit is not None:
        lxl = xl_emit(xl_fm)
    else:
        lxl = p.dma(xl_fm[:, :], d_xl[:, :])

    bank_last = {i: [None] for i in range(8)}
    vrow_last = [None]
    vcol_last = [None]
    sq_last = [None]

    def vec_ln(src, dst, eps_ap, gname=None, bname=None, waits=()):
        sq = p.s(lambda e: e.activation(sq_fm[:, :], src[:, :], AF.Square),
                 [waits, sq_last[0]], tag="vln_sq")
        m1 = p.t(lambda e: e.matmul(ps[6][0:1, 0:KD], ones_col[:, :],
                                    src[:, :], start=True, stop=True),
                 [waits, ms1] + bank_last[6], tag="vln_s1")
        m2 = p.t(lambda e: e.matmul(ps[7][0:1, 0:KD], ones_col[:, :],
                                    sq_fm[:, :], start=True, stop=True),
                 [sq, ms1] + bank_last[7], tag="vln_s2")
        r1 = p.v(lambda e: e.reduce_sum(vrow[0:1, 0:1], ps[6][0:1, 0:KD],
                                        axis=mybir.AxisListType.X),
                 [m1, vrow_last[0]], tag="vln_r1")
        r2 = p.v(lambda e: e.reduce_sum(vrow[0:1, 1:2], ps[7][0:1, 0:KD],
                                        axis=mybir.AxisListType.X), [m2, r1],
                 tag="vln_r2")
        bank_last[6] = [r1]
        bank_last[7] = [r2]
        r3 = p.v(lambda e: e.tensor_scalar_mul(vrow[0:1, 0:1], vrow[0:1, 0:1],
                                               1.0 / D), [r1], tag="vln_m")
        r4 = p.v(lambda e: e.tensor_mul(vrow[0:1, 2:3], vrow[0:1, 0:1],
                                        vrow[0:1, 0:1]), [r3])
        r5 = p.v(lambda e: e.scalar_tensor_tensor(vrow[0:1, 1:2],
                                                  vrow[0:1, 1:2], 1.0 / D,
                                                  vrow[0:1, 2:3], ALU.mult,
                                                  ALU.subtract), [r4, r2])
        r6 = p.s(lambda e: e.activation(vrow[0:1, 1:2], vrow[0:1, 1:2], AF.Ln,
                                        bias=lb.cns[0:1, 0:1] if eps_ap is None
                                        else eps_ap), [r5, lb.init, msc],
                 tag="vln_l")
        r7 = p.s(lambda e: e.activation(vrow[0:1, 1:2], vrow[0:1, 1:2], AF.Exp,
                                        scale=-0.5), [r6])
        vrow_last[0] = r7
        bm = p.t(lambda e: e.matmul(ps[6][:, 0:2], ones_rowf[0:1, :],
                                    vrow[0:1, 0:2], start=True, stop=True),
                 [r7, r3, ms2] + bank_last[6], tag="vln_bc")
        bc_ = p.v(lambda e: e.tensor_copy(vcol[:, 0:2], ps[6][:, 0:2]),
                  [bm, vcol_last[0]], tag="vln_bcc")
        bank_last[6] = [bc_]
        ap_ = p.v(lambda e, dst=dst: e.tensor_scalar(
            dst[:, :], src[:, :], vcol[:, 0:1], vcol[:, 1:2],
            ALU.subtract, ALU.mult), [bc_, waits], tag="vln_ap")
        vcol_last[0] = ap_
        sq_last[0] = ap_
        out_tok = ap_
        if gname is not None:
            g1 = p.v(lambda e, dst=dst, gname=gname: e.tensor_tensor(
                dst[:, :], dst[:, :], vg[gname][:, :], ALU.mult),
                [ap_, ldt[gname]], tag="vln_g")
            g2 = p.v(lambda e, dst=dst, bname=bname: e.tensor_tensor(
                dst[:, :], dst[:, :], vg[bname][:, :], ALU.add),
                [g1, ldt[bname]], tag="vln_b")
            out_tok = g2
        return out_tok

    chunk_toks_all = [[] for _ in range(KD)]
    for nb in range(2):
        for b in range(4):
            lb.last[("st", b)] = bank_last[b]
        chunk_toks, lnall, strd = emit_ln_block(p, lb, xi, 1024 * nb, AR, 0,
                                                pitch=2048, boff=1024 * nb)
        for b in range(4):
            bank_last[b] = [strd[b]]
        for k in range(KD):
            chunk_toks_all[k].append(chunk_toks[k])

    gwsl = [sb(f"gwsl{i}", [128, D], BF16) for i in range(2)]
    gwsl_last = [None, None]
    gw_ld = {}
    go_toks = {}
    gstate = {"i": 0}

    def emit_gate_groups(n):
        for _ in range(n):
            i = gstate["i"]
            if i >= 64:
                return
            gstate["i"] = i + 1
            m, qc = i // 4, i % 4
            par = m % 2
            if qc == 0:
                gw_ld[m] = p.dma(gwsl[par][:, :], d_gw[m], [gwsl_last[par]],
                                 tag="gww")
            lw = gw_ld[m]
            bank = 2 + (qc % 2)
            mm = None
            for k in range(KD):
                mm = p.t(lambda e, k=k, qc=qc, par=par, bank=bank: e.matmul(
                    ps[bank][:, 0:512], gwsl[par][:, 128 * k:128 * (k + 1)],
                    xln_ap(k, 512 * qc, 512 * (qc + 1)),
                    start=(k == 0), stop=(k == KD - 1)),
                    [chunk_toks_all[k], lw] +
                    (bank_last[bank] if k == 0 else []), tag="gmm")
            sg = p.s(lambda e, m=m, qc=qc, bank=bank: e.activation(
                o_ap(m, 512 * qc, 512 * (qc + 1)), ps[bank][:, 0:512],
                AF.Sigmoid, bias=vg["gb"][:, m:m + 1]),
                [mm, ldt["gb"], go_toks.get((m, qc))], tag="gsig")
            bank_last[bank] = [sg]
            go_toks[(m, qc)] = sg
            if qc == 3:
                gwsl_last[par] = mm

    emit_gate_groups(4)
    ln1 = vec_ln(xl_fm, xn_fm, None, waits=[lxl])
    emit_gate_groups(3)
    ln2 = vec_ln(xn_fm, xn_fm, None, gname="ing", bname="inb", waits=[ln1])
    emit_gate_groups(3)
    xn_bf_tok = p.v(lambda e: e.tensor_copy(tn_bf[:, :], xn_fm[:, :]), [ln2],
                    tag="xnbf")

    wsl_last = [None, None]
    th_tok = None

    def vec_matmul(dst_fm, bias_fm, dram_w, nm_, src_bf, src_tok):
        toks = []
        for m in range(nm_):
            if m % 4 == 3:
                emit_gate_groups(1)
            par = m % 2
            lw = p.dma(wsl[par][:, :], dram_w[m], [wsl_last[par]], tag="vmw")
            bank = m % 2
            mm = None
            for k in range(KD):
                mm = p.t(lambda e, k=k, par=par, bank=bank: e.matmul(
                    ps[bank][:, 0:1], wsl[par][:, 128 * k:128 * (k + 1)],
                    src_bf[:, k:k + 1], start=(k == 0), stop=(k == KD - 1)),
                    [src_tok, lw] + (bank_last[bank] if k == 0 else []),
                    tag="vmm")
            wsl_last[par] = mm
            ev = p.s(lambda e, m=m, bank=bank, dst_fm=dst_fm, bias_fm=bias_fm:
                     e.activation(dst_fm[:, m:m + 1], ps[bank][:, 0:1],
                                  AF.Identity, bias=bias_fm[:, m:m + 1]),
                     [mm, toks], tag="vmev")
            bank_last[bank] = [ev]
            toks.append(ev)
        return toks

    agg_t = vec_matmul(th_fm, vg["aggb"], d_agg, KD, tn_bf, [xn_bf_tok])
    th_tok = agg_t
    cl_prev = None
    lkr_prev = None
    thp_last = None
    thr_last = None
    for step in range(4):
        lnp = vec_ln(th_fm, tn_fm, None, gname="png", bname="pnb",
                     waits=[th_tok])
        emit_gate_groups(2)
        tbf = p.v(lambda e: e.tensor_copy(tn_bf[:, :], tn_fm[:, :]), [lnp],
                  tag="tbf")
        g_toks = []
        for m in range(32):
            if m % 8 == 7:
                emit_gate_groups(1)
            par = m % 2
            lw = p.dma(wsl[par][:, :], d_d1[m], [wsl_last[par]], tag="d1w")
            bank = m % 2
            mm = None
            for k in range(KD):
                mm = p.t(lambda e, k=k, par=par, bank=bank: e.matmul(
                    ps[bank][:, 0:1], wsl[par][:, 128 * k:128 * (k + 1)],
                    tn_bf[:, k:k + 1], start=(k == 0), stop=(k == KD - 1)),
                    [tbf, lw] + (bank_last[bank] if k == 0 else []), tag="d1m")
            wsl_last[par] = mm
            ev = p.s(lambda e, m=m, bank=bank: e.activation(
                g_bf[:, m:m + 1], ps[bank][:, 0:1], AF.Gelu,
                bias=d1b_sb[:, m:m + 1]), [mm, ld1b, g_toks], tag="d1ev")
            bank_last[bank] = [ev]
            g_toks.append(ev)
        d2_toks = []
        for m2 in range(KD):
            if m2 % 4 == 3:
                emit_gate_groups(1)
            par = m2 % 2
            lw = p.dma(wsl2[par][:, :], d_d2[m2], [wsl2_last[par]], tag="d2w")
            bank = m2 % 2
            mm = None
            for k2 in range(32):
                mm = p.t(lambda e, k2=k2, par=par, bank=bank: e.matmul(
                    ps[bank][:, 0:1], wsl2[par][:, 128 * k2:128 * (k2 + 1)],
                    g_bf[:, k2:k2 + 1], start=(k2 == 0), stop=(k2 == 31)),
                    [g_toks, lw] + (bank_last[bank] if k2 == 0 else []),
                    tag="d2m")
            wsl2_last[par] = mm
            ev = p.s(lambda e, m2=m2, bank=bank: e.activation(
                thp[:, m2:m2 + 1], ps[bank][:, 0:1], AF.Copy),
                [mm, d2_toks, thp_last], tag="d2ev")
            bank_last[bank] = [ev]
            d2_toks.append(ev)
        wr = p.dma(thki_h.ap()[:, :], thp[:, :], [d2_toks, cl_prev],
                   tag="thwr")
        thp_last = wr
        cl = p.coll("AllReduce", ALU.add, PAIRS,
                    [thki_h.ap().opt()], [thko_h.ap().opt()],
                    waits=[wr, lkr_prev], tag="thar")
        cl_prev = cl
        lkr = p.dma(thr[:, :], thko_h.ap()[:, :], [cl, thr_last], tag="thrd")
        lkr_prev = lkr
        a1 = p.v(lambda e: e.tensor_tensor(thr[:, :], thr[:, :],
                                           vg["d2b"][:, :], ALU.add),
                 [lkr, ldt["d2b"]], tag="thb")
        a2 = p.v(lambda e: e.tensor_tensor(th_fm[:, :], th_fm[:, :],
                                           thr[:, :], ALU.add),
                 [a1, lnp], tag="thacc")
        thr_last = a2
        th_tok = a2
    thbf = p.v(lambda e: e.tensor_copy(tn_bf[:, :], th_fm[:, :]), [th_tok],
               tag="thbf")
    bc_t = vec_matmul(bc_fm, vg["bcb"], d_bc, KD, tn_bf, [thbf])

    emit_gate_groups(64)
    xres_last = [None, None]
    o_toks = {}
    for m in range(KD):
        for qc in range(4):
            ot = p.v(lambda e, m=m, qc=qc: e.scalar_tensor_tensor(
                o_ap(m, 512 * qc, 512 * (qc + 1)),
                o_ap(m, 512 * qc, 512 * (qc + 1)),
                bc_fm[:, m:m + 1], xln_ap(m, 512 * qc, 512 * (qc + 1)),
                ALU.mult, ALU.add), [go_toks[(m, qc)], bc_t,
                                     chunk_toks_all[m]], tag="gate_o")
            o_toks[(m, qc)] = ot

    for nb in range(2):
        stat_mms = []
        sq_tl = {}
        for k in range(KD):
            otk = [o_toks[(k, 2 * nb)], o_toks[(k, 2 * nb + 1)]]
            for sub in range(2):
                sqo = p.s(lambda e, k=k, nb=nb, sub=sub: e.activation(
                    lb.sqbf[sub][:, 0:512],
                    o_ap(k, 1024 * nb + 512 * sub,
                         1024 * nb + 512 * (sub + 1)), AF.Square),
                    [otk, sq_tl.get(sub)], tag="osq")
                m1 = p.t(lambda e, k=k, nb=nb, sub=sub: e.matmul(
                    ps[0 + sub][0:1, 0:512], lb.ones_colb[:, :],
                    o_ap(k, 1024 * nb + 512 * sub,
                         1024 * nb + 512 * (sub + 1)),
                    start=(k == 0), stop=(k == KD - 1)),
                    [otk, lb.init] + (bank_last[0 + sub] if k == 0 else []),
                    tag="ost1")
                m2 = p.t(lambda e, sub=sub, k=k: e.matmul(
                    ps[2 + sub][0:1, 0:512], lb.ones_colb[:, :],
                    lb.sqbf[sub][:, 0:512],
                    start=(k == 0), stop=(k == KD - 1)),
                    [sqo, lb.init] + (bank_last[2 + sub] if k == 0 else []),
                    tag="ost2")
                sq_tl[sub] = m2
                stat_mms += [m1, m2]
        bct = []
        R = lb.srow
        for sub in range(2):
            m_ = p.v(lambda e, sub=sub: e.tensor_scalar_mul(
                R[0:1, 0:512], ps[0 + sub][0:1, 0:512], 1.0 / D),
                [stat_mms, lb.last.get("rows")], tag="ornm")
            ms_ = p.v(lambda e: e.tensor_mul(R[0:1, 512:1024], R[0:1, 0:512],
                                             R[0:1, 0:512]), [m_])
            var_ = p.v(lambda e, sub=sub: e.scalar_tensor_tensor(
                R[0:1, 512:1024], ps[2 + sub][0:1, 0:512], 1.0 / D,
                R[0:1, 512:1024], ALU.mult, ALU.subtract), [ms_, stat_mms])
            lnv = p.s(lambda e: e.activation(R[0:1, 512:1024],
                                             R[0:1, 512:1024], AF.Ln,
                                             bias=cns2[0:1, 0:1]),
                      [var_, msc], tag="olnv")
            rstd = p.s(lambda e: e.activation(R[0:1, 512:1024],
                                              R[0:1, 512:1024], AF.Exp,
                                              scale=-0.5), [lnv])
            c1c = p.v(lambda e: e.tensor_copy(lb.srow_bf[0:1, 0:512],
                                              R[0:1, 512:1024]),
                      [rstd, lb.last.get("srowbf")], tag="oc1")
            c0c = p.v(lambda e: e.scalar_tensor_tensor(
                lb.srow_bf[0:1, 512:1024], R[0:1, 0:512], -1.0,
                R[0:1, 512:1024], ALU.mult, ALU.mult),
                [rstd, m_, lb.last.get("srowbf")], tag="oc0")
            bm1 = p.t(lambda e: e.matmul(ps[4][:, 0:512], lb.ones_rowb[0:1, :],
                                         lb.srow_bf[0:1, 0:512], start=True,
                                         stop=True),
                      [c1c, lb.init, lb.last.get("psbc")], tag="obc1")
            bc1 = p.v(lambda e, sub=sub: e.tensor_copy(lb.cb1[sub][:, :],
                                                       ps[4][:, 0:512]),
                      [bm1, lb.last.get(("cb1", sub))], tag="obc1c")
            bm2 = p.t(lambda e: e.matmul(ps[4][:, 0:512], lb.ones_rowb[0:1, :],
                                         lb.srow_bf[0:1, 512:1024], start=True,
                                         stop=True), [c0c, bc1], tag="obc2")
            bc0 = p.v(lambda e, sub=sub: e.tensor_copy(lb.cb0[sub][:, :],
                                                       ps[4][:, 0:512]),
                      [bm2, lb.last.get(("cb0", sub))], tag="obc0c")
            lb.last["psbc"] = bc0
            lb.last["srowbf"] = [bm1, bm2]
            lb.last["rows"] = [var_, c0c, rstd, c1c]
            bank_last[0 + sub] = [m_]
            bank_last[2 + sub] = [var_]
            bct.append((bc1, bc0))
        apl = []
        for k in range(KD):
            for sub in range(2):
                bc1, bc0 = bct[sub]
                qc = 2 * nb + sub
                pr = sub
                otk = [o_toks[(k, 2 * nb)], o_toks[(k, 2 * nb + 1)]]
                lx = p.dma(xres[pr][:, :], xi[128 * k:128 * (k + 1),
                                              512 * qc:512 * (qc + 1)],
                           [xres_last[pr]], tag="oxld")
                t1 = p.v(lambda e, k=k, nb=nb, sub=sub: e.tensor_tensor(
                    oscr[:, 0:512],
                    o_ap(k, 1024 * nb + 512 * sub,
                         1024 * nb + 512 * (sub + 1)),
                    lb.cb1[sub][:, :], ALU.mult), [bc1, otk], tag="oap1")
                t2 = p.v(lambda e, sub=sub: e.tensor_tensor(
                    oscr[:, 0:512], oscr[:, 0:512], lb.cb0[sub][:, :],
                    ALU.add), [t1, bc0], tag="oap2")
                t3 = p.v(lambda e, k=k: e.tensor_scalar(
                    oscr[:, 0:512], oscr[:, 0:512], vg["outg"][:, k:k + 1],
                    vg["outb"][:, k:k + 1], ALU.mult, ALU.add),
                    [t2, ldt["outg"], ldt["outb"]], tag="oap3")
                t4 = p.v(lambda e, pr=pr: e.tensor_tensor(
                    xres[pr][:, :], xres[pr][:, :], oscr[:, 0:512], ALU.add),
                    [t3, lx], tag="oap4")
                st = p.dmag(xo[128 * k:128 * (k + 1),
                               512 * qc:512 * (qc + 1)], xres[pr][:, :],
                            [t4], tag="oxst")
                xres_last[pr] = st
                apl.append(st)
        for sub in range(2):
            lb.last[("cb1", sub)] = apl
            lb.last[("cb0", sub)] = apl
    return


def p5f_inputs(inp, h):
    def fm(name, n=KD):
        return np.ascontiguousarray(
            np.asarray(inp[name], np.float32).reshape(n, 128).T)
    return {
        "ing": fm("L1_ffn_in_gamma"), "inb": fm("L1_ffn_in_beta"),
        "png": fm("L1_ffn_pn_gamma"), "pnb": fm("L1_ffn_pn_beta"),
        "outg": fm("L1_ffn_out_gamma"), "outb": fm("L1_ffn_out_beta"),
        "agg": tile_lhsT(np.asarray(inp["L1_ffn_agg_w"], np.float32)),
        "aggb": fm("L1_ffn_agg_b"),
        "d1w": np.ascontiguousarray(tile_lhsT(
            np.asarray(inp["L1_ffn_d1_w"], np.float32))[32 * h:32 * (h + 1)]),
        "d1b": np.ascontiguousarray(
            fm("L1_ffn_d1_b", 64)[:, 32 * h:32 * (h + 1)]),
        "d2w": np.ascontiguousarray(tile_lhsT(
            np.asarray(inp["L1_ffn_d2_w"],
                       np.float32))[:, :, 4096 * h:4096 * (h + 1)]),
        "d2b": fm("L1_ffn_d2_b"),
        "bcw": tile_lhsT(np.asarray(inp["L1_ffn_bc_w"], np.float32)),
        "bcb": fm("L1_ffn_bc_b"),
        "gw": tile_lhsT(np.asarray(inp["L1_ffn_gate_w"], np.float32)),
        "gb": fm("L1_ffn_gate_b"),
    }



def emit_p6f(nc, ctx, p, dr, pfx=""):
    SCALE = 1.0 / float(np.sqrt(128.0))

    xi = dr["xi"]
    d_wq, d_wk, d_wv, d_wo = dr["wq"], dr["wk"], dr["wv"], dr["wo"]
    d_cq, d_sq = dr["cosq"], dr["sinq"]
    xo = dr["xo"]
    dq = dr["dq"]
    dko_h, dkg_h = dr["dko_h"], dr["dkg_h"]
    dvo_h, dvg_h = dr["dvo_h"], dr["dvg_h"]
    dko, dkg = dko_h.ap(), dkg_h.ap()
    dvo, dvg = dvo_h.ap(), dvg_h.ap()

    def sb(name, shape, dtype=F32):
        return ctx.enter_context(nc.sbuf_tensor(pfx + name, shape, dtype))

    AR = sb("AR", [128, 65536], BF16)

    def kth(hp, lo, hi):
        return AR[:, 32768 + hp * 4096 + lo: 32768 + hp * 4096 + hi]

    def vth(hp, lo, hi):
        return AR[:, 40960 + hp * 4096 + lo: 40960 + hp * 4096 + hi]

    def qth(hp, lo, hi):
        return AR[:, 49152 + hp * 2048 + lo: 49152 + hp * 2048 + hi]

    def pt_ap(pi, lo, hi):
        base = 53248 + pi * 512 if pi < 3 else 63616
        return AR[:, base + lo: base + hi]

    dinv_b = AR[:, 54784: 55296]

    def att_ap(h, qc):
        return AR[:, 2048 * h + 512 * qc: 2048 * h + 512 * (qc + 1)]

    def xln_ap(k, lo, hi):
        return AR[:, 2048 * k + lo: 2048 * k + hi]

    wsl = [sb(f"wsl{i}", [128, D], BF16) for i in range(2)]
    wvres = sb("wvres", [128, KD * 512], BF16)
    prst = [sb(f"prst{i}", [128, 512], BF16) for i in range(2)]
    ctb = [sb(f"ctb{i}", [128, 512], BF16) for i in range(2)]
    stb = [sb(f"stb{i}", [128, 512], BF16) for i in range(2)]
    rsc = [sb(f"rsc{i}", [128, 512]) for i in range(2)]
    drow = sb("drow", [1, 512])
    drow_bf = sb("drow_bf", [1, 512], BF16)
    ones_rowb = sb("ones_rowb", [1, 128], BF16)
    ones_colb = sb("ones_colb", [128, 1], BF16)
    xres = [sb(f"xres{i}", [128, 512]) for i in range(2)]

    ps = [ctx.enter_context(nc.psum_tensor(pfx + f"ps{i}", [128, 512], F32))
          for i in range(8)]

    lb = LnBufs(nc, ctx, pfx + "ln_",
                ps_st=[ps[0][0:1, 0:512], ps[1][0:1, 0:512],
                       ps[2][0:1, 0:512], ps[3][0:1, 0:512]],
                ps_bc=ps[4][:, 0:512])
    lb.emit_init(p)

    ms1 = p.v(lambda e: e.memset(ones_rowb[:, :], 1.0))
    ms2 = p.v(lambda e: e.memset(ones_colb[:, :], 1.0))

    bank_last = {i: [None] for i in range(8)}

    chunk_toks_all = [[] for _ in range(KD)]
    for nb in range(2):
        for b in range(4):
            lb.last[("st", b)] = bank_last[b]
        chunk_toks, lnall, strd = emit_ln_block(p, lb, xi, 1024 * nb, AR, 0,
                                                pitch=2048, boff=1024 * nb)
        for b in range(4):
            bank_last[b] = [strd[b]]
        for k in range(KD):
            chunk_toks_all[k].append(chunk_toks[k])

    v_store_toks = []
    lws = []
    for k in range(KD):
        lws.append(p.dma(wvres[:, 512 * k:512 * (k + 1)], d_wv[k],
                         tag="wvld"))
    prst_last = [None, None]
    wv_last = None
    for t in range(16):
        bank = 5 + (t % 2)
        mm = None
        for k in range(KD):
            mm = p.t(lambda e, k=k, t=t, bank=bank: e.matmul(
                ps[bank][:, 0:512],
                xln_ap(k, 128 * t, 128 * (t + 1)),
                wvres[:, 512 * k:512 * (k + 1)],
                start=(k == 0), stop=(k == KD - 1)),
                [chunk_toks_all[k], lws[k]] +
                (bank_last[bank] if k == 0 else []), tag="vmm")
        pr = t % 2
        ev = p.s(lambda e, pr=pr, bank=bank: e.activation(
            prst[pr][:, :], ps[bank][:, 0:512], AF.Copy),
            [mm, prst_last[pr]], tag="vev")
        bank_last[bank] = [ev]
        sts = []
        for j in range(4):
            sts.append(p.dmag(dvo[j, t], prst[pr][:, 128 * j:128 * (j + 1)],
                              [ev], tag="vst"))
        prst_last[pr] = sts
        v_store_toks += sts
        wv_last = mm

    wsl_last = [None, None]
    st_by_head_q = {}
    st_by_head_k = {}
    ctb_last = [None, None]
    stb_last = [None, None]
    rsc_last = [None, None]

    def proj_rope(dram_w, dram_out, nblocks, ntok, d_cos, d_sin, stmap, tag):
        nonlocal wsl_last
        for m in range(nblocks):
            par = m % 2
            lw = p.dma(wsl[par][:, :], dram_w[m], [wsl_last[par]], tag=tag + "w")
            stoks = []
            mm = None
            for qc in range(ntok // 512):
                bank = 5 + (qc % 2)
                for k in range(KD):
                    mm = p.t(lambda e, k=k, qc=qc, par=par, bank=bank: e.matmul(
                        ps[bank][:, 0:512], wsl[par][:, 128 * k:128 * (k + 1)],
                        xln_ap(k, 512 * qc, 512 * (qc + 1)),
                        start=(k == 0), stop=(k == KD - 1)),
                        [chunk_toks_all[k], lw] +
                        (bank_last[bank] if k == 0 else []), tag=tag)
                pr = qc % 2
                lc = p.dma(ctb[pr][:, :], d_cos[:, 512 * qc:512 * (qc + 1)],
                           [ctb_last[pr]], tag=tag + "ct")
                ls = p.dma(stb[pr][:, :], d_sin[:, 512 * qc:512 * (qc + 1)],
                           [stb_last[pr]], tag=tag + "st")
                c1 = p.v(lambda e, pr=pr, bank=bank: e.tensor_copy(
                    rsc[pr][0:64, 0:512], ps[bank][64:128, 0:512]),
                    [mm, rsc_last[pr]], tag=tag + "r1")
                c2 = p.v(lambda e, pr=pr, bank=bank: e.tensor_copy(
                    rsc[pr][64:128, 0:512], ps[bank][0:64, 0:512]),
                    [c1], tag=tag + "r2")
                m2_ = p.v(lambda e, pr=pr: e.tensor_tensor(
                    rsc[pr][:, 0:512], rsc[pr][:, 0:512], stb[pr][:, :],
                    ALU.mult), [c2, ls], tag=tag + "r3")
                m1_ = p.v(lambda e, pr=pr, bank=bank: e.tensor_tensor(
                    ps[bank][:, 0:512], ps[bank][:, 0:512], ctb[pr][:, :],
                    ALU.mult), [m2_, lc, mm], tag=tag + "r4")
                a1 = p.v(lambda e, pr=pr, bank=bank: e.tensor_tensor(
                    prst[pr][:, :], ps[bank][:, 0:512], rsc[pr][:, 0:512],
                    ALU.add), [m1_, prst_last[pr]], tag=tag + "r5")
                st = p.dmag(dram_out[128 * m:128 * (m + 1),
                                     512 * qc:512 * (qc + 1)],
                            prst[pr][:, :], [a1], tag=tag + "sto")
                bank_last[bank] = [a1]
                prst_last[pr] = st
                ctb_last[pr] = m1_
                stb_last[pr] = m2_
                rsc_last[pr] = a1
                stoks.append(st)
            stmap[m] = stoks
            wsl_last[par] = mm

    proj_rope(d_wq, dq, KD, TOWN, d_cq, d_sq, st_by_head_q, "qp")
    proj_rope(d_wk, dko, 4, TOWN, d_cq, d_sq, st_by_head_k, "kp")

    coll_k = p.coll("AllGather", ALU.bypass, PAIRS,
                    [dko_h.ap().opt()], [dkg_h.ap().opt()],
                    waits=[_flat(list(st_by_head_k.values()))], tag="agk")
    coll_v = p.coll("AllGather", ALU.bypass, PAIRS,
                    [dvo_h.ap().opt()], [dvg_h.ap().opt()],
                    waits=[v_store_toks], tag="agv")

    all_proj = _flat(list(st_by_head_q.values()) + list(st_by_head_k.values())) \
        + v_store_toks
    ar_last = {"k": [None, None], "v": [None, None], "q": [None, None]}
    pt_last = [None, None, None, None]
    att_last = {}
    dinv_last = None
    drow_last = None
    drow_bf_last = None
    den_pend = None
    first_waits = all_proj

    for g in range(4):
        gp = g % 2
        lkA = p.dma(kth(gp, 0, 2048), dkg[0, 128 * g:128 * (g + 1), :],
                    [coll_k, ar_last["k"][gp],
                     first_waits if g < 2 else None], tag="kld")
        lkB = p.dma(kth(gp, 2048, 4096), dkg[1, 128 * g:128 * (g + 1), :],
                    [coll_k], tag="kld")
        lk = [lkA, lkB]
        lvs = []
        for t in range(32):
            lvs.append(p.dma(vth(gp, 128 * t, 128 * (t + 1)),
                             dvg[t // 16, g, t % 16],
                             [coll_v, ar_last["v"][gp],
                              first_waits if g < 2 else None], tag="vld"))
        kreaders = []
        vreaders = []
        for hh in range(4):
            h = 4 * g + hh
            hp = h % 2
            lq = p.dma(qth(hp, 0, 2048), dq[128 * h:128 * (h + 1), :],
                       [st_by_head_q[h], ar_last["q"][hp],
                        first_waits if h < 2 else None], tag="qld")
            qreaders = []
            for qc in range(4):
                pvbank = 2 + (qc % 2)
                dbank = 4 + (qc % 2)
                dmm = pvm = None
                pend = None

                def make_den(h, qc, dmm, pvm, pvbank, dbank):
                    def go():
                        nonlocal drow_last, drow_bf_last, dinv_last
                        rcp = p.v(lambda e, dbank=dbank: e.reciprocal(
                            drow[0:1, :], ps[dbank][0:1, 0:512]),
                            [dmm, drow_last], tag="rcp")
                        rcb = p.v(lambda e: e.tensor_copy(drow_bf[0:1, :],
                                                          drow[0:1, :]),
                                  [rcp, drow_bf_last], tag="rcb")
                        bank_last[dbank] = [rcp]
                        bcm = p.t(lambda e: e.matmul(ps[6][:, 0:512],
                                                     ones_rowb[0:1, :],
                                                     drow_bf[0:1, :],
                                                     start=True, stop=True),
                                  [rcb, ms1] + bank_last[6], tag="dbc")
                        dcp = p.v(lambda e: e.tensor_copy(dinv_b,
                                                          ps[6][:, 0:512]),
                                  [bcm, dinv_last], tag="dcp")
                        bank_last[6] = [dcp]
                        drow_last = rcb
                        drow_bf_last = bcm
                        amul = p.v(lambda e, h=h, qc=qc, pvbank=pvbank:
                                   e.tensor_tensor(
                            att_ap(h, qc), ps[pvbank][:, 0:512], dinv_b,
                            ALU.mult),
                            [pvm, dcp, att_last.get((h, qc))], tag="attdiv")
                        dinv_last = amul
                        bank_last[pvbank] = [amul]
                        att_last[(h, qc)] = amul
                    return go

                def mk_pend(t, pi, ex, last):
                    def go():
                        dm = p.t(lambda e, pi=pi, dbank=dbank, first=(t == 0),
                                 last=last: e.matmul(
                            ps[dbank][0:1, 0:512], ones_colb[:, :],
                            pt_ap(pi, 0, 512), start=first, stop=last),
                            [ex, ms2] + (bank_last[dbank] if t == 0 else []),
                            tag="dmm")
                        pv = p.t(lambda e, pi=pi, t=t, gp=gp, pvbank=pvbank,
                                 first=(t == 0), last=last: e.matmul(
                            ps[pvbank][:, 0:512],
                            vth(gp, 128 * t, 128 * (t + 1)),
                            pt_ap(pi, 0, 512), start=first, stop=last),
                            [ex, lvs[t]] +
                            (bank_last[pvbank] if t == 0 else []),
                            tag="pvm")
                        vreaders.append(pv)
                        pt_last[pi] = [dm, pv]
                        return dm, pv
                    return go

                for t in range(32):
                    sbank = t % 2
                    smm = p.t(lambda e, t=t, gp=gp, hp=hp, qc=qc, sbank=sbank:
                              e.matmul(ps[sbank][:, 0:512],
                                       kth(gp, 128 * t, 128 * (t + 1)),
                                       qth(hp, 512 * qc, 512 * (qc + 1)),
                                       start=True, stop=True),
                              [lk, lq] + bank_last[sbank], tag="smm")
                    kreaders.append(smm)
                    qreaders.append(smm)
                    pi = t % 4
                    ex = p.s(lambda e, pi=pi, sbank=sbank: e.activation(
                        pt_ap(pi, 0, 512), ps[sbank][:, 0:512], AF.Exp,
                        scale=SCALE), [smm, pt_last[pi]], tag="exp")
                    bank_last[sbank] = [ex]
                    if t == 4 and den_pend is not None:
                        den_pend()
                        den_pend = None
                    if pend is not None:
                        dmm, pvm = pend()
                    pend = mk_pend(t, pi, ex, t == 31)
                dmm, pvm = pend()
                den_pend = make_den(h, qc, dmm, pvm, pvbank, dbank)
            ar_last["q"][hp] = qreaders[-1]
        ar_last["k"][gp] = kreaders[-1]
        ar_last["v"][gp] = vreaders
    if den_pend is not None:
        den_pend()
        den_pend = None

    xres_last = [None, None]
    for m in range(KD):
        par = m % 2
        lw = p.dma(wsl[par][:, :], d_wo[m], [wsl_last[par]], tag="wow")
        mm = None
        for qc in range(4):
            bank = 5 + (qc % 2)
            for k in range(KD):
                mm = p.t(lambda e, k=k, qc=qc, par=par, bank=bank: e.matmul(
                    ps[bank][:, 0:512], wsl[par][:, 128 * k:128 * (k + 1)],
                    att_ap(k, qc), start=(k == 0), stop=(k == KD - 1)),
                    [att_last[(k, qc)], lw] +
                    (bank_last[bank] if k == 0 else []), tag="omm")
            pr = qc % 2
            lx = p.dma(xres[pr][:, :], xi[128 * m:128 * (m + 1),
                                          512 * qc:512 * (qc + 1)],
                       [xres_last[pr]], tag="oxld")
            yv = p.v(lambda e, pr=pr, bank=bank: e.tensor_tensor(
                xres[pr][:, :], xres[pr][:, :], ps[bank][:, 0:512], ALU.add),
                [lx, mm], tag="oadd")
            st = p.dmag(xo[128 * m:128 * (m + 1), 512 * qc:512 * (qc + 1)],
                        xres[pr][:, :], [yv], tag="oxst")
            bank_last[bank] = [yv]
            xres_last[pr] = st
        wsl_last[par] = mm
    return


def p6f_inputs(inp, h):
    own = cols_of(h)
    invf = 1.0 / (10000.0 ** (np.arange(0, 128, 2) / 128.0))
    pos = np.arange(S)
    fr = pos[None, :] * np.concatenate([invf, invf])[:, None]
    cos = np.cos(fr).astype(np.float32)
    sin = np.sin(fr).astype(np.float32)
    sin_signed = sin.copy()
    sin_signed[0:64] *= -1.0
    return {
        "wq": tile_lhsT(np.asarray(inp["L1_attn_b_wq"], np.float32).T),
        "wk": tile_lhsT(np.asarray(inp["L1_attn_b_wk"], np.float32).T),
        "wv": to_bf(np.ascontiguousarray(
            np.asarray(inp["L1_attn_b_wv"], np.float32).T).reshape(KD, 128, 512)),
        "wo": tile_lhsT(np.asarray(inp["L1_attn_b_wo"], np.float32).T),
        "cosq": to_bf(cos[:, own]),
        "sinq": to_bf(sin_signed[:, own]),
    }
